# version 07
# speedup vs baseline: 1.2443x; 1.1564x over previous
import numpy as np
from contextlib import ExitStack
import concourse.bass as bass
import concourse.mybir as mybir
from concourse.bass_utils import run_bass_kernel_spmd
from concourse.bass_core_rust import IndirectOffsetOnAxis

F32 = mybir.dt.float32
BF16 = mybir.dt.bfloat16
I32 = mybir.dt.int32
U32 = mybir.dt.uint32
AF = mybir.ActivationFunctionType
ALU = mybir.AluOpType
AX = mybir.AxisListType

T = 2048
D = 1024
NT = T // 128
EPS_LN = 1e-5
ENGINES = ("sync", "tensor", "vector", "scalar", "gpsimd")


class Sem:
    __slots__ = ("h", "n", "name")
    def __init__(self, ctx, nc, name):
        self.h = ctx.enter_context(nc.semaphore(name))
        self.n = 0
        self.name = name


class Prog:
    def __init__(self):
        self.nc = bass.Bass("TRN2", target_bir_lowering=False, num_devices=8)
        self.ctx = ExitStack()
        self.uid = 0

    def sem(self, name):
        return Sem(self.ctx, self.nc, self.name(name))

    def get_sem(self, name):
        if not hasattr(self, "_sems"):
            self._sems = {}
        if name not in self._sems:
            self._sems[name] = Sem(self.ctx, self.nc, name)
        return self._sems[name]

    def sb(self, name, shape, dtype):
        return self.ctx.enter_context(self.nc.sbuf_tensor(self.name(name), shape, dtype))[:]

    def ps(self, name, shape, dtype):
        return self.ctx.enter_context(self.nc.psum_tensor(self.name(name), shape, dtype))[:]

    def name(self, base):
        self.uid += 1
        return f"{base}_{self.uid}"


class Sched:
    def __init__(self, p: Prog):
        self.p = p
        self.items = {e: [] for e in ENGINES}
        if not hasattr(p, "_waited"):
            p._waited = {e: {} for e in ENGINES}

    def emit(self, eng, fn, waits=(), incs=()):
        for s, k in incs:
            s.n += k
        seen = self.p._waited[eng]
        best = {}
        for s, v in waits:
            if v > 0 and v > best.get(s.h, -1):
                best[s.h] = v
        ws = []
        for h, v in best.items():
            if v > seen.get(h, 0):
                ws.append((h, v))
                seen[h] = v
        self.items[eng].append((fn, ws, [(s.h, k) for s, k in incs]))

    def flush(self, name=None):
        p = self.p
        if not any(self.items.values()):
            return
        with p.nc.Block(name) as blk:
            for eng in ENGINES:
                items = self.items[eng]
                if not items:
                    continue
                def body(e, items=items):
                    for fn, waits, incs in items:
                        for h, v in waits:
                            e.wait_ge(h, v)
                        inst = fn(e)
                        for h, k in incs:
                            if inst is None:
                                inst = e.nop()
                            inst.then_inc(h, k)
                getattr(blk, eng)(body)
        self.items = {e: [] for e in ENGINES}


def w(s: Sem, v=None):
    return (s, s.n if v is None else v)


def _emit_ln_inner(sch, x_ap, out_ap, stat, scr, s_stats, done=None,
            gamma=None, beta=None, scalar_waits=(), vector_waits=(), eps_ap=None):
    sum_ = stat[:, 0:1]; negmean = stat[:, 1:2]; ssq = stat[:, 2:3]; rstd = stat[:, 3:4]

    def sc(e):
        e.activation(out=scr, in_=x_ap, func=AF.Identity, accum_out=sum_)
        e.drain()
        e.activation(out=negmean, in_=sum_, func=AF.Identity, scale=sch.p.cv(2))
        e.drain()
        e.activation(out=scr, in_=x_ap, func=AF.Square, bias=negmean, accum_out=ssq)
        e.drain()
        return e.activation(out=rstd, in_=ssq, func=AF.Sqrt, scale=sch.p.cv(3), bias=eps_ap)
    sch.emit("scalar", sc, waits=list(scalar_waits), incs=[(s_stats, 1)])

    def ve(e):
        e.reciprocal(out=rstd, in_=rstd)
        e.drain()
        if gamma is None and beta is None:
            return e.tensor_scalar(out=out_ap, in0=x_ap, scalar1=negmean, scalar2=rstd,
                                   op0=ALU.add, op1=ALU.mult)
        e.tensor_scalar(out=scr, in0=x_ap, scalar1=negmean, scalar2=rstd,
                        op0=ALU.add, op1=ALU.mult)
        if gamma is not None:
            e.tensor_tensor(out=(scr if beta is not None else out_ap), in0=scr, in1=gamma, op=ALU.mult)
        if beta is not None:
            return e.tensor_tensor(out=out_ap, in0=scr, in1=beta, op=ALU.add)
    incs = [(done, 1)] if done is not None else []
    sch.emit("vector", ve, waits=[w(s_stats)] + list(vector_waits), incs=incs)


def emit_ln(sch, x_ap, out_ap, stat, scr, s_stats, done=None,
            gamma=None, beta=None, scalar_waits=(), vector_waits=()):
    _emit_ln_inner(sch, x_ap, out_ap, stat, scr, s_stats, done=done, gamma=gamma,
                   beta=beta, scalar_waits=scalar_waits, vector_waits=vector_waits,
                   eps_ap=sch.p.epsap)


def emit_bcast_load(sch, dram_row, sbuf_tile, s_ld, eng="sync", waits=()):
    def f(e, dram_row=dram_row, sbuf_tile=sbuf_tile):
        return e.dma_start(out=sbuf_tile, in_=dram_row.to_broadcast([sbuf_tile.shape[0]] + list(dram_row.shape[1:])))
    sch.emit(eng, f, waits=waits, incs=[(s_ld, 16)])


def emit_l1_ffn(p, x_src, x_dst, weights, consts):
    ident_bf = consts["ident_bf"]
    nc = p.nc
    TH = 1024
    NTH = TH // 128
    sch = Sched(p)
    ctx = ExitStack()
    sb = lambda n, s, d: ctx.enter_context(nc.sbuf_tensor(p.name(n), s, d))[:]
    ps = lambda n, s, d: ctx.enter_context(nc.psum_tensor(p.name(n), s, d))[:]

    g_bc = sb("g_bc", [128, D], F32)
    b_bc = sb("b_bc", [128, D], F32)
    b2_bc = sb("b2_bc", [128, D], F32)
    b1T = sb("b1T", [128, 32], F32)
    hT = sb("hT", [128, 8 * TH], BF16)
    z_acc = sb("z_acc", [128, NTH * D], F32)
    scr = sb("scr", [128, D], F32)
    scr2 = sb("scr2", [128, D], F32)
    xtmp = sb("xtmp", [128, D], F32)
    stat = sb("stat", [128, 8], F32)
    xn_bf = sb("xn_bf", [128, D], BF16)
    w1q = [sb("w1q", [128, 512], I32) for _ in range(2)]
    w1f = sb("w1f", [128, 512], F32)
    w1s = [sb("w1s", [128, 512], F32) for _ in range(2)]
    w1c = sb("w1c", [128, 8 * 512], BF16)
    w2q = [sb("w2q", [128, 1024], I32) for _ in range(2)]
    w2f = sb("w2f", [128, 1024], F32)
    w2s = [sb("w2s", [128, 1024], F32) for _ in range(2)]
    w2c = sb("w2c", [128, 4 * 1024], BF16)
    y2 = sb("y2", [128, 8 * 512], BF16)
    pt0 = ps("pt0", [128, 128], BF16)
    pt1 = ps("pt1", [128, 128], BF16)
    py = [ps("py", [128, 512], F32) for _ in range(2)]
    pz = [ps("pz", [128, 512], F32) for _ in range(2)]

    x_res = sb("x_res", [128, NT * D], F32)
    s_io = p.get_sem("io")
    s_ld = p.sem("ld")
    s_stats = p.get_sem("st")
    s_xn = p.get_sem("xn")
    s_tr = p.get_sem("tr")
    s_evac = p.get_sem("ev")
    s_w1 = p.get_sem("w1")
    s_w2 = p.get_sem("w2")
    s_wq = [p.sem("wqa"), p.sem("wqb")]
    s_ws = [p.sem("wsa"), p.sem("wsb")]
    s_wq2 = [p.sem("wq2a"), p.sem("wq2b")]
    s_ws2 = [p.sem("ws2a"), p.sem("ws2b")]
    s_dq1 = p.get_sem("dq1")
    s_dq2 = p.get_sem("dq2")
    s_mmy = p.get_sem("mmy")
    s_silu = p.get_sem("silu")
    s_mmz = p.get_sem("mmz")
    s_zacc = p.get_sem("za")
    s_ep = p.get_sem("ep")

    def ldx(e):
        return e.dma_start(out=x_res.rearrange("p (n d) -> p n d", d=D),
                           in_=x_src.rearrange("(n p) d -> p n d", p=128))
    sch.emit("sync", ldx, waits=[w(s_io)], incs=[(s_io, 16)])
    emit_bcast_load(sch, weights["ln_gamma"], g_bc, s_ld)
    emit_bcast_load(sch, weights["ln_beta"], b_bc, s_ld)
    emit_bcast_load(sch, weights["fc2_bias"], b2_bc, s_ld)
    sch.emit("sync", lambda e: e.dma_start(out=b1T, in_=weights["fc1_biasT"]),
             incs=[(s_ld, 16)])
    ld_done = w(s_ld)
    first_scalar_wait = [ld_done, w(s_io)]

    for half in range(2):
        for ii in range(NTH):
            i = half * NTH + ii
            x_t = x_res[:, i * D:(i + 1) * D]
            z_t = z_acc[:, ii * D:(ii + 1) * D]
            prev_half_fence = [w(s_ep)] if (half > 0 and ii == 0) else []
            emit_ln(sch, x_t, xtmp, stat, scr, s_stats, done=s_xn,
                    scalar_waits=(first_scalar_wait if i == 0 else [w(s_xn)]))
            first_scalar_wait = None
            emit_ln(sch, xtmp, xn_bf, stat, scr, s_stats, done=s_xn,
                    gamma=g_bc, beta=b_bc,
                    scalar_waits=[w(s_xn)], vector_waits=[w(s_tr)])
            def zseed(e, z_t=z_t, xtmp=xtmp):
                return e.scalar_tensor_tensor(out=z_t, in0=xtmp, scalar=2.0,
                                              in1=b2_bc, op0=ALU.mult, op1=ALU.add)
            sch.emit("vector", zseed, waits=prev_half_fence, incs=[(s_zacc, 1)])
            for k in range(8):
                pbank = pt0 if (k % 2 == 0) else pt1
                evac_snapshot = s_evac.n
                waits = [w(s_xn)] if k == 0 else []
                if evac_snapshot >= 2:
                    waits.append((s_evac, evac_snapshot - 1))
                def tr(e, k=k, pbank=pbank):
                    return e.transpose(out=pbank, in_=xn_bf[:, k * 128:(k + 1) * 128], identity=ident_bf)
                sch.emit("tensor", tr, waits=waits, incs=[(s_tr, 1)])
                dst = hT[:, k * TH + ii * 128: k * TH + (ii + 1) * 128]
                def ev(e, dst=dst, pbank=pbank):
                    return e.tensor_copy(out=dst, in_=pbank)
                sch.emit("vector", ev, waits=[w(s_tr)], incs=[(s_evac, 1)])
        sch.flush(f"l1ffn_prep{half}")
        hT_ready = w(s_evac)
        zseed_done = w(s_zacc)

        for n in range(8):
            mmy_fence = w(s_mmy)
            mmz_fence = w(s_mmz)
            for k in range(8):
                qb = w1q[k % 2]; sbuf_s = w1s[k % 2]
                kk = (half * 8 + n) * 8 + k
                def ldq(e, n=n, k=k, qb=qb):
                    return e.dma_start(out=qb, in_=weights["fc1_q"][k * 128:(k + 1) * 128, n * 512:(n + 1) * 512])
                sch.emit("sync", ldq, waits=[(s_dq1, s_dq1.n - 1)] if kk >= 2 else [], incs=[(s_wq[k % 2], 16)])
                def lds(e, n=n, k=k, sbuf_s=sbuf_s):
                    return e.dma_start(out=sbuf_s, in_=weights["fc1_scales"][k:k + 1, n * 512:(n + 1) * 512].to_broadcast([128, 512]))
                sch.emit("sync", lds, waits=[(s_w1, s_w1.n - 1)] if kk >= 2 else [], incs=[(s_ws[k % 2], 16)])
                def deq_s(e, qb=qb):
                    return e.activation(out=w1f, in_=qb, func=AF.Identity, bias=p.cv(1))
                sch.emit("scalar", deq_s,
                         waits=[w(s_wq[k % 2])] + ([w(s_w1)] if kk > 0 else []),
                         incs=[(s_dq1, 1)])
                def deq_v(e, k=k, sbuf_s=sbuf_s):
                    return e.tensor_tensor(out=w1c[:, k * 512:(k + 1) * 512], in0=w1f, in1=sbuf_s, op=ALU.mult)
                vw = [w(s_dq1), w(s_ws[k % 2])]
                if k == 0 and (n > 0 or half > 0):
                    vw.append(mmy_fence)
                sch.emit("vector", deq_v, waits=vw, incs=[(s_w1, 1)])
            w1_ready = w(s_w1)

            for j in range(4):
                for g in range(2):
                    idx = (half * 8 + n) * 8 + j * 2 + g
                    pyb = py[idx % 2]
                    waits = [w1_ready, hT_ready] if (j == 0 and g == 0) else []
                    if idx >= 2:
                        waits.append((s_silu, s_silu.n - 1))
                    def mm(e, j=j, g=g, pyb=pyb):
                        inst = None
                        for k in range(8):
                            inst = e.matmul(out=pyb,
                                            lhsT=w1c[:, k * 512 + j * 128: k * 512 + (j + 1) * 128],
                                            rhs=hT[:, k * TH + g * 512: k * TH + (g + 1) * 512],
                                            start=(k == 0), stop=(k == 7))
                        return inst
                    sch.emit("tensor", mm, waits=waits, incs=[(s_mmy, 1)])
                    ntile_abs = n * 4 + j
                    def silu(e, j=j, g=g, pyb=pyb, ntile_abs=ntile_abs):
                        return e.activation(out=y2[:, (j * 2 + g) * 512:(j * 2 + g + 1) * 512],
                                            in_=pyb, func=AF.Silu,
                                            bias=b1T[:, ntile_abs:ntile_abs + 1])
                    sw = [w(s_mmy)]
                    if j == 0 and g == 0 and (n > 0 or half > 0):
                        sw.append(mmz_fence)
                    sch.emit("scalar", silu, waits=sw, incs=[(s_silu, 1)])

            for jj in range(4):
                qb = w2q[jj % 2]; sbuf_s = w2s[jj % 2]
                gidx = n * 4 + jj
                ggidx = half * 32 + gidx
                def ldq2(e, gidx=gidx, qb=qb):
                    return e.dma_start(out=qb, in_=weights["fc2_q"][gidx * 128:(gidx + 1) * 128, :])
                sch.emit("sync", ldq2, waits=[(s_dq2, s_dq2.n - 1)] if ggidx >= 2 else [], incs=[(s_wq2[jj % 2], 16)])
                def lds2(e, gidx=gidx, sbuf_s=sbuf_s):
                    return e.dma_start(out=sbuf_s, in_=weights["fc2_scales"][gidx:gidx + 1, :].to_broadcast([128, 1024]))
                sch.emit("sync", lds2, waits=[(s_w2, s_w2.n - 1)] if ggidx >= 2 else [], incs=[(s_ws2[jj % 2], 16)])
                def deq2_s(e, qb=qb):
                    return e.activation(out=w2f, in_=qb, func=AF.Identity, bias=p.cv(1))
                sch.emit("scalar", deq2_s,
                         waits=[w(s_wq2[jj % 2])] + ([w(s_w2)] if ggidx > 0 else []),
                         incs=[(s_dq2, 1)])
                def deq2_v(e, jj=jj, sbuf_s=sbuf_s):
                    return e.tensor_tensor(out=w2c[:, jj * 1024:(jj + 1) * 1024], in0=w2f, in1=sbuf_s, op=ALU.mult)
                vw = [w(s_dq2), w(s_ws2[jj % 2])]
                if jj == 0 and (n > 0 or half > 0):
                    vw.append(mmz_fence)
                sch.emit("vector", deq2_v, waits=vw, incs=[(s_w2, 1)])
            w2_ready = w(s_w2)

            silu_all = w(s_silu)
            for g in range(2):
                for mt in range(4):
                    for hc in range(2):
                        idx = (half * 8 + n) * 16 + g * 8 + mt * 2 + hc
                        pzb = pz[idx % 2]
                        waits = [silu_all, w2_ready] if (g == 0 and mt == 0 and hc == 0) else []
                        if idx >= 2:
                            waits.append((s_zacc, s_zacc.n - 1))
                        if n == 0 and g == 0 and mt == 0 and hc == 0:
                            waits.append(zseed_done)
                        def mm2(e, g=g, mt=mt, hc=hc, pzb=pzb):
                            inst = None
                            for j in range(4):
                                inst = e.matmul(out=pzb,
                                                lhsT=y2[:, (j * 2 + g) * 512 + mt * 128:(j * 2 + g) * 512 + (mt + 1) * 128],
                                                rhs=w2c[:, j * 1024 + hc * 512: j * 1024 + (hc + 1) * 512],
                                                start=(j == 0), stop=(j == 3))
                            return inst
                        sch.emit("tensor", mm2, waits=waits, incs=[(s_mmz, 1)])
                        mtile = g * 4 + mt
                        dst = z_acc[:, mtile * D + hc * 512: mtile * D + (hc + 1) * 512]
                        def zadd(e, dst=dst, pzb=pzb):
                            return e.tensor_tensor(out=dst, in0=dst, in1=pzb, op=ALU.add)
                        sch.emit("vector", zadd, waits=[w(s_mmz)], incs=[(s_zacc, 1)])
            sch.flush(f"l1ffn_h{half}n{n}")

        z_done = w(s_zacc)
        for ii in range(NTH):
            i = half * NTH + ii
            x_t = x_res[:, i * D:(i + 1) * D]
            z_t = z_acc[:, ii * D:(ii + 1) * D]
            def ep1(e, z_t=z_t):
                return e.tensor_scalar(out=scr2, in0=z_t, scalar1=0.5, scalar2=None, op0=ALU.mult)
            sch.emit("vector", ep1, waits=[z_done] if ii == 0 else [], incs=[(s_ep, 1)])
            emit_ln(sch, scr2, scr2, stat, scr, s_stats, done=s_ep,
                    scalar_waits=[w(s_ep)])
            def ep2(e, x_t=x_t):
                return e.tensor_tensor(out=x_t, in0=x_t, in1=scr2, op=ALU.add)
            sch.emit("vector", ep2, incs=[(s_ep, 1)])
        sch.flush(f"l1ffn_ep{half}")
    def stx(e):
        return e.dma_start(out=x_dst.rearrange("(n p) d -> p n d", p=128),
                           in_=x_res.rearrange("p (n d) -> p n d", d=D))
    sch.emit("sync", stx, waits=[w(s_ep)], incs=[(s_io, 16)])
    sch.emit("sync", lambda e: e.nop(), waits=[w(s_io)])
    sch.flush("l1ffn_store")
    ctx.close()
    return s_io



def emit_l1_attn(p, x_src, x_dst, weights, consts):
    nc = p.nc
    ident_bf = consts["ident_bf"]
    NH = 16
    sch = Sched(p)
    ctx = ExitStack()
    sb = lambda n, s, d: ctx.enter_context(nc.sbuf_tensor(p.name(n), s, d))[:]
    ps = lambda n, s, d: ctx.enter_context(nc.psum_tensor(p.name(n), s, d))[:]

    qT = sb("qT", [128, 8 * T], BF16)
    kT = sb("kT", [128, 8 * T], BF16)
    v_ = sb("v", [128, NT * D], BF16)
    oT = sb("oT", [128, 8 * T], BF16)
    pen9 = sb("pen9", [128, 128], F32)
    bqT = sb("bqT", [128, 8], F32)
    bo_bc = sb("bo_bc", [128, D], F32)
    stat = sb("stat", [128, 8], F32)
    scr = sb("scr", [128, D], F32)
    scr2 = sb("scr2", [128, D], F32)
    lrow = sb("lrow", [128, 4], F32)

    s_io = p.get_sem("io")
    s_ld = p.sem("ld")
    s_stats = p.get_sem("st")
    s_xn = p.get_sem("xn")
    s_tr = p.get_sem("tr")
    s_evac = p.get_sem("ev")
    s_wld = p.sem("wld")
    s_xt = [p.sem("xta"), p.sem("xtb")]
    s_st = [p.sem("sta"), p.sem("stb")]
    s_proj = p.get_sem("mmy")
    s_pevac = p.get_sem("silu")
    s_ab = p.get_sem("ab")
    s_smm = p.get_sem("mmz")
    s_sevac = p.get_sem("za")
    s_devac = p.get_sem("zad")
    s_max = p.get_sem("w1")
    s_exp = p.get_sem("dq1")
    s_ptr = p.get_sem("tr2")
    s_ptev = p.get_sem("ev2")
    s_pv = p.get_sem("w2")
    s_o = p.get_sem("dq2")
    s_oT = p.get_sem("tr3")
    s_zmm = p.get_sem("zmm")
    s_ep = p.get_sem("ep")

    with nc.sbuf_tensor(p.name("xnT"), [128, 8 * T], BF16) as xnT_t, \
         nc.sbuf_tensor(p.name("xt"), [128, 2 * D], F32) as xt_t, \
         nc.sbuf_tensor(p.name("wbf"), [128, 8 * D], BF16) as wbf_t, \
         nc.sbuf_tensor(p.name("xnbf"), [128, D], BF16) as xnbf_t, \
         nc.psum_tensor(p.name("pt0"), [128, 128], BF16) as pt0_t, \
         nc.psum_tensor(p.name("pt1"), [128, 128], BF16) as pt1_t, \
         nc.psum_tensor(p.name("pp0"), [128, 512], F32) as pp0_t, \
         nc.psum_tensor(p.name("pp1"), [128, 512], F32) as pp1_t:
        xnT = xnT_t[:]; wbf = wbf_t[:]; xnbf = xnbf_t[:]
        pt = (pt0_t[:], pt1_t[:])
        pp = (pp0_t[:], pp1_t[:])
        xt = xt_t[:]

        sch.emit("sync", lambda e: e.dma_start(out=pen9, in_=weights["pen9"]), incs=[(s_ld, 16)])
        sch.emit("sync", lambda e: e.dma_start(out=bqT, in_=weights["bqT"]), incs=[(s_ld, 16)])
        emit_bcast_load(sch, weights["bo"], bo_bc, s_ld)
        consts_ready = w(s_ld)
        sch.emit("scalar", lambda e: e.nop(), waits=[consts_ready])
        sch.emit("vector", lambda e: e.nop(), waits=[consts_ready])

        for i in range(NT):
            xti = xt[:, (i % 2) * D:(i % 2 + 1) * D]
            ldw = []
            if i >= 2:
                ldw = [(s_xn, s_xn.n - 1)]
            def ldx(e, xti=xti, i=i):
                return e.dma_start(out=xti, in_=x_src[i * 128:(i + 1) * 128, :])
            sl = s_xt[i % 2]
            sch.emit("sync", ldx, waits=([w(s_io)] if i == 0 else ldw), incs=[(sl, 16)])
            emit_ln(sch, xti, xnbf, stat, scr, s_stats, done=s_xn,
                    scalar_waits=[w(sl), w(s_xn)], vector_waits=[w(s_tr)])
            for k in range(8):
                pbank = pt[k % 2]
                evac_snapshot = s_evac.n
                waits = [w(s_xn)] if k == 0 else []
                if evac_snapshot >= 2:
                    waits.append((s_evac, evac_snapshot - 1))
                def tr(e, k=k, pbank=pbank):
                    return e.transpose(out=pbank, in_=xnbf[:, k * 128:(k + 1) * 128], identity=ident_bf)
                sch.emit("tensor", tr, waits=waits, incs=[(s_tr, 1)])
                dst = xnT[:, k * T + i * 128: k * T + (i + 1) * 128]
                def ev(e, dst=dst, pbank=pbank):
                    return e.tensor_copy(out=dst, in_=pbank)
                sch.emit("vector", ev, waits=[w(s_tr)], incs=[(s_evac, 1)])
        sch.flush("l1a_ln")
        xnT_ready = w(s_evac)

        for pi, (wname, outbuf, kind) in enumerate([
                ("Wq", qT, "q"), ("Wk", kT, "k"), ("Wv", v_, "v")]):
            wdap = weights[wname]
            wfence = w(s_proj)
            for kc in range(8):
                def ldw_(e, wdap=wdap, kc=kc):
                    return e.dma_start(out=wbf[:, kc * D:(kc + 1) * D],
                                       in_=wdap[kc * 128:(kc + 1) * 128, :])
                sch.emit("gpsimd", ldw_, waits=[wfence] if kc == 0 else [], incs=[(s_wld, 16)])
            wld_done = w(s_wld)
            if kind in ("q", "k"):
                for c in range(8):
                    for tc in range(4):
                        idx = pi * 32 + c * 4 + tc
                        pb = pp[idx % 2]
                        waits = [wld_done, xnT_ready] if (c == 0 and tc == 0) else []
                        if True:
                            waits = waits + [(s_pevac, s_pevac.n - 1)] if s_pevac.n >= 2 else waits
                        def mm(e, c=c, tc=tc, pb=pb):
                            inst = None
                            for kc in range(8):
                                inst = e.matmul(out=pb,
                                                lhsT=wbf[:, kc * D + c * 128: kc * D + (c + 1) * 128],
                                                rhs=xnT[:, kc * T + tc * 512: kc * T + (tc + 1) * 512],
                                                start=(kc == 0), stop=(kc == 7))
                            return inst
                        sch.emit("tensor", mm, waits=waits, incs=[(s_proj, 1)])
                        dst = outbuf[:, c * T + tc * 512: c * T + (tc + 1) * 512]
                        if kind == "q":
                            def evq(e, dst=dst, pb=pb, c=c):
                                return e.activation(out=dst, in_=pb, func=AF.Identity,
                                                    bias=bqT[:, c:c + 1])
                            sch.emit("scalar", evq, waits=[w(s_proj)], incs=[(s_pevac, 1)])
                        else:
                            def evk(e, dst=dst, pb=pb):
                                return e.activation(out=dst, in_=pb, func=AF.Identity)
                            sch.emit("scalar", evk, waits=[w(s_proj)], incs=[(s_pevac, 1)])
            else:
                for tt in range(16):
                    for hc in range(2):
                        idx = c_idx = tt * 2 + hc
                        pb = pp[idx % 2]
                        waits = [wld_done] if idx == 0 else []
                        if s_pevac.n >= 2:
                            waits = waits + [(s_pevac, s_pevac.n - 1)]
                        def mmv(e, tt=tt, hc=hc, pb=pb):
                            inst = None
                            for kc in range(8):
                                inst = e.matmul(out=pb,
                                                lhsT=xnT[:, kc * T + tt * 128: kc * T + (tt + 1) * 128],
                                                rhs=wbf[:, kc * D + hc * 512: kc * D + (hc + 1) * 512],
                                                start=(kc == 0), stop=(kc == 7))
                            return inst
                        sch.emit("tensor", mmv, waits=waits, incs=[(s_proj, 1)])
                        dst = v_[:, tt * D + hc * 512: tt * D + (hc + 1) * 512]
                        def evv(e, dst=dst, pb=pb):
                            return e.activation(out=dst, in_=pb, func=AF.Identity)
                        sch.emit("scalar", evv, waits=[w(s_proj)], incs=[(s_pevac, 1)])
        sch.flush("l1a_proj")
    proj_done = w(s_pevac)

    with nc.sbuf_tensor(p.name("S_row"), [128, 2 * T], F32) as S_t, \
         nc.sbuf_tensor(p.name("P_row"), [128, 2 * T], BF16) as P_t, \
         nc.sbuf_tensor(p.name("PT"), [128, 2 * 512], BF16) as PT_t, \
         nc.sbuf_tensor(p.name("alibi"), [128, T], F32) as arow_t, \
         nc.sbuf_tensor(p.name("obf"), [128, 2 * 64], BF16) as obf_t, \
         nc.psum_tensor(p.name("ps0"), [128, 512], F32) as ps0_t, \
         nc.psum_tensor(p.name("ps1"), [128, 512], F32) as ps1_t, \
         nc.psum_tensor(p.name("ppv"), [128, 64], F32) as ppv_t, \
         nc.psum_tensor(p.name("ptp0"), [128, 512], BF16) as ptp0_t, \
         nc.psum_tensor(p.name("ptp1"), [128, 512], BF16) as ptp1_t, \
         nc.psum_tensor(p.name("pot"), [64, 128], BF16) as pot_t:
        S_row2 = S_t[:]; P_row2 = P_t[:]; PT2 = PT_t[:]
        alibi_h = arow_t[:]; obf2 = obf_t[:]
        psb = (ps0_t[:], ps1_t[:])
        ppv = ppv_t[:]
        ptp = (ptp0_t[:], ptp1_t[:])
        pot = pot_t[:]

        s_abd = p.sem("abd")
        PTW = 512

        def stageA(h, q):
            par = (h * NT + q) % 2
            S_row = S_row2[:, par * T:par * T + T]
            P_row = P_row2[:, par * T:par * T + T]
            l_ = lrow[:, 2 * par:2 * par + 1]
            negm = lrow[:, 2 * par + 1:2 * par + 2]
            klen = (q + 1) * 128
            rem = klen % 512
            if rem == 0:
                rem = 512
            nfull = (klen - rem) // 512
            widths = [512] * nfull + [rem]
            strip = h // 2
            pbase = (h % 2) * 64
            qsl = qT[pbase:pbase + 64, strip * T + q * 128: strip * T + (q + 1) * 128]
            for bi, bw in enumerate(widths):
                boff = bi * 512
                pb = psb[(s_smm.n) % 2]
                mmw = []
                if h == 0 and q == 0 and bi == 0:
                    mmw = [proj_done, w(s_ld)]
                if bi == 0 and q == 0:
                    mmw = mmw + [(s_abd, 16 * (h + 1))]
                if s_smm.n >= 2:
                    mmw = mmw + [(s_sevac, s_sevac.n - 1)]
                def smm(e, qsl=qsl, pb=pb, strip=strip, pbase=pbase, boff=boff, bw=bw):
                    return e.matmul(out=pb[:, 0:bw], lhsT=qsl,
                                    rhs=kT[pbase:pbase + 64, strip * T + boff: strip * T + boff + bw],
                                    start=True, stop=True)
                sch.emit("tensor", smm, waits=mmw, incs=[(s_smm, 1)])
                dst = S_row[:, boff:boff + bw]
                last = bi == len(widths) - 1
                def sev(e, dst=dst, pb=pb, bw=bw, boff=boff, last=last, klen=klen, S_row=S_row):
                    e.scalar_tensor_tensor(out=dst, in0=pb[:, 0:bw], scalar=0.125,
                                           in1=alibi_h[:, boff:boff + bw],
                                           op0=ALU.mult, op1=ALU.add)
                    if last:
                        e.drain()
                        e.tensor_tensor(out=S_row[:, klen - 128:klen],
                                        in0=S_row[:, klen - 128:klen], in1=pen9, op=ALU.add)
                    return e.drain()
                evw = [w(s_smm)]
                if s_exp.n >= 1:
                    evw.append((s_exp, s_exp.n - 1))
                sch.emit("vector", sev, waits=evw, incs=[(s_sevac, 1)])
            def vmax(e, S_row=S_row, negm=negm, klen=klen):
                e.tensor_reduce(out=negm, in_=S_row[:, 0:klen], axis=AX.X, op=ALU.max)
                e.drain()
                return e.tensor_scalar(out=negm, in0=negm, scalar1=-1.0, scalar2=None, op0=ALU.mult)
            vmw = [w(s_sevac)]
            if s_exp.n >= 1:
                vmw.append((s_exp, s_exp.n - 1))
            sch.emit("vector", vmax, waits=vmw, incs=[(s_max, 1)])
            def sexp(e, S_row=S_row, P_row=P_row, negm=negm, l_=l_, klen=klen):
                return e.activation(out=P_row[:, 0:klen], in_=S_row[:, 0:klen],
                                    func=AF.Exp, bias=negm, accum_out=l_)
            sxw = [w(s_max)]
            if len(ptr_hist) >= 1:
                sxw.append((s_ptr, ptr_hist[-1]))
            if s_o.n >= 1:
                sxw.append((s_o, s_o.n))
            sch.emit("scalar", sexp, waits=sxw, incs=[(s_exp, 1)])
            return dict(exp=s_exp.n)

        def stageB(h, q, tok):
            par = (h * NT + q) % 2
            P_row = P_row2[:, par * T:par * T + T]
            l_ = lrow[:, 2 * par:2 * par + 1]
            klen = (q + 1) * 128
            nc_chunks = klen // 128
            strip = h // 2
            pbase = (h % 2) * 64
            for g0 in range(0, nc_chunks, 4):
                gn = min(4, nc_chunks - g0)
                bank = ptp[(s_ptev.n) % 2]
                for j in range(gn):
                    c = g0 + j
                    ptw = []
                    if j == 0:
                        ptw.append((s_exp, tok["exp"]))
                        if s_ptev.n >= 2:
                            ptw.append((s_ptev, s_ptev.n - 1))
                    def ptr_(e, P_row=P_row, c=c, bank=bank, j=j):
                        return e.transpose(out=bank[:, j * 128:(j + 1) * 128],
                                           in_=P_row[:, c * 128:(c + 1) * 128], identity=ident_bf)
                    sch.emit("tensor", ptr_, waits=ptw, incs=[(s_ptr, 1)])
                dst = PT2[:, (s_ptev.n % 2) * PTW:(s_ptev.n % 2) * PTW + gn * 128]
                def ptev(e, dst=dst, bank=bank, gn=gn):
                    return e.copy(out=dst, in_=bank[:, 0:gn * 128])
                pw = [w(s_ptr)]
                if len(pv_bank_hist) >= 2:
                    pw.append((s_pv, pv_bank_hist[-2]))
                sch.emit("scalar", ptev, waits=pw, incs=[(s_ptev, 1)])
                pv_bank_hist.append(s_pv.n + gn)
                for j in range(gn):
                    c = g0 + j
                    def pvm(e, c=c, h=h, dst=dst, j=j, nc_chunks=nc_chunks):
                        return e.matmul(out=ppv, lhsT=dst[:, j * 128:(j + 1) * 128],
                                        rhs=v_[:, c * D + h * 64: c * D + h * 64 + 64],
                                        start=(c == 0), stop=(c == nc_chunks - 1))
                    pvw = [w(s_ptev)] if j == 0 else []
                    if c == 0 and s_o.n >= 1:
                        pvw.append((s_o, s_o.n))
                    sch.emit("tensor", pvm, waits=pvw, incs=[(s_pv, 1)])
            ptr_hist.append(s_ptr.n)
            obf = obf2[:, par * 64:par * 64 + 64]
            def vo(e, l_=l_, obf=obf):
                e.reciprocal(out=l_, in_=l_)
                e.drain()
                return e.tensor_scalar(out=obf, in0=ppv, scalar1=l_, scalar2=None, op0=ALU.mult)
            sch.emit("vector", vo, waits=[w(s_pv)] + ([(s_oT, s_oT.n - 1)] if s_o.n >= 2 else []),
                     incs=[(s_o, 1)])
            def otr(e, obf=obf):
                return e.transpose(out=pot, in_=obf, identity=ident_bf)
            sch.emit("tensor", otr, waits=[w(s_o), (s_oT, s_oT.n)], incs=[(s_oT, 1)])
            dsto = oT[pbase:pbase + 64, strip * T + q * 128: strip * T + (q + 1) * 128]
            def otev(e, dsto=dsto):
                return e.tensor_copy(out=dsto, in_=pot)
            sch.emit("vector", otev, waits=[w(s_oT)], incs=[(s_oT, 1)])

        ptr_hist = []
        pv_bank_hist = []
        pending = None
        for h in range(NH):
            def bab(e, h=h):
                return e.dma_start(out=alibi_h, in_=weights["alibi"][h:h + 1, :].to_broadcast([128, T]))
            abw = [proj_done] if h == 0 else [(s_smm, s_smm.n)]
            sch.emit("sync", bab, waits=abw, incs=[(s_abd, 16)])
            for q in range(NT):
                tok = stageA(h, q)
                if pending is not None:
                    stageB(*pending)
                pending = (h, q, tok)
            sch.flush(f"l1a_h{h}")
        stageB(*pending)
    flash_done = w(s_oT)

    with nc.sbuf_tensor(p.name("wof"), [128, 8 * D], BF16) as wof_t, \
         nc.sbuf_tensor(p.name("xt2"), [128, 2 * D], F32) as xt2_t, \
         nc.psum_tensor(p.name("pz0"), [128, 512], F32) as pz0_t, \
         nc.psum_tensor(p.name("pz1"), [128, 512], F32) as pz1_t:
        wof = wof_t[:]; xt2 = xt2_t[:]
        pz = (pz0_t[:], pz1_t[:])
        for kc in range(8):
            def ldwo(e, kc=kc):
                return e.dma_start(out=wof[:, kc * D:(kc + 1) * D],
                                   in_=weights["Wo"][kc * 128:(kc + 1) * 128, :])
            sch.emit("gpsimd", ldwo, incs=[(s_wld, 16)])
        wo_done = w(s_wld)
        for i in range(NT):
            xti = xt2[:, (i % 2) * D:(i % 2 + 1) * D]
            ldw2 = [] if i < 2 else [(s_ep, s_ep.n - 2), w(s_st[i % 2])]
            sl2 = s_xt[i % 2]
            def ldx2(e, xti=xti, i=i):
                return e.dma_start(out=xti, in_=x_src[i * 128:(i + 1) * 128, :])
            sch.emit("sync", ldx2, waits=ldw2, incs=[(sl2, 16)])
            for hc in range(2):
                pb = pz[hc]
                mmw = [wo_done, flash_done] if (i == 0 and hc == 0) else []
                if i > 0:
                    mmw = mmw + [(s_ep, s_ep.n - (1 + hc))]
                def zmm(e, i=i, hc=hc, pb=pb):
                    inst = None
                    for c in range(8):
                        inst = e.matmul(out=pb, lhsT=oT[:, c * T + i * 128: c * T + (i + 1) * 128],
                                        rhs=wof[:, c * D + hc * 512: c * D + (hc + 1) * 512],
                                        start=(c == 0), stop=(c == 7))
                    return inst
                sch.emit("tensor", zmm, waits=mmw, incs=[(s_zmm, 1)])
            def zev(e, i=i):
                e.tensor_tensor(out=scr2[:, 0:512], in0=pz[0], in1=bo_bc[:, 0:512], op=ALU.add)
                return e.tensor_tensor(out=scr2[:, 512:1024], in0=pz[1], in1=bo_bc[:, 512:1024], op=ALU.add)
            sch.emit("vector", zev, waits=[w(s_zmm), w(s_ep)], incs=[(s_ep, 2)])
            emit_ln(sch, scr2, scr2, stat, scr, s_stats, done=s_ep,
                    scalar_waits=[w(s_ep)])
            def xadd(e, xti=xti):
                return e.tensor_tensor(out=xti, in0=xti, in1=scr2, op=ALU.add)
            sch.emit("vector", xadd, waits=[w(sl2), w(s_ep)], incs=[(s_ep, 1)])
            def stx(e, xti=xti, i=i):
                return e.dma_start(out=x_dst[i * 128:(i + 1) * 128, :], in_=xti)
            sch.emit("sync", stx, waits=[w(s_ep)], incs=[(s_st[i % 2], 16)])
        sch.emit("sync", lambda e: e.nop(), waits=[w(s_st[0]), w(s_st[1])], incs=[(s_io, 16)])
        sch.flush("l1a_z")
    ctx.close()
    return s_io


def make_cvec():
    cv = np.zeros((128, 8), dtype=np.float32)
    cv[:, 0] = EPS_LN
    cv[:, 1] = -128.0
    cv[:, 2] = -1.0 / D
    cv[:, 3] = 1.0 / D
    cv[:, 4] = 1.0
    cv[:, 5] = 0.5
    cv[:, 6] = 0.125
    return cv


def build_program(blocks):
    p = Prog()
    nc = p.nc
    x_in = nc.dram_tensor("x_in", [T, D], F32, kind="ExternalInput").ap()
    x_out = nc.dram_tensor("x_out", [T, D], F32, kind="ExternalOutput").ap()
    ident_in = nc.dram_tensor("ident_in", [128, 128], BF16, kind="ExternalInput").ap()
    identf_in = nc.dram_tensor("identf_in", [128, 128], F32, kind="ExternalInput").ap()
    cvec_in = nc.dram_tensor("cvec_in", [128, 8], F32, kind="ExternalInput").ap()

    ident_bf = p.sb("ident_bf", [128, 128], BF16)
    ident_f = p.sb("ident_f", [128, 128], F32)
    onesf = p.sb("onesf", [1, 128], F32)
    cvec = p.sb("cvec", [128, 8], F32)
    p.epsap = cvec[:, 0:1]
    p.cv = lambda i: cvec[:, i:i + 1]

    s_io = p.get_sem("io")
    sch = Sched(p)
    sch.emit("sync", lambda e: e.dma_start(out=ident_bf, in_=ident_in), incs=[(s_io, 16)])
    sch.emit("sync", lambda e: e.dma_start(out=ident_f, in_=identf_in), incs=[(s_io, 16)])
    sch.emit("sync", lambda e: e.dma_start(out=cvec, in_=cvec_in), incs=[(s_io, 16)])
    sch.emit("vector", lambda e: e.memset(onesf, 1.0), incs=[(s_io, 1)])
    for eng in ("scalar", "vector", "tensor", "gpsimd"):
        sch.emit(eng, lambda e: e.nop(), waits=[w(s_io)])
    sch.flush("init")

    consts = {"ident_bf": ident_bf, "ident_f": ident_f, "onesf": onesf, "cvec": cvec}
    n = len(blocks)
    bufs = [x_in]
    for i in range(n - 1):
        bufs.append(nc.dram_tensor(f"xbuf{i}", [T, D], F32, kind="Internal").ap())
    bufs.append(x_out)

    for i, (emit_fn, wdecl) in enumerate(blocks):
        weights = {}
        for name, (shape, dtype) in wdecl.items():
            weights[name] = nc.dram_tensor(f"b{i}_{name}", list(shape), dtype, kind="ExternalInput").ap()
        emit_fn(p, bufs[i], bufs[i + 1], weights, consts)
    p.ctx.close()
    return p


L1_FFN_DECL = {
    "ln_gamma": ([1, D], F32), "ln_beta": ([1, D], F32),
    "fc1_q": ([1024, 4096], I32), "fc1_scales": ([8, 4096], F32),
    "fc1_biasT": ([128, 32], F32), "fc2_q": ([4096, 1024], I32),
    "fc2_scales": ([32, 1024], F32), "fc2_bias": ([1, D], F32),
}

L1_ATTN_DECL = {
    "Wq": ([1024, 1024], F32), "Wk": ([1024, 1024], F32),
    "Wv": ([1024, 1024], F32), "Wo": ([1024, 1024], F32),
    "bqT": ([128, 8], F32), "bo": ([1, D], F32),
    "pen9": ([128, 128], F32), "alibi": ([16, T], F32),
}


def l1_ffn_weights(inp, pre=""):
    return {
        pre + "ln_gamma": inp["L1_ffn_ln_gamma"].reshape(1, D),
        pre + "ln_beta": inp["L1_ffn_ln_beta"].reshape(1, D),
        pre + "fc1_q": inp["L1_ffn_fc1_q"],
        pre + "fc1_scales": inp["L1_ffn_fc1_scales"],
        pre + "fc1_biasT": np.ascontiguousarray(inp["L1_ffn_fc1_bias"].reshape(32, 128).T),
        pre + "fc2_q": inp["L1_ffn_fc2_q"],
        pre + "fc2_scales": inp["L1_ffn_fc2_scales"],
        pre + "fc2_bias": inp["L1_ffn_fc2_bias"].reshape(1, D),
    }


def l1_attn_weights(inp, pre=""):
    NH = 16
    slopes = np.power(2.0 ** (-8.0 / NH), np.arange(1, NH + 1, dtype=np.float32))
    tk = np.arange(T, dtype=np.float64)
    pen9 = np.where(np.arange(128)[None, :] <= np.arange(128)[:, None], 0.0, -1e9).astype(np.float32)
    return {
        pre + "Wq": inp["L1_attn_Wq"], pre + "Wk": inp["L1_attn_Wk"],
        pre + "Wv": inp["L1_attn_Wv"], pre + "Wo": inp["L1_attn_Wo"],
        pre + "bqT": np.ascontiguousarray(inp["L1_attn_bq"].reshape(8, 128).T),
        pre + "bo": inp["L1_attn_bo"].reshape(1, D),
        pre + "pen9": pen9,
        pre + "alibi": (slopes[:, None].astype(np.float64) * tk[None, :]).astype(np.float32),
    }


def base_inputs():
    import ml_dtypes
    return {"ident_in": np.eye(128, dtype=ml_dtypes.bfloat16),
            "identf_in": np.eye(128, dtype=np.float32), "cvec_in": make_cvec()}


def emit_l0_ffn(p, x_src, x_dst, weights, consts):
    nc = p.nc
    ident_bf = consts["ident_bf"]
    ident_f = consts["ident_f"]
    onesf = consts["onesf"]
    LOGK = float(np.log(np.float32(1152.0)))
    SQD = 32.0
    sch = Sched(p)
    ctx = ExitStack()
    sb = lambda n, s, d: ctx.enter_context(nc.sbuf_tensor(p.name(n), s, d))[:]
    dram = lambda n, s, d: nc.dram_tensor(p.name(n), s, d, kind="Internal").ap()

    z_acc = sb("z_acc", [128, NT * D], F32)
    scr = sb("scr", [128, D], F32)
    scr2 = sb("scr2", [128, D], F32)
    stat = sb("stat", [128, 8], F32)
    s_sb = sb("s_sb", [128, 16], F32)
    maskcol = sb("maskcol", [128, 16], F32)
    maskbc = sb("maskbc", [128, T], F32)
    b2l_bc = sb("b2l_bc", [128, D], F32)
    b2h_bc = sb("b2h_bc", [128, D], F32)
    b1lT = sb("b1lT", [128, 4], F32)
    b1hT = sb("b1hT", [128, 32], F32)
    xnhT = sb("xnhT", [128, 8 * T], BF16)
    s32 = sb("s32", [32, 64], F32)
    b32 = sb("b32", [32, 64], F32)
    t32 = sb("t32", [32, 64], F32)
    u32 = sb("u32", [32, 64], F32)
    m32 = sb("m32", [32, 64], F32)
    pos32 = sb("pos32", [32, 64], F32)
    cum32 = sb("cum32", [32, 64], F32)
    zero32 = sb("zero32", [32, 64], F32)
    un32 = sb("un32", [32, 64], F32)
    sq32 = sb("sq32", [32, 32], F32)
    tq32 = sb("tq32", [32, 32], F32)
    srow = sb("srow", [1, 64], F32)
    orow = sb("orow", [1, 32], F32)
    lohi = sb("lohi", [1, 8], F32)
    pu1 = sb("pu1", [1, 1], U32)
    maskrow = sb("maskrow", [1, 512], F32)
    s16 = sb("s16", [16, 128], F32)

    d_s = dram("l0f_s", [16, 128], F32)
    d_m = dram("l0f_m", [32, 64], F32)

    s_io = p.get_sem("io")
    s_ld = p.sem("l0ld")
    s_xt = [p.sem("l0xa"), p.sem("l0xb")]
    s_st6 = [p.sem("l0sa"), p.sem("l0sb")]
    s_stats = p.get_sem("st")
    s_xn = p.get_sem("xn")
    s_tr = p.get_sem("tr")
    s_evac = p.get_sem("ev")
    s_sv = p.get_sem("sv")
    s_desc = p.get_sem("desc")
    s_dsc = p.get_sem("dsc")
    s_sdma = p.sem("l0sd")
    s_mdma = p.sem("l0md")
    s_wl = p.sem("l0wl")
    s_wh = [p.sem("l0wha"), p.sem("l0whb")]
    s_w2h = [p.sem("l0w2a"), p.sem("l0w2b")]
    s_mm1 = p.get_sem("mmy")
    s_gel = p.get_sem("silu")
    s_mm2 = p.get_sem("mmz")
    s_za = p.get_sem("za")
    s_msk = p.get_sem("msk")
    s_mbc = p.get_sem("mbc")
    s_ep = p.get_sem("ep")

    emit_bcast_load(sch, weights["b2_l"], b2l_bc, s_ld)
    emit_bcast_load(sch, weights["b2_h"], b2h_bc, s_ld)
    sch.emit("sync", lambda e: e.dma_start(out=b1lT, in_=weights["b1_lT"]), incs=[(s_ld, 16)])
    sch.emit("sync", lambda e: e.dma_start(out=b1hT, in_=weights["b1_hT"]), incs=[(s_ld, 16)])
    def vinit(e):
        e.memset(orow, 1.0)
        e.drain()
        e.memset(zero32, 0.0)
        e.drain()
        e.memset(sq32, 0.0)
        e.drain()
        e.memset(tq32, 0.0)
        e.drain()
        e.memset(srow, 0.0)
        e.drain()
        return e.memset(lohi, 0.0)
    sch.emit("vector", vinit, incs=[(s_desc, 1)])

    xnlT_cm = nc.sbuf_tensor(p.name("xnlT"), [128, 8 * T], BF16)
    xnlT = xnlT_cm.__enter__()[:]
    with nc.sbuf_tensor(p.name("xt"), [128, 2 * D], F32) as xt_t, \
         nc.sbuf_tensor(p.name("xinb"), [128, 2 * D], F32) as xin_t, \
         nc.sbuf_tensor(p.name("xnb"), [128, D], BF16) as xnb_t, \
         nc.sbuf_tensor(p.name("rbc3"), [128, 3 * D], F32) as rbc3_t, \
         nc.psum_tensor(p.name("pt0"), [128, 128], BF16) as pt0_t, \
         nc.psum_tensor(p.name("pt1"), [128, 128], BF16) as pt1_t:
        xt2 = xt_t[:]; xin2 = xin_t[:]; xnb = xnb_t[:]
        rbc3 = rbc3_t[:]
        r_bc = rbc3[:, 0:D]; gl_bc = rbc3[:, D:2 * D]; gh_bc = rbc3[:, 2 * D:3 * D]
        emit_bcast_load(sch, weights["routing_token"], r_bc, s_ld)
        emit_bcast_load(sch, weights["gamma_l"], gl_bc, s_ld)
        emit_bcast_load(sch, weights["gamma_h"], gh_bc, s_ld)
        ld_done = w(s_ld)
        sch.emit("scalar", lambda e: e.nop(), waits=[ld_done])
        sch.emit("vector", lambda e: e.nop(), waits=[ld_done])
        sch.emit("gpsimd", lambda e: e.nop(), waits=[ld_done])
        pt = (pt0_t[:], pt1_t[:])
        for i in range(NT):
            xti = xt2[:, (i % 2) * D:(i % 2 + 1) * D]
            xini = xin2[:, (i % 2) * D:(i % 2 + 1) * D]
            sl = s_xt[i % 2]
            ldw = [w(s_io)] if i == 0 else ([] if i < 2 else [(s_xn, s_xn.n - 2)])
            def ldx(e, xti=xti, i=i):
                return e.dma_start(out=xti, in_=x_src[i * 128:(i + 1) * 128, :])
            sch.emit("sync", ldx, waits=ldw, incs=[(sl, 16)])
            emit_ln(sch, xti, xini, stat, scr, s_stats, done=s_xn,
                    scalar_waits=[w(sl), w(s_xn), w(s_sv)])
            def sv(e, xini=xini, i=i):
                e.tensor_tensor(out=scr, in0=xini, in1=r_bc, op=ALU.mult)
                e.drain()
                return e.tensor_reduce(out=s_sb[:, i:i + 1], in_=scr, axis=AX.X, op=ALU.add)
            sch.emit("vector", sv, waits=[w(s_xn)], incs=[(s_sv, 1)])
            ssq = stat[:, 4:5]; rn = stat[:, 5:6]
            def rms_s(e, xini=xini, ssq=ssq):
                e.activation(out=scr2, in_=xini, func=AF.Square, accum_out=ssq)
                e.drain()
                return e.activation(out=ssq, in_=ssq, func=AF.Sqrt)
            sch.emit("scalar", rms_s, waits=[w(s_sv)], incs=[(s_dsc, 1)])
            def rms_v(e, ssq=ssq, rn=rn):
                e.tensor_scalar(out=rn, in0=ssq, scalar1=1e-12, scalar2=None, op0=ALU.max)
                e.drain()
                e.reciprocal(out=rn, in_=rn)
                e.drain()
                return e.tensor_scalar(out=rn, in0=rn, scalar1=SQD, scalar2=None, op0=ALU.mult)
            sch.emit("vector", rms_v, waits=[w(s_dsc)], incs=[(s_desc, 1)])
            for gbc, dstT in ((gl_bc, xnlT), (gh_bc, xnhT)):
                def mknorm(e, xini=xini, gbc=gbc, rn=rn):
                    e.tensor_scalar(out=scr2, in0=xini, scalar1=rn, scalar2=None, op0=ALU.mult)
                    e.drain()
                    return e.tensor_tensor(out=xnb, in0=scr2, in1=gbc, op=ALU.mult)
                sch.emit("vector", mknorm, waits=[w(s_desc), w(s_tr)], incs=[(s_xn, 1)])
                for k in range(8):
                    pbank = pt[k % 2]
                    evac_snapshot = s_evac.n
                    waits = [w(s_xn)] if k == 0 else []
                    if evac_snapshot >= 2:
                        waits.append((s_evac, evac_snapshot - 1))
                    def tr(e, k=k, pbank=pbank):
                        return e.transpose(out=pbank, in_=xnb[:, k * 128:(k + 1) * 128], identity=ident_bf)
                    sch.emit("tensor", tr, waits=waits, incs=[(s_tr, 1)])
                    dst = dstT[:, k * T + i * 128: k * T + (i + 1) * 128]
                    def ev(e, dst=dst, pbank=pbank):
                        return e.tensor_copy(out=dst, in_=pbank)
                    sch.emit("vector", ev, waits=[w(s_tr)], incs=[(s_evac, 1)])
        sch.flush("l0f_p1")
    xT_ready = w(s_evac)

    with nc.psum_tensor(p.name("ptsf"), [16, 128], F32) as ptsf_t:
        ptsf = ptsf_t[:]
        def str_(e):
            return e.transpose(out=ptsf, in_=s_sb, identity=ident_f)
        sch.emit("tensor", str_, waits=[w(s_sv)], incs=[(s_tr, 1)])
        def sev_(e):
            return e.tensor_copy(out=s16, in_=ptsf)
        sch.emit("vector", sev_, waits=[w(s_tr)], incs=[(s_desc, 1)])
        def sst(e):
            return e.dma_start(out=d_s, in_=s16)
        sch.emit("sync", sst, waits=[w(s_desc)], incs=[(s_sdma, 16)])
        def sld(e):
            return e.dma_start(out=s32, in_=d_s.rearrange("a b -> (a b)").rearrange("(P F) -> P F", P=32))
        sch.emit("sync", sld, waits=[w(s_sdma)], incs=[(s_sdma, 16)])
        sch.flush("l0f_p2")
    s32_ready = w(s_sdma)

    with nc.sbuf_tensor(p.name("wl"), [128, 8 * 512], BF16) as wl_t, \
         nc.sbuf_tensor(p.name("w2l"), [128, 4 * D], BF16) as w2l_t, \
         nc.sbuf_tensor(p.name("hlT"), [128, 4 * T], BF16) as hlT_t, \
         nc.psum_tensor(p.name("pl0"), [128, 512], F32) as pl0_t, \
         nc.psum_tensor(p.name("pl1"), [128, 512], F32) as pl1_t:
        wl = wl_t[:]; w2l = w2l_t[:]; hlT = hlT_t[:]
        pl = (pl0_t[:], pl1_t[:])
        for kc in range(8):
            def lw(e, kc=kc):
                return e.dma_start(out=wl[:, kc * 512:(kc + 1) * 512],
                                   in_=weights["w1_lT"][kc * 128:(kc + 1) * 128, :])
            sch.emit("gpsimd", lw, incs=[(s_wl, 16)])
        for j in range(4):
            def lw2(e, j=j):
                return e.dma_start(out=w2l[:, j * D:(j + 1) * D],
                                   in_=weights["w2_lT"][j * 128:(j + 1) * 128, :])
            sch.emit("gpsimd", lw2, incs=[(s_wl, 16)])
        wl_done = w(s_wl)
        for f4 in range(4):
            for mc in range(4):
                idx = f4 * 4 + mc
                pb = pl[idx % 2]
                waits = [wl_done, xT_ready] if idx == 0 else []
                if idx >= 2:
                    waits.append((s_gel, s_gel.n - 1))
                def mm(e, f4=f4, mc=mc, pb=pb):
                    inst = None
                    for kc in range(8):
                        inst = e.matmul(out=pb,
                                        lhsT=wl[:, kc * 512 + f4 * 128: kc * 512 + (f4 + 1) * 128],
                                        rhs=xnlT[:, kc * T + mc * 512: kc * T + (mc + 1) * 512],
                                        start=(kc == 0), stop=(kc == 7))
                    return inst
                sch.emit("tensor", mm, waits=waits, incs=[(s_mm1, 1)])
                dst = hlT[:, f4 * T + mc * 512: f4 * T + (mc + 1) * 512]
                def gel(e, dst=dst, pb=pb, f4=f4):
                    return e.activation(out=dst, in_=pb, func=AF.Gelu, bias=b1lT[:, f4:f4 + 1])
                sch.emit("scalar", gel, waits=[w(s_mm1)], incs=[(s_gel, 1)])
        gel_done = w(s_gel)
        for mt in range(16):
            for dc in range(2):
                idx = mt * 2 + dc
                pb = pl[idx % 2]
                waits = [gel_done] if idx == 0 else []
                if idx >= 2:
                    waits.append((s_za, s_za.n - 1))
                def mm2(e, mt=mt, dc=dc, pb=pb):
                    inst = None
                    for f4 in range(4):
                        inst = e.matmul(out=pb,
                                        lhsT=hlT[:, f4 * T + mt * 128: f4 * T + (mt + 1) * 128],
                                        rhs=w2l[:, f4 * D + dc * 512: f4 * D + (dc + 1) * 512],
                                        start=(f4 == 0), stop=(f4 == 3))
                    return inst
                sch.emit("tensor", mm2, waits=waits, incs=[(s_mm2, 1)])
                dst = z_acc[:, mt * D + dc * 512: mt * D + (dc + 1) * 512]
                def zseed(e, dst=dst, pb=pb, dc=dc):
                    return e.tensor_tensor(out=dst, in0=pb, in1=b2l_bc[:, dc * 512:(dc + 1) * 512], op=ALU.add)
                sch.emit("vector", zseed, waits=[w(s_mm2)], incs=[(s_za, 1)])
        sch.flush("l0f_p4")

    xnlT_cm.__exit__(None, None, None)

    def col_to_row(e, col_ap, row_dst):
        e.tensor_copy(out=sq32[:, 0:1], in_=col_ap)
        e.drain()
        e.transpose(out=tq32, in_=sq32)
        e.drain()
        e.tensor_copy(out=row_dst, in_=tq32[0:1, 0:32])
        return e.drain()

    def scalar_to_col(e, sc_ap, col_dst):
        e.tensor_scalar(out=tq32[0:1, 0:32], in0=orow[0:1, 0:32], scalar1=sc_ap, scalar2=None, op0=ALU.mult)
        e.drain()
        e.transpose(out=sq32, in_=tq32)
        e.drain()
        e.tensor_copy(out=col_dst, in_=sq32[:, 0:1])
        return e.drain()

    def d0(e):
        return e.tensor_scalar(out=b32, in0=s32, scalar1=-1.0, scalar2=None, op0=ALU.mult)
    sch.emit("vector", d0, waits=[s32_ready], incs=[(s_desc, 1)])
    psum_ = stat[0:32, 6:7]
    acol = stat[0:32, 7:8]
    for it in range(50):
        def d1(e):
            return e.tensor_add(out=t32, in0=s32, in1=b32)
        sch.emit("vector", d1, waits=[w(s_desc)], incs=[(s_desc, 1)])
        def d2(e):
            return e.activation(out=t32, in_=t32, func=AF.Exp, accum_out=psum_)
        sch.emit("scalar", d2, waits=[w(s_desc)] + ([w(s_dsc)] if it > 0 else []), incs=[(s_dsc, 1)])
        def d3(e):
            col_to_row(e, psum_, srow[0:1, 0:32])
            e.tensor_reduce(out=lohi[:, 7:8], in_=srow[0:1, 0:32], axis=AX.X, op=ALU.add)
            return e.drain()
        sch.emit("vector", d3, waits=[w(s_dsc)], incs=[(s_desc, 1)])
        def d4(e):
            return e.activation(out=lohi[:, 7:8], in_=lohi[:, 7:8], func=AF.Ln)
        sch.emit("scalar", d4, waits=[w(s_desc)], incs=[(s_dsc, 1)])
        def d5(e):
            e.tensor_scalar(out=lohi[:, 7:8], in0=lohi[:, 7:8], scalar1=-1.0, scalar2=LOGK,
                            op0=ALU.mult, op1=ALU.add)
            e.drain()
            scalar_to_col(e, lohi[:, 7:8], acol)
            e.tensor_scalar(out=b32, in0=s32, scalar1=acol, scalar2=None, op0=ALU.add)
            e.drain()
            return e.tensor_scalar(out=b32, in0=b32, scalar1=0.0, scalar2=-1.0, op0=ALU.max, op1=ALU.mult)
        sch.emit("vector", d5, waits=[w(s_dsc)], incs=[(s_desc, 1)])
    def du(e):
        return e.tensor_scalar(out=u32, in0=s32, scalar1=acol, scalar2=None, op0=ALU.add)
    sch.emit("vector", du, waits=[w(s_desc)], incs=[(s_desc, 1)])
    sch.flush("l0f_p3")

    def m0(e):
        e.tensor_scalar(out=pos32, in0=u32, scalar1=0.0, scalar2=None, op0=ALU.is_ge)
        e.drain()
        e.tensor_tensor_scan(out=cum32, data0=pos32, data1=zero32, initial=0.0,
                             op0=ALU.add, op1=ALU.add)
        e.drain()
        col_to_row(e, cum32[:, 63:64], srow[0:1, 0:32])
        e.tensor_tensor_scan(out=srow[0:1, 0:32], data0=srow[0:1, 0:32], data1=orow[0:1, 0:32],
                             op0=ALU.add, op1=ALU.mult, initial=0.0)
        e.drain()
        e.tensor_copy(out=lohi[:, 6:7], in_=srow[0:1, 31:32])
        e.drain()
        e.memset(srow[:, 32:33], 0.0)
        e.drain()
        e.tensor_copy(out=srow[:, 33:64], in_=srow[0:1, 0:31])
        e.drain()
        e.tensor_copy(out=tq32[0:1, 0:32], in_=srow[0:1, 32:64])
        e.drain()
        e.transpose(out=sq32, in_=tq32)
        e.drain()
        e.tensor_scalar(out=cum32, in0=cum32, scalar1=sq32[:, 0:1], scalar2=None, op0=ALU.add)
        e.drain()
        e.tensor_scalar(out=m32, in0=cum32, scalar1=1024.0, scalar2=None, op0=ALU.is_le)
        e.drain()
        e.tensor_tensor(out=m32, in0=m32, in1=pos32, op=ALU.mult)
        e.drain()
        e.tensor_scalar(out=lohi[:, 4:5], in0=lohi[:, 6:7], scalar1=-1.0, scalar2=1024.0,
                        op0=ALU.mult, op1=ALU.add)
        e.drain()
        e.tensor_scalar(out=lohi[:, 5:6], in0=lohi[:, 4:5], scalar1=0.0, scalar2=None, op0=ALU.is_gt)
        e.drain()
        e.tensor_scalar(out=t32, in0=pos32, scalar1=-1e38, scalar2=None, op0=ALU.mult)
        e.drain()
        e.tensor_tensor(out=un32, in0=u32, in1=t32, op=ALU.min)
        e.drain()
        e.tensor_tensor(out=t32, in0=u32, in1=pos32, op=ALU.mult)
        e.drain()
        e.tensor_sub(out=t32, in0=u32, in1=t32)
        e.drain()
        e.tensor_reduce(out=stat[0:32, 6:7], in_=t32, axis=AX.X, op=ALU.min)
        e.drain()
        col_to_row(e, stat[0:32, 6:7], srow[0:1, 0:32])
        e.tensor_reduce(out=lohi[:, 0:1], in_=srow[0:1, 0:32], axis=AX.X, op=ALU.min)
        e.drain()
        return e.memset(lohi[:, 1:2], 0.0)
    sch.emit("vector", m0, waits=[w(s_desc)], incs=[(s_desc, 1)])
    mcol = stat[0:32, 6:7]
    for r_ in range(45):
        def br(e):
            e.tensor_add(out=lohi[:, 2:3], in0=lohi[:, 0:1], in1=lohi[:, 1:2])
            e.drain()
            e.tensor_scalar(out=lohi[:, 2:3], in0=lohi[:, 2:3], scalar1=0.5, scalar2=None, op0=ALU.mult)
            e.drain()
            scalar_to_col(e, lohi[:, 2:3], mcol)
            e.tensor_scalar(out=t32, in0=un32, scalar1=mcol, scalar2=None, op0=ALU.is_gt)
            e.drain()
            e.tensor_reduce(out=stat[0:32, 7:8], in_=t32, axis=AX.X, op=ALU.add)
            e.drain()
            col_to_row(e, stat[0:32, 7:8], srow[0:1, 0:32])
            e.tensor_reduce(out=lohi[:, 3:4], in_=srow[0:1, 0:32], axis=AX.X, op=ALU.add)
            e.drain()
            e.tensor_tensor(out=pu1, in0=lohi[:, 3:4], in1=lohi[:, 4:5], op=ALU.is_ge)
            e.drain()
            e.select(out=lohi[:, 0:1], mask=pu1, on_true=lohi[:, 2:3], on_false=lohi[:, 0:1], add_drain=True)
            e.drain()
            e.select(out=lohi[:, 1:2], mask=pu1, on_true=lohi[:, 1:2], on_false=lohi[:, 2:3], add_drain=True)
            return e.drain()
        sch.emit("vector", br, waits=[w(s_desc)], incs=[(s_desc, 1)])
    def mfin(e):
        scalar_to_col(e, lohi[:, 0:1], mcol)
        e.tensor_scalar(out=t32, in0=un32, scalar1=mcol, scalar2=None, op0=ALU.is_gt)
        e.drain()
        scalar_to_col(e, lohi[:, 5:6], mcol)
        e.tensor_scalar(out=t32, in0=t32, scalar1=mcol, scalar2=None, op0=ALU.mult)
        e.drain()
        e.tensor_add(out=m32, in0=m32, in1=t32)
        return e.drain()
    sch.emit("vector", mfin, waits=[w(s_desc)], incs=[(s_desc, 1)])
    def mst(e):
        return e.dma_start(out=d_m, in_=m32)
    sch.emit("sync", mst, waits=[w(s_desc)], incs=[(s_mdma, 16)])
    def mld2(e):
        return e.dma_start(out=s16, in_=d_m.rearrange("a b -> (a b)").rearrange("(a b) -> a b", a=16))
    sch.emit("sync", mld2, waits=[w(s_mdma)], incs=[(s_mdma, 16)])
    with nc.psum_tensor(p.name("ptm"), [128, 16], F32) as ptm_t, \
         nc.psum_tensor(p.name("pmb"), [128, 512], F32) as pmb_t:
        ptm = ptm_t[:]; pmb = pmb_t[:]
        def mtr(e):
            return e.transpose(out=ptm, in_=s16, identity=ident_f[0:16, 0:16])
        sch.emit("tensor", mtr, waits=[w(s_mdma)], incs=[(s_tr, 1)])
        def mtc(e):
            return e.tensor_copy(out=maskcol, in_=ptm)
        sch.emit("vector", mtc, waits=[w(s_tr)], incs=[(s_mbc, 1)])
        dmrow = d_m.rearrange("a b -> (a b)").rearrange("(c t) -> c t", c=4)
        for c in range(4):
            def mldr(e, c=c):
                return e.dma_start(out=maskrow, in_=dmrow[c:c + 1, :])
            sch.emit("sync", mldr, waits=[w(s_mdma)] + ([(s_mbc, s_mbc.n)] if c > 0 else []),
                     incs=[(s_mdma, 16)])
            def mbm(e, c=c):
                return e.matmul(out=pmb, lhsT=onesf, rhs=maskrow,
                                start=True, stop=True)
            sch.emit("tensor", mbm, waits=[w(s_mdma)] + ([(s_mbc, s_mbc.n)] if c > 0 else []),
                     incs=[(s_tr, 1)])
            def mbe(e, c=c):
                return e.tensor_copy(out=maskbc[:, c * 512:(c + 1) * 512], in_=pmb)
            sch.emit("vector", mbe, waits=[w(s_tr)], incs=[(s_mbc, 1)])
        sch.flush("l0f_p3b")
    mask_ready = w(s_mbc)

    with nc.sbuf_tensor(p.name("whh"), [128, 2 * 8 * 512], BF16) as whh_t, \
         nc.sbuf_tensor(p.name("w2h"), [128, 2 * 4 * D], BF16) as w2h_t, \
         nc.sbuf_tensor(p.name("hhT"), [128, 4 * T], BF16) as hhT_t, \
         nc.psum_tensor(p.name("ph0"), [128, 512], F32) as ph0_t, \
         nc.psum_tensor(p.name("ph1"), [128, 512], F32) as ph1_t:
        whh2 = whh_t[:]; w2h2 = w2h_t[:]; hhT = hhT_t[:]
        ph = (ph0_t[:], ph1_t[:])
        mm1_hist = []
        mm2_hist = []
        for fc in range(8):
            slot = fc % 2
            whh = whh2[:, slot * 8 * 512:(slot + 1) * 8 * 512]
            w2h = w2h2[:, slot * 4 * D:(slot + 1) * 4 * D]
            lw_waits = [] if fc < 2 else [(s_mm1, mm1_hist[fc - 2])]
            for kc in range(8):
                def lwh(e, fc=fc, kc=kc, whh=whh):
                    return e.dma_start(out=whh[:, kc * 512:(kc + 1) * 512],
                                       in_=weights["w1_hT"][kc * 128:(kc + 1) * 128, fc * 512:(fc + 1) * 512])
                sch.emit("gpsimd", lwh, waits=lw_waits if kc == 0 else [], incs=[(s_wh[slot], 16)])
            lw2_waits = [] if fc < 2 else [(s_mm2, mm2_hist[fc - 2])]
            for j in range(4):
                def lw2h(e, fc=fc, j=j, w2h=w2h):
                    return e.dma_start(out=w2h[:, j * D:(j + 1) * D],
                                       in_=weights["w2_hT"][fc * 512 + j * 128: fc * 512 + (j + 1) * 128, :])
                sch.emit("gpsimd", lw2h, waits=lw2_waits if j == 0 else [], incs=[(s_w2h[slot], 16)])
            wh_done = w(s_wh[slot])
            w2h_done = w(s_w2h[slot])
            mm2_fence = w(s_mm2)
            for f4 in range(4):
                for mc in range(4):
                    idx = fc * 16 + f4 * 4 + mc
                    pb = ph[idx % 2]
                    waits = [wh_done] if (f4 == 0 and mc == 0) else []
                    if idx >= 2:
                        waits.append((s_gel, s_gel.n - 1))
                    def mmh(e, f4=f4, mc=mc, pb=pb, whh=whh):
                        inst = None
                        for kc in range(8):
                            inst = e.matmul(out=pb,
                                            lhsT=whh[:, kc * 512 + f4 * 128: kc * 512 + (f4 + 1) * 128],
                                            rhs=xnhT[:, kc * T + mc * 512: kc * T + (mc + 1) * 512],
                                            start=(kc == 0), stop=(kc == 7))
                        return inst
                    sch.emit("tensor", mmh, waits=waits, incs=[(s_mm1, 1)])
                    dst = hhT[:, f4 * T + mc * 512: f4 * T + (mc + 1) * 512]
                    bidx = fc * 4 + f4
                    gw = [w(s_mm1)]
                    if f4 == 0 and mc == 0 and fc > 0:
                        gw.append(mm2_fence)
                        gw.append(w(s_msk))
                    def gelh(e, dst=dst, pb=pb, bidx=bidx):
                        return e.activation(out=dst, in_=pb, func=AF.Gelu, bias=b1hT[:, bidx:bidx + 1])
                    sch.emit("scalar", gelh, waits=gw, incs=[(s_gel, 1)])
            mm1_hist.append(s_mm1.n)
            gel_done_h = w(s_gel)
            for f4 in range(4):
                def mskp(e, f4=f4):
                    return e.tensor_tensor(out=hhT[:, f4 * T:(f4 + 1) * T],
                                           in0=hhT[:, f4 * T:(f4 + 1) * T],
                                           in1=maskbc, op=ALU.mult)
                sch.emit("vector", mskp, waits=([gel_done_h, mask_ready] if f4 == 0 else []),
                         incs=[(s_msk, 1)])
            msk_done = w(s_msk)
            for mt in range(16):
                for dc in range(2):
                    idx = fc * 32 + mt * 2 + dc
                    pb = ph[idx % 2]
                    waits = [msk_done, w2h_done] if (mt == 0 and dc == 0) else []
                    if idx >= 2:
                        waits.append((s_za, s_za.n - 1))
                    def mmh2(e, mt=mt, dc=dc, pb=pb, w2h=w2h):
                        inst = None
                        for f4 in range(4):
                            inst = e.matmul(out=pb,
                                            lhsT=hhT[:, f4 * T + mt * 128: f4 * T + (mt + 1) * 128],
                                            rhs=w2h[:, f4 * D + dc * 512: f4 * D + (dc + 1) * 512],
                                            start=(f4 == 0), stop=(f4 == 3))
                        return inst
                    sch.emit("tensor", mmh2, waits=waits, incs=[(s_mm2, 1)])
                    dst = z_acc[:, mt * D + dc * 512: mt * D + (dc + 1) * 512]
                    def zadd(e, dst=dst, pb=pb):
                        return e.tensor_tensor(out=dst, in0=dst, in1=pb, op=ALU.add)
                    sch.emit("vector", zadd, waits=[w(s_mm2)], incs=[(s_za, 1)])
            mm2_hist.append(s_mm2.n)
            sch.flush(f"l0f_p5_{fc}")

    with nc.sbuf_tensor(p.name("xt6"), [128, 2 * D], F32) as xt6_t:
        xt6 = xt6_t[:]
        z_done = w(s_za)
        for i in range(NT):
            xti = xt6[:, (i % 2) * D:(i % 2 + 1) * D]
            sl = s_xt[i % 2]
            ldw = [w(s_za, 0)] if False else ([] if i < 2 else [(s_ep, s_ep.n - 2), w(s_st6[i % 2])])
            def ldx6(e, xti=xti, i=i):
                return e.dma_start(out=xti, in_=x_src[i * 128:(i + 1) * 128, :])
            sch.emit("sync", ldx6, waits=ldw, incs=[(sl, 16)])
            z_t = z_acc[:, i * D:(i + 1) * D]
            def brank(e, z_t=z_t, i=i):
                return e.scalar_tensor_tensor(out=z_t, in0=b2h_bc, scalar=maskcol[:, i:i + 1],
                                              in1=z_t, op0=ALU.mult, op1=ALU.add)
            bw_ = [z_done, mask_ready] if i == 0 else [w(s_ep)]
            sch.emit("vector", brank, waits=bw_, incs=[(s_ep, 1)])
            emit_ln(sch, z_t, scr2, stat, scr, s_stats, done=s_ep,
                    scalar_waits=[w(s_ep)])
            def xadd(e, xti=xti):
                return e.tensor_tensor(out=xti, in0=xti, in1=scr2, op=ALU.add)
            sch.emit("vector", xadd, waits=[w(sl), w(s_ep)], incs=[(s_ep, 1)])
            def stx(e, xti=xti, i=i):
                return e.dma_start(out=x_dst[i * 128:(i + 1) * 128, :], in_=xti)
            sch.emit("sync", stx, waits=[w(s_ep)], incs=[(s_st6[i % 2], 16)])
        sch.emit("sync", lambda e: e.nop(), waits=[w(s_st6[0]), w(s_st6[1])], incs=[(s_io, 16)])
        sch.flush("l0f_p6")
    ctx.close()
    return s_io


L0_FFN_DECL = {
    "routing_token": ([1, D], F32), "gamma_l": ([1, D], F32), "gamma_h": ([1, D], F32),
    "w1_lT": ([1024, 512], F32), "b1_lT": ([128, 4], F32),
    "w2_lT": ([512, 1024], F32), "b2_l": ([1, D], F32),
    "w1_hT": ([1024, 4096], F32), "b1_hT": ([128, 32], F32),
    "w2_hT": ([4096, 1024], F32), "b2_h": ([1, D], F32),
}


def l0_ffn_weights(inp, pre=""):
    return {
        pre + "routing_token": inp["L0_ffn_routing_token"].reshape(1, D),
        pre + "gamma_l": inp["L0_ffn_gamma_l"].reshape(1, D),
        pre + "gamma_h": inp["L0_ffn_gamma_h"].reshape(1, D),
        pre + "w1_lT": np.ascontiguousarray(inp["L0_ffn_w1_l"].T),
        pre + "b1_lT": np.ascontiguousarray(inp["L0_ffn_b1_l"].reshape(4, 128).T),
        pre + "w2_lT": np.ascontiguousarray(inp["L0_ffn_w2_l"].T),
        pre + "b2_l": inp["L0_ffn_b2_l"].reshape(1, D),
        pre + "w1_hT": np.ascontiguousarray(inp["L0_ffn_w1_h"].T),
        pre + "b1_hT": np.ascontiguousarray(inp["L0_ffn_b1_h"].reshape(32, 128).T),
        pre + "w2_hT": np.ascontiguousarray(inp["L0_ffn_w2_h"].T),
        pre + "b2_h": inp["L0_ffn_b2_h"].reshape(1, D),
    }


def emit_l0_attn(p, x_src, x_dst, weights, consts, use_collective=True, n_cores=8, mode="mono"):
    nc = p.nc
    ident_bf = consts["ident_bf"]
    ident_f = consts["ident_f"]
    NH, DK, CH = 16, 64, 64
    NCH = T // CH
    E = 8
    CAP = 5120.0
    sch = Sched(p)
    ctx = ExitStack()
    sb = lambda n, s, d: ctx.enter_context(nc.sbuf_tensor(p.name(n), s, d))[:]
    dram = lambda n, s, d: nc.dram_tensor(p.name(n), s, d, kind="Internal").ap()

    if mode == "a":
        d_xmid = weights["xmid_out"]
    elif mode == "b":
        d_xmid = weights["xmid_in"]
    else:
        d_xmid = dram("l0a_xmid", [T, D], F32)
    d_cnt = dram("l0a_cnt", [E, 1], F32)
    d_gath = dram("l0a_gath", [n_cores * E, 1], F32)

    scr = sb("scr", [128, D], F32)
    scr2 = sb("scr2", [128, D], F32)
    stat = sb("stat", [128, 8], F32)

    s_io = p.get_sem("io")
    s_ld = p.sem("a0ld")
    s_xt = [p.sem("a0xa"), p.sem("a0xb")]
    s_st6 = [p.sem("a0sa"), p.sem("a0sb")]
    s_stats = p.get_sem("st")
    s_xn = p.get_sem("xn")
    s_tr = p.get_sem("tr")
    s_evac = p.get_sem("ev")
    s_wld = p.sem("a0wld")
    s_w2ld = [p.sem("a0w2a"), p.sem("a0w2b")]
    s_proj = p.get_sem("mmy")
    s_pev = p.get_sem("silu")
    s_rope = p.get_sem("desc")
    s_phis = p.get_sem("dsc")
    s_mmS = p.get_sem("mmz")
    s_Sacc = p.get_sem("za")
    s_mmN = p.get_sem("w1")
    s_epi = p.get_sem("w2")
    s_aoT = p.get_sem("ab")
    s_gt = p.get_sem("msk")
    s_gtT = p.get_sem("mbc")
    s_coll = p.sem("a0coll")
    s_cldma = p.sem("a0cd")
    s_za = p.get_sem("za")
    s_mm1 = p.get_sem("zmm")
    s_mm2 = p.get_sem("mmzz")
    s_ep = p.get_sem("ep")

    if mode != "b":
        aoT_cm = nc.sbuf_tensor(p.name("aoT"), [128, 8 * T], BF16)
        aoT = aoT_cm.__enter__()[:]
    if mode != "b":
        NHD = NH * 72
        bs_ctx = ExitStack()
        _sb = lambda n, sh, d: bs_ctx.enter_context(nc.sbuf_tensor(p.name(n), sh, d))[:]
        _ps = lambda n, sh, d: bs_ctx.enter_context(nc.psum_tensor(p.name(n), sh, d))[:]
        if True:
            wall = _sb("wqkvg", [128, 4 * 8 * D], BF16)
            bqkg = _sb("bqkg", [128, 4 * D], F32)
            xt2 = _sb("xt", [128, 2 * D], F32)
            xin2 = _sb("xinb", [128, 2 * D], F32)
            xnb = _sb("xnb", [128, D], BF16)
            n1bc = _sb("n1bc", [128, D], F32)
            rcos = _sb("rcos", [128, NT * 32], F32)
            rsin = _sb("rsin", [128, NT * 32], F32)
            rsc = _sb("rsc", [128, 2 * 512], F32)
            rmsTt2 = _sb("rmsTt", [128, 2 * 8 * 128], BF16)
            qkvg2 = _sb("qkvg", [128, 2 * 4 * D], BF16)
            Sacc = _sb("Sacc", [64, NHD], F32)
            Sbf = _sb("Sbf", [64, NHD], BF16)
            ones64 = _sb("ones64", [128, 1], BF16)
            qtb2 = _sb("qtb", [64, 2 * 64], BF16)
            aob2 = _sb("aob", [64, 2 * 64], BF16)
            gf2 = _sb("gf", [64, 2 * 64], F32)
            nsc2 = _sb("nsc", [64, 2 * 72], F32)
            pt = (_ps("pt0", [128, 128], BF16), _ps("pt1", [128, 128], BF16))
            pp = (_ps("pp0", [128, 512], F32), _ps("pp1", [128, 512], F32))
            pS = (_ps("pS0", [64, 72], F32), _ps("pS1", [64, 72], F32))
            pN = (_ps("pN0", [64, 72], F32), _ps("pN1", [64, 72], F32))

            emit_bcast_load(sch, weights["norm1_scale"], n1bc, s_ld)
            sch.emit("sync", lambda e: e.dma_start(
                out=rcos.rearrange("p (n f) -> p n f", f=32),
                in_=weights["ropecos"].rearrange("(n p) f -> p n f", p=128)), incs=[(s_ld, 16)])
            sch.emit("sync", lambda e: e.dma_start(
                out=rsin.rearrange("p (n f) -> p n f", f=32),
                in_=weights["ropesin"].rearrange("(n p) f -> p n f", p=128)), incs=[(s_ld, 16)])
            emit_bcast_load(sch, weights["bq"], bqkg[:, 0:D], s_ld)
            emit_bcast_load(sch, weights["bk"], bqkg[:, D:2 * D], s_ld)
            emit_bcast_load(sch, weights["bv"], bqkg[:, 2 * D:3 * D], s_ld)
            emit_bcast_load(sch, weights["bg"], bqkg[:, 3 * D:4 * D], s_ld)
            for pi, wname in enumerate(["Wq", "Wk", "Wv", "Wg"]):
                for kc in range(8):
                    def ldw_(e, wname=wname, pi=pi, kc=kc):
                        return e.dma_start(out=wall[:, (pi * 8 + kc) * D:(pi * 8 + kc + 1) * D],
                                           in_=weights[wname][kc * 128:(kc + 1) * 128, :])
                    sch.emit("gpsimd", ldw_, incs=[(s_wld, 16)])
            ld_done = w(s_ld)
            wld_done = w(s_wld)
            sch.emit("scalar", lambda e: e.nop(), waits=[ld_done])
            def vi(e):
                e.memset(ones64, 1.0)
                e.drain()
                return e.memset(Sacc, 0.0)
            sch.emit("vector", vi, waits=[ld_done], incs=[(s_Sacc, 1)])

            pev_hist = []
            rsc_hist = []
            for tt in range(NT):
                tp = tt % 2
                xti = xt2[:, tp * D:(tp + 1) * D]
                xini = xin2[:, tp * D:(tp + 1) * D]
                sl = s_xt[tp]
                ldw = [w(s_io)] if tt == 0 else ([] if tt < 2 else [(s_xn, s_xn.n - 2)])
                def ldx(e, xti=xti, tt=tt):
                    return e.dma_start(out=xti, in_=x_src[tt * 128:(tt + 1) * 128, :])
                sch.emit("sync", ldx, waits=ldw, incs=[(sl, 16)])
                emit_ln(sch, xti, xini, stat, scr2, s_stats, done=s_xn,
                        scalar_waits=[w(sl), w(s_xn)])
                ssq = stat[:, 4:5]; rr = stat[:, 5:6]
                def rms_s(e, xini=xini, ssq=ssq):
                    e.activation(out=scr2, in_=xini, func=AF.Square, accum_out=ssq)
                    e.drain()
                    return e.activation(out=ssq, in_=ssq, func=AF.Sqrt, scale=p.cv(3))
                sch.emit("scalar", rms_s, waits=[w(s_xn)], incs=[(s_phis, 1)])
                def rms_v(e, rr=rr, ssq=ssq):
                    e.reciprocal(out=rr, in_=ssq)
                    return e.drain()
                sch.emit("vector", rms_v, waits=[w(s_phis)], incs=[(s_rope, 1)])
                def mknorm(e, xini=xini, rr=rr):
                    e.tensor_scalar(out=scr2, in0=xini, scalar1=rr, scalar2=None, op0=ALU.mult)
                    e.drain()
                    return e.tensor_tensor(out=xnb, in0=scr2, in1=n1bc, op=ALU.mult)
                sch.emit("vector", mknorm, waits=[w(s_rope), w(s_tr)], incs=[(s_xn, 1)])
                rmsTt = rmsTt2[:, tp * 8 * 128:(tp + 1) * 8 * 128]
                for k in range(8):
                    pbank = pt[k % 2]
                    evac_snapshot = s_evac.n
                    waits = [w(s_xn)] if k == 0 else []
                    if evac_snapshot >= 2:
                        waits.append((s_evac, evac_snapshot - 1))
                    def tr(e, k=k, pbank=pbank):
                        return e.transpose(out=pbank, in_=xnb[:, k * 128:(k + 1) * 128], identity=ident_bf)
                    sch.emit("tensor", tr, waits=waits, incs=[(s_tr, 1)])
                    dst = rmsTt[:, k * 128:(k + 1) * 128]
                    def ev(e, dst=dst, pbank=pbank):
                        return e.tensor_copy(out=dst, in_=pbank)
                    sch.emit("vector", ev, waits=[w(s_tr)], incs=[(s_evac, 1)])
                rmsTt_ready = w(s_evac)
                qkvg = qkvg2[:, tp * 4 * D:(tp + 1) * 4 * D]
                for pi in range(4):
                    for hc in range(2):
                        idx = tt * 8 + pi * 2 + hc
                        pb = pp[idx % 2]
                        waits = [rmsTt_ready]
                        if pi == 0 and hc == 0 and tt == 0:
                            waits.append(wld_done)
                        if idx >= 2:
                            waits.append((s_pev, pev_hist[idx - 2]))
                        def mmv(e, pi=pi, hc=hc, pb=pb, rmsTt=rmsTt):
                            inst = None
                            for kc in range(8):
                                inst = e.matmul(out=pb,
                                                lhsT=rmsTt[:, kc * 128:(kc + 1) * 128],
                                                rhs=wall[:, (pi * 8 + kc) * D + hc * 512:(pi * 8 + kc) * D + (hc + 1) * 512],
                                                start=(kc == 0), stop=(kc == 7))
                            return inst
                        sch.emit("tensor", mmv, waits=waits, incs=[(s_proj, 1)])
                        dst = qkvg[:, pi * D + hc * 512: pi * D + (hc + 1) * 512]
                        bsl = bqkg[:, pi * D + hc * 512: pi * D + (hc + 1) * 512]
                        rbuf = rsc[:, (idx % 2) * 512:(idx % 2 + 1) * 512]
                        if pi == 3:
                            def evg1(e, pb=pb, bsl=bsl, rbuf=rbuf):
                                return e.tensor_tensor(out=rbuf, in0=pb, in1=bsl, op=ALU.add)
                            gvw = [w(s_proj)]
                            if len(rsc_hist) >= 2:
                                gvw.append((s_phis, rsc_hist[-2]))
                            sch.emit("vector", evg1, waits=gvw, incs=[(s_pev, 1)])
                            pev_hist.append(s_pev.n)
                            def evg2(e, dst=dst, rbuf=rbuf):
                                return e.activation(out=dst, in_=rbuf, func=AF.Sigmoid)
                            sch.emit("scalar", evg2, waits=[w(s_pev)], incs=[(s_phis, 1)])
                            rsc_hist.append(s_phis.n)
                        elif pi == 2:
                            def evv(e, dst=dst, pb=pb, bsl=bsl):
                                return e.tensor_tensor(out=dst, in0=pb, in1=bsl, op=ALU.add)
                            sch.emit("vector", evv, waits=[w(s_proj)], incs=[(s_pev, 1)])
                            pev_hist.append(s_pev.n)
                        else:
                            def evqk(e, pb=pb, bsl=bsl, rbuf=rbuf):
                                return e.tensor_tensor(out=rbuf, in0=pb, in1=bsl, op=ALU.add)
                            evw = [w(s_proj)]
                            if len(rsc_hist) >= 2:
                                evw.append((s_phis, rsc_hist[-2]))
                            sch.emit("vector", evqk, waits=evw, incs=[(s_pev, 1)])
                            pev_hist.append(s_pev.n)
                            cossl = rcos[:, tt * 32:(tt + 1) * 32]
                            sinsl = rsin[:, tt * 32:(tt + 1) * 32]
                            def rope(e, rbuf=rbuf, cossl=cossl, sinsl=sinsl):
                                v3 = lambda ap: ap.rearrange("p (h f) -> p h f", f=64)
                                lo = v3(rbuf)[:, :, 0:32]
                                hi = v3(rbuf)[:, :, 32:64]
                                cb = cossl.unsqueeze(1).to_broadcast([128, 8, 32])
                                sbx = sinsl.unsqueeze(1).to_broadcast([128, 8, 32])
                                s1 = v3(scr[:, 0:512])[:, :, 0:32]
                                s2 = v3(scr[:, 0:512])[:, :, 32:64]
                                e.tensor_tensor(out=s1, in0=lo, in1=cb, op=ALU.mult)
                                e.drain()
                                e.tensor_tensor(out=s2, in0=hi, in1=sbx, op=ALU.mult)
                                e.drain()
                                e.tensor_sub(out=s1, in0=s1, in1=s2)
                                e.drain()
                                e.tensor_tensor(out=s2, in0=lo, in1=sbx, op=ALU.mult)
                                e.drain()
                                e.tensor_tensor(out=lo, in0=hi, in1=cb, op=ALU.mult)
                                e.drain()
                                e.tensor_add(out=s2, in0=s2, in1=lo)
                                e.drain()
                                e.tensor_scalar(out=scr2[:, 0:512], in0=scr[:, 0:512], scalar1=0.0, scalar2=None, op0=ALU.max)
                                e.drain()
                                e.tensor_sub(out=rbuf, in0=scr[:, 0:512], in1=scr2[:, 0:512])
                                return e.drain()
                            sch.emit("vector", rope, waits=[w(s_pev), w(s_phis)], incs=[(s_rope, 1)])
                            def phiexp(e, rbuf=rbuf):
                                return e.activation(out=rbuf, in_=rbuf, func=AF.Exp)
                            sch.emit("scalar", phiexp, waits=[w(s_rope)], incs=[(s_phis, 1)])
                            rsc_hist.append(s_phis.n)
                            def phiadd(e, dst=dst, rbuf=rbuf):
                                return e.tensor_add(out=dst, in0=rbuf, in1=scr2[:, 0:512])
                            sch.emit("vector", phiadd, waits=[w(s_phis)], incs=[(s_rope, 1)])
                proj_tile_done = w(s_rope)
                pev_tile_done = w(s_pev)
                g_ready = w(s_phis)
                q_tm = qkvg[:, 0:D]
                k_tm = qkvg[:, D:2 * D]
                v_tm = qkvg[:, 2 * D:3 * D]
                g_tm = qkvg[:, 3 * D:4 * D]
                for cc in range(2):
                    c = tt * 2 + cc
                    pbase = cc * 64
                    for h in range(NH):
                        it = c * NH + h
                        par = it % 2
                        Ssl = Sacc[:, h * 72: h * 72 + 65]
                        Sbsl = Sbf[:, h * 72: h * 72 + 65]
                        pSb = pS[par]; pNb = pN[par]
                        ksl = k_tm[pbase:pbase + 64, h * 64:h * 64 + 64]
                        vsl = v_tm[pbase:pbase + 64, h * 64:h * 64 + 64]
                        gsl = g_tm[pbase:pbase + 64, h * 64:h * 64 + 64]
                        qck = q_tm[pbase:pbase + 64, h * 64:h * 64 + 64]
                        mw = [pev_tile_done, proj_tile_done] if (cc == 0 and h == 0) else []
                        if it == 0:
                            mw.append(w(s_Sacc))
                        if it >= 2:
                            mw.append((s_Sacc, s_Sacc.n - 1))
                        def mmS(e, ksl=ksl, vsl=vsl, pSb=pSb, pbase=pbase):
                            e.matmul(out=pSb[:, 0:64], lhsT=ksl, rhs=vsl, start=True, stop=True)
                            return e.matmul(out=pSb[:, 64:65], lhsT=ksl, rhs=ones64[pbase:pbase + 64, :], start=True, stop=True)
                        sch.emit("tensor", mmS, waits=mw, incs=[(s_mmS, 1)])
                        def qtr(e, qck=qck, pbase=pbase):
                            return e.transpose(out=pt[0][0:64, 0:64], in_=qck,
                                               identity=ident_bf[pbase:pbase + 64, pbase:pbase + 64])
                        qw = [(s_mmN, s_mmN.n)] if it >= 1 else []
                        if s_tr.n >= 1 and it == 0 and tt == 0:
                            qw.append(w(s_evac))
                        sch.emit("tensor", qtr, waits=qw, incs=[(s_mmN, 1)])
                        qtsl = qtb2[:, par * 64:par * 64 + 64]
                        def qev(e, qtsl=qtsl):
                            return e.tensor_copy(out=qtsl, in_=pt[0][0:64, 0:64])
                        sch.emit("vector", qev, waits=[w(s_mmN)], incs=[(s_mmN, 1)])
                        gfsl = gf2[:, par * 64:par * 64 + 64]
                        def sadd(e, Ssl=Ssl, Sbsl=Sbsl, pSb=pSb, gfsl=gfsl, gsl=gsl):
                            e.tensor_tensor(out=Ssl, in0=Ssl, in1=pSb[:, 0:65], op=ALU.add)
                            e.drain()
                            e.tensor_copy(out=Sbsl, in_=Ssl)
                            e.drain()
                            return e.tensor_copy(out=gfsl, in_=gsl)
                        sw = [w(s_mmS)]
                        if cc == 0 and h == 0:
                            sw.append(g_ready)
                        if it >= 2:
                            sw.append((s_epi, s_epi.n - 1))
                        sch.emit("vector", sadd, waits=sw, incs=[(s_Sacc, 1)])
                        def mmN(e, qtsl=qtsl, Sbsl=Sbsl, pNb=pNb):
                            return e.matmul(out=pNb[:, 0:65], lhsT=qtsl, rhs=Sbsl, start=True, stop=True)
                        nw = [w(s_Sacc), w(s_mmN)]
                        if it >= 2:
                            nw.append((s_epi, s_epi.n - 1))
                        sch.emit("tensor", mmN, waits=nw, incs=[(s_mmN, 1)])
                        aob = aob2[:, par * 64:par * 64 + 64]
                        nsc = nsc2[:, par * 72:par * 72 + 72]
                        def epi(e, pNb=pNb, gfsl=gfsl, aob=aob, nsc=nsc):
                            den = nsc[:, 64:65]
                            e.tensor_scalar(out=den, in0=pNb[:, 64:65], scalar1=1e-6, scalar2=None, op0=ALU.add)
                            e.drain()
                            e.reciprocal(out=den, in_=den)
                            e.drain()
                            e.tensor_scalar(out=nsc[:, 0:64], in0=pNb[:, 0:64], scalar1=den, scalar2=None, op0=ALU.mult)
                            e.drain()
                            return e.tensor_tensor(out=aob, in0=nsc[:, 0:64], in1=gfsl, op=ALU.mult)
                        ew = [w(s_mmN)]
                        if it >= 2:
                            ew.append((s_aoT, s_aoT.n))
                        sch.emit("vector", epi, waits=ew, incs=[(s_epi, 1)])
                        strip = h // 2
                        obase = (h % 2) * 64
                        def otr(e, aob=aob, obase=obase):
                            return e.transpose(out=pt[1][obase:obase + 64, 0:64], in_=aob, identity=ident_bf[0:64, 0:64])
                        ow = [w(s_epi)]
                        if it >= 1:
                            ow.append((s_aoT, s_aoT.n))
                        sch.emit("tensor", otr, waits=ow, incs=[(s_aoT, 1)])
                        dsto = aoT[obase:obase + 64, strip * T + c * 64: strip * T + (c + 1) * 64]
                        def otev(e, dsto=dsto, obase=obase):
                            return e.tensor_copy(out=dsto, in_=pt[1][obase:obase + 64, 0:64])
                        sch.emit("vector", otev, waits=[w(s_aoT)], incs=[(s_aoT, 1)])
                sch.flush(f"l0a_bs{tt}")
            if tt == NT - 1:
                pass
        scan_done = w(s_aoT)

        bs_ctx.close()

        with nc.sbuf_tensor(p.name("wof"), [128, 8 * D], BF16) as wof_t, \
             nc.sbuf_tensor(p.name("bobc"), [128, D], F32) as bobc_t, \
             nc.sbuf_tensor(p.name("xt2"), [128, 2 * D], F32) as xt2_t, \
             nc.sbuf_tensor(p.name("xin3"), [128, D], F32) as xin3_t, \
             nc.psum_tensor(p.name("pz0"), [128, 512], F32) as pz0_t, \
             nc.psum_tensor(p.name("pz1"), [128, 512], F32) as pz1_t:
            wof = wof_t[:]; bo_bcast = bobc_t[:]; xt3 = xt2_t[:]; xin3 = xin3_t[:]
            pz = (pz0_t[:], pz1_t[:])
            for kc in range(8):
                def ldwo(e, kc=kc):
                    return e.dma_start(out=wof[:, kc * D:(kc + 1) * D],
                                       in_=weights["Wo"][kc * 128:(kc + 1) * 128, :])
                sch.emit("gpsimd", ldwo, incs=[(s_wld, 16)])
            s_bo = p.sem("a0bo")
            emit_bcast_load(sch, weights["bo"], bo_bcast, s_bo)
            wo_done = w(s_wld)
            bo_done = w(s_bo)
            for i in range(NT):
                xti = xt3[:, (i % 2) * D:(i % 2 + 1) * D]
                sl = s_xt[i % 2]
                ldw2 = [] if i < 2 else [(s_xn, s_xn.n - 2)]
                def ldx2(e, xti=xti, i=i):
                    return e.dma_start(out=xti, in_=x_src[i * 128:(i + 1) * 128, :])
                sch.emit("sync", ldx2, waits=ldw2, incs=[(sl, 16)])
                emit_ln(sch, xti, xin3, stat, scr, s_stats, done=s_xn,
                        scalar_waits=[w(sl), w(s_xn)], vector_waits=[w(s_ep)])
                for hc in range(2):
                    pb = pz[hc]
                    mmw = [wo_done, scan_done] if (i == 0 and hc == 0) else []
                    if i > 0:
                        mmw = mmw + [(s_ep, s_ep.n - (1 + hc))]
                    def zmm(e, i=i, hc=hc, pb=pb):
                        inst = None
                        for c8 in range(8):
                            inst = e.matmul(out=pb, lhsT=aoT[:, c8 * T + i * 128: c8 * T + (i + 1) * 128],
                                            rhs=wof[:, c8 * D + hc * 512: c8 * D + (hc + 1) * 512],
                                            start=(c8 == 0), stop=(c8 == 7))
                        return inst
                    sch.emit("tensor", zmm, waits=mmw, incs=[(s_gt, 1)])
                def zev(e, i=i):
                    e.tensor_tensor(out=scr2[:, 0:512], in0=pz[0], in1=bo_bcast[:, 0:512], op=ALU.add)
                    e.drain()
                    e.tensor_tensor(out=scr2[:, 512:1024], in0=pz[1], in1=bo_bcast[:, 512:1024], op=ALU.add)
                    e.drain()
                    return e.tensor_add(out=scr2, in0=scr2, in1=xin3)
                zvw = [w(s_gt), w(s_xn)]
                if i == 0:
                    zvw.append(bo_done)
                sch.emit("vector", zev, waits=zvw, incs=[(s_ep, 2)])
                def stm(e, i=i):
                    return e.dma_start(out=d_xmid[i * 128:(i + 1) * 128, :], in_=scr2)
                sch.emit("sync", stm, waits=[w(s_ep), (s_st6[i % 2], s_st6[i % 2].n)], incs=[(s_st6[i % 2], 16)])
                sch.emit("vector", lambda e: e.nop(), waits=[w(s_st6[i % 2])], incs=[(s_ep, 1)])
            sch.flush("l0a_phC")
        aoT_cm.__exit__(None, None, None)


    s_rw = p.sem("a0rw")
    s_xm = [p.sem("a0ma"), p.sem("a0mb")]
    s_whe = [p.sem("a0wea"), p.sem("a0web")]
    sl_i32 = sb("sl_i32", [128, 2 * NT], I32)
    g12p = sb("g12p", [128, 2 * NT], F32)
    d_xslots = dram("l0a_xsl", [8 * 1024 + 128, D], BF16)
    d_zslots = dram("l0a_zsl", [8 * 1024 + 128, D], F32)
    gts = sb("gts", [128, NT * 8], F32)
    zrow8 = sb("zrow8", [8, T], F32)
    gb6a_e = sb("gb6a", [128, D], F32)
    gb6b_e = sb("gb6b", [128, D], F32)
    six0 = sb("six0", [128, 1], I32)
    six1 = sb("six1", [128, 1], I32)
    gix0 = sb("gix0", [128, 1], I32)
    gix1 = sb("gix1", [128, 1], I32)
    offrow = sb("offrow", [1, 8], F32)
    offbc = sb("offbc", [128, 8], F32)
    meT = sb("meT", [8, T], F32)
    exT = sb("exT", [8, T], F32)
    eq1s = sb("eq1s", [128, NT * 8], F32)
    eq2s = sb("eq2s", [128, NT * 8], F32)
    g12 = sb("g12", [128, NT * 2], F32)
    ioe_bc = sb("ioe_bc", [128, 8], F32)
    rb_bc = sb("rb_bc", [128, 8], F32)
    b2all = sb("b2all", [8, D], F32)
    cmask = sb("cmask", [8, 1], F32)
    gth = sb("gth", [8, 8], F32)
    offc = sb("offc", [8, 1], F32)
    lgs = sb("lgs", [128, 8], F32)
    prb = sb("prb", [128, 8], F32)
    pr2 = sb("pr2", [128, 8], F32)
    tp8 = sb("tp8", [128, 8], F32)
    iu1 = sb("iu1", [128, 8], U32)
    iu2 = sb("iu2", [128, 8], U32)
    mr8 = sb("mr8", [128, 8], F32)
    xmn_tm_cm = nc.sbuf_tensor(p.name("xmn_tm"), [128, NT * D], BF16)
    xmn_tm = xmn_tm_cm.__enter__()[:]
    xmnT_cm = nc.sbuf_tensor(p.name("xmnT"), [128, 8 * T], BF16)
    xmnT = xmnT_cm.__enter__()[:]

    s_zf = p.sem("a0zf")
    s_sc8 = p.sem("a0sc8")
    s_zm = p.sem("a0zm")
    zt_cm = nc.sbuf_tensor(p.name("zbf"), [128, D], BF16)
    zbf = zt_cm.__enter__()[:]
    def zfill0(e):
        e.memset(zbf, 0.0)
        e.drain()
        return e.memset(scr2, 0.0)
    sch.emit("vector", zfill0, incs=[(s_zm, 1)])
    for r_ in range(65):
        def zfill(e, r_=r_):
            return e.dma_start(out=d_xslots[r_ * 128:(r_ + 1) * 128, :], in_=zbf)
        sch.emit("sync", zfill, waits=[w(s_zm)] if r_ == 0 else [], incs=[(s_zf, 16)])
    def zfill2(e):
        return e.dma_start(out=d_zslots[8192:8320, :], in_=scr2)
    sch.emit("sync", zfill2, waits=[w(s_zm)], incs=[(s_zf, 16)])
    emit_bcast_load(sch, weights["ioe"], ioe_bc, s_rw)
    emit_bcast_load(sch, weights["router_b"], rb_bc, s_rw)
    sch.emit("sync", lambda e: e.dma_start(out=b2all, in_=weights["b2"]), incs=[(s_rw, 16)])
    sch.emit("sync", lambda e: e.dma_start(out=cmask, in_=weights["coremask"]), incs=[(s_rw, 16)])

    with nc.sbuf_tensor(p.name("xm"), [128, 2 * D], F32) as xm_t, \
         nc.sbuf_tensor(p.name("xnb2"), [128, D], BF16) as xnb2_t, \
         nc.sbuf_tensor(p.name("n2bc"), [128, D], F32) as n2bc_t, \
         nc.sbuf_tensor(p.name("rwbf"), [128, 8 * 8], BF16) as rwbf_t, \
         nc.psum_tensor(p.name("pt0b"), [128, 128], BF16) as pt0_t, \
         nc.psum_tensor(p.name("pt1b"), [128, 128], BF16) as pt1_t, \
         nc.psum_tensor(p.name("plg"), [128, 8], F32) as plg_t, \
         nc.psum_tensor(p.name("pme"), [8, 128], F32) as pme_t:
        xm2 = xm_t[:]; xnb2 = xnb2_t[:]; n2bc = n2bc_t[:]; rwbf = rwbf_t[:]
        pt = (pt0_t[:], pt1_t[:])
        plg = plg_t[:]; pme = pme_t[:]
        emit_bcast_load(sch, weights["norm2_scale"], n2bc, s_rw)
        s_rwg = p.sem("a0rwg")
        sch.emit("gpsimd", lambda e: e.dma_start(
            out=rwbf.rearrange("p (kc e) -> p kc e", e=8),
            in_=weights["router_W"].rearrange("(kc p) e -> p kc e", p=128)), incs=[(s_rwg, 16)])
        rw_done = w(s_rw)
        rwg_done = w(s_rwg)
        sch.emit("scalar", lambda e: e.nop(), waits=[rw_done])
        sch.emit("vector", lambda e: e.nop(), waits=[rw_done])
        for i in range(NT):
            xmi = xm2[:, (i % 2) * D:(i % 2 + 1) * D]
            sl = s_xm[i % 2]
            ldw = [] if i < 2 else [(s_xn, s_xn.n - 1)]
            def ldxm(e, xmi=xmi, i=i):
                return e.dma_start(out=xmi, in_=d_xmid[i * 128:(i + 1) * 128, :])
            sch.emit("sync", ldxm, waits=ldw, incs=[(sl, 16)])
            ssq = stat[:, 4:5]; rr = stat[:, 5:6]
            def rms_s(e, xmi=xmi, ssq=ssq):
                e.activation(out=scr2, in_=xmi, func=AF.Square, accum_out=ssq)
                e.drain()
                return e.activation(out=ssq, in_=ssq, func=AF.Sqrt, scale=p.cv(3))
            sch.emit("scalar", rms_s, waits=[w(sl), w(s_xn), w(s_gt)] + ([w(s_zf)] if i == 0 else []),
                     incs=[(s_phis, 1)])
            def rms_v(e, rr=rr, ssq=ssq):
                e.reciprocal(out=rr, in_=ssq)
                return e.drain()
            sch.emit("vector", rms_v, waits=[w(s_phis)], incs=[(s_rope, 1)])
            def mknorm(e, xmi=xmi, rr=rr, i=i):
                e.tensor_scalar(out=scr2, in0=xmi, scalar1=rr, scalar2=None, op0=ALU.mult)
                e.drain()
                e.tensor_tensor(out=xnb2, in0=scr2, in1=n2bc, op=ALU.mult)
                e.drain()
                return e.tensor_copy(out=xmn_tm[:, i * D:(i + 1) * D], in_=xnb2)
            sch.emit("vector", mknorm, waits=[w(s_rope), w(s_tr)], incs=[(s_xn, 1)])
            for k in range(8):
                pbank = pt[k % 2]
                evac_snapshot = s_evac.n
                waits = [w(s_xn)] if k == 0 else []
                if evac_snapshot >= 2:
                    waits.append((s_evac, evac_snapshot - 1))
                def tr(e, k=k, pbank=pbank):
                    return e.transpose(out=pbank, in_=xnb2[:, k * 128:(k + 1) * 128], identity=ident_bf)
                sch.emit("tensor", tr, waits=waits, incs=[(s_tr, 1)])
                dst = xmnT[:, k * T + i * 128: k * T + (i + 1) * 128]
                def ev(e, dst=dst, pbank=pbank):
                    return e.tensor_copy(out=dst, in_=pbank)
                sch.emit("vector", ev, waits=[w(s_tr)], incs=[(s_evac, 1)])
            def lmm(e, i=i):
                inst = None
                for kc in range(8):
                    inst = e.matmul(out=plg, lhsT=xmnT[:, kc * T + i * 128: kc * T + (i + 1) * 128],
                                    rhs=rwbf[:, kc * 8:(kc + 1) * 8],
                                    start=(kc == 0), stop=(kc == 7))
                return inst
            sch.emit("tensor", lmm, waits=[w(s_evac)] + ([(s_gt, s_gt.n)] if i > 0 else [rw_done, rwg_done]),
                     incs=[(s_mmS, 1)])
            def sm1(e):
                e.tensor_tensor(out=lgs, in0=plg, in1=rb_bc, op=ALU.add)
                e.drain()
                e.tensor_reduce(out=stat[:, 6:7], in_=lgs, axis=AX.X, op=ALU.max, negate=True)
                return e.drain()
            sch.emit("vector", sm1, waits=[w(s_mmS), w(s_phis)], incs=[(s_gt, 1)])
            def sm2(e):
                return e.activation(out=prb, in_=lgs, func=AF.Exp, bias=stat[:, 6:7], accum_out=stat[:, 7:8])
            sch.emit("scalar", sm2, waits=[w(s_gt)], incs=[(s_phis, 1)])
            def top2(e, i=i):
                e.reciprocal(out=stat[:, 7:8], in_=stat[:, 7:8])
                e.drain()
                e.tensor_scalar(out=prb, in0=prb, scalar1=stat[:, 7:8], scalar2=None, op0=ALU.mult)
                e.drain()
                e.memset(mr8, -1e30)
                e.drain()
                e.tensor_reduce(out=tp8[:, 0:1], in_=prb, axis=AX.X, op=ALU.max)
                e.drain()
                e.tensor_copy(out=mr8[:, 0:1], in_=tp8[:, 0:1])
                e.drain()
                e.match_replace(out=pr2, in_to_replace=mr8, in_values=prb, imm_value=-1e30)
                e.drain()
                e.tensor_reduce(out=tp8[:, 1:2], in_=pr2, axis=AX.X, op=ALU.max)
                e.drain()
                e.max_index(out=iu1, in_max=mr8, in_values=prb)
                e.drain()
                e.tensor_copy(out=mr8[:, 0:1], in_=tp8[:, 1:2])
                e.drain()
                e.max_index(out=iu2, in_max=mr8, in_values=pr2)
                e.drain()
                e.tensor_add(out=tp8[:, 2:3], in0=tp8[:, 0:1], in1=tp8[:, 1:2])
                e.drain()
                e.reciprocal(out=tp8[:, 2:3], in_=tp8[:, 2:3])
                e.drain()
                e.tensor_tensor(out=g12[:, i * 2:i * 2 + 1], in0=tp8[:, 0:1], in1=tp8[:, 2:3], op=ALU.mult)
                e.drain()
                e.tensor_tensor(out=g12[:, i * 2 + 1:i * 2 + 2], in0=tp8[:, 1:2], in1=tp8[:, 2:3], op=ALU.mult)
                e.drain()
                e.tensor_copy(out=tp8[:, 3:4], in_=iu1[:, 0:1])
                e.drain()
                e.tensor_copy(out=tp8[:, 4:5], in_=iu2[:, 0:1])
                e.drain()
                e.tensor_scalar(out=eq1s[:, i * 8:(i + 1) * 8], in0=ioe_bc, scalar1=tp8[:, 3:4], scalar2=None, op0=ALU.is_equal)
                e.drain()
                e.tensor_scalar(out=eq2s[:, i * 8:(i + 1) * 8], in0=ioe_bc, scalar1=tp8[:, 4:5], scalar2=None, op0=ALU.is_equal)
                e.drain()
                return e.tensor_add(out=lgs, in0=eq1s[:, i * 8:(i + 1) * 8], in1=eq2s[:, i * 8:(i + 1) * 8])
            sch.emit("vector", top2, waits=[w(s_phis)], incs=[(s_gt, 1)])
            def metr(e):
                return e.transpose(out=pme, in_=lgs, identity=ident_f)
            sch.emit("tensor", metr, waits=[w(s_gt)] + ([(s_gtT, s_gtT.n)] if i > 0 else []), incs=[(s_mmS, 1)])
            def meev(e, i=i):
                return e.tensor_copy(out=meT[:, i * 128:(i + 1) * 128], in_=pme)
            sch.emit("vector", meev, waits=[w(s_mmS)], incs=[(s_gtT, 1)])
        sch.flush("l0a_d1")

        def csum3(e):
            e.tensor_tensor_scan(out=exT, data0=meT, data1=zrow8, initial=0.0,
                                 op0=ALU.add, op1=ALU.add)
            e.drain()
            e.tensor_copy(out=gth[:, 0:1], in_=exT[:, T - 1:T])
            e.drain()
            e.tensor_sub(out=exT, in0=exT, in1=meT)
            return e.drain()

        def zr(e):
            e.memset(zrow8, 0.0)
            e.drain()
            return e.memset(gth, 0.0)
        sch.emit("vector", zr, incs=[(s_gtT, 1)])
        sch.emit("vector", csum3, waits=[w(s_gtT)], incs=[(s_gtT, 1)])
        def cst(e):
            return e.dma_start(out=(weights["cnt_out"] if mode == "a" else d_cnt), in_=gth[:, 0:1])
        sch.emit("sync", cst, waits=[w(s_gtT)], incs=[(s_cldma, 16)])
        if mode == "a":
            sch.emit("sync", lambda e: e.nop(), waits=[w(s_cldma)], incs=[(s_io, 16)])
            sch.flush("l0a_a_end")
            return s_io
        if mode == "b":
            def cldb(e):
                return e.dma_start(out=gth, in_=weights["gathin"])
            sch.emit("sync", cldb, waits=[w(s_cldma), w(s_gtT)], incs=[(s_cldma, 16)])
        elif use_collective:
            def coll(e):
                return e.collective_compute("AllGather", ALU.bypass,
                                            replica_groups=[list(range(n_cores))],
                                            ins=[d_cnt], outs=[d_gath])
            sch.emit("gpsimd", coll, waits=[w(s_cldma)], incs=[(s_coll, 16)])
            def cld(e):
                return e.dma_start(out=gth, in_=d_gath.rearrange("(r e) o -> r (e o)", r=n_cores))
            sch.emit("sync", cld, waits=[w(s_coll), w(s_gtT)], incs=[(s_cldma, 16)])
        else:
            def cld1(e):
                return e.dma_start(out=gth[0:1, :], in_=d_cnt.rearrange("(o e) u -> o (e u)", o=1))
            sch.emit("sync", cld1, waits=[w(s_cldma), w(s_gtT)], incs=[(s_cldma, 16)])
        with nc.psum_tensor(p.name("poff"), [8, 1], F32) as poff_t, \
             nc.psum_tensor(p.name("poff2"), [1, 8], F32) as poff2_t, \
             nc.psum_tensor(p.name("pofb"), [128, 8], F32) as pofb_t:
            poff = poff_t[:]
            poff2 = poff2_t[:]
            pofb = pofb_t[:]
            def offmm(e):
                return e.matmul(out=poff, lhsT=gth, rhs=cmask, start=True, stop=True)
            sch.emit("tensor", offmm, waits=[w(s_cldma), rw_done], incs=[(s_mmS, 1)])
            def offev(e):
                return e.tensor_copy(out=offc, in_=poff)
            sch.emit("vector", offev, waits=[w(s_mmS)], incs=[(s_gtT, 1)])
            def offtr(e):
                return e.transpose(out=pme[0:1, 0:8] if False else poff2, in_=offc, identity=ident_f[0:8, 0:8])
            sch.emit("tensor", offtr, waits=[w(s_gtT)], incs=[(s_mmS, 1)])
            def offr(e):
                return e.tensor_copy(out=offrow, in_=poff2)
            sch.emit("vector", offr, waits=[w(s_mmS)], incs=[(s_gtT, 1)])
            def offbm(e):
                return e.matmul(out=pofb, lhsT=consts["onesf"], rhs=offrow, start=True, stop=True)
            sch.emit("tensor", offbm, waits=[w(s_gtT)], incs=[(s_mmS, 1)])
            def offbe(e):
                return e.tensor_copy(out=offbc, in_=pofb)
            sch.emit("vector", offbe, waits=[w(s_mmS)], incs=[(s_gtT, 1)])
            sch.flush("l0a_d2")

        with nc.psum_tensor(p.name("pex"), [128, 8], F32) as pex_t:
            pex = pex_t[:]
            for i in range(NT):
                def extr(e, i=i):
                    return e.transpose(out=pex, in_=exT[:, i * 128:(i + 1) * 128], identity=ident_f[0:8, 0:8])
                sch.emit("tensor", extr, waits=[w(s_gtT)] + ([(s_gt, s_gt.n)] if i > 0 else []),
                         incs=[(s_mmS, 1)])
                def capg(e, i=i):
                    eq1 = eq1s[:, i * 8:(i + 1) * 8]
                    eq2 = eq2s[:, i * 8:(i + 1) * 8]
                    e.tensor_tensor(out=lgs, in0=eq1, in1=pex, op=ALU.mult)
                    e.drain()
                    e.tensor_reduce(out=tp8[:, 0:1], in_=lgs, axis=AX.X, op=ALU.add)
                    e.drain()
                    e.tensor_tensor(out=lgs, in0=eq2, in1=pex, op=ALU.mult)
                    e.drain()
                    e.tensor_reduce(out=tp8[:, 1:2], in_=lgs, axis=AX.X, op=ALU.add)
                    e.drain()
                    e.tensor_tensor(out=lgs, in0=eq1, in1=offbc, op=ALU.mult)
                    e.drain()
                    e.tensor_reduce(out=tp8[:, 6:7], in_=lgs, axis=AX.X, op=ALU.add)
                    e.drain()
                    e.tensor_tensor(out=lgs, in0=eq2, in1=offbc, op=ALU.mult)
                    e.drain()
                    e.tensor_reduce(out=tp8[:, 7:8], in_=lgs, axis=AX.X, op=ALU.add)
                    e.drain()
                    e.tensor_tensor(out=tp8[:, 6:7], in0=tp8[:, 6:7], in1=tp8[:, 0:1], op=ALU.add)
                    e.drain()
                    e.tensor_tensor(out=tp8[:, 7:8], in0=tp8[:, 7:8], in1=tp8[:, 1:2], op=ALU.add)
                    e.drain()
                    e.tensor_scalar(out=tp8[:, 2:3], in0=tp8[:, 6:7], scalar1=CAP, scalar2=None, op0=ALU.is_lt)
                    e.drain()
                    e.tensor_scalar(out=tp8[:, 3:4], in0=tp8[:, 7:8], scalar1=CAP, scalar2=None, op0=ALU.is_lt)
                    e.drain()
                    e.tensor_scalar(out=stat[:, 4:5], in0=tp8[:, 0:1], scalar1=1024.0, scalar2=None, op0=ALU.is_lt)
                    e.drain()
                    e.tensor_scalar(out=stat[:, 5:6], in0=tp8[:, 1:2], scalar1=1024.0, scalar2=None, op0=ALU.is_lt)
                    e.drain()
                    e.tensor_tensor(out=tp8[:, 2:3], in0=tp8[:, 2:3], in1=stat[:, 4:5], op=ALU.mult)
                    e.drain()
                    e.tensor_tensor(out=tp8[:, 3:4], in0=tp8[:, 3:4], in1=stat[:, 5:6], op=ALU.mult)
                    e.drain()
                    e.tensor_tensor(out=g12p[:, 2 * i:2 * i + 1], in0=g12[:, i * 2:i * 2 + 1], in1=tp8[:, 2:3], op=ALU.mult)
                    e.drain()
                    e.tensor_tensor(out=g12p[:, 2 * i + 1:2 * i + 2], in0=g12[:, i * 2 + 1:i * 2 + 2], in1=tp8[:, 3:4], op=ALU.mult)
                    e.drain()
                    e.tensor_tensor(out=lgs, in0=eq1, in1=ioe_bc, op=ALU.mult)
                    e.drain()
                    e.tensor_reduce(out=tp8[:, 4:5], in_=lgs, axis=AX.X, op=ALU.add)
                    e.drain()
                    e.tensor_tensor(out=lgs, in0=eq2, in1=ioe_bc, op=ALU.mult)
                    e.drain()
                    e.tensor_reduce(out=tp8[:, 5:6], in_=lgs, axis=AX.X, op=ALU.add)
                    e.drain()
                    e.tensor_scalar(out=tp8[:, 4:5], in0=tp8[:, 4:5], scalar1=1024.0, scalar2=tp8[:, 0:1], op0=ALU.mult, op1=ALU.add)
                    e.drain()
                    e.tensor_scalar(out=tp8[:, 5:6], in0=tp8[:, 5:6], scalar1=1024.0, scalar2=tp8[:, 1:2], op0=ALU.mult, op1=ALU.add)
                    e.drain()
                    e.tensor_scalar(out=stat[:, 4:5], in0=stat[:, 4:5], scalar1=-1.0, scalar2=1.0, op0=ALU.mult, op1=ALU.add)
                    e.drain()
                    e.tensor_scalar(out=stat[:, 5:6], in0=stat[:, 5:6], scalar1=-1.0, scalar2=1.0, op0=ALU.mult, op1=ALU.add)
                    e.drain()
                    e.scalar_tensor_tensor(out=tp8[:, 4:5], in0=stat[:, 4:5], scalar=1e9, in1=tp8[:, 4:5], op0=ALU.mult, op1=ALU.add)
                    e.drain()
                    e.scalar_tensor_tensor(out=tp8[:, 5:6], in0=stat[:, 5:6], scalar=1e9, in1=tp8[:, 5:6], op0=ALU.mult, op1=ALU.add)
                    e.drain()
                    e.tensor_scalar(out=tp8[:, 4:5], in0=tp8[:, 4:5], scalar1=8192.0, scalar2=None, op0=ALU.min)
                    e.drain()
                    e.tensor_scalar(out=tp8[:, 5:6], in0=tp8[:, 5:6], scalar1=8192.0, scalar2=None, op0=ALU.min)
                    e.drain()
                    e.tensor_copy(out=sl_i32[:, 2 * i:2 * i + 1], in_=tp8[:, 4:5])
                    e.drain()
                    e.tensor_copy(out=sl_i32[:, 2 * i + 1:2 * i + 2], in_=tp8[:, 5:6])
                    e.drain()
                    e.tensor_scalar(out=lgs, in0=eq1, scalar1=g12p[:, 2 * i:2 * i + 1], scalar2=None, op0=ALU.mult)
                    e.drain()
                    e.tensor_scalar(out=prb, in0=eq2, scalar1=g12p[:, 2 * i + 1:2 * i + 2], scalar2=None, op0=ALU.mult)
                    e.drain()
                    return e.tensor_add(out=gts[:, i * 8:(i + 1) * 8], in0=lgs, in1=prb)
                sch.emit("vector", capg, waits=[w(s_mmS)], incs=[(s_gt, 1)])
                for sc in range(2):
                    sixb = six0 if sc == 0 else six1
                    def scpy(e, i=i, sc=sc, sixb=sixb):
                        return e.tensor_copy(out=sixb, in_=sl_i32[:, 2 * i + sc:2 * i + sc + 1])
                    sch.emit("vector", scpy, waits=[w(s_gt), w(s_sc8)], incs=[(s_gt, 1)])
                    def scat(e, i=i, sc=sc, sixb=sixb):
                        return e.indirect_dma_start(
                            out=d_xslots, out_offset=IndirectOffsetOnAxis(sixb, 0),
                            in_=xmn_tm[:, i * D:(i + 1) * D], in_offset=None)
                    sch.emit("gpsimd", scat, waits=[w(s_gt), w(s_zf)], incs=[(s_sc8, 16)])
                def gtr(e, i=i):
                    return e.transpose(out=pme, in_=gts[:, i * 8:(i + 1) * 8], identity=ident_f)
                sch.emit("tensor", gtr, waits=[w(s_gt)] + [(s_gtT, s_gtT.n)], incs=[(s_mmS, 1)])
                def gev(e, i=i):
                    return e.tensor_copy(out=meT[:, i * 128:(i + 1) * 128], in_=pme)
                sch.emit("vector", gev, waits=[w(s_mmS)], incs=[(s_gtT, 1)])
            sch.flush("l0a_d3")
    zt_cm.__exit__(None, None, None)
    xmnT_cm.__exit__(None, None, None)
    xmn_tm_cm.__exit__(None, None, None)
    gates_done = w(s_gtT)

    with nc.sbuf_tensor(p.name("whe"), [128, 2 * 8 * 512], BF16) as whe_t, \
         nc.sbuf_tensor(p.name("w2s"), [128, 4 * 512], BF16) as w2s_t, \
         nc.sbuf_tensor(p.name("hmF"), [128, 32 * 1024], BF16) as hmF_t, \
         nc.sbuf_tensor(p.name("xesT"), [128, 2 * 8 * 1024], BF16) as xesT_t, \
         nc.sbuf_tensor(p.name("xsl"), [128, 2 * D], BF16) as xsl_t, \
         nc.sbuf_tensor(p.name("zev6"), [128, 8 * 512], F32) as zev6_t, \
         nc.sbuf_tensor(p.name("b1eT"), [128, 8 * 32], F32) as b1eT_t, \
         nc.psum_tensor(p.name("pe0"), [128, 512], F32) as pe0_t, \
         nc.psum_tensor(p.name("pe1"), [128, 512], F32) as pe1_t, \
         nc.psum_tensor(p.name("pz0"), [128, 512], F32) as pz0_t, \
         nc.psum_tensor(p.name("pz1"), [128, 512], F32) as pz1_t, \
         nc.psum_tensor(p.name("pz2"), [128, 512], F32) as pz2_t, \
         nc.psum_tensor(p.name("pz3"), [128, 512], F32) as pz3_t, \
         nc.psum_tensor(p.name("pts0"), [128, 128], BF16) as pts0_t, \
         nc.psum_tensor(p.name("pts1"), [128, 128], BF16) as pts1_t:
        whe2 = whe_t[:]; w2s = w2s_t[:]; hmF = hmF_t[:]
        xesT2 = xesT_t[:]; xsl2 = xsl_t[:]; zev6 = zev6_t[:]
        b1eT = b1eT_t[:]
        pe = (pe0_t[:], pe1_t[:])
        pz = (pz0_t[:], pz1_t[:], pz2_t[:], pz3_t[:])
        pts = (pts0_t[:], pts1_t[:])
        s_xld = [p.sem("a0xla"), p.sem("a0xlb")]
        s_xtr = p.get_sem("tr")
        s_xev = p.get_sem("ev")
        s_w2t = [p.sem("a0w2t0"), p.sem("a0w2t1"), p.sem("a0w2t2"), p.sem("a0w2t3")]
        s_ze = [p.get_sem("ze0"), p.get_sem("ze1"), p.get_sem("ze2"), p.get_sem("ze3")]
        s_zsl6 = [p.sem("a0z6a"), p.sem("a0z6b")]
        sch.emit("sync", lambda e: e.dma_start(out=b1eT, in_=weights["b1T"]), incs=[(s_rw, 16)])
        b1_done = w(s_rw)
        scat_done = w(s_sc8)
        mm1_hist = []
        mm1_hist_f = []
        mm2_e_hist = []
        g_tile = 0
        ps_idx = 0
        for ei in range(E):
            slote = ei % 2
            xesT = xesT2[:, slote * 8 * 1024:(slote + 1) * 8 * 1024]
            for mt in range(8):
                xsl = xsl2[:, (mt % 2) * D:(mt % 2 + 1) * D]
                xw = [scat_done] if (ei == 0 and mt == 0) else []
                if ei * 8 + mt >= 2:
                    xw.append((s_xtr, s_xtr.n - 8))
                def xld(e, ei=ei, mt=mt, xsl=xsl):
                    return e.dma_start(out=xsl, in_=d_xslots[ei * 1024 + mt * 128: ei * 1024 + (mt + 1) * 128, :])
                sch.emit("sync", xld, waits=xw, incs=[(s_xld[mt % 2], 16)])
                for k in range(8):
                    pbank = pts[k % 2]
                    tw = [w(s_xld[mt % 2])] if k == 0 else []
                    if s_xev.n >= 2:
                        tw.append((s_xev, s_xev.n - 1))
                    if ei >= 2 and mt == 0 and k == 0:
                        tw.append((s_mm1, mm1_hist[ei - 2]))
                    def xtr(e, k=k, pbank=pbank, xsl=xsl):
                        return e.transpose(out=pbank, in_=xsl[:, k * 128:(k + 1) * 128], identity=ident_bf)
                    sch.emit("tensor", xtr, waits=tw, incs=[(s_xtr, 1)])
                    dst = xesT[:, k * 1024 + mt * 128: k * 1024 + (mt + 1) * 128]
                    def xev(e, dst=dst, pbank=pbank):
                        return e.tensor_copy(out=dst, in_=pbank)
                    sch.emit("vector", xev, waits=[w(s_xtr)], incs=[(s_xev, 1)])
            xesT_ready = w(s_xev)
            for fc in range(8):
                ci = ei * 8 + fc
                slot = ci % 2
                whe = whe2[:, slot * 8 * 512:(slot + 1) * 8 * 512]
                lw_waits = [] if ci < 2 else [(s_mm1, mm1_hist_f[ci - 2])]
                for kc in range(8):
                    def lwh(e, ei=ei, fc=fc, kc=kc, whe=whe):
                        return e.dma_start(out=whe[:, kc * 512:(kc + 1) * 512],
                                           in_=weights["W1"][ei * 1024 + kc * 128: ei * 1024 + (kc + 1) * 128,
                                                             fc * 512:(fc + 1) * 512])
                    sch.emit("gpsimd", lwh, waits=lw_waits if kc == 0 else [], incs=[(s_whe[slot], 16)])
                wh_done = w(s_whe[slot])
                for f4 in range(4):
                    for mc in range(2):
                        idx = ci * 8 + f4 * 2 + mc
                        pb = pe[idx % 2]
                        waits = [wh_done] if (f4 == 0 and mc == 0) else []
                        if fc == 0 and f4 == 0 and mc == 0:
                            waits.append(xesT_ready)
                        if ci == 0 and f4 == 0 and mc == 0:
                            waits.append(b1_done)
                        if idx >= 2:
                            waits.append((s_pev, s_pev.n - 1))
                        def mmh(e, f4=f4, mc=mc, pb=pb, whe=whe, xesT=xesT):
                            inst = None
                            for kc in range(8):
                                inst = e.matmul(out=pb,
                                                lhsT=whe[:, kc * 512 + f4 * 128: kc * 512 + (f4 + 1) * 128],
                                                rhs=xesT[:, kc * 1024 + mc * 512: kc * 1024 + (mc + 1) * 512],
                                                start=(kc == 0), stop=(kc == 7))
                            return inst
                        sch.emit("tensor", mmh, waits=waits, incs=[(s_mm1, 1)])
                        t_h = fc * 4 + f4
                        dst = hmF[:, t_h * 1024 + mc * 512: t_h * 1024 + (mc + 1) * 512]
                        bidx = ei * 32 + fc * 4 + f4
                        gw = [w(s_mm1)]
                        if ei > 0 and fc == 0 and f4 == 0 and mc == 0:
                            gw.append((s_mm2, mm2_e_hist[ei - 1]))
                        def gelh(e, dst=dst, pb=pb, bidx=bidx):
                            return e.activation(out=dst, in_=pb, func=AF.Gelu, bias=b1eT[:, bidx:bidx + 1])
                        sch.emit("scalar", gelh, waits=gw, incs=[(s_pev, 1)])
                mm1_hist_f.append(s_mm1.n)
                if fc == 3 or fc == 7:
                    sch.flush(f"l0a_d5f1_{ei}_{fc}")
            mm1_hist.append(s_mm1.n)
            gel_done = w(s_pev)
            for dc in range(2):
                for mtg in range(2):
                    for t in range(32):
                        par = g_tile % 4
                        lw2 = []
                        if g_tile >= 4:
                            lw2.append((s_mm2, s_mm2.n - 12))
                        def lw2t(e, ei=ei, dc=dc, t=t, par=par):
                            return e.dma_start(out=w2s[:, par * 512:(par + 1) * 512],
                                               in_=weights["W2"][ei * 4096 + t * 128: ei * 4096 + (t + 1) * 128,
                                                                 dc * 512:(dc + 1) * 512])
                        sch.emit("gpsimd", lw2t, waits=lw2, incs=[(s_w2t[par], 16)])
                        for mt in range(4):
                            mw = [w(s_w2t[par])] if mt == 0 else []
                            if t == 0:
                                if dc == 0 and mtg == 0:
                                    mw.append(gel_done)
                                if ps_idx > 0:
                                    mw.append((s_ze[mt], s_ze[mt].n))
                            def mmz(e, mt=mt, mtg=mtg, t=t, par=par):
                                return e.matmul(out=pz[mt],
                                                lhsT=hmF[:, t * 1024 + (mtg * 4 + mt) * 128: t * 1024 + (mtg * 4 + mt + 1) * 128],
                                                rhs=w2s[:, par * 512:(par + 1) * 512],
                                                start=(t == 0), stop=(t == 31))
                            sch.emit("tensor", mmz, waits=mw, incs=[(s_mm2, 1)])
                        g_tile += 1
                    n_mm2 = s_mm2.n
                    for mt in range(4):
                        zt6 = zev6[:, ((ps_idx % 2) * 4 + mt) * 512: ((ps_idx % 2) * 4 + mt + 1) * 512]
                        ew = [(s_mm2, n_mm2 - 3 + mt)]
                        if ps_idx >= 2 and mt == 0:
                            ew.append(w(s_zsl6[ps_idx % 2]))
                        def zev_(e, zt6=zt6, mt=mt):
                            return e.tensor_copy(out=zt6, in_=pz[mt])
                        sch.emit("vector", zev_, waits=ew, incs=[(s_ze[mt], 1)])
                        def zst6(e, ei=ei, dc=dc, mtg=mtg, mt=mt, zt6=zt6):
                            return e.dma_start(
                                out=d_zslots[ei * 1024 + (mtg * 4 + mt) * 128: ei * 1024 + (mtg * 4 + mt + 1) * 128,
                                             dc * 512:(dc + 1) * 512],
                                in_=zt6)
                        sch.emit("sync", zst6, waits=[(s_ze[mt], s_ze[mt].n)], incs=[(s_zsl6[ps_idx % 2], 16)])
                    ps_idx += 1
                sch.flush(f"l0a_d5f2_{ei}_{dc}")
            mm2_e_hist.append(s_mm2.n)
        sch.flush("l0a_d5_end")
        zsl6_done0 = w(s_zsl6[0]); zsl6_done1 = w(s_zsl6[1])

    with nc.sbuf_tensor(p.name("xt7"), [128, 2 * D], F32) as xt7_t, \
         nc.sbuf_tensor(p.name("xm6"), [128, 2 * D], F32) as xm6_t, \
         nc.sbuf_tensor(p.name("zb6"), [128, 2 * D], F32) as zb6_t, \
         nc.psum_tensor(p.name("pb20"), [128, 512], F32) as pb20_t, \
         nc.psum_tensor(p.name("pb21"), [128, 512], F32) as pb21_t:
        xt7 = xt7_t[:]; xm6 = xm6_t[:]; zb6 = zb6_t[:]
        pb2 = (pb20_t[:], pb21_t[:])
        gb6 = (gb6a_e, gb6b_e)
        s_g6 = p.sem("a0g6")
        s_g6v = p.get_sem("msk")
        def g6init(e):
            e.memset(gb6[0], 0.0)
            e.drain()
            return e.memset(gb6[1], 0.0)
        sch.emit("vector", g6init, incs=[(s_g6v, 1)])
        for i in range(NT):
            xmi = xm6[:, (i % 2) * D:(i % 2 + 1) * D]
            z_t = zb6[:, (i % 2) * D:(i % 2 + 1) * D]
            xti = xt7[:, (i % 2) * D:(i % 2 + 1) * D]
            sl = s_xm[i % 2]
            ldw = [] if i < 2 else [(s_za, s_za.n - 2)]
            def ldxm(e, xmi=xmi, i=i):
                return e.dma_start(out=xmi, in_=d_xmid[i * 128:(i + 1) * 128, :])
            sch.emit("sync", ldxm, waits=ldw, incs=[(sl, 16)])
            for hc in range(2):
                pb = pb2[hc]
                mmw = [gates_done] if (i == 0 and hc == 0) else []
                if i > 0:
                    mmw.append((s_za, s_za.n - 1 + hc))
                def bmm(e, i=i, hc=hc, pb=pb):
                    return e.matmul(out=pb, lhsT=meT[:, i * 128:(i + 1) * 128],
                                    rhs=b2all[:, hc * 512:(hc + 1) * 512], start=True, stop=True)
                sch.emit("tensor", bmm, waits=mmw, incs=[(s_mmS, 1)])
            zw0 = [w(s_mmS), w(sl)]
            if i >= 2:
                zw0.append((s_ep, s_ep.n - 2))
            def zs0(e, i=i, xmi=xmi, z_t=z_t):
                return e.tensor_tensor(out=z_t[:, 0:512], in0=pb2[0], in1=xmi[:, 0:512], op=ALU.add)
            sch.emit("vector", zs0, waits=zw0, incs=[(s_za, 1)])
            def zs1(e, i=i, xmi=xmi, z_t=z_t):
                return e.tensor_tensor(out=z_t[:, 512:1024], in0=pb2[1], in1=xmi[:, 512:1024], op=ALU.add)
            sch.emit("vector", zs1, incs=[(s_za, 1)])
            def ldx7(e, xti=xti, i=i):
                return e.dma_start(out=xti, in_=x_src[i * 128:(i + 1) * 128, :])
            sch.emit("sync", ldx7, waits=[] if i < 2 else [(s_ep, s_ep.n - 2), w(s_st6[i % 2])],
                     incs=[(s_xt[i % 2], 16)])
            for sc in range(2):
                gb = gb6[sc]
                gw6 = [zsl6_done0, zsl6_done1, w(s_g6v)] if (i == 0 and sc == 0) else [(s_g6v, s_g6v.n)]
                gixb = gix0 if sc == 0 else gix1
                def g6c(e, i=i, sc=sc, gixb=gixb):
                    return e.tensor_copy(out=gixb, in_=sl_i32[:, 2 * i + sc:2 * i + sc + 1])
                sch.emit("vector", g6c, waits=gw6 + [w(s_g6)], incs=[(s_g6v, 1)])
                def g6(e, gb=gb, gixb=gixb):
                    return e.indirect_dma_start(
                        out=gb, out_offset=None,
                        in_=d_zslots, in_offset=IndirectOffsetOnAxis(gixb, 0))
                sch.emit("gpsimd", g6, waits=[w(s_g6v)], incs=[(s_g6, 16)])
                def g6a(e, i=i, sc=sc, gb=gb, z_t=z_t):
                    return e.scalar_tensor_tensor(out=z_t, in0=gb, scalar=g12p[:, 2 * i + sc:2 * i + sc + 1],
                                                  in1=z_t, op0=ALU.mult, op1=ALU.add)
                sch.emit("vector", g6a, waits=[w(s_g6), (s_za, s_za.n)], incs=[(s_g6v, 1)])
            emit_ln(sch, z_t, scr2, stat, scr, s_stats, done=s_ep,
                    scalar_waits=[w(s_g6v), w(s_ep), w(s_xn)] if i == 0 else [w(s_g6v), w(s_ep)],
                    vector_waits=[w(s_ep)] if i > 0 else [])
            def xadd(e, xti=xti):
                return e.tensor_tensor(out=xti, in0=xti, in1=scr2, op=ALU.add)
            sch.emit("vector", xadd, waits=[w(s_xt[i % 2]), w(s_ep)], incs=[(s_ep, 1)])
            def stx(e, xti=xti, i=i):
                return e.dma_start(out=x_dst[i * 128:(i + 1) * 128, :], in_=xti)
            sch.emit("sync", stx, waits=[w(s_ep)], incs=[(s_st6[i % 2], 16)])
            if i % 4 == 3:
                sch.flush(f"l0a_d6_{i}")
        sch.emit("sync", lambda e: e.nop(), waits=[w(s_st6[0]), w(s_st6[1])], incs=[(s_io, 16)])
        sch.flush("l0a_d6")
    ctx.close()
    return s_io

L0_ATTN_DECL = {
    "norm1_scale": ([1, D], F32), "norm2_scale": ([1, D], F32),
    "Wq": ([1024, 1024], F32), "Wk": ([1024, 1024], F32),
    "Wv": ([1024, 1024], F32), "Wg": ([1024, 1024], F32), "Wo": ([1024, 1024], F32),
    "bq": ([1, D], F32), "bk": ([1, D], F32), "bv": ([1, D], F32), "bg": ([1, D], F32),
    "bo": ([1, D], F32),
    "ropecos": ([T, 32], F32), "ropesin": ([T, 32], F32),
    "router_W": ([1024, 8], F32), "router_b": ([1, 8], F32),
    "ioe": ([1, 8], F32), "coremask": ([8, 1], F32),
    "W1": ([8 * 1024, 4096], F32), "b1T": ([128, 8 * 32], F32),
    "W2": ([8 * 4096, 1024], F32), "b2": ([8, 1024], F32),
}


def l0_attn_weights(inp, core_id, pre=""):
    half = 32
    inv = 1.0 / 10000.0 ** (np.arange(half, dtype=np.float32) / half)
    ang = np.arange(T, dtype=np.float32)[:, None] * inv[None, :]
    cmask = np.zeros((8, 1), np.float32)
    cmask[:core_id, 0] = 1.0
    return {
        pre + "norm1_scale": inp["L0_attn_norm1_scale"].reshape(1, D),
        pre + "norm2_scale": inp["L0_attn_norm2_scale"].reshape(1, D),
        pre + "Wq": inp["L0_attn_Wq"], pre + "Wk": inp["L0_attn_Wk"],
        pre + "Wv": inp["L0_attn_Wv"], pre + "Wg": inp["L0_attn_Wg"],
        pre + "Wo": inp["L0_attn_Wo"],
        pre + "bq": inp["L0_attn_bq"].reshape(1, D), pre + "bk": inp["L0_attn_bk"].reshape(1, D),
        pre + "bv": inp["L0_attn_bv"].reshape(1, D), pre + "bg": inp["L0_attn_bg"].reshape(1, D),
        pre + "bo": inp["L0_attn_bo"].reshape(1, D),
        pre + "ropecos": np.cos(ang), pre + "ropesin": np.sin(ang),
        pre + "router_W": inp["L0_attn_router_W"],
        pre + "router_b": inp["L0_attn_router_b"].reshape(1, 8),
        pre + "ioe": np.arange(8, dtype=np.float32).reshape(1, 8),
        pre + "coremask": cmask,
        pre + "W1": np.ascontiguousarray(inp["L0_attn_W1"].reshape(8 * 1024, 4096)),
        pre + "b1T": np.ascontiguousarray(inp["L0_attn_b1"].reshape(8, 32, 128).transpose(2, 0, 1).reshape(128, 256)),
        pre + "W2": np.ascontiguousarray(inp["L0_attn_W2"].reshape(8 * 4096, 1024)),
        pre + "b2": inp["L0_attn_b2"],
    }


def emit_l0_attn_b(p, x_src, x_dst, weights, consts):
    return emit_l0_attn(p, x_src, x_dst, weights, consts, mode="b")


L0_ATTN_B_DECL = dict(L0_ATTN_DECL)
L0_ATTN_B_DECL["xmid_in"] = ([T, D], F32)
L0_ATTN_B_DECL["gathin"] = ([8, 8], F32)
_PROGS = {}


def _build_l0a_program():
    p = Prog()
    nc = p.nc
    x_in = nc.dram_tensor("x_in", [T, D], F32, kind="ExternalInput").ap()
    ident_in = nc.dram_tensor("ident_in", [128, 128], BF16, kind="ExternalInput").ap()
    identf_in = nc.dram_tensor("identf_in", [128, 128], F32, kind="ExternalInput").ap()
    cvec_in = nc.dram_tensor("cvec_in", [128, 8], F32, kind="ExternalInput").ap()
    wmap = {}
    for name, (shape, dt) in L0_ATTN_DECL.items():
        wmap[name] = nc.dram_tensor("b0_" + name, list(shape), dt, kind="ExternalInput").ap()
    wmap["xmid_out"] = nc.dram_tensor("xmid_out", [T, D], F32, kind="ExternalOutput").ap()
    wmap["cnt_out"] = nc.dram_tensor("cnt_out", [8, 1], F32, kind="ExternalOutput").ap()
    ident_bf = p.sb("ident_bf", [128, 128], BF16)
    ident_f = p.sb("ident_f", [128, 128], F32)
    onesf = p.sb("onesf", [1, 128], F32)
    cvec = p.sb("cvec", [128, 8], F32)
    p.epsap = cvec[:, 0:1]
    p.cv = lambda i: cvec[:, i:i + 1]
    s_io = p.get_sem("io")
    sch = Sched(p)
    sch.emit("sync", lambda e: e.dma_start(out=ident_bf, in_=ident_in), incs=[(s_io, 16)])
    sch.emit("sync", lambda e: e.dma_start(out=ident_f, in_=identf_in), incs=[(s_io, 16)])
    sch.emit("sync", lambda e: e.dma_start(out=cvec, in_=cvec_in), incs=[(s_io, 16)])
    sch.emit("vector", lambda e: e.memset(onesf, 1.0), incs=[(s_io, 1)])
    for eng in ("scalar", "vector", "tensor", "gpsimd"):
        sch.emit(eng, lambda e: e.nop(), waits=[w(s_io)])
    sch.flush("init")
    consts = {"ident_bf": ident_bf, "ident_f": ident_f, "onesf": onesf, "cvec": cvec}
    emit_l0_attn(p, x_in, None, wmap, consts, mode="a")
    return p


def _get_progs():
    if "v2" not in _PROGS:
        _PROGS["v2"] = {
            "l0a": _build_l0a_program(),
            "rest": build_program([
                (emit_l0_attn_b, L0_ATTN_B_DECL),
                (emit_l0_ffn, L0_FFN_DECL),
                (emit_l1_attn, L1_ATTN_DECL),
                (emit_l1_ffn, L1_FFN_DECL),
            ]),
        }
    return _PROGS["v2"]


def kernel(**inputs):
    progs = _get_progs()
    base = base_inputs()
    x = np.ascontiguousarray(inputs["L0_attn_x"], dtype=np.float32)
    n = 8
    l0w = [l0_attn_weights(inputs, core_id=c) for c in range(n)]
    l0fw = l0_ffn_weights(inputs)
    l1aw = l1_attn_weights(inputs)
    l1fw = l1_ffn_weights(inputs)

    in_maps = []
    for c in range(n):
        m = dict(base)
        m["x_in"] = np.ascontiguousarray(x[c])
        for k, v in l0w[c].items():
            m["b0_" + k] = v
        in_maps.append(m)
    resA = run_bass_kernel_spmd(progs["l0a"].nc, in_maps, core_ids=list(range(n)))
    gath = np.ascontiguousarray(
        np.concatenate([resA.results[c]["cnt_out"].reshape(1, 8) for c in range(n)], axis=0))

    in_maps = []
    for c in range(n):
        m = dict(base)
        m["x_in"] = np.ascontiguousarray(x[c])
        for k, v in l0w[c].items():
            m["b0_" + k] = v
        m["b0_xmid_in"] = resA.results[c]["xmid_out"]
        m["b0_gathin"] = gath
        for k, v in l0fw.items():
            m["b1_" + k] = v
        for k, v in l1aw.items():
            m["b2_" + k] = v
        for k, v in l1fw.items():
            m["b3_" + k] = v
        in_maps.append(m)
    res = run_bass_kernel_spmd(progs["rest"].nc, in_maps, core_ids=list(range(n)))
    return np.stack([res.results[c]["x_out"] for c in range(n)]).astype(np.float32)


# version 09
# speedup vs baseline: 1.3715x; 1.1022x over previous
import numpy as np
from contextlib import ExitStack
import concourse.bass as bass
import concourse.mybir as mybir
from concourse.bass_utils import run_bass_kernel_spmd
from concourse.bass_core_rust import IndirectOffsetOnAxis

F32 = mybir.dt.float32
BF16 = mybir.dt.bfloat16
I32 = mybir.dt.int32
U32 = mybir.dt.uint32
AF = mybir.ActivationFunctionType
ALU = mybir.AluOpType
AX = mybir.AxisListType

T = 2048
D = 1024
NT = T // 128
EPS_LN = 1e-5
ENGINES = ("sync", "tensor", "vector", "scalar", "gpsimd")


class Sem:
    __slots__ = ("h", "n", "name")
    def __init__(self, ctx, nc, name):
        self.h = ctx.enter_context(nc.semaphore(name))
        self.n = 0
        self.name = name


class Prog:
    def __init__(self):
        self.nc = bass.Bass("TRN2", target_bir_lowering=False, num_devices=8)
        self.ctx = ExitStack()
        self.uid = 0

    def sem(self, name):
        return Sem(self.ctx, self.nc, self.name(name))

    def get_sem(self, name):
        if not hasattr(self, "_sems"):
            self._sems = {}
        if name not in self._sems:
            self._sems[name] = Sem(self.ctx, self.nc, name)
        return self._sems[name]

    def sb(self, name, shape, dtype):
        return self.ctx.enter_context(self.nc.sbuf_tensor(self.name(name), shape, dtype))[:]

    def ps(self, name, shape, dtype):
        return self.ctx.enter_context(self.nc.psum_tensor(self.name(name), shape, dtype))[:]

    def name(self, base):
        self.uid += 1
        return f"{base}_{self.uid}"


class Sched:
    def __init__(self, p: Prog):
        self.p = p
        self.items = {e: [] for e in ENGINES}
        if not hasattr(p, "_waited"):
            p._waited = {e: {} for e in ENGINES}

    def emit(self, eng, fn, waits=(), incs=()):
        for s, k in incs:
            s.n += k
        seen = self.p._waited[eng]
        best = {}
        for s, v in waits:
            if v > 0 and v > best.get(s.h, -1):
                best[s.h] = v
        ws = []
        for h, v in best.items():
            if v > seen.get(h, 0):
                ws.append((h, v))
                seen[h] = v
        self.items[eng].append((fn, ws, [(s.h, k) for s, k in incs]))

    def flush(self, name=None):
        p = self.p
        if not any(self.items.values()):
            return
        with p.nc.Block(name) as blk:
            for eng in ENGINES:
                items = self.items[eng]
                if not items:
                    continue
                def body(e, items=items):
                    for fn, waits, incs in items:
                        for h, v in waits:
                            e.wait_ge(h, v)
                        inst = fn(e)
                        for h, k in incs:
                            if inst is None:
                                inst = e.nop()
                            inst.then_inc(h, k)
                getattr(blk, eng)(body)
        self.items = {e: [] for e in ENGINES}


def w(s: Sem, v=None):
    return (s, s.n if v is None else v)


def _emit_ln_inner(sch, x_ap, out_ap, stat, scr, s_stats, done=None,
            gamma=None, beta=None, scalar_waits=(), vector_waits=(), eps_ap=None):
    sum_ = stat[:, 0:1]; negmean = stat[:, 1:2]; ssq = stat[:, 2:3]; rstd = stat[:, 3:4]

    def sc(e):
        e.activation(out=scr, in_=x_ap, func=AF.Identity, accum_out=sum_)
        e.drain()
        e.activation(out=negmean, in_=sum_, func=AF.Identity, scale=sch.p.cv(2))
        e.drain()
        e.activation(out=scr, in_=x_ap, func=AF.Square, bias=negmean, accum_out=ssq)
        e.drain()
        return e.activation(out=rstd, in_=ssq, func=AF.Sqrt, scale=sch.p.cv(3), bias=eps_ap)
    sch.emit("scalar", sc, waits=list(scalar_waits), incs=[(s_stats, 1)])

    def ve(e):
        e.reciprocal(out=rstd, in_=rstd)
        e.drain()
        if gamma is None and beta is None:
            return e.tensor_scalar(out=out_ap, in0=x_ap, scalar1=negmean, scalar2=rstd,
                                   op0=ALU.add, op1=ALU.mult)
        e.tensor_scalar(out=scr, in0=x_ap, scalar1=negmean, scalar2=rstd,
                        op0=ALU.add, op1=ALU.mult)
        if gamma is not None:
            e.tensor_tensor(out=(scr if beta is not None else out_ap), in0=scr, in1=gamma, op=ALU.mult)
        if beta is not None:
            return e.tensor_tensor(out=out_ap, in0=scr, in1=beta, op=ALU.add)
    incs = [(done, 1)] if done is not None else []
    sch.emit("vector", ve, waits=[w(s_stats)] + list(vector_waits), incs=incs)


def emit_ln(sch, x_ap, out_ap, stat, scr, s_stats, done=None,
            gamma=None, beta=None, scalar_waits=(), vector_waits=()):
    _emit_ln_inner(sch, x_ap, out_ap, stat, scr, s_stats, done=done, gamma=gamma,
                   beta=beta, scalar_waits=scalar_waits, vector_waits=vector_waits,
                   eps_ap=sch.p.epsap)


def emit_bcast_load(sch, dram_row, sbuf_tile, s_ld, eng="sync", waits=()):
    def f(e, dram_row=dram_row, sbuf_tile=sbuf_tile):
        return e.dma_start(out=sbuf_tile, in_=dram_row.to_broadcast([sbuf_tile.shape[0]] + list(dram_row.shape[1:])))
    sch.emit(eng, f, waits=waits, incs=[(s_ld, 16)])


def emit_l1_ffn(p, x_src, x_dst, weights, consts):
    ident_bf = consts["ident_bf"]
    nc = p.nc
    TH = 1024
    NTH = TH // 128
    sch = Sched(p)
    ctx = ExitStack()
    sb = lambda n, s, d: ctx.enter_context(nc.sbuf_tensor(p.name(n), s, d))[:]
    ps = lambda n, s, d: ctx.enter_context(nc.psum_tensor(p.name(n), s, d))[:]

    g_bc = sb("g_bc", [128, D], F32)
    b_bc = sb("b_bc", [128, D], F32)
    b2_bc = sb("b2_bc", [128, D], F32)
    b1T = sb("b1T", [128, 32], F32)
    hT = sb("hT", [128, 8 * TH], BF16)
    z_acc = sb("z_acc", [128, NTH * D], F32)
    scr = sb("scr", [128, D], F32)
    scr2 = sb("scr2", [128, D], F32)
    xtmp = sb("xtmp", [128, D], F32)
    stat = sb("stat", [128, 8], F32)
    xn_bf = sb("xn_bf", [128, D], BF16)
    w1q = [sb("w1q", [128, 512], I32) for _ in range(2)]
    w1f = sb("w1f", [128, 512], F32)
    w1s = [sb("w1s", [128, 512], F32) for _ in range(2)]
    w1c = sb("w1c", [128, 8 * 512], BF16)
    w2q = [sb("w2q", [128, 1024], I32) for _ in range(2)]
    w2f = sb("w2f", [128, 1024], F32)
    w2s = [sb("w2s", [128, 1024], F32) for _ in range(2)]
    w2c = sb("w2c", [128, 4 * 1024], BF16)
    y2 = sb("y2", [128, 8 * 512], BF16)
    pt0 = ps("pt0", [128, 128], BF16)
    pt1 = ps("pt1", [128, 128], BF16)
    py = [ps("py", [128, 512], F32) for _ in range(2)]
    pz = [ps("pz", [128, 512], F32) for _ in range(2)]

    x_res = sb("x_res", [128, NT * D], F32)
    s_io = p.get_sem("io")
    s_ld = p.sem("ld")
    s_stats = p.get_sem("st")
    s_xn = p.get_sem("xn")
    s_tr = p.get_sem("tr")
    s_evac = p.get_sem("ev")
    s_w1 = p.get_sem("w1")
    s_w2 = p.get_sem("w2")
    s_wq = [p.sem("wqa"), p.sem("wqb")]
    s_ws = [p.sem("wsa"), p.sem("wsb")]
    s_wq2 = [p.sem("wq2a"), p.sem("wq2b")]
    s_ws2 = [p.sem("ws2a"), p.sem("ws2b")]
    s_dq1 = p.get_sem("dq1")
    s_dq2 = p.get_sem("dq2")
    s_mmy = p.get_sem("mmy")
    s_silu = p.get_sem("silu")
    s_mmz = p.get_sem("mmz")
    s_zacc = p.get_sem("za")
    s_ep = p.get_sem("ep")

    def ldx(e):
        return e.dma_start(out=x_res.rearrange("p (n d) -> p n d", d=D),
                           in_=x_src.rearrange("(n p) d -> p n d", p=128))
    sch.emit("sync", ldx, waits=[w(s_io)], incs=[(s_io, 16)])
    emit_bcast_load(sch, weights["ln_gamma"], g_bc, s_ld)
    emit_bcast_load(sch, weights["ln_beta"], b_bc, s_ld)
    emit_bcast_load(sch, weights["fc2_bias"], b2_bc, s_ld)
    sch.emit("sync", lambda e: e.dma_start(out=b1T, in_=weights["fc1_biasT"]),
             incs=[(s_ld, 16)])
    ld_done = w(s_ld)
    first_scalar_wait = [ld_done, w(s_io)]

    for half in range(2):
        for ii in range(NTH):
            i = half * NTH + ii
            x_t = x_res[:, i * D:(i + 1) * D]
            z_t = z_acc[:, ii * D:(ii + 1) * D]
            prev_half_fence = [w(s_ep)] if (half > 0 and ii == 0) else []
            emit_ln(sch, x_t, xtmp, stat, scr, s_stats, done=s_xn,
                    scalar_waits=(first_scalar_wait if i == 0 else [w(s_xn)]))
            first_scalar_wait = None
            emit_ln(sch, xtmp, xn_bf, stat, scr, s_stats, done=s_xn,
                    gamma=g_bc, beta=b_bc,
                    scalar_waits=[w(s_xn)], vector_waits=[w(s_tr)])
            def zseed(e, z_t=z_t, xtmp=xtmp):
                return e.scalar_tensor_tensor(out=z_t, in0=xtmp, scalar=2.0,
                                              in1=b2_bc, op0=ALU.mult, op1=ALU.add)
            sch.emit("vector", zseed, waits=prev_half_fence, incs=[(s_zacc, 1)])
            for k in range(8):
                pbank = pt0 if (k % 2 == 0) else pt1
                evac_snapshot = s_evac.n
                waits = [w(s_xn)] if k == 0 else []
                if evac_snapshot >= 2:
                    waits.append((s_evac, evac_snapshot - 1))
                def tr(e, k=k, pbank=pbank):
                    return e.transpose(out=pbank, in_=xn_bf[:, k * 128:(k + 1) * 128], identity=ident_bf)
                sch.emit("tensor", tr, waits=waits, incs=[(s_tr, 1)])
                dst = hT[:, k * TH + ii * 128: k * TH + (ii + 1) * 128]
                def ev(e, dst=dst, pbank=pbank):
                    return e.tensor_copy(out=dst, in_=pbank)
                sch.emit("vector", ev, waits=[w(s_tr)], incs=[(s_evac, 1)])
        sch.flush(f"l1ffn_prep{half}")
        hT_ready = w(s_evac)
        zseed_done = w(s_zacc)

        for n in range(8):
            mmy_fence = w(s_mmy)
            mmz_fence = w(s_mmz)
            for k in range(8):
                qb = w1q[k % 2]; sbuf_s = w1s[k % 2]
                kk = (half * 8 + n) * 8 + k
                def ldq(e, n=n, k=k, qb=qb):
                    return e.dma_start(out=qb, in_=weights["fc1_q"][k * 128:(k + 1) * 128, n * 512:(n + 1) * 512])
                sch.emit("sync", ldq, waits=[(s_dq1, s_dq1.n - 1)] if kk >= 2 else [], incs=[(s_wq[k % 2], 16)])
                def lds(e, n=n, k=k, sbuf_s=sbuf_s):
                    return e.dma_start(out=sbuf_s, in_=weights["fc1_scales"][k:k + 1, n * 512:(n + 1) * 512].to_broadcast([128, 512]))
                sch.emit("sync", lds, waits=[(s_w1, s_w1.n - 1)] if kk >= 2 else [], incs=[(s_ws[k % 2], 16)])
                def deq_s(e, qb=qb):
                    return e.activation(out=w1f, in_=qb, func=AF.Identity, bias=p.cv(1))
                sch.emit("scalar", deq_s,
                         waits=[w(s_wq[k % 2])] + ([w(s_w1)] if kk > 0 else []),
                         incs=[(s_dq1, 1)])
                def deq_v(e, k=k, sbuf_s=sbuf_s):
                    return e.tensor_tensor(out=w1c[:, k * 512:(k + 1) * 512], in0=w1f, in1=sbuf_s, op=ALU.mult)
                vw = [w(s_dq1), w(s_ws[k % 2])]
                if k == 0 and (n > 0 or half > 0):
                    vw.append(mmy_fence)
                sch.emit("vector", deq_v, waits=vw, incs=[(s_w1, 1)])
            w1_ready = w(s_w1)

            for j in range(4):
                for g in range(2):
                    idx = (half * 8 + n) * 8 + j * 2 + g
                    pyb = py[idx % 2]
                    waits = [w1_ready, hT_ready] if (j == 0 and g == 0) else []
                    if idx >= 2:
                        waits.append((s_silu, s_silu.n - 1))
                    def mm(e, j=j, g=g, pyb=pyb):
                        inst = None
                        for k in range(8):
                            inst = e.matmul(out=pyb,
                                            lhsT=w1c[:, k * 512 + j * 128: k * 512 + (j + 1) * 128],
                                            rhs=hT[:, k * TH + g * 512: k * TH + (g + 1) * 512],
                                            start=(k == 0), stop=(k == 7))
                        return inst
                    sch.emit("tensor", mm, waits=waits, incs=[(s_mmy, 1)])
                    ntile_abs = n * 4 + j
                    def silu(e, j=j, g=g, pyb=pyb, ntile_abs=ntile_abs):
                        return e.activation(out=y2[:, (j * 2 + g) * 512:(j * 2 + g + 1) * 512],
                                            in_=pyb, func=AF.Silu,
                                            bias=b1T[:, ntile_abs:ntile_abs + 1])
                    sw = [w(s_mmy)]
                    if j == 0 and g == 0 and (n > 0 or half > 0):
                        sw.append(mmz_fence)
                    sch.emit("scalar", silu, waits=sw, incs=[(s_silu, 1)])

            for jj in range(4):
                qb = w2q[jj % 2]; sbuf_s = w2s[jj % 2]
                gidx = n * 4 + jj
                ggidx = half * 32 + gidx
                def ldq2(e, gidx=gidx, qb=qb):
                    return e.dma_start(out=qb, in_=weights["fc2_q"][gidx * 128:(gidx + 1) * 128, :])
                sch.emit("sync", ldq2, waits=[(s_dq2, s_dq2.n - 1)] if ggidx >= 2 else [], incs=[(s_wq2[jj % 2], 16)])
                def lds2(e, gidx=gidx, sbuf_s=sbuf_s):
                    return e.dma_start(out=sbuf_s, in_=weights["fc2_scales"][gidx:gidx + 1, :].to_broadcast([128, 1024]))
                sch.emit("sync", lds2, waits=[(s_w2, s_w2.n - 1)] if ggidx >= 2 else [], incs=[(s_ws2[jj % 2], 16)])
                def deq2_s(e, qb=qb):
                    return e.activation(out=w2f, in_=qb, func=AF.Identity, bias=p.cv(1))
                sch.emit("scalar", deq2_s,
                         waits=[w(s_wq2[jj % 2])] + ([w(s_w2)] if ggidx > 0 else []),
                         incs=[(s_dq2, 1)])
                def deq2_v(e, jj=jj, sbuf_s=sbuf_s):
                    return e.tensor_tensor(out=w2c[:, jj * 1024:(jj + 1) * 1024], in0=w2f, in1=sbuf_s, op=ALU.mult)
                vw = [w(s_dq2), w(s_ws2[jj % 2])]
                if jj == 0 and (n > 0 or half > 0):
                    vw.append(mmz_fence)
                sch.emit("vector", deq2_v, waits=vw, incs=[(s_w2, 1)])
            w2_ready = w(s_w2)

            silu_all = w(s_silu)
            for g in range(2):
                for mt in range(4):
                    for hc in range(2):
                        idx = (half * 8 + n) * 16 + g * 8 + mt * 2 + hc
                        pzb = pz[idx % 2]
                        waits = [silu_all, w2_ready] if (g == 0 and mt == 0 and hc == 0) else []
                        if idx >= 2:
                            waits.append((s_zacc, s_zacc.n - 1))
                        if n == 0 and g == 0 and mt == 0 and hc == 0:
                            waits.append(zseed_done)
                        def mm2(e, g=g, mt=mt, hc=hc, pzb=pzb):
                            inst = None
                            for j in range(4):
                                inst = e.matmul(out=pzb,
                                                lhsT=y2[:, (j * 2 + g) * 512 + mt * 128:(j * 2 + g) * 512 + (mt + 1) * 128],
                                                rhs=w2c[:, j * 1024 + hc * 512: j * 1024 + (hc + 1) * 512],
                                                start=(j == 0), stop=(j == 3))
                            return inst
                        sch.emit("tensor", mm2, waits=waits, incs=[(s_mmz, 1)])
                        mtile = g * 4 + mt
                        dst = z_acc[:, mtile * D + hc * 512: mtile * D + (hc + 1) * 512]
                        def zadd(e, dst=dst, pzb=pzb):
                            return e.tensor_tensor(out=dst, in0=dst, in1=pzb, op=ALU.add)
                        sch.emit("vector", zadd, waits=[w(s_mmz)], incs=[(s_zacc, 1)])
            sch.flush(f"l1ffn_h{half}n{n}")

        z_done = w(s_zacc)
        for ii in range(NTH):
            i = half * NTH + ii
            x_t = x_res[:, i * D:(i + 1) * D]
            z_t = z_acc[:, ii * D:(ii + 1) * D]
            def ep1(e, z_t=z_t):
                return e.tensor_scalar(out=scr2, in0=z_t, scalar1=0.5, scalar2=None, op0=ALU.mult)
            sch.emit("vector", ep1, waits=[z_done] if ii == 0 else [], incs=[(s_ep, 1)])
            emit_ln(sch, scr2, scr2, stat, scr, s_stats, done=s_ep,
                    scalar_waits=[w(s_ep)])
            def ep2(e, x_t=x_t):
                return e.tensor_tensor(out=x_t, in0=x_t, in1=scr2, op=ALU.add)
            sch.emit("vector", ep2, incs=[(s_ep, 1)])
        sch.flush(f"l1ffn_ep{half}")
    def stx(e):
        return e.dma_start(out=x_dst.rearrange("(n p) d -> p n d", p=128),
                           in_=x_res.rearrange("p (n d) -> p n d", d=D))
    sch.emit("sync", stx, waits=[w(s_ep)], incs=[(s_io, 16)])
    sch.emit("sync", lambda e: e.nop(), waits=[w(s_io)])
    sch.flush("l1ffn_store")
    ctx.close()
    return s_io



def emit_l1_attn(p, x_src, x_dst, weights, consts):
    nc = p.nc
    ident_bf = consts["ident_bf"]
    NH = 16
    sch = Sched(p)
    ctx = ExitStack()
    sb = lambda n, s, d: ctx.enter_context(nc.sbuf_tensor(p.name(n), s, d))[:]
    ps = lambda n, s, d: ctx.enter_context(nc.psum_tensor(p.name(n), s, d))[:]

    qT = sb("qT", [128, 8 * T], BF16)
    kT = sb("kT", [128, 8 * T], BF16)
    v_ = sb("v", [128, NT * D], BF16)
    oT = sb("oT", [128, 8 * T], BF16)
    pen9 = sb("pen9", [128, 128], F32)
    bqT = sb("bqT", [128, 8], F32)
    bo_bc = sb("bo_bc", [128, D], F32)
    stat = sb("stat", [128, 8], F32)
    scr = sb("scr", [128, D], F32)
    scr2 = sb("scr2", [128, D], F32)
    lrow = sb("lrow", [128, 4], F32)

    s_io = p.get_sem("io")
    s_ld = p.sem("ld")
    s_stats = p.get_sem("st")
    s_xn = p.get_sem("xn")
    s_tr = p.get_sem("tr")
    s_evac = p.get_sem("ev")
    s_wld = p.sem("wld")
    s_xt = [p.sem("xta"), p.sem("xtb")]
    s_st = [p.sem("sta"), p.sem("stb")]
    s_proj = p.get_sem("mmy")
    s_pevac = p.get_sem("silu")
    s_ab = p.get_sem("ab")
    s_smm = p.get_sem("mmz")
    s_sevac = p.get_sem("za")
    s_devac = p.get_sem("zad")
    s_max = p.get_sem("w1")
    s_exp = p.get_sem("dq1")
    s_ptr = p.get_sem("tr2")
    s_ptev = p.get_sem("ev2")
    s_pv = p.get_sem("w2")
    s_o = p.get_sem("dq2")
    s_oT = p.get_sem("tr3")
    s_zmm = p.get_sem("zmm")
    s_ep = p.get_sem("ep")

    with nc.sbuf_tensor(p.name("xnT"), [128, 8 * T], BF16) as xnT_t, \
         nc.sbuf_tensor(p.name("xt"), [128, 2 * D], F32) as xt_t, \
         nc.sbuf_tensor(p.name("wbf"), [128, 8 * D], BF16) as wbf_t, \
         nc.sbuf_tensor(p.name("xnbf"), [128, D], BF16) as xnbf_t, \
         nc.psum_tensor(p.name("pt0"), [128, 128], BF16) as pt0_t, \
         nc.psum_tensor(p.name("pt1"), [128, 128], BF16) as pt1_t, \
         nc.psum_tensor(p.name("pp0"), [128, 512], F32) as pp0_t, \
         nc.psum_tensor(p.name("pp1"), [128, 512], F32) as pp1_t:
        xnT = xnT_t[:]; wbf = wbf_t[:]; xnbf = xnbf_t[:]
        pt = (pt0_t[:], pt1_t[:])
        pp = (pp0_t[:], pp1_t[:])
        xt = xt_t[:]

        sch.emit("sync", lambda e: e.dma_start(out=pen9, in_=weights["pen9"]), incs=[(s_ld, 16)])
        sch.emit("sync", lambda e: e.dma_start(out=bqT, in_=weights["bqT"]), incs=[(s_ld, 16)])
        emit_bcast_load(sch, weights["bo"], bo_bc, s_ld)
        consts_ready = w(s_ld)
        sch.emit("scalar", lambda e: e.nop(), waits=[consts_ready])
        sch.emit("vector", lambda e: e.nop(), waits=[consts_ready])

        for i in range(NT):
            xti = xt[:, (i % 2) * D:(i % 2 + 1) * D]
            ldw = []
            if i >= 2:
                ldw = [(s_xn, s_xn.n - 1)]
            def ldx(e, xti=xti, i=i):
                return e.dma_start(out=xti, in_=x_src[i * 128:(i + 1) * 128, :])
            sl = s_xt[i % 2]
            sch.emit("sync", ldx, waits=([w(s_io)] if i == 0 else ldw), incs=[(sl, 16)])
            emit_ln(sch, xti, xnbf, stat, scr, s_stats, done=s_xn,
                    scalar_waits=[w(sl), w(s_xn)], vector_waits=[w(s_tr)])
            for k in range(8):
                pbank = pt[k % 2]
                evac_snapshot = s_evac.n
                waits = [w(s_xn)] if k == 0 else []
                if evac_snapshot >= 2:
                    waits.append((s_evac, evac_snapshot - 1))
                def tr(e, k=k, pbank=pbank):
                    return e.transpose(out=pbank, in_=xnbf[:, k * 128:(k + 1) * 128], identity=ident_bf)
                sch.emit("tensor", tr, waits=waits, incs=[(s_tr, 1)])
                dst = xnT[:, k * T + i * 128: k * T + (i + 1) * 128]
                def ev(e, dst=dst, pbank=pbank):
                    return e.tensor_copy(out=dst, in_=pbank)
                sch.emit("vector", ev, waits=[w(s_tr)], incs=[(s_evac, 1)])
        sch.flush("l1a_ln")
        xnT_ready = w(s_evac)

        for pi, (wname, outbuf, kind) in enumerate([
                ("Wq", qT, "q"), ("Wk", kT, "k"), ("Wv", v_, "v")]):
            wdap = weights[wname]
            wfence = w(s_proj)
            for kc in range(8):
                def ldw_(e, wdap=wdap, kc=kc):
                    return e.dma_start(out=wbf[:, kc * D:(kc + 1) * D],
                                       in_=wdap[kc * 128:(kc + 1) * 128, :])
                sch.emit("gpsimd", ldw_, waits=[wfence] if kc == 0 else [], incs=[(s_wld, 16)])
            wld_done = w(s_wld)
            if kind in ("q", "k"):
                for c in range(8):
                    for tc in range(4):
                        idx = pi * 32 + c * 4 + tc
                        pb = pp[idx % 2]
                        waits = [wld_done, xnT_ready] if (c == 0 and tc == 0) else []
                        if True:
                            waits = waits + [(s_pevac, s_pevac.n - 1)] if s_pevac.n >= 2 else waits
                        def mm(e, c=c, tc=tc, pb=pb):
                            inst = None
                            for kc in range(8):
                                inst = e.matmul(out=pb,
                                                lhsT=wbf[:, kc * D + c * 128: kc * D + (c + 1) * 128],
                                                rhs=xnT[:, kc * T + tc * 512: kc * T + (tc + 1) * 512],
                                                start=(kc == 0), stop=(kc == 7))
                            return inst
                        sch.emit("tensor", mm, waits=waits, incs=[(s_proj, 1)])
                        dst = outbuf[:, c * T + tc * 512: c * T + (tc + 1) * 512]
                        if kind == "q":
                            def evq(e, dst=dst, pb=pb, c=c):
                                return e.activation(out=dst, in_=pb, func=AF.Identity,
                                                    bias=bqT[:, c:c + 1])
                            sch.emit("scalar", evq, waits=[w(s_proj)], incs=[(s_pevac, 1)])
                        else:
                            def evk(e, dst=dst, pb=pb):
                                return e.activation(out=dst, in_=pb, func=AF.Identity)
                            sch.emit("scalar", evk, waits=[w(s_proj)], incs=[(s_pevac, 1)])
            else:
                for tt in range(16):
                    for hc in range(2):
                        idx = c_idx = tt * 2 + hc
                        pb = pp[idx % 2]
                        waits = [wld_done] if idx == 0 else []
                        if s_pevac.n >= 2:
                            waits = waits + [(s_pevac, s_pevac.n - 1)]
                        def mmv(e, tt=tt, hc=hc, pb=pb):
                            inst = None
                            for kc in range(8):
                                inst = e.matmul(out=pb,
                                                lhsT=xnT[:, kc * T + tt * 128: kc * T + (tt + 1) * 128],
                                                rhs=wbf[:, kc * D + hc * 512: kc * D + (hc + 1) * 512],
                                                start=(kc == 0), stop=(kc == 7))
                            return inst
                        sch.emit("tensor", mmv, waits=waits, incs=[(s_proj, 1)])
                        dst = v_[:, tt * D + hc * 512: tt * D + (hc + 1) * 512]
                        def evv(e, dst=dst, pb=pb):
                            return e.activation(out=dst, in_=pb, func=AF.Identity)
                        sch.emit("scalar", evv, waits=[w(s_proj)], incs=[(s_pevac, 1)])
        sch.flush("l1a_proj")
    proj_done = w(s_pevac)

    with nc.sbuf_tensor(p.name("S_row"), [128, 2 * T], F32) as S_t, \
         nc.sbuf_tensor(p.name("P_row"), [128, 2 * T], BF16) as P_t, \
         nc.sbuf_tensor(p.name("PT"), [128, 2 * 512], BF16) as PT_t, \
         nc.sbuf_tensor(p.name("alibi"), [128, T], F32) as arow_t, \
         nc.sbuf_tensor(p.name("obf"), [128, 2 * 64], BF16) as obf_t, \
         nc.psum_tensor(p.name("ps0"), [128, 512], F32) as ps0_t, \
         nc.psum_tensor(p.name("ps1"), [128, 512], F32) as ps1_t, \
         nc.psum_tensor(p.name("ppv"), [128, 64], F32) as ppv_t, \
         nc.psum_tensor(p.name("ptp0"), [128, 512], BF16) as ptp0_t, \
         nc.psum_tensor(p.name("ptp1"), [128, 512], BF16) as ptp1_t, \
         nc.psum_tensor(p.name("pot"), [64, 128], BF16) as pot_t:
        S_row2 = S_t[:]; P_row2 = P_t[:]; PT2 = PT_t[:]
        alibi_h = arow_t[:]; obf2 = obf_t[:]
        psb = (ps0_t[:], ps1_t[:])
        ppv = ppv_t[:]
        ptp = (ptp0_t[:], ptp1_t[:])
        pot = pot_t[:]

        s_abd = p.sem("abd")
        PTW = 512

        def stageA(h, q):
            par = (h * NT + q) % 2
            S_row = S_row2[:, par * T:par * T + T]
            P_row = P_row2[:, par * T:par * T + T]
            l_ = lrow[:, 2 * par:2 * par + 1]
            negm = lrow[:, 2 * par + 1:2 * par + 2]
            klen = (q + 1) * 128
            rem = klen % 512
            if rem == 0:
                rem = 512
            nfull = (klen - rem) // 512
            widths = [512] * nfull + [rem]
            strip = h // 2
            pbase = (h % 2) * 64
            qsl = qT[pbase:pbase + 64, strip * T + q * 128: strip * T + (q + 1) * 128]
            for bi, bw in enumerate(widths):
                boff = bi * 512
                pb = psb[(s_smm.n) % 2]
                mmw = []
                if h == 0 and q == 0 and bi == 0:
                    mmw = [proj_done, w(s_ld)]
                if bi == 0 and q == 0:
                    mmw = mmw + [(s_abd, 16 * (h + 1))]
                if s_smm.n >= 2:
                    mmw = mmw + [(s_sevac, s_sevac.n - 1)]
                def smm(e, qsl=qsl, pb=pb, strip=strip, pbase=pbase, boff=boff, bw=bw):
                    return e.matmul(out=pb[:, 0:bw], lhsT=qsl,
                                    rhs=kT[pbase:pbase + 64, strip * T + boff: strip * T + boff + bw],
                                    start=True, stop=True)
                sch.emit("tensor", smm, waits=mmw, incs=[(s_smm, 1)])
                dst = S_row[:, boff:boff + bw]
                last = bi == len(widths) - 1
                def sev(e, dst=dst, pb=pb, bw=bw, boff=boff, last=last, klen=klen, S_row=S_row):
                    e.scalar_tensor_tensor(out=dst, in0=pb[:, 0:bw], scalar=0.125,
                                           in1=alibi_h[:, boff:boff + bw],
                                           op0=ALU.mult, op1=ALU.add)
                    if last:
                        e.drain()
                        e.tensor_tensor(out=S_row[:, klen - 128:klen],
                                        in0=S_row[:, klen - 128:klen], in1=pen9, op=ALU.add)
                    return e.drain()
                evw = [w(s_smm)]
                if s_exp.n >= 1:
                    evw.append((s_exp, s_exp.n - 1))
                sch.emit("vector", sev, waits=evw, incs=[(s_sevac, 1)])
            def vmax(e, S_row=S_row, negm=negm, klen=klen):
                e.tensor_reduce(out=negm, in_=S_row[:, 0:klen], axis=AX.X, op=ALU.max)
                e.drain()
                return e.tensor_scalar(out=negm, in0=negm, scalar1=-1.0, scalar2=None, op0=ALU.mult)
            vmw = [w(s_sevac)]
            if s_exp.n >= 1:
                vmw.append((s_exp, s_exp.n - 1))
            sch.emit("vector", vmax, waits=vmw, incs=[(s_max, 1)])
            def sexp(e, S_row=S_row, P_row=P_row, negm=negm, l_=l_, klen=klen):
                return e.activation(out=P_row[:, 0:klen], in_=S_row[:, 0:klen],
                                    func=AF.Exp, bias=negm, accum_out=l_)
            sxw = [w(s_max)]
            if len(ptr_hist) >= 1:
                sxw.append((s_ptr, ptr_hist[-1]))
            if s_o.n >= 1:
                sxw.append((s_o, s_o.n))
            sch.emit("scalar", sexp, waits=sxw, incs=[(s_exp, 1)])
            return dict(exp=s_exp.n)

        def stageB(h, q, tok):
            par = (h * NT + q) % 2
            P_row = P_row2[:, par * T:par * T + T]
            l_ = lrow[:, 2 * par:2 * par + 1]
            klen = (q + 1) * 128
            nc_chunks = klen // 128
            strip = h // 2
            pbase = (h % 2) * 64
            for g0 in range(0, nc_chunks, 4):
                gn = min(4, nc_chunks - g0)
                bank = ptp[(s_ptev.n) % 2]
                for j in range(gn):
                    c = g0 + j
                    ptw = []
                    if j == 0:
                        ptw.append((s_exp, tok["exp"]))
                        if s_ptev.n >= 2:
                            ptw.append((s_ptev, s_ptev.n - 1))
                    def ptr_(e, P_row=P_row, c=c, bank=bank, j=j):
                        return e.transpose(out=bank[:, j * 128:(j + 1) * 128],
                                           in_=P_row[:, c * 128:(c + 1) * 128], identity=ident_bf)
                    sch.emit("tensor", ptr_, waits=ptw, incs=[(s_ptr, 1)])
                dst = PT2[:, (s_ptev.n % 2) * PTW:(s_ptev.n % 2) * PTW + gn * 128]
                def ptev(e, dst=dst, bank=bank, gn=gn):
                    return e.copy(out=dst, in_=bank[:, 0:gn * 128])
                pw = [w(s_ptr)]
                if len(pv_bank_hist) >= 2:
                    pw.append((s_pv, pv_bank_hist[-2]))
                sch.emit("scalar", ptev, waits=pw, incs=[(s_ptev, 1)])
                pv_bank_hist.append(s_pv.n + gn)
                for j in range(gn):
                    c = g0 + j
                    def pvm(e, c=c, h=h, dst=dst, j=j, nc_chunks=nc_chunks):
                        return e.matmul(out=ppv, lhsT=dst[:, j * 128:(j + 1) * 128],
                                        rhs=v_[:, c * D + h * 64: c * D + h * 64 + 64],
                                        start=(c == 0), stop=(c == nc_chunks - 1))
                    pvw = [w(s_ptev)] if j == 0 else []
                    if c == 0 and s_o.n >= 1:
                        pvw.append((s_o, s_o.n))
                    sch.emit("tensor", pvm, waits=pvw, incs=[(s_pv, 1)])
            ptr_hist.append(s_ptr.n)
            obf = obf2[:, par * 64:par * 64 + 64]
            def vo(e, l_=l_, obf=obf):
                e.reciprocal(out=l_, in_=l_)
                e.drain()
                return e.tensor_scalar(out=obf, in0=ppv, scalar1=l_, scalar2=None, op0=ALU.mult)
            sch.emit("vector", vo, waits=[w(s_pv)] + ([(s_oT, s_oT.n - 1)] if s_o.n >= 2 else []),
                     incs=[(s_o, 1)])
            def otr(e, obf=obf):
                return e.transpose(out=pot, in_=obf, identity=ident_bf)
            sch.emit("tensor", otr, waits=[w(s_o), (s_oT, s_oT.n)], incs=[(s_oT, 1)])
            dsto = oT[pbase:pbase + 64, strip * T + q * 128: strip * T + (q + 1) * 128]
            def otev(e, dsto=dsto):
                return e.tensor_copy(out=dsto, in_=pot)
            sch.emit("vector", otev, waits=[w(s_oT)], incs=[(s_oT, 1)])

        ptr_hist = []
        pv_bank_hist = []
        pending = None
        for h in range(NH):
            def bab(e, h=h):
                return e.dma_start(out=alibi_h, in_=weights["alibi"][h:h + 1, :].to_broadcast([128, T]))
            abw = [proj_done] if h == 0 else [(s_smm, s_smm.n)]
            sch.emit("sync", bab, waits=abw, incs=[(s_abd, 16)])
            for q in range(NT):
                tok = stageA(h, q)
                if pending is not None:
                    stageB(*pending)
                pending = (h, q, tok)
            sch.flush(f"l1a_h{h}")
        stageB(*pending)
    flash_done = w(s_oT)

    with nc.sbuf_tensor(p.name("wof"), [128, 8 * D], BF16) as wof_t, \
         nc.sbuf_tensor(p.name("xt2"), [128, 2 * D], F32) as xt2_t, \
         nc.psum_tensor(p.name("pz0"), [128, 512], F32) as pz0_t, \
         nc.psum_tensor(p.name("pz1"), [128, 512], F32) as pz1_t:
        wof = wof_t[:]; xt2 = xt2_t[:]
        pz = (pz0_t[:], pz1_t[:])
        for kc in range(8):
            def ldwo(e, kc=kc):
                return e.dma_start(out=wof[:, kc * D:(kc + 1) * D],
                                   in_=weights["Wo"][kc * 128:(kc + 1) * 128, :])
            sch.emit("gpsimd", ldwo, incs=[(s_wld, 16)])
        wo_done = w(s_wld)
        for i in range(NT):
            xti = xt2[:, (i % 2) * D:(i % 2 + 1) * D]
            ldw2 = [] if i < 2 else [(s_ep, s_ep.n - 2), w(s_st[i % 2])]
            sl2 = s_xt[i % 2]
            def ldx2(e, xti=xti, i=i):
                return e.dma_start(out=xti, in_=x_src[i * 128:(i + 1) * 128, :])
            sch.emit("sync", ldx2, waits=ldw2, incs=[(sl2, 16)])
            for hc in range(2):
                pb = pz[hc]
                mmw = [wo_done, flash_done] if (i == 0 and hc == 0) else []
                if i > 0:
                    mmw = mmw + [(s_ep, s_ep.n - (1 + hc))]
                def zmm(e, i=i, hc=hc, pb=pb):
                    inst = None
                    for c in range(8):
                        inst = e.matmul(out=pb, lhsT=oT[:, c * T + i * 128: c * T + (i + 1) * 128],
                                        rhs=wof[:, c * D + hc * 512: c * D + (hc + 1) * 512],
                                        start=(c == 0), stop=(c == 7))
                    return inst
                sch.emit("tensor", zmm, waits=mmw, incs=[(s_zmm, 1)])
            def zev(e, i=i):
                e.tensor_tensor(out=scr2[:, 0:512], in0=pz[0], in1=bo_bc[:, 0:512], op=ALU.add)
                return e.tensor_tensor(out=scr2[:, 512:1024], in0=pz[1], in1=bo_bc[:, 512:1024], op=ALU.add)
            sch.emit("vector", zev, waits=[w(s_zmm), w(s_ep)], incs=[(s_ep, 2)])
            emit_ln(sch, scr2, scr2, stat, scr, s_stats, done=s_ep,
                    scalar_waits=[w(s_ep)])
            def xadd(e, xti=xti):
                return e.tensor_tensor(out=xti, in0=xti, in1=scr2, op=ALU.add)
            sch.emit("vector", xadd, waits=[w(sl2), w(s_ep)], incs=[(s_ep, 1)])
            def stx(e, xti=xti, i=i):
                return e.dma_start(out=x_dst[i * 128:(i + 1) * 128, :], in_=xti)
            sch.emit("sync", stx, waits=[w(s_ep)], incs=[(s_st[i % 2], 16)])
        sch.emit("sync", lambda e: e.nop(), waits=[w(s_st[0]), w(s_st[1])], incs=[(s_io, 16)])
        sch.flush("l1a_z")
    ctx.close()
    return s_io


def make_cvec():
    cv = np.zeros((128, 8), dtype=np.float32)
    cv[:, 0] = EPS_LN
    cv[:, 1] = -128.0
    cv[:, 2] = -1.0 / D
    cv[:, 3] = 1.0 / D
    cv[:, 4] = 1.0
    cv[:, 5] = 0.5
    cv[:, 6] = 0.125
    return cv


def build_program(blocks):
    p = Prog()
    nc = p.nc
    x_in = nc.dram_tensor("x_in", [T, D], F32, kind="ExternalInput").ap()
    x_out = nc.dram_tensor("x_out", [T, D], F32, kind="ExternalOutput").ap()
    ident_in = nc.dram_tensor("ident_in", [128, 128], BF16, kind="ExternalInput").ap()
    identf_in = nc.dram_tensor("identf_in", [128, 128], F32, kind="ExternalInput").ap()
    cvec_in = nc.dram_tensor("cvec_in", [128, 8], F32, kind="ExternalInput").ap()

    ident_bf = p.sb("ident_bf", [128, 128], BF16)
    ident_f = p.sb("ident_f", [128, 128], F32)
    onesf = p.sb("onesf", [1, 128], F32)
    cvec = p.sb("cvec", [128, 8], F32)
    p.epsap = cvec[:, 0:1]
    p.cv = lambda i: cvec[:, i:i + 1]

    s_io = p.get_sem("io")
    sch = Sched(p)
    sch.emit("sync", lambda e: e.dma_start(out=ident_bf, in_=ident_in), incs=[(s_io, 16)])
    sch.emit("sync", lambda e: e.dma_start(out=ident_f, in_=identf_in), incs=[(s_io, 16)])
    sch.emit("sync", lambda e: e.dma_start(out=cvec, in_=cvec_in), incs=[(s_io, 16)])
    sch.emit("vector", lambda e: e.memset(onesf, 1.0), incs=[(s_io, 1)])
    for eng in ("scalar", "vector", "tensor", "gpsimd"):
        sch.emit(eng, lambda e: e.nop(), waits=[w(s_io)])
    sch.flush("init")

    consts = {"ident_bf": ident_bf, "ident_f": ident_f, "onesf": onesf, "cvec": cvec}
    n = len(blocks)
    bufs = [x_in]
    for i in range(n - 1):
        bufs.append(nc.dram_tensor(f"xbuf{i}", [T, D], F32, kind="Internal").ap())
    bufs.append(x_out)

    for i, (emit_fn, wdecl) in enumerate(blocks):
        weights = {}
        for name, (shape, dtype) in wdecl.items():
            weights[name] = nc.dram_tensor(f"b{i}_{name}", list(shape), dtype, kind="ExternalInput").ap()
        emit_fn(p, bufs[i], bufs[i + 1], weights, consts)
    p.ctx.close()
    return p


L1_FFN_DECL = {
    "ln_gamma": ([1, D], F32), "ln_beta": ([1, D], F32),
    "fc1_q": ([1024, 4096], I32), "fc1_scales": ([8, 4096], F32),
    "fc1_biasT": ([128, 32], F32), "fc2_q": ([4096, 1024], I32),
    "fc2_scales": ([32, 1024], F32), "fc2_bias": ([1, D], F32),
}

L1_ATTN_DECL = {
    "Wq": ([1024, 1024], F32), "Wk": ([1024, 1024], F32),
    "Wv": ([1024, 1024], F32), "Wo": ([1024, 1024], F32),
    "bqT": ([128, 8], F32), "bo": ([1, D], F32),
    "pen9": ([128, 128], F32), "alibi": ([16, T], F32),
}


def l1_ffn_weights(inp, pre=""):
    return {
        pre + "ln_gamma": inp["L1_ffn_ln_gamma"].reshape(1, D),
        pre + "ln_beta": inp["L1_ffn_ln_beta"].reshape(1, D),
        pre + "fc1_q": inp["L1_ffn_fc1_q"],
        pre + "fc1_scales": inp["L1_ffn_fc1_scales"],
        pre + "fc1_biasT": np.ascontiguousarray(inp["L1_ffn_fc1_bias"].reshape(32, 128).T),
        pre + "fc2_q": inp["L1_ffn_fc2_q"],
        pre + "fc2_scales": inp["L1_ffn_fc2_scales"],
        pre + "fc2_bias": inp["L1_ffn_fc2_bias"].reshape(1, D),
    }


def l1_attn_weights(inp, pre=""):
    NH = 16
    slopes = np.power(2.0 ** (-8.0 / NH), np.arange(1, NH + 1, dtype=np.float32))
    tk = np.arange(T, dtype=np.float64)
    pen9 = np.where(np.arange(128)[None, :] <= np.arange(128)[:, None], 0.0, -1e9).astype(np.float32)
    return {
        pre + "Wq": inp["L1_attn_Wq"], pre + "Wk": inp["L1_attn_Wk"],
        pre + "Wv": inp["L1_attn_Wv"], pre + "Wo": inp["L1_attn_Wo"],
        pre + "bqT": np.ascontiguousarray(inp["L1_attn_bq"].reshape(8, 128).T),
        pre + "bo": inp["L1_attn_bo"].reshape(1, D),
        pre + "pen9": pen9,
        pre + "alibi": (slopes[:, None].astype(np.float64) * tk[None, :]).astype(np.float32),
    }


def base_inputs():
    import ml_dtypes
    return {"ident_in": np.eye(128, dtype=ml_dtypes.bfloat16),
            "identf_in": np.eye(128, dtype=np.float32), "cvec_in": make_cvec()}


def emit_l0_ffn(p, x_src, x_dst, weights, consts):
    nc = p.nc
    ident_bf = consts["ident_bf"]
    ident_f = consts["ident_f"]
    onesf = consts["onesf"]
    LOGK = float(np.log(np.float32(1152.0)))
    SQD = 32.0
    sch = Sched(p)
    ctx = ExitStack()
    sb = lambda n, s, d: ctx.enter_context(nc.sbuf_tensor(p.name(n), s, d))[:]
    dram = lambda n, s, d: nc.dram_tensor(p.name(n), s, d, kind="Internal").ap()

    z_acc = sb("z_acc", [128, NT * D], F32)
    scr = sb("scr", [128, D], F32)
    scr2 = sb("scr2", [128, D], F32)
    stat = sb("stat", [128, 8], F32)
    s_sb = sb("s_sb", [128, 16], F32)
    maskcol = sb("maskcol", [128, 16], F32)
    maskbc = sb("maskbc", [128, T], F32)
    b2l_bc = sb("b2l_bc", [128, D], F32)
    b2h_bc = sb("b2h_bc", [128, D], F32)
    b1lT = sb("b1lT", [128, 4], F32)
    b1hT = sb("b1hT", [128, 32], F32)
    xnhT = sb("xnhT", [128, 8 * T], BF16)
    s32 = sb("s32", [32, 64], F32)
    b32 = sb("b32", [32, 64], F32)
    t32 = sb("t32", [32, 64], F32)
    u32 = sb("u32", [32, 64], F32)
    m32 = sb("m32", [32, 64], F32)
    pos32 = sb("pos32", [32, 64], F32)
    cum32 = sb("cum32", [32, 64], F32)
    zero32 = sb("zero32", [32, 64], F32)
    un32 = sb("un32", [32, 64], F32)
    sq32 = sb("sq32", [32, 32], F32)
    tq32 = sb("tq32", [32, 32], F32)
    srow = sb("srow", [1, 64], F32)
    orow = sb("orow", [1, 32], F32)
    lohi = sb("lohi", [1, 8], F32)
    pu1 = sb("pu1", [1, 1], U32)
    maskrow = sb("maskrow", [1, 512], F32)
    s16 = sb("s16", [16, 128], F32)

    d_s = dram("l0f_s", [16, 128], F32)
    d_m = dram("l0f_m", [32, 64], F32)

    s_io = p.get_sem("io")
    s_ld = p.sem("l0ld")
    s_xt = [p.sem("l0xa"), p.sem("l0xb")]
    s_st6 = [p.sem("l0sa"), p.sem("l0sb")]
    s_stats = p.get_sem("st")
    s_xn = p.get_sem("xn")
    s_tr = p.get_sem("tr")
    s_evac = p.get_sem("ev")
    s_sv = p.get_sem("sv")
    s_desc = p.get_sem("desc")
    s_dsc = p.get_sem("dsc")
    s_sdma = p.sem("l0sd")
    s_mdma = p.sem("l0md")
    s_wl = p.sem("l0wl")
    s_wh = [p.sem("l0wha"), p.sem("l0whb")]
    s_w2h = [p.sem("l0w2a"), p.sem("l0w2b")]
    s_mm1 = p.get_sem("mmy")
    s_gel = p.get_sem("silu")
    s_mm2 = p.get_sem("mmz")
    s_za = p.get_sem("za")
    s_msk = p.get_sem("msk")
    s_mbc = p.get_sem("mbc")
    s_ep = p.get_sem("ep")

    emit_bcast_load(sch, weights["b2_l"], b2l_bc, s_ld)
    emit_bcast_load(sch, weights["b2_h"], b2h_bc, s_ld)
    sch.emit("sync", lambda e: e.dma_start(out=b1lT, in_=weights["b1_lT"]), incs=[(s_ld, 16)])
    sch.emit("sync", lambda e: e.dma_start(out=b1hT, in_=weights["b1_hT"]), incs=[(s_ld, 16)])
    def vinit(e):
        e.memset(orow, 1.0)
        e.drain()
        e.memset(zero32, 0.0)
        e.drain()
        e.memset(sq32, 0.0)
        e.drain()
        e.memset(tq32, 0.0)
        e.drain()
        e.memset(srow, 0.0)
        e.drain()
        return e.memset(lohi, 0.0)
    sch.emit("vector", vinit, incs=[(s_desc, 1)])

    xnlT_cm = nc.sbuf_tensor(p.name("xnlT"), [128, 8 * T], BF16)
    xnlT = xnlT_cm.__enter__()[:]
    with nc.sbuf_tensor(p.name("xt"), [128, 2 * D], F32) as xt_t, \
         nc.sbuf_tensor(p.name("xinb"), [128, 2 * D], F32) as xin_t, \
         nc.sbuf_tensor(p.name("xnb"), [128, D], BF16) as xnb_t, \
         nc.sbuf_tensor(p.name("rbc3"), [128, 3 * D], F32) as rbc3_t, \
         nc.psum_tensor(p.name("pt0"), [128, 128], BF16) as pt0_t, \
         nc.psum_tensor(p.name("pt1"), [128, 128], BF16) as pt1_t:
        xt2 = xt_t[:]; xin2 = xin_t[:]; xnb = xnb_t[:]
        rbc3 = rbc3_t[:]
        r_bc = rbc3[:, 0:D]; gl_bc = rbc3[:, D:2 * D]; gh_bc = rbc3[:, 2 * D:3 * D]
        emit_bcast_load(sch, weights["routing_token"], r_bc, s_ld)
        emit_bcast_load(sch, weights["gamma_l"], gl_bc, s_ld)
        emit_bcast_load(sch, weights["gamma_h"], gh_bc, s_ld)
        ld_done = w(s_ld)
        sch.emit("scalar", lambda e: e.nop(), waits=[ld_done])
        sch.emit("vector", lambda e: e.nop(), waits=[ld_done])
        sch.emit("gpsimd", lambda e: e.nop(), waits=[ld_done])
        pt = (pt0_t[:], pt1_t[:])
        for i in range(NT):
            xti = xt2[:, (i % 2) * D:(i % 2 + 1) * D]
            xini = xin2[:, (i % 2) * D:(i % 2 + 1) * D]
            sl = s_xt[i % 2]
            ldw = [w(s_io)] if i == 0 else ([] if i < 2 else [(s_xn, s_xn.n - 2)])
            def ldx(e, xti=xti, i=i):
                return e.dma_start(out=xti, in_=x_src[i * 128:(i + 1) * 128, :])
            sch.emit("sync", ldx, waits=ldw, incs=[(sl, 16)])
            emit_ln(sch, xti, xini, stat, scr, s_stats, done=s_xn,
                    scalar_waits=[w(sl), w(s_xn), w(s_sv)])
            def sv(e, xini=xini, i=i):
                e.tensor_tensor(out=scr, in0=xini, in1=r_bc, op=ALU.mult)
                e.drain()
                return e.tensor_reduce(out=s_sb[:, i:i + 1], in_=scr, axis=AX.X, op=ALU.add)
            sch.emit("vector", sv, waits=[w(s_xn)], incs=[(s_sv, 1)])
            ssq = stat[:, 4:5]; rn = stat[:, 5:6]
            def rms_s(e, xini=xini, ssq=ssq):
                e.activation(out=scr2, in_=xini, func=AF.Square, accum_out=ssq)
                e.drain()
                return e.activation(out=ssq, in_=ssq, func=AF.Sqrt)
            sch.emit("scalar", rms_s, waits=[w(s_sv)], incs=[(s_dsc, 1)])
            def rms_v(e, ssq=ssq, rn=rn):
                e.tensor_scalar(out=rn, in0=ssq, scalar1=1e-12, scalar2=None, op0=ALU.max)
                e.drain()
                e.reciprocal(out=rn, in_=rn)
                e.drain()
                return e.tensor_scalar(out=rn, in0=rn, scalar1=SQD, scalar2=None, op0=ALU.mult)
            sch.emit("vector", rms_v, waits=[w(s_dsc)], incs=[(s_desc, 1)])
            for gbc, dstT in ((gl_bc, xnlT), (gh_bc, xnhT)):
                def mknorm(e, xini=xini, gbc=gbc, rn=rn):
                    e.tensor_scalar(out=scr2, in0=xini, scalar1=rn, scalar2=None, op0=ALU.mult)
                    e.drain()
                    return e.tensor_tensor(out=xnb, in0=scr2, in1=gbc, op=ALU.mult)
                sch.emit("vector", mknorm, waits=[w(s_desc), w(s_tr)], incs=[(s_xn, 1)])
                for k in range(8):
                    pbank = pt[k % 2]
                    evac_snapshot = s_evac.n
                    waits = [w(s_xn)] if k == 0 else []
                    if evac_snapshot >= 2:
                        waits.append((s_evac, evac_snapshot - 1))
                    def tr(e, k=k, pbank=pbank):
                        return e.transpose(out=pbank, in_=xnb[:, k * 128:(k + 1) * 128], identity=ident_bf)
                    sch.emit("tensor", tr, waits=waits, incs=[(s_tr, 1)])
                    dst = dstT[:, k * T + i * 128: k * T + (i + 1) * 128]
                    def ev(e, dst=dst, pbank=pbank):
                        return e.tensor_copy(out=dst, in_=pbank)
                    sch.emit("vector", ev, waits=[w(s_tr)], incs=[(s_evac, 1)])
        sch.flush("l0f_p1")
    xT_ready = w(s_evac)

    with nc.psum_tensor(p.name("ptsf"), [16, 128], F32) as ptsf_t:
        ptsf = ptsf_t[:]
        def str_(e):
            return e.transpose(out=ptsf, in_=s_sb, identity=ident_f)
        sch.emit("tensor", str_, waits=[w(s_sv)], incs=[(s_tr, 1)])
        def sev_(e):
            return e.tensor_copy(out=s16, in_=ptsf)
        sch.emit("vector", sev_, waits=[w(s_tr)], incs=[(s_desc, 1)])
        def sst(e):
            return e.dma_start(out=d_s, in_=s16)
        sch.emit("sync", sst, waits=[w(s_desc)], incs=[(s_sdma, 16)])
        def sld(e):
            return e.dma_start(out=s32, in_=d_s.rearrange("a b -> (a b)").rearrange("(P F) -> P F", P=32))
        sch.emit("sync", sld, waits=[w(s_sdma)], incs=[(s_sdma, 16)])
        sch.flush("l0f_p2")
    s32_ready = w(s_sdma)

    with nc.sbuf_tensor(p.name("wl"), [128, 8 * 512], BF16) as wl_t, \
         nc.sbuf_tensor(p.name("w2l"), [128, 4 * D], BF16) as w2l_t, \
         nc.sbuf_tensor(p.name("hlT"), [128, 4 * T], BF16) as hlT_t, \
         nc.psum_tensor(p.name("pl0"), [128, 512], F32) as pl0_t, \
         nc.psum_tensor(p.name("pl1"), [128, 512], F32) as pl1_t:
        wl = wl_t[:]; w2l = w2l_t[:]; hlT = hlT_t[:]
        pl = (pl0_t[:], pl1_t[:])
        for kc in range(8):
            def lw(e, kc=kc):
                return e.dma_start(out=wl[:, kc * 512:(kc + 1) * 512],
                                   in_=weights["w1_lT"][kc * 128:(kc + 1) * 128, :])
            sch.emit("gpsimd", lw, incs=[(s_wl, 16)])
        for j in range(4):
            def lw2(e, j=j):
                return e.dma_start(out=w2l[:, j * D:(j + 1) * D],
                                   in_=weights["w2_lT"][j * 128:(j + 1) * 128, :])
            sch.emit("gpsimd", lw2, incs=[(s_wl, 16)])
        wl_done = w(s_wl)
        for f4 in range(4):
            for mc in range(4):
                idx = f4 * 4 + mc
                pb = pl[idx % 2]
                waits = [wl_done, xT_ready] if idx == 0 else []
                if idx >= 2:
                    waits.append((s_gel, s_gel.n - 1))
                def mm(e, f4=f4, mc=mc, pb=pb):
                    inst = None
                    for kc in range(8):
                        inst = e.matmul(out=pb,
                                        lhsT=wl[:, kc * 512 + f4 * 128: kc * 512 + (f4 + 1) * 128],
                                        rhs=xnlT[:, kc * T + mc * 512: kc * T + (mc + 1) * 512],
                                        start=(kc == 0), stop=(kc == 7))
                    return inst
                sch.emit("tensor", mm, waits=waits, incs=[(s_mm1, 1)])
                dst = hlT[:, f4 * T + mc * 512: f4 * T + (mc + 1) * 512]
                def gel(e, dst=dst, pb=pb, f4=f4):
                    return e.activation(out=dst, in_=pb, func=AF.Gelu, bias=b1lT[:, f4:f4 + 1])
                sch.emit("scalar", gel, waits=[w(s_mm1)], incs=[(s_gel, 1)])
        gel_done = w(s_gel)
        for mt in range(16):
            for dc in range(2):
                idx = mt * 2 + dc
                pb = pl[idx % 2]
                waits = [gel_done] if idx == 0 else []
                if idx >= 2:
                    waits.append((s_za, s_za.n - 1))
                def mm2(e, mt=mt, dc=dc, pb=pb):
                    inst = None
                    for f4 in range(4):
                        inst = e.matmul(out=pb,
                                        lhsT=hlT[:, f4 * T + mt * 128: f4 * T + (mt + 1) * 128],
                                        rhs=w2l[:, f4 * D + dc * 512: f4 * D + (dc + 1) * 512],
                                        start=(f4 == 0), stop=(f4 == 3))
                    return inst
                sch.emit("tensor", mm2, waits=waits, incs=[(s_mm2, 1)])
                dst = z_acc[:, mt * D + dc * 512: mt * D + (dc + 1) * 512]
                def zseed(e, dst=dst, pb=pb, dc=dc):
                    return e.tensor_tensor(out=dst, in0=pb, in1=b2l_bc[:, dc * 512:(dc + 1) * 512], op=ALU.add)
                sch.emit("vector", zseed, waits=[w(s_mm2)], incs=[(s_za, 1)])
        sch.flush("l0f_p4")

    xnlT_cm.__exit__(None, None, None)

    def col_to_row(e, col_ap, row_dst):
        e.tensor_copy(out=sq32[:, 0:1], in_=col_ap)
        e.drain()
        e.transpose(out=tq32, in_=sq32)
        e.drain()
        e.tensor_copy(out=row_dst, in_=tq32[0:1, 0:32])
        return e.drain()

    def scalar_to_col(e, sc_ap, col_dst):
        e.tensor_scalar(out=tq32[0:1, 0:32], in0=orow[0:1, 0:32], scalar1=sc_ap, scalar2=None, op0=ALU.mult)
        e.drain()
        e.transpose(out=sq32, in_=tq32)
        e.drain()
        e.tensor_copy(out=col_dst, in_=sq32[:, 0:1])
        return e.drain()

    def d0(e):
        return e.tensor_scalar(out=b32, in0=s32, scalar1=-1.0, scalar2=None, op0=ALU.mult)
    sch.emit("vector", d0, waits=[s32_ready], incs=[(s_desc, 1)])
    psum_ = stat[0:32, 6:7]
    acol = stat[0:32, 7:8]
    for it in range(50):
        def d1(e):
            return e.tensor_add(out=t32, in0=s32, in1=b32)
        sch.emit("vector", d1, waits=[w(s_desc)], incs=[(s_desc, 1)])
        def d2(e):
            return e.activation(out=t32, in_=t32, func=AF.Exp, accum_out=psum_)
        sch.emit("scalar", d2, waits=[w(s_desc)] + ([w(s_dsc)] if it > 0 else []), incs=[(s_dsc, 1)])
        def d3(e):
            col_to_row(e, psum_, srow[0:1, 0:32])
            e.tensor_reduce(out=lohi[:, 7:8], in_=srow[0:1, 0:32], axis=AX.X, op=ALU.add)
            return e.drain()
        sch.emit("vector", d3, waits=[w(s_dsc)], incs=[(s_desc, 1)])
        def d4(e):
            return e.activation(out=lohi[:, 7:8], in_=lohi[:, 7:8], func=AF.Ln)
        sch.emit("scalar", d4, waits=[w(s_desc)], incs=[(s_dsc, 1)])
        def d5(e):
            e.tensor_scalar(out=lohi[:, 7:8], in0=lohi[:, 7:8], scalar1=-1.0, scalar2=LOGK,
                            op0=ALU.mult, op1=ALU.add)
            e.drain()
            scalar_to_col(e, lohi[:, 7:8], acol)
            e.tensor_scalar(out=b32, in0=s32, scalar1=acol, scalar2=None, op0=ALU.add)
            e.drain()
            return e.tensor_scalar(out=b32, in0=b32, scalar1=0.0, scalar2=-1.0, op0=ALU.max, op1=ALU.mult)
        sch.emit("vector", d5, waits=[w(s_dsc)], incs=[(s_desc, 1)])
    def du(e):
        return e.tensor_scalar(out=u32, in0=s32, scalar1=acol, scalar2=None, op0=ALU.add)
    sch.emit("vector", du, waits=[w(s_desc)], incs=[(s_desc, 1)])
    sch.flush("l0f_p3")

    def m0(e):
        e.tensor_scalar(out=pos32, in0=u32, scalar1=0.0, scalar2=None, op0=ALU.is_ge)
        e.drain()
        e.tensor_tensor_scan(out=cum32, data0=pos32, data1=zero32, initial=0.0,
                             op0=ALU.add, op1=ALU.add)
        e.drain()
        col_to_row(e, cum32[:, 63:64], srow[0:1, 0:32])
        e.tensor_tensor_scan(out=srow[0:1, 0:32], data0=srow[0:1, 0:32], data1=orow[0:1, 0:32],
                             op0=ALU.add, op1=ALU.mult, initial=0.0)
        e.drain()
        e.tensor_copy(out=lohi[:, 6:7], in_=srow[0:1, 31:32])
        e.drain()
        e.memset(srow[:, 32:33], 0.0)
        e.drain()
        e.tensor_copy(out=srow[:, 33:64], in_=srow[0:1, 0:31])
        e.drain()
        e.tensor_copy(out=tq32[0:1, 0:32], in_=srow[0:1, 32:64])
        e.drain()
        e.transpose(out=sq32, in_=tq32)
        e.drain()
        e.tensor_scalar(out=cum32, in0=cum32, scalar1=sq32[:, 0:1], scalar2=None, op0=ALU.add)
        e.drain()
        e.tensor_scalar(out=m32, in0=cum32, scalar1=1024.0, scalar2=None, op0=ALU.is_le)
        e.drain()
        e.tensor_tensor(out=m32, in0=m32, in1=pos32, op=ALU.mult)
        e.drain()
        e.tensor_scalar(out=lohi[:, 4:5], in0=lohi[:, 6:7], scalar1=-1.0, scalar2=1024.0,
                        op0=ALU.mult, op1=ALU.add)
        e.drain()
        e.tensor_scalar(out=lohi[:, 5:6], in0=lohi[:, 4:5], scalar1=0.0, scalar2=None, op0=ALU.is_gt)
        e.drain()
        e.tensor_scalar(out=t32, in0=pos32, scalar1=-1e38, scalar2=None, op0=ALU.mult)
        e.drain()
        e.tensor_tensor(out=un32, in0=u32, in1=t32, op=ALU.min)
        e.drain()
        e.tensor_tensor(out=t32, in0=u32, in1=pos32, op=ALU.mult)
        e.drain()
        e.tensor_sub(out=t32, in0=u32, in1=t32)
        e.drain()
        e.tensor_reduce(out=stat[0:32, 6:7], in_=t32, axis=AX.X, op=ALU.min)
        e.drain()
        col_to_row(e, stat[0:32, 6:7], srow[0:1, 0:32])
        e.tensor_reduce(out=lohi[:, 0:1], in_=srow[0:1, 0:32], axis=AX.X, op=ALU.min)
        e.drain()
        return e.memset(lohi[:, 1:2], 0.0)
    sch.emit("vector", m0, waits=[w(s_desc)], incs=[(s_desc, 1)])
    mcol = stat[0:32, 6:7]
    for r_ in range(45):
        def br(e):
            e.tensor_add(out=lohi[:, 2:3], in0=lohi[:, 0:1], in1=lohi[:, 1:2])
            e.drain()
            e.tensor_scalar(out=lohi[:, 2:3], in0=lohi[:, 2:3], scalar1=0.5, scalar2=None, op0=ALU.mult)
            e.drain()
            scalar_to_col(e, lohi[:, 2:3], mcol)
            e.tensor_scalar(out=t32, in0=un32, scalar1=mcol, scalar2=None, op0=ALU.is_gt)
            e.drain()
            e.tensor_reduce(out=stat[0:32, 7:8], in_=t32, axis=AX.X, op=ALU.add)
            e.drain()
            col_to_row(e, stat[0:32, 7:8], srow[0:1, 0:32])
            e.tensor_reduce(out=lohi[:, 3:4], in_=srow[0:1, 0:32], axis=AX.X, op=ALU.add)
            e.drain()
            e.tensor_tensor(out=pu1, in0=lohi[:, 3:4], in1=lohi[:, 4:5], op=ALU.is_ge)
            e.drain()
            e.select(out=lohi[:, 0:1], mask=pu1, on_true=lohi[:, 2:3], on_false=lohi[:, 0:1], add_drain=True)
            e.drain()
            e.select(out=lohi[:, 1:2], mask=pu1, on_true=lohi[:, 1:2], on_false=lohi[:, 2:3], add_drain=True)
            return e.drain()
        sch.emit("vector", br, waits=[w(s_desc)], incs=[(s_desc, 1)])
    def mfin(e):
        scalar_to_col(e, lohi[:, 0:1], mcol)
        e.tensor_scalar(out=t32, in0=un32, scalar1=mcol, scalar2=None, op0=ALU.is_gt)
        e.drain()
        scalar_to_col(e, lohi[:, 5:6], mcol)
        e.tensor_scalar(out=t32, in0=t32, scalar1=mcol, scalar2=None, op0=ALU.mult)
        e.drain()
        e.tensor_add(out=m32, in0=m32, in1=t32)
        return e.drain()
    sch.emit("vector", mfin, waits=[w(s_desc)], incs=[(s_desc, 1)])
    def mst(e):
        return e.dma_start(out=d_m, in_=m32)
    sch.emit("sync", mst, waits=[w(s_desc)], incs=[(s_mdma, 16)])
    def mld2(e):
        return e.dma_start(out=s16, in_=d_m.rearrange("a b -> (a b)").rearrange("(a b) -> a b", a=16))
    sch.emit("sync", mld2, waits=[w(s_mdma)], incs=[(s_mdma, 16)])
    with nc.psum_tensor(p.name("ptm"), [128, 16], F32) as ptm_t, \
         nc.psum_tensor(p.name("pmb"), [128, 512], F32) as pmb_t:
        ptm = ptm_t[:]; pmb = pmb_t[:]
        def mtr(e):
            return e.transpose(out=ptm, in_=s16, identity=ident_f[0:16, 0:16])
        sch.emit("tensor", mtr, waits=[w(s_mdma)], incs=[(s_tr, 1)])
        def mtc(e):
            return e.tensor_copy(out=maskcol, in_=ptm)
        sch.emit("vector", mtc, waits=[w(s_tr)], incs=[(s_mbc, 1)])
        dmrow = d_m.rearrange("a b -> (a b)").rearrange("(c t) -> c t", c=4)
        for c in range(4):
            def mldr(e, c=c):
                return e.dma_start(out=maskrow, in_=dmrow[c:c + 1, :])
            sch.emit("sync", mldr, waits=[w(s_mdma)] + ([(s_mbc, s_mbc.n)] if c > 0 else []),
                     incs=[(s_mdma, 16)])
            def mbm(e, c=c):
                return e.matmul(out=pmb, lhsT=onesf, rhs=maskrow,
                                start=True, stop=True)
            sch.emit("tensor", mbm, waits=[w(s_mdma)] + ([(s_mbc, s_mbc.n)] if c > 0 else []),
                     incs=[(s_tr, 1)])
            def mbe(e, c=c):
                return e.tensor_copy(out=maskbc[:, c * 512:(c + 1) * 512], in_=pmb)
            sch.emit("vector", mbe, waits=[w(s_tr)], incs=[(s_mbc, 1)])
        sch.flush("l0f_p3b")
    mask_ready = w(s_mbc)

    with nc.sbuf_tensor(p.name("whh"), [128, 2 * 8 * 512], BF16) as whh_t, \
         nc.sbuf_tensor(p.name("w2h"), [128, 2 * 4 * D], BF16) as w2h_t, \
         nc.sbuf_tensor(p.name("hhT"), [128, 4 * T], BF16) as hhT_t, \
         nc.psum_tensor(p.name("ph0"), [128, 512], F32) as ph0_t, \
         nc.psum_tensor(p.name("ph1"), [128, 512], F32) as ph1_t:
        whh2 = whh_t[:]; w2h2 = w2h_t[:]; hhT = hhT_t[:]
        ph = (ph0_t[:], ph1_t[:])
        mm1_hist = []
        mm2_hist = []
        for fc in range(8):
            slot = fc % 2
            whh = whh2[:, slot * 8 * 512:(slot + 1) * 8 * 512]
            w2h = w2h2[:, slot * 4 * D:(slot + 1) * 4 * D]
            lw_waits = [] if fc < 2 else [(s_mm1, mm1_hist[fc - 2])]
            for kc in range(8):
                def lwh(e, fc=fc, kc=kc, whh=whh):
                    return e.dma_start(out=whh[:, kc * 512:(kc + 1) * 512],
                                       in_=weights["w1_hT"][kc * 128:(kc + 1) * 128, fc * 512:(fc + 1) * 512])
                sch.emit("gpsimd", lwh, waits=lw_waits if kc == 0 else [], incs=[(s_wh[slot], 16)])
            lw2_waits = [] if fc < 2 else [(s_mm2, mm2_hist[fc - 2])]
            for j in range(4):
                def lw2h(e, fc=fc, j=j, w2h=w2h):
                    return e.dma_start(out=w2h[:, j * D:(j + 1) * D],
                                       in_=weights["w2_hT"][fc * 512 + j * 128: fc * 512 + (j + 1) * 128, :])
                sch.emit("gpsimd", lw2h, waits=lw2_waits if j == 0 else [], incs=[(s_w2h[slot], 16)])
            wh_done = w(s_wh[slot])
            w2h_done = w(s_w2h[slot])
            mm2_fence = w(s_mm2)
            for f4 in range(4):
                for mc in range(4):
                    idx = fc * 16 + f4 * 4 + mc
                    pb = ph[idx % 2]
                    waits = [wh_done] if (f4 == 0 and mc == 0) else []
                    if idx >= 2:
                        waits.append((s_gel, s_gel.n - 1))
                    def mmh(e, f4=f4, mc=mc, pb=pb, whh=whh):
                        inst = None
                        for kc in range(8):
                            inst = e.matmul(out=pb,
                                            lhsT=whh[:, kc * 512 + f4 * 128: kc * 512 + (f4 + 1) * 128],
                                            rhs=xnhT[:, kc * T + mc * 512: kc * T + (mc + 1) * 512],
                                            start=(kc == 0), stop=(kc == 7))
                        return inst
                    sch.emit("tensor", mmh, waits=waits, incs=[(s_mm1, 1)])
                    dst = hhT[:, f4 * T + mc * 512: f4 * T + (mc + 1) * 512]
                    bidx = fc * 4 + f4
                    gw = [w(s_mm1)]
                    if f4 == 0 and mc == 0 and fc > 0:
                        gw.append(mm2_fence)
                        gw.append(w(s_msk))
                    def gelh(e, dst=dst, pb=pb, bidx=bidx):
                        return e.activation(out=dst, in_=pb, func=AF.Gelu, bias=b1hT[:, bidx:bidx + 1])
                    sch.emit("scalar", gelh, waits=gw, incs=[(s_gel, 1)])
            mm1_hist.append(s_mm1.n)
            gel_done_h = w(s_gel)
            for f4 in range(4):
                def mskp(e, f4=f4):
                    return e.tensor_tensor(out=hhT[:, f4 * T:(f4 + 1) * T],
                                           in0=hhT[:, f4 * T:(f4 + 1) * T],
                                           in1=maskbc, op=ALU.mult)
                sch.emit("vector", mskp, waits=([gel_done_h, mask_ready] if f4 == 0 else []),
                         incs=[(s_msk, 1)])
            msk_done = w(s_msk)
            for mt in range(16):
                for dc in range(2):
                    idx = fc * 32 + mt * 2 + dc
                    pb = ph[idx % 2]
                    waits = [msk_done, w2h_done] if (mt == 0 and dc == 0) else []
                    if idx >= 2:
                        waits.append((s_za, s_za.n - 1))
                    def mmh2(e, mt=mt, dc=dc, pb=pb, w2h=w2h):
                        inst = None
                        for f4 in range(4):
                            inst = e.matmul(out=pb,
                                            lhsT=hhT[:, f4 * T + mt * 128: f4 * T + (mt + 1) * 128],
                                            rhs=w2h[:, f4 * D + dc * 512: f4 * D + (dc + 1) * 512],
                                            start=(f4 == 0), stop=(f4 == 3))
                        return inst
                    sch.emit("tensor", mmh2, waits=waits, incs=[(s_mm2, 1)])
                    dst = z_acc[:, mt * D + dc * 512: mt * D + (dc + 1) * 512]
                    def zadd(e, dst=dst, pb=pb):
                        return e.tensor_tensor(out=dst, in0=dst, in1=pb, op=ALU.add)
                    sch.emit("vector", zadd, waits=[w(s_mm2)], incs=[(s_za, 1)])
            mm2_hist.append(s_mm2.n)
            sch.flush(f"l0f_p5_{fc}")

    with nc.sbuf_tensor(p.name("xt6"), [128, 2 * D], F32) as xt6_t:
        xt6 = xt6_t[:]
        z_done = w(s_za)
        for i in range(NT):
            xti = xt6[:, (i % 2) * D:(i % 2 + 1) * D]
            sl = s_xt[i % 2]
            ldw = [w(s_za, 0)] if False else ([] if i < 2 else [(s_ep, s_ep.n - 2), w(s_st6[i % 2])])
            def ldx6(e, xti=xti, i=i):
                return e.dma_start(out=xti, in_=x_src[i * 128:(i + 1) * 128, :])
            sch.emit("sync", ldx6, waits=ldw, incs=[(sl, 16)])
            z_t = z_acc[:, i * D:(i + 1) * D]
            def brank(e, z_t=z_t, i=i):
                return e.scalar_tensor_tensor(out=z_t, in0=b2h_bc, scalar=maskcol[:, i:i + 1],
                                              in1=z_t, op0=ALU.mult, op1=ALU.add)
            bw_ = [z_done, mask_ready] if i == 0 else [w(s_ep)]
            sch.emit("vector", brank, waits=bw_, incs=[(s_ep, 1)])
            emit_ln(sch, z_t, scr2, stat, scr, s_stats, done=s_ep,
                    scalar_waits=[w(s_ep)])
            def xadd(e, xti=xti):
                return e.tensor_tensor(out=xti, in0=xti, in1=scr2, op=ALU.add)
            sch.emit("vector", xadd, waits=[w(sl), w(s_ep)], incs=[(s_ep, 1)])
            def stx(e, xti=xti, i=i):
                return e.dma_start(out=x_dst[i * 128:(i + 1) * 128, :], in_=xti)
            sch.emit("sync", stx, waits=[w(s_ep)], incs=[(s_st6[i % 2], 16)])
        sch.emit("sync", lambda e: e.nop(), waits=[w(s_st6[0]), w(s_st6[1])], incs=[(s_io, 16)])
        sch.flush("l0f_p6")
    ctx.close()
    return s_io


L0_FFN_DECL = {
    "routing_token": ([1, D], F32), "gamma_l": ([1, D], F32), "gamma_h": ([1, D], F32),
    "w1_lT": ([1024, 512], F32), "b1_lT": ([128, 4], F32),
    "w2_lT": ([512, 1024], F32), "b2_l": ([1, D], F32),
    "w1_hT": ([1024, 4096], F32), "b1_hT": ([128, 32], F32),
    "w2_hT": ([4096, 1024], F32), "b2_h": ([1, D], F32),
}


def l0_ffn_weights(inp, pre=""):
    return {
        pre + "routing_token": inp["L0_ffn_routing_token"].reshape(1, D),
        pre + "gamma_l": inp["L0_ffn_gamma_l"].reshape(1, D),
        pre + "gamma_h": inp["L0_ffn_gamma_h"].reshape(1, D),
        pre + "w1_lT": np.ascontiguousarray(inp["L0_ffn_w1_l"].T),
        pre + "b1_lT": np.ascontiguousarray(inp["L0_ffn_b1_l"].reshape(4, 128).T),
        pre + "w2_lT": np.ascontiguousarray(inp["L0_ffn_w2_l"].T),
        pre + "b2_l": inp["L0_ffn_b2_l"].reshape(1, D),
        pre + "w1_hT": np.ascontiguousarray(inp["L0_ffn_w1_h"].T),
        pre + "b1_hT": np.ascontiguousarray(inp["L0_ffn_b1_h"].reshape(32, 128).T),
        pre + "w2_hT": np.ascontiguousarray(inp["L0_ffn_w2_h"].T),
        pre + "b2_h": inp["L0_ffn_b2_h"].reshape(1, D),
    }


def emit_l0_attn(p, x_src, x_dst, weights, consts, use_collective=True, n_cores=8, mode="mono"):
    nc = p.nc
    ident_bf = consts["ident_bf"]
    ident_f = consts["ident_f"]
    NH, DK, CH = 16, 64, 64
    NCH = T // CH
    E = 8
    CAP = 5120.0
    sch = Sched(p)
    ctx = ExitStack()
    sb = lambda n, s, d: ctx.enter_context(nc.sbuf_tensor(p.name(n), s, d))[:]
    dram = lambda n, s, d: nc.dram_tensor(p.name(n), s, d, kind="Internal").ap()

    if mode == "a":
        d_xmid = weights["xmid_out"]
    elif mode == "b":
        d_xmid = weights["xmid_in"]
    else:
        d_xmid = dram("l0a_xmid", [T, D], F32)
    d_cnt = dram("l0a_cnt", [E, 1], F32)
    d_gath = dram("l0a_gath", [n_cores * E, 1], F32)

    scr = sb("scr", [128, D], F32)
    scr2 = sb("scr2", [128, D], F32)
    stat = sb("stat", [128, 8], F32)

    s_io = p.get_sem("io")
    s_ld = p.sem("a0ld")
    s_xt = [p.sem("a0xa"), p.sem("a0xb")]
    s_st6 = [p.sem("a0sa"), p.sem("a0sb")]
    s_stats = p.get_sem("st")
    s_xn = p.get_sem("xn")
    s_tr = p.get_sem("tr")
    s_evac = p.get_sem("ev")
    s_wld = p.sem("a0wld")
    s_w2ld = [p.sem("a0w2a"), p.sem("a0w2b")]
    s_proj = p.get_sem("mmy")
    s_pev = p.get_sem("silu")
    s_rope = p.get_sem("desc")
    s_phis = p.get_sem("dsc")
    s_mmS = p.get_sem("mmz")
    s_Sacc = p.get_sem("za")
    s_mmN = p.get_sem("w1")
    s_epi = p.get_sem("w2")
    s_aoT = p.get_sem("ab")
    s_gt = p.get_sem("msk")
    s_gtT = p.get_sem("mbc")
    s_coll = p.sem("a0coll")
    s_cldma = p.sem("a0cd")
    s_za = p.get_sem("za")
    s_mm1 = p.get_sem("zmm")
    s_mm2 = p.get_sem("mmzz")
    s_ep = p.get_sem("ep")

    if mode != "b":
        aoT_cm = nc.sbuf_tensor(p.name("aoT"), [128, 8 * T], BF16)
        aoT = aoT_cm.__enter__()[:]
    if mode != "b":
        NHD = NH * 72
        bs_ctx = ExitStack()
        _sb = lambda n, sh, d: bs_ctx.enter_context(nc.sbuf_tensor(p.name(n), sh, d))[:]
        _ps = lambda n, sh, d: bs_ctx.enter_context(nc.psum_tensor(p.name(n), sh, d))[:]
        if True:
            wall = _sb("wqkvg", [128, 4 * 8 * D], BF16)
            bqkg = _sb("bqkg", [128, 4 * D], F32)
            xt2 = _sb("xt", [128, 2 * D], F32)
            xin2 = _sb("xinb", [128, 2 * D], F32)
            xnb = _sb("xnb", [128, D], BF16)
            n1bc = _sb("n1bc", [128, D], F32)
            rcos = _sb("rcos", [128, NT * 32], F32)
            rsin = _sb("rsin", [128, NT * 32], F32)
            rsc = _sb("rsc", [128, 2 * 512], F32)
            rmsTt2 = _sb("rmsTt", [128, 2 * 8 * 128], BF16)
            qkvg2 = _sb("qkvg", [128, 2 * 4 * D], BF16)
            SaccS = _sb("SaccS", [128, 512], F32)
            SaccZ = _sb("SaccZ", [128, 8], F32)
            SbfS = _sb("SbfS", [128, 512], BF16)
            SbfZ = _sb("SbfZ", [128, 8], BF16)
            ones64 = _sb("ones64", [128, 1], BF16)
            qtb2 = _sb("qtb", [128, 2 * 512], BF16)
            gext = _sb("gext", [128, 512], F32)
            nscb = _sb("nscb", [128, 512], F32)
            recb = _sb("recb", [128, 8], F32)
            aobb = _sb("aobb", [128, 512], BF16)
            pt = (_ps("pt0", [128, 128], BF16), _ps("pt1", [128, 128], BF16))
            pp = (_ps("pp0", [128, 512], F32), _ps("pp1", [128, 512], F32))
            pSb = _ps("pSb", [128, 512], F32)
            pZb = _ps("pZb", [128, 8], F32)
            pNb = _ps("pNb", [128, 512], F32)
            pNzb = _ps("pNzb", [128, 8], F32)

            emit_bcast_load(sch, weights["norm1_scale"], n1bc, s_ld)
            sch.emit("sync", lambda e: e.dma_start(
                out=rcos.rearrange("p (n f) -> p n f", f=32),
                in_=weights["ropecos"].rearrange("(n p) f -> p n f", p=128)), incs=[(s_ld, 16)])
            sch.emit("sync", lambda e: e.dma_start(
                out=rsin.rearrange("p (n f) -> p n f", f=32),
                in_=weights["ropesin"].rearrange("(n p) f -> p n f", p=128)), incs=[(s_ld, 16)])
            emit_bcast_load(sch, weights["bq"], bqkg[:, 0:D], s_ld)
            emit_bcast_load(sch, weights["bk"], bqkg[:, D:2 * D], s_ld)
            emit_bcast_load(sch, weights["bv"], bqkg[:, 2 * D:3 * D], s_ld)
            emit_bcast_load(sch, weights["bg"], bqkg[:, 3 * D:4 * D], s_ld)
            for pi, wname in enumerate(["Wq", "Wk", "Wv", "Wg"]):
                for kc in range(8):
                    def ldw_(e, wname=wname, pi=pi, kc=kc):
                        return e.dma_start(out=wall[:, (pi * 8 + kc) * D:(pi * 8 + kc + 1) * D],
                                           in_=weights[wname][kc * 128:(kc + 1) * 128, :])
                    sch.emit("gpsimd", ldw_, incs=[(s_wld, 16)])
            ld_done = w(s_ld)
            wld_done = w(s_wld)
            sch.emit("scalar", lambda e: e.nop(), waits=[ld_done])
            def vi(e):
                e.memset(ones64, 1.0)
                e.drain()
                e.memset(SaccS, 0.0)
                e.drain()
                return e.memset(SaccZ, 0.0)
            sch.emit("vector", vi, waits=[ld_done], incs=[(s_Sacc, 1)])

            pev_hist = []
            rsc_hist = []
            nw_hist = []
            sadd_hist = []
            epi_hist = []
            qev_hist = []
            for tt in range(NT):
                tp = tt % 2
                xti = xt2[:, tp * D:(tp + 1) * D]
                xini = xin2[:, tp * D:(tp + 1) * D]
                sl = s_xt[tp]
                ldw = [w(s_io)] if tt == 0 else ([] if tt < 2 else [(s_xn, s_xn.n - 2)])
                def ldx(e, xti=xti, tt=tt):
                    return e.dma_start(out=xti, in_=x_src[tt * 128:(tt + 1) * 128, :])
                sch.emit("sync", ldx, waits=ldw, incs=[(sl, 16)])
                emit_ln(sch, xti, xini, stat, scr2, s_stats, done=s_xn,
                        scalar_waits=[w(sl), w(s_xn)])
                ssq = stat[:, 4:5]; rr = stat[:, 5:6]
                def rms_s(e, xini=xini, ssq=ssq):
                    e.activation(out=scr2, in_=xini, func=AF.Square, accum_out=ssq)
                    e.drain()
                    return e.activation(out=ssq, in_=ssq, func=AF.Sqrt, scale=p.cv(3))
                sch.emit("scalar", rms_s, waits=[w(s_xn)], incs=[(s_phis, 1)])
                def rms_v(e, rr=rr, ssq=ssq):
                    e.reciprocal(out=rr, in_=ssq)
                    return e.drain()
                sch.emit("vector", rms_v, waits=[w(s_phis)], incs=[(s_rope, 1)])
                def mknorm(e, xini=xini, rr=rr):
                    e.tensor_scalar(out=scr2, in0=xini, scalar1=rr, scalar2=None, op0=ALU.mult)
                    e.drain()
                    return e.tensor_tensor(out=xnb, in0=scr2, in1=n1bc, op=ALU.mult)
                sch.emit("vector", mknorm, waits=[w(s_rope), w(s_tr)], incs=[(s_xn, 1)])
                rmsTt = rmsTt2[:, tp * 8 * 128:(tp + 1) * 8 * 128]
                for k in range(8):
                    pbank = pt[k % 2]
                    evac_snapshot = s_evac.n
                    waits = [w(s_xn)] if k == 0 else []
                    if k == 0 and tt > 0:
                        waits.append((s_aoT, s_aoT.n))
                    if evac_snapshot >= 2:
                        waits.append((s_evac, evac_snapshot - 1))
                    def tr(e, k=k, pbank=pbank):
                        return e.transpose(out=pbank, in_=xnb[:, k * 128:(k + 1) * 128], identity=ident_bf)
                    sch.emit("tensor", tr, waits=waits, incs=[(s_tr, 1)])
                    dst = rmsTt[:, k * 128:(k + 1) * 128]
                    def ev(e, dst=dst, pbank=pbank):
                        return e.tensor_copy(out=dst, in_=pbank)
                    sch.emit("vector", ev, waits=[w(s_tr)], incs=[(s_evac, 1)])
                rmsTt_ready = w(s_evac)
                qkvg = qkvg2[:, tp * 4 * D:(tp + 1) * 4 * D]
                for pi in range(4):
                    for hc in range(2):
                        idx = tt * 8 + pi * 2 + hc
                        pb = pp[idx % 2]
                        waits = [rmsTt_ready]
                        if pi == 0 and hc == 0 and tt == 0:
                            waits.append(wld_done)
                        if idx >= 2:
                            waits.append((s_pev, pev_hist[idx - 2]))
                        def mmv(e, pi=pi, hc=hc, pb=pb, rmsTt=rmsTt):
                            inst = None
                            for kc in range(8):
                                inst = e.matmul(out=pb,
                                                lhsT=rmsTt[:, kc * 128:(kc + 1) * 128],
                                                rhs=wall[:, (pi * 8 + kc) * D + hc * 512:(pi * 8 + kc) * D + (hc + 1) * 512],
                                                start=(kc == 0), stop=(kc == 7))
                            return inst
                        sch.emit("tensor", mmv, waits=waits, incs=[(s_proj, 1)])
                        dst = qkvg[:, pi * D + hc * 512: pi * D + (hc + 1) * 512]
                        bsl = bqkg[:, pi * D + hc * 512: pi * D + (hc + 1) * 512]
                        rbuf = rsc[:, (idx % 2) * 512:(idx % 2 + 1) * 512]
                        if pi == 3:
                            def evg1(e, pb=pb, bsl=bsl, rbuf=rbuf):
                                return e.tensor_tensor(out=rbuf, in0=pb, in1=bsl, op=ALU.add)
                            gvw = [w(s_proj)]
                            if len(rsc_hist) >= 2:
                                gvw.append((s_phis, rsc_hist[-2]))
                            sch.emit("vector", evg1, waits=gvw, incs=[(s_pev, 1)])
                            pev_hist.append(s_pev.n)
                            def evg2(e, dst=dst, rbuf=rbuf):
                                return e.activation(out=dst, in_=rbuf, func=AF.Sigmoid)
                            sch.emit("scalar", evg2, waits=[w(s_pev)], incs=[(s_phis, 1)])
                            rsc_hist.append(s_phis.n)
                        elif pi == 2:
                            def evv(e, dst=dst, pb=pb, bsl=bsl):
                                return e.tensor_tensor(out=dst, in0=pb, in1=bsl, op=ALU.add)
                            sch.emit("vector", evv, waits=[w(s_proj)], incs=[(s_pev, 1)])
                            pev_hist.append(s_pev.n)
                        else:
                            def evqk(e, pb=pb, bsl=bsl, rbuf=rbuf):
                                return e.tensor_tensor(out=rbuf, in0=pb, in1=bsl, op=ALU.add)
                            evw = [w(s_proj)]
                            if len(rsc_hist) >= 2:
                                evw.append((s_phis, rsc_hist[-2]))
                            sch.emit("vector", evqk, waits=evw, incs=[(s_pev, 1)])
                            pev_hist.append(s_pev.n)
                            cossl = rcos[:, tt * 32:(tt + 1) * 32]
                            sinsl = rsin[:, tt * 32:(tt + 1) * 32]
                            def rope(e, rbuf=rbuf, cossl=cossl, sinsl=sinsl):
                                v3 = lambda ap: ap.rearrange("p (h f) -> p h f", f=64)
                                lo = v3(rbuf)[:, :, 0:32]
                                hi = v3(rbuf)[:, :, 32:64]
                                cb = cossl.unsqueeze(1).to_broadcast([128, 8, 32])
                                sbx = sinsl.unsqueeze(1).to_broadcast([128, 8, 32])
                                s1 = v3(scr[:, 0:512])[:, :, 0:32]
                                s2 = v3(scr[:, 0:512])[:, :, 32:64]
                                e.tensor_tensor(out=s1, in0=lo, in1=cb, op=ALU.mult)
                                e.drain()
                                e.tensor_tensor(out=s2, in0=hi, in1=sbx, op=ALU.mult)
                                e.drain()
                                e.tensor_sub(out=s1, in0=s1, in1=s2)
                                e.drain()
                                e.tensor_tensor(out=s2, in0=lo, in1=sbx, op=ALU.mult)
                                e.drain()
                                e.tensor_tensor(out=lo, in0=hi, in1=cb, op=ALU.mult)
                                e.drain()
                                e.tensor_add(out=s2, in0=s2, in1=lo)
                                e.drain()
                                e.tensor_scalar(out=scr2[:, 0:512], in0=scr[:, 0:512], scalar1=0.0, scalar2=None, op0=ALU.max)
                                e.drain()
                                e.tensor_sub(out=rbuf, in0=scr[:, 0:512], in1=scr2[:, 0:512])
                                return e.drain()
                            sch.emit("vector", rope, waits=[w(s_pev), w(s_phis)], incs=[(s_rope, 1)])
                            def phiexp(e, rbuf=rbuf):
                                return e.activation(out=rbuf, in_=rbuf, func=AF.Exp)
                            sch.emit("scalar", phiexp, waits=[w(s_rope)], incs=[(s_phis, 1)])
                            rsc_hist.append(s_phis.n)
                            def phiadd(e, dst=dst, rbuf=rbuf):
                                return e.tensor_add(out=dst, in0=rbuf, in1=scr2[:, 0:512])
                            sch.emit("vector", phiadd, waits=[w(s_phis)], incs=[(s_rope, 1)])
                proj_tile_done = w(s_rope)
                pev_tile_done = w(s_pev)
                g_ready = w(s_phis)
                q_tm = qkvg[:, 0:D]
                k_tm = qkvg[:, D:2 * D]
                v_tm = qkvg[:, 2 * D:3 * D]
                g_tm = qkvg[:, 3 * D:4 * D]
                aoT3 = aoT.rearrange("p (s t) -> p s t", s=8)
                s_qq = p.get_sem("qq")
                for bi in range(8):
                    cc = bi // 4
                    u = bi % 4
                    pbase = cc * 64
                    bank = pt[bi % 2]
                    qw = []
                    if bi == 0:
                        qw = [proj_tile_done, (s_evac, s_evac.n)]
                        if tt > 0:
                            qw.append((s_mmN, nw_hist[-1]))
                    elif bi >= 2:
                        qw = [(s_qq, s_qq.n - 1)]
                    def qtrb(e, cc=cc, u=u, pbase=pbase, bank=bank):
                        e.transpose(out=bank[:, 0:64],
                                    in_=q_tm[pbase:pbase + 64, (4 * u) * 64:(4 * u) * 64 + 128],
                                    identity=ident_bf[pbase:pbase + 64, pbase:pbase + 64])
                        return e.transpose(out=bank[:, 64:128],
                                           in_=q_tm[pbase:pbase + 64, (4 * u + 2) * 64:(4 * u + 2) * 64 + 128],
                                           identity=ident_bf[pbase:pbase + 64, pbase:pbase + 64])
                    sch.emit("tensor", qtrb, waits=qw, incs=[(s_mmN, 1)])
                    def qevb(e, cc=cc, u=u, bank=bank):
                        return e.tensor_copy(out=qtb2[:, cc * 512 + (2 * u) * 64: cc * 512 + (2 * u) * 64 + 128],
                                             in_=bank)
                    sch.emit("vector", qevb, waits=[w(s_mmN)], incs=[(s_qq, 1)])
                qev_hist.append(s_qq.n)
                for cc in range(2):
                    c = tt * 2 + cc
                    pbase = cc * 64
                    qtb = qtb2[:, cc * 512:(cc + 1) * 512]
                    g3 = g_tm.rearrange("p (j two f) -> p two j f", two=2, f=64)
                    def gxc(e, pbase=pbase, g3=g3):
                        e.tensor_copy(out=gext[0:64, :].rearrange("p (one j f) -> p one j f", one=1, f=64),
                                      in_=g3[pbase:pbase + 64, 0:1, :, :])
                        e.drain()
                        return e.tensor_copy(out=gext[64:128, :].rearrange("p (one j f) -> p one j f", one=1, f=64),
                                             in_=g3[pbase:pbase + 64, 1:2, :, :])
                    sch.emit("vector", gxc, waits=([g_ready] if cc == 0 else []) + [(s_epi, s_epi.n)], incs=[(s_epi, 1)])
                    mw = []
                    if cc == 0 and tt == 0:
                        mw = [pev_tile_done, proj_tile_done, w(s_Sacc)]
                    elif cc == 0:
                        mw = [pev_tile_done, proj_tile_done, (s_Sacc, sadd_hist[-1])]
                    else:
                        mw = [(s_Sacc, sadd_hist[-1])]
                    def mmS(e, pbase=pbase):
                        inst = None
                        for pq in range(2):
                            for j in range(8):
                                h = 2 * j + pq
                                e.matmul(out=pSb[pq * 64:pq * 64 + 64, j * 64:(j + 1) * 64],
                                         lhsT=k_tm[pbase:pbase + 64, h * 64:(h + 1) * 64],
                                         rhs=v_tm[pbase:pbase + 64, h * 64:(h + 1) * 64],
                                         start=True, stop=True)
                                inst = e.matmul(out=pZb[pq * 64:pq * 64 + 64, j:j + 1],
                                                lhsT=k_tm[pbase:pbase + 64, h * 64:(h + 1) * 64],
                                                rhs=ones64[pbase:pbase + 64, :],
                                                start=True, stop=True)
                        return inst
                    sch.emit("tensor", mmS, waits=mw, incs=[(s_mmS, 1)])
                    def sadd(e):
                        e.tensor_tensor(out=SaccS, in0=SaccS, in1=pSb, op=ALU.add)
                        e.drain()
                        return e.tensor_tensor(out=SaccZ, in0=SaccZ, in1=pZb, op=ALU.add)
                    sch.emit("vector", sadd, waits=[w(s_mmS), (s_Sacc, s_Sacc.n)], incs=[(s_Sacc, 1)])
                    sadd_hist.append(s_Sacc.n)
                    scw = ([(s_mmN, nw_hist[-1])] if c > 0 else []) + [(s_Sacc, s_Sacc.n)]
                    def scast(e):
                        e.tensor_copy(out=SbfS, in_=SaccS)
                        e.drain()
                        return e.tensor_copy(out=SbfZ, in_=SaccZ)
                    sch.emit("vector", scast, waits=scw, incs=[(s_Sacc, 1)])
                    nw = [w(s_Sacc)]
                    if cc == 0:
                        nw.append((s_qq, qev_hist[-1]))
                    if c > 0:
                        nw.append((s_epi, epi_hist[-1]))
                    def mmN(e, qtb=qtb):
                        inst = None
                        for pq in range(2):
                            for j in range(8):
                                e.matmul(out=pNb[pq * 64:pq * 64 + 64, j * 64:(j + 1) * 64],
                                         lhsT=qtb[pq * 64:pq * 64 + 64, j * 64:(j + 1) * 64],
                                         rhs=SbfS[pq * 64:pq * 64 + 64, j * 64:(j + 1) * 64],
                                         start=True, stop=True)
                                inst = e.matmul(out=pNzb[pq * 64:pq * 64 + 64, j:j + 1],
                                                lhsT=qtb[pq * 64:pq * 64 + 64, j * 64:(j + 1) * 64],
                                                rhs=SbfZ[pq * 64:pq * 64 + 64, j:j + 1],
                                                start=True, stop=True)
                        return inst
                    sch.emit("tensor", mmN, waits=nw, incs=[(s_mmN, 1)])
                    nw_hist.append(s_mmN.n)
                    def epi1(e):
                        e.tensor_scalar(out=recb, in0=pNzb, scalar1=1e-6, scalar2=None, op0=ALU.add)
                        e.drain()
                        return e.reciprocal(out=recb, in_=recb)
                    sch.emit("vector", epi1, waits=[w(s_mmN)], incs=[(s_epi, 1)])
                    ew2 = ([(s_aoT, s_aoT.n)] if c > 0 else []) + [(s_epi, s_epi.n)]
                    def epi2(e):
                        e.tensor_tensor(out=nscb.rearrange("p (j f) -> p j f", f=64),
                                        in0=pNb.rearrange("p (j f) -> p j f", f=64),
                                        in1=recb.unsqueeze(2).to_broadcast([128, 8, 64]),
                                        op=ALU.mult)
                        e.drain()
                        return e.tensor_tensor(out=aobb, in0=nscb, in1=gext, op=ALU.mult)
                    sch.emit("vector", epi2, waits=ew2, incs=[(s_epi, 1)])
                    epi_hist.append(s_epi.n)
                    for r in range(4):
                        ow = [w(s_epi)]
                        if r > 0 or c > 0:
                            ow.append((s_aoT, s_aoT.n))
                        def otr(e, r=r):
                            inst = None
                            for pq in range(2):
                                for jj in range(2):
                                    j = 2 * r + jj
                                    inst = e.transpose(out=pt[1][pq * 64:pq * 64 + 64, jj * 64:(jj + 1) * 64],
                                                       in_=aobb[pq * 64:pq * 64 + 64, j * 64:(j + 1) * 64],
                                                       identity=ident_bf[pq * 64:pq * 64 + 64, pq * 64:pq * 64 + 64])
                            return inst
                        sch.emit("tensor", otr, waits=ow, incs=[(s_aoT, 1)])
                        def otev(e, r=r, c=c):
                            e.tensor_copy(out=aoT3[0:64, 2 * r:2 * r + 2, c * 64:(c + 1) * 64],
                                          in_=pt[1][0:64, 0:128].rearrange("p (s t) -> p s t", s=2))
                            e.drain()
                            return e.tensor_copy(out=aoT3[64:128, 2 * r:2 * r + 2, c * 64:(c + 1) * 64],
                                                 in_=pt[1][64:128, 0:128].rearrange("p (s t) -> p s t", s=2))
                        sch.emit("vector", otev, waits=[w(s_aoT)], incs=[(s_aoT, 1)])
                sch.flush(f"l0a_bs{tt}")
            if tt == NT - 1:
                pass
        scan_done = w(s_aoT)

        bs_ctx.close()

        with nc.sbuf_tensor(p.name("wof"), [128, 8 * D], BF16) as wof_t, \
             nc.sbuf_tensor(p.name("bobc"), [128, D], F32) as bobc_t, \
             nc.sbuf_tensor(p.name("xt2"), [128, 2 * D], F32) as xt2_t, \
             nc.sbuf_tensor(p.name("xin3"), [128, D], F32) as xin3_t, \
             nc.psum_tensor(p.name("pz0"), [128, 512], F32) as pz0_t, \
             nc.psum_tensor(p.name("pz1"), [128, 512], F32) as pz1_t:
            wof = wof_t[:]; bo_bcast = bobc_t[:]; xt3 = xt2_t[:]; xin3 = xin3_t[:]
            pz = (pz0_t[:], pz1_t[:])
            for kc in range(8):
                def ldwo(e, kc=kc):
                    return e.dma_start(out=wof[:, kc * D:(kc + 1) * D],
                                       in_=weights["Wo"][kc * 128:(kc + 1) * 128, :])
                sch.emit("gpsimd", ldwo, incs=[(s_wld, 16)])
            s_bo = p.sem("a0bo")
            emit_bcast_load(sch, weights["bo"], bo_bcast, s_bo)
            wo_done = w(s_wld)
            bo_done = w(s_bo)
            for i in range(NT):
                xti = xt3[:, (i % 2) * D:(i % 2 + 1) * D]
                sl = s_xt[i % 2]
                ldw2 = [] if i < 2 else [(s_xn, s_xn.n - 2)]
                def ldx2(e, xti=xti, i=i):
                    return e.dma_start(out=xti, in_=x_src[i * 128:(i + 1) * 128, :])
                sch.emit("sync", ldx2, waits=ldw2, incs=[(sl, 16)])
                emit_ln(sch, xti, xin3, stat, scr, s_stats, done=s_xn,
                        scalar_waits=[w(sl), w(s_xn)], vector_waits=[w(s_ep)])
                for hc in range(2):
                    pb = pz[hc]
                    mmw = [wo_done, scan_done] if (i == 0 and hc == 0) else []
                    if i > 0:
                        mmw = mmw + [(s_ep, s_ep.n - (1 + hc))]
                    def zmm(e, i=i, hc=hc, pb=pb):
                        inst = None
                        for c8 in range(8):
                            inst = e.matmul(out=pb, lhsT=aoT[:, c8 * T + i * 128: c8 * T + (i + 1) * 128],
                                            rhs=wof[:, c8 * D + hc * 512: c8 * D + (hc + 1) * 512],
                                            start=(c8 == 0), stop=(c8 == 7))
                        return inst
                    sch.emit("tensor", zmm, waits=mmw, incs=[(s_gt, 1)])
                def zev(e, i=i):
                    e.tensor_tensor(out=scr2[:, 0:512], in0=pz[0], in1=bo_bcast[:, 0:512], op=ALU.add)
                    e.drain()
                    e.tensor_tensor(out=scr2[:, 512:1024], in0=pz[1], in1=bo_bcast[:, 512:1024], op=ALU.add)
                    e.drain()
                    return e.tensor_add(out=scr2, in0=scr2, in1=xin3)
                zvw = [w(s_gt), w(s_xn)]
                if i == 0:
                    zvw.append(bo_done)
                sch.emit("vector", zev, waits=zvw, incs=[(s_ep, 2)])
                def stm(e, i=i):
                    return e.dma_start(out=d_xmid[i * 128:(i + 1) * 128, :], in_=scr2)
                sch.emit("sync", stm, waits=[w(s_ep), (s_st6[i % 2], s_st6[i % 2].n)], incs=[(s_st6[i % 2], 16)])
                sch.emit("vector", lambda e: e.nop(), waits=[w(s_st6[i % 2])], incs=[(s_ep, 1)])
            sch.flush("l0a_phC")
        aoT_cm.__exit__(None, None, None)


    s_rw = p.sem("a0rw")
    s_xm = [p.sem("a0ma"), p.sem("a0mb")]
    s_whe = [p.sem("a0wea"), p.sem("a0web")]
    sl_i32 = sb("sl_i32", [128, 2 * NT], I32)
    g12p = sb("g12p", [128, 2 * NT], F32)
    d_xslots = dram("l0a_xsl", [8 * 1024 + 128, D], BF16)
    d_zslots = dram("l0a_zsl", [8 * 1024 + 128, D], F32)
    gts = sb("gts", [128, NT * 8], F32)
    zrow8 = sb("zrow8", [8, T], F32)
    gb6a_e = sb("gb6a", [128, D], F32)
    gb6b_e = sb("gb6b", [128, D], F32)
    six0 = sb("six0", [128, 1], I32)
    six1 = sb("six1", [128, 1], I32)
    gix0 = sb("gix0", [128, 1], I32)
    gix1 = sb("gix1", [128, 1], I32)
    offrow = sb("offrow", [1, 8], F32)
    offbc = sb("offbc", [128, 8], F32)
    meT = sb("meT", [8, T], F32)
    exT = sb("exT", [8, T], F32)
    eq1s = sb("eq1s", [128, NT * 8], F32)
    eq2s = sb("eq2s", [128, NT * 8], F32)
    g12 = sb("g12", [128, NT * 2], F32)
    ioe_bc = sb("ioe_bc", [128, 8], F32)
    rb_bc = sb("rb_bc", [128, 8], F32)
    b2all = sb("b2all", [8, D], F32)
    cmask = sb("cmask", [8, 1], F32)
    gth = sb("gth", [8, 8], F32)
    offc = sb("offc", [8, 1], F32)
    lgs = sb("lgs", [128, 8], F32)
    prb = sb("prb", [128, 8], F32)
    pr2 = sb("pr2", [128, 8], F32)
    tp8 = sb("tp8", [128, 8], F32)
    iu1 = sb("iu1", [128, 8], U32)
    iu2 = sb("iu2", [128, 8], U32)
    mr8 = sb("mr8", [128, 8], F32)
    xmn_tm_cm = nc.sbuf_tensor(p.name("xmn_tm"), [128, NT * D], BF16)
    xmn_tm = xmn_tm_cm.__enter__()[:]
    xmnT_cm = nc.sbuf_tensor(p.name("xmnT"), [128, 8 * T], BF16)
    xmnT = xmnT_cm.__enter__()[:]

    s_zf = p.sem("a0zf")
    s_sc8 = p.sem("a0sc8")
    s_zm = p.sem("a0zm")
    zt_cm = nc.sbuf_tensor(p.name("zbf"), [128, D], BF16)
    zbf = zt_cm.__enter__()[:]
    def zfill0(e):
        e.memset(zbf, 0.0)
        e.drain()
        return e.memset(scr2, 0.0)
    sch.emit("vector", zfill0, incs=[(s_zm, 1)])
    for r_ in range(65):
        def zfill(e, r_=r_):
            return e.dma_start(out=d_xslots[r_ * 128:(r_ + 1) * 128, :], in_=zbf)
        sch.emit("sync", zfill, waits=[w(s_zm)] if r_ == 0 else [], incs=[(s_zf, 16)])
    def zfill2(e):
        return e.dma_start(out=d_zslots[8192:8320, :], in_=scr2)
    sch.emit("sync", zfill2, waits=[w(s_zm)], incs=[(s_zf, 16)])
    emit_bcast_load(sch, weights["ioe"], ioe_bc, s_rw)
    emit_bcast_load(sch, weights["router_b"], rb_bc, s_rw)
    sch.emit("sync", lambda e: e.dma_start(out=b2all, in_=weights["b2"]), incs=[(s_rw, 16)])
    sch.emit("sync", lambda e: e.dma_start(out=cmask, in_=weights["coremask"]), incs=[(s_rw, 16)])

    with nc.sbuf_tensor(p.name("xm"), [128, 2 * D], F32) as xm_t, \
         nc.sbuf_tensor(p.name("xnb2"), [128, D], BF16) as xnb2_t, \
         nc.sbuf_tensor(p.name("n2bc"), [128, D], F32) as n2bc_t, \
         nc.sbuf_tensor(p.name("rwbf"), [128, 8 * 8], BF16) as rwbf_t, \
         nc.psum_tensor(p.name("pt0b"), [128, 128], BF16) as pt0_t, \
         nc.psum_tensor(p.name("pt1b"), [128, 128], BF16) as pt1_t, \
         nc.psum_tensor(p.name("plg"), [128, 8], F32) as plg_t, \
         nc.psum_tensor(p.name("pme"), [8, 128], F32) as pme_t:
        xm2 = xm_t[:]; xnb2 = xnb2_t[:]; n2bc = n2bc_t[:]; rwbf = rwbf_t[:]
        pt = (pt0_t[:], pt1_t[:])
        plg = plg_t[:]; pme = pme_t[:]
        emit_bcast_load(sch, weights["norm2_scale"], n2bc, s_rw)
        s_rwg = p.sem("a0rwg")
        sch.emit("gpsimd", lambda e: e.dma_start(
            out=rwbf.rearrange("p (kc e) -> p kc e", e=8),
            in_=weights["router_W"].rearrange("(kc p) e -> p kc e", p=128)), incs=[(s_rwg, 16)])
        rw_done = w(s_rw)
        rwg_done = w(s_rwg)
        sch.emit("scalar", lambda e: e.nop(), waits=[rw_done])
        sch.emit("vector", lambda e: e.nop(), waits=[rw_done])
        for i in range(NT):
            xmi = xm2[:, (i % 2) * D:(i % 2 + 1) * D]
            sl = s_xm[i % 2]
            ldw = [] if i < 2 else [(s_xn, s_xn.n - 1)]
            def ldxm(e, xmi=xmi, i=i):
                return e.dma_start(out=xmi, in_=d_xmid[i * 128:(i + 1) * 128, :])
            sch.emit("sync", ldxm, waits=ldw, incs=[(sl, 16)])
            ssq = stat[:, 4:5]; rr = stat[:, 5:6]
            def rms_s(e, xmi=xmi, ssq=ssq):
                e.activation(out=scr2, in_=xmi, func=AF.Square, accum_out=ssq)
                e.drain()
                return e.activation(out=ssq, in_=ssq, func=AF.Sqrt, scale=p.cv(3))
            sch.emit("scalar", rms_s, waits=[w(sl), w(s_xn), w(s_gt)] + ([w(s_zf)] if i == 0 else []),
                     incs=[(s_phis, 1)])
            def rms_v(e, rr=rr, ssq=ssq):
                e.reciprocal(out=rr, in_=ssq)
                return e.drain()
            sch.emit("vector", rms_v, waits=[w(s_phis)], incs=[(s_rope, 1)])
            def mknorm(e, xmi=xmi, rr=rr, i=i):
                e.tensor_scalar(out=scr2, in0=xmi, scalar1=rr, scalar2=None, op0=ALU.mult)
                e.drain()
                e.tensor_tensor(out=xnb2, in0=scr2, in1=n2bc, op=ALU.mult)
                e.drain()
                return e.tensor_copy(out=xmn_tm[:, i * D:(i + 1) * D], in_=xnb2)
            sch.emit("vector", mknorm, waits=[w(s_rope), w(s_tr)], incs=[(s_xn, 1)])
            for k in range(8):
                pbank = pt[k % 2]
                evac_snapshot = s_evac.n
                waits = [w(s_xn)] if k == 0 else []
                if evac_snapshot >= 2:
                    waits.append((s_evac, evac_snapshot - 1))
                def tr(e, k=k, pbank=pbank):
                    return e.transpose(out=pbank, in_=xnb2[:, k * 128:(k + 1) * 128], identity=ident_bf)
                sch.emit("tensor", tr, waits=waits, incs=[(s_tr, 1)])
                dst = xmnT[:, k * T + i * 128: k * T + (i + 1) * 128]
                def ev(e, dst=dst, pbank=pbank):
                    return e.tensor_copy(out=dst, in_=pbank)
                sch.emit("vector", ev, waits=[w(s_tr)], incs=[(s_evac, 1)])
            def lmm(e, i=i):
                inst = None
                for kc in range(8):
                    inst = e.matmul(out=plg, lhsT=xmnT[:, kc * T + i * 128: kc * T + (i + 1) * 128],
                                    rhs=rwbf[:, kc * 8:(kc + 1) * 8],
                                    start=(kc == 0), stop=(kc == 7))
                return inst
            sch.emit("tensor", lmm, waits=[w(s_evac)] + ([(s_gt, s_gt.n)] if i > 0 else [rw_done, rwg_done]),
                     incs=[(s_mmS, 1)])
            def sm1(e):
                e.tensor_tensor(out=lgs, in0=plg, in1=rb_bc, op=ALU.add)
                e.drain()
                e.tensor_reduce(out=stat[:, 6:7], in_=lgs, axis=AX.X, op=ALU.max, negate=True)
                return e.drain()
            sch.emit("vector", sm1, waits=[w(s_mmS), w(s_phis)], incs=[(s_gt, 1)])
            def sm2(e):
                return e.activation(out=prb, in_=lgs, func=AF.Exp, bias=stat[:, 6:7], accum_out=stat[:, 7:8])
            sch.emit("scalar", sm2, waits=[w(s_gt)], incs=[(s_phis, 1)])
            def top2(e, i=i):
                e.reciprocal(out=stat[:, 7:8], in_=stat[:, 7:8])
                e.drain()
                e.tensor_scalar(out=prb, in0=prb, scalar1=stat[:, 7:8], scalar2=None, op0=ALU.mult)
                e.drain()
                e.memset(mr8, -1e30)
                e.drain()
                e.tensor_reduce(out=tp8[:, 0:1], in_=prb, axis=AX.X, op=ALU.max)
                e.drain()
                e.tensor_copy(out=mr8[:, 0:1], in_=tp8[:, 0:1])
                e.drain()
                e.match_replace(out=pr2, in_to_replace=mr8, in_values=prb, imm_value=-1e30)
                e.drain()
                e.tensor_reduce(out=tp8[:, 1:2], in_=pr2, axis=AX.X, op=ALU.max)
                e.drain()
                e.max_index(out=iu1, in_max=mr8, in_values=prb)
                e.drain()
                e.tensor_copy(out=mr8[:, 0:1], in_=tp8[:, 1:2])
                e.drain()
                e.max_index(out=iu2, in_max=mr8, in_values=pr2)
                e.drain()
                e.tensor_add(out=tp8[:, 2:3], in0=tp8[:, 0:1], in1=tp8[:, 1:2])
                e.drain()
                e.reciprocal(out=tp8[:, 2:3], in_=tp8[:, 2:3])
                e.drain()
                e.tensor_tensor(out=g12[:, i * 2:i * 2 + 1], in0=tp8[:, 0:1], in1=tp8[:, 2:3], op=ALU.mult)
                e.drain()
                e.tensor_tensor(out=g12[:, i * 2 + 1:i * 2 + 2], in0=tp8[:, 1:2], in1=tp8[:, 2:3], op=ALU.mult)
                e.drain()
                e.tensor_copy(out=tp8[:, 3:4], in_=iu1[:, 0:1])
                e.drain()
                e.tensor_copy(out=tp8[:, 4:5], in_=iu2[:, 0:1])
                e.drain()
                e.tensor_scalar(out=eq1s[:, i * 8:(i + 1) * 8], in0=ioe_bc, scalar1=tp8[:, 3:4], scalar2=None, op0=ALU.is_equal)
                e.drain()
                e.tensor_scalar(out=eq2s[:, i * 8:(i + 1) * 8], in0=ioe_bc, scalar1=tp8[:, 4:5], scalar2=None, op0=ALU.is_equal)
                e.drain()
                return e.tensor_add(out=lgs, in0=eq1s[:, i * 8:(i + 1) * 8], in1=eq2s[:, i * 8:(i + 1) * 8])
            sch.emit("vector", top2, waits=[w(s_phis)], incs=[(s_gt, 1)])
            def metr(e):
                return e.transpose(out=pme, in_=lgs, identity=ident_f)
            sch.emit("tensor", metr, waits=[w(s_gt)] + ([(s_gtT, s_gtT.n)] if i > 0 else []), incs=[(s_mmS, 1)])
            def meev(e, i=i):
                return e.tensor_copy(out=meT[:, i * 128:(i + 1) * 128], in_=pme)
            sch.emit("vector", meev, waits=[w(s_mmS)], incs=[(s_gtT, 1)])
        sch.flush("l0a_d1")

        def csum3(e):
            e.tensor_tensor_scan(out=exT, data0=meT, data1=zrow8, initial=0.0,
                                 op0=ALU.add, op1=ALU.add)
            e.drain()
            e.tensor_copy(out=gth[:, 0:1], in_=exT[:, T - 1:T])
            e.drain()
            e.tensor_sub(out=exT, in0=exT, in1=meT)
            return e.drain()

        def zr(e):
            e.memset(zrow8, 0.0)
            e.drain()
            return e.memset(gth, 0.0)
        sch.emit("vector", zr, incs=[(s_gtT, 1)])
        sch.emit("vector", csum3, waits=[w(s_gtT)], incs=[(s_gtT, 1)])
        def cst(e):
            return e.dma_start(out=(weights["cnt_out"] if mode == "a" else d_cnt), in_=gth[:, 0:1])
        sch.emit("sync", cst, waits=[w(s_gtT)], incs=[(s_cldma, 16)])
        if mode == "a":
            sch.emit("sync", lambda e: e.nop(), waits=[w(s_cldma)], incs=[(s_io, 16)])
            sch.flush("l0a_a_end")
            return s_io
        if mode == "b":
            def cldb(e):
                return e.dma_start(out=gth, in_=weights["gathin"])
            sch.emit("sync", cldb, waits=[w(s_cldma), w(s_gtT)], incs=[(s_cldma, 16)])
        elif use_collective:
            def coll(e):
                return e.collective_compute("AllGather", ALU.bypass,
                                            replica_groups=[list(range(n_cores))],
                                            ins=[d_cnt], outs=[d_gath])
            sch.emit("gpsimd", coll, waits=[w(s_cldma)], incs=[(s_coll, 16)])
            def cld(e):
                return e.dma_start(out=gth, in_=d_gath.rearrange("(r e) o -> r (e o)", r=n_cores))
            sch.emit("sync", cld, waits=[w(s_coll), w(s_gtT)], incs=[(s_cldma, 16)])
        else:
            def cld1(e):
                return e.dma_start(out=gth[0:1, :], in_=d_cnt.rearrange("(o e) u -> o (e u)", o=1))
            sch.emit("sync", cld1, waits=[w(s_cldma), w(s_gtT)], incs=[(s_cldma, 16)])
        with nc.psum_tensor(p.name("poff"), [8, 1], F32) as poff_t, \
             nc.psum_tensor(p.name("poff2"), [1, 8], F32) as poff2_t, \
             nc.psum_tensor(p.name("pofb"), [128, 8], F32) as pofb_t:
            poff = poff_t[:]
            poff2 = poff2_t[:]
            pofb = pofb_t[:]
            def offmm(e):
                return e.matmul(out=poff, lhsT=gth, rhs=cmask, start=True, stop=True)
            sch.emit("tensor", offmm, waits=[w(s_cldma), rw_done], incs=[(s_mmS, 1)])
            def offev(e):
                return e.tensor_copy(out=offc, in_=poff)
            sch.emit("vector", offev, waits=[w(s_mmS)], incs=[(s_gtT, 1)])
            def offtr(e):
                return e.transpose(out=pme[0:1, 0:8] if False else poff2, in_=offc, identity=ident_f[0:8, 0:8])
            sch.emit("tensor", offtr, waits=[w(s_gtT)], incs=[(s_mmS, 1)])
            def offr(e):
                return e.tensor_copy(out=offrow, in_=poff2)
            sch.emit("vector", offr, waits=[w(s_mmS)], incs=[(s_gtT, 1)])
            def offbm(e):
                return e.matmul(out=pofb, lhsT=consts["onesf"], rhs=offrow, start=True, stop=True)
            sch.emit("tensor", offbm, waits=[w(s_gtT)], incs=[(s_mmS, 1)])
            def offbe(e):
                return e.tensor_copy(out=offbc, in_=pofb)
            sch.emit("vector", offbe, waits=[w(s_mmS)], incs=[(s_gtT, 1)])
            sch.flush("l0a_d2")

        with nc.psum_tensor(p.name("pex"), [128, 8], F32) as pex_t:
            pex = pex_t[:]
            for i in range(NT):
                def extr(e, i=i):
                    return e.transpose(out=pex, in_=exT[:, i * 128:(i + 1) * 128], identity=ident_f[0:8, 0:8])
                sch.emit("tensor", extr, waits=[w(s_gtT)] + ([(s_gt, s_gt.n)] if i > 0 else []),
                         incs=[(s_mmS, 1)])
                def capg(e, i=i):
                    eq1 = eq1s[:, i * 8:(i + 1) * 8]
                    eq2 = eq2s[:, i * 8:(i + 1) * 8]
                    e.tensor_tensor(out=lgs, in0=eq1, in1=pex, op=ALU.mult)
                    e.drain()
                    e.tensor_reduce(out=tp8[:, 0:1], in_=lgs, axis=AX.X, op=ALU.add)
                    e.drain()
                    e.tensor_tensor(out=lgs, in0=eq2, in1=pex, op=ALU.mult)
                    e.drain()
                    e.tensor_reduce(out=tp8[:, 1:2], in_=lgs, axis=AX.X, op=ALU.add)
                    e.drain()
                    e.tensor_tensor(out=lgs, in0=eq1, in1=offbc, op=ALU.mult)
                    e.drain()
                    e.tensor_reduce(out=tp8[:, 6:7], in_=lgs, axis=AX.X, op=ALU.add)
                    e.drain()
                    e.tensor_tensor(out=lgs, in0=eq2, in1=offbc, op=ALU.mult)
                    e.drain()
                    e.tensor_reduce(out=tp8[:, 7:8], in_=lgs, axis=AX.X, op=ALU.add)
                    e.drain()
                    e.tensor_tensor(out=tp8[:, 6:7], in0=tp8[:, 6:7], in1=tp8[:, 0:1], op=ALU.add)
                    e.drain()
                    e.tensor_tensor(out=tp8[:, 7:8], in0=tp8[:, 7:8], in1=tp8[:, 1:2], op=ALU.add)
                    e.drain()
                    e.tensor_scalar(out=tp8[:, 2:3], in0=tp8[:, 6:7], scalar1=CAP, scalar2=None, op0=ALU.is_lt)
                    e.drain()
                    e.tensor_scalar(out=tp8[:, 3:4], in0=tp8[:, 7:8], scalar1=CAP, scalar2=None, op0=ALU.is_lt)
                    e.drain()
                    e.tensor_scalar(out=stat[:, 4:5], in0=tp8[:, 0:1], scalar1=1024.0, scalar2=None, op0=ALU.is_lt)
                    e.drain()
                    e.tensor_scalar(out=stat[:, 5:6], in0=tp8[:, 1:2], scalar1=1024.0, scalar2=None, op0=ALU.is_lt)
                    e.drain()
                    e.tensor_tensor(out=tp8[:, 2:3], in0=tp8[:, 2:3], in1=stat[:, 4:5], op=ALU.mult)
                    e.drain()
                    e.tensor_tensor(out=tp8[:, 3:4], in0=tp8[:, 3:4], in1=stat[:, 5:6], op=ALU.mult)
                    e.drain()
                    e.tensor_tensor(out=g12p[:, 2 * i:2 * i + 1], in0=g12[:, i * 2:i * 2 + 1], in1=tp8[:, 2:3], op=ALU.mult)
                    e.drain()
                    e.tensor_tensor(out=g12p[:, 2 * i + 1:2 * i + 2], in0=g12[:, i * 2 + 1:i * 2 + 2], in1=tp8[:, 3:4], op=ALU.mult)
                    e.drain()
                    e.tensor_tensor(out=lgs, in0=eq1, in1=ioe_bc, op=ALU.mult)
                    e.drain()
                    e.tensor_reduce(out=tp8[:, 4:5], in_=lgs, axis=AX.X, op=ALU.add)
                    e.drain()
                    e.tensor_tensor(out=lgs, in0=eq2, in1=ioe_bc, op=ALU.mult)
                    e.drain()
                    e.tensor_reduce(out=tp8[:, 5:6], in_=lgs, axis=AX.X, op=ALU.add)
                    e.drain()
                    e.tensor_scalar(out=tp8[:, 4:5], in0=tp8[:, 4:5], scalar1=1024.0, scalar2=tp8[:, 0:1], op0=ALU.mult, op1=ALU.add)
                    e.drain()
                    e.tensor_scalar(out=tp8[:, 5:6], in0=tp8[:, 5:6], scalar1=1024.0, scalar2=tp8[:, 1:2], op0=ALU.mult, op1=ALU.add)
                    e.drain()
                    e.tensor_scalar(out=stat[:, 4:5], in0=stat[:, 4:5], scalar1=-1.0, scalar2=1.0, op0=ALU.mult, op1=ALU.add)
                    e.drain()
                    e.tensor_scalar(out=stat[:, 5:6], in0=stat[:, 5:6], scalar1=-1.0, scalar2=1.0, op0=ALU.mult, op1=ALU.add)
                    e.drain()
                    e.scalar_tensor_tensor(out=tp8[:, 4:5], in0=stat[:, 4:5], scalar=1e9, in1=tp8[:, 4:5], op0=ALU.mult, op1=ALU.add)
                    e.drain()
                    e.scalar_tensor_tensor(out=tp8[:, 5:6], in0=stat[:, 5:6], scalar=1e9, in1=tp8[:, 5:6], op0=ALU.mult, op1=ALU.add)
                    e.drain()
                    e.tensor_scalar(out=tp8[:, 4:5], in0=tp8[:, 4:5], scalar1=8192.0, scalar2=None, op0=ALU.min)
                    e.drain()
                    e.tensor_scalar(out=tp8[:, 5:6], in0=tp8[:, 5:6], scalar1=8192.0, scalar2=None, op0=ALU.min)
                    e.drain()
                    e.tensor_copy(out=sl_i32[:, 2 * i:2 * i + 1], in_=tp8[:, 4:5])
                    e.drain()
                    e.tensor_copy(out=sl_i32[:, 2 * i + 1:2 * i + 2], in_=tp8[:, 5:6])
                    e.drain()
                    e.tensor_scalar(out=lgs, in0=eq1, scalar1=g12p[:, 2 * i:2 * i + 1], scalar2=None, op0=ALU.mult)
                    e.drain()
                    e.tensor_scalar(out=prb, in0=eq2, scalar1=g12p[:, 2 * i + 1:2 * i + 2], scalar2=None, op0=ALU.mult)
                    e.drain()
                    return e.tensor_add(out=gts[:, i * 8:(i + 1) * 8], in0=lgs, in1=prb)
                sch.emit("vector", capg, waits=[w(s_mmS)], incs=[(s_gt, 1)])
                for sc in range(2):
                    sixb = six0 if sc == 0 else six1
                    def scpy(e, i=i, sc=sc, sixb=sixb):
                        return e.tensor_copy(out=sixb, in_=sl_i32[:, 2 * i + sc:2 * i + sc + 1])
                    sch.emit("vector", scpy, waits=[w(s_gt), w(s_sc8)], incs=[(s_gt, 1)])
                    def scat(e, i=i, sc=sc, sixb=sixb):
                        return e.indirect_dma_start(
                            out=d_xslots, out_offset=IndirectOffsetOnAxis(sixb, 0),
                            in_=xmn_tm[:, i * D:(i + 1) * D], in_offset=None)
                    sch.emit("gpsimd", scat, waits=[w(s_gt), w(s_zf)], incs=[(s_sc8, 16)])
                def gtr(e, i=i):
                    return e.transpose(out=pme, in_=gts[:, i * 8:(i + 1) * 8], identity=ident_f)
                sch.emit("tensor", gtr, waits=[w(s_gt)] + [(s_gtT, s_gtT.n)], incs=[(s_mmS, 1)])
                def gev(e, i=i):
                    return e.tensor_copy(out=meT[:, i * 128:(i + 1) * 128], in_=pme)
                sch.emit("vector", gev, waits=[w(s_mmS)], incs=[(s_gtT, 1)])
            sch.flush("l0a_d3")
    zt_cm.__exit__(None, None, None)
    xmnT_cm.__exit__(None, None, None)
    xmn_tm_cm.__exit__(None, None, None)
    gates_done = w(s_gtT)

    with nc.sbuf_tensor(p.name("whe"), [128, 2 * 8 * 512], BF16) as whe_t, \
         nc.sbuf_tensor(p.name("w2s"), [128, 4 * 512], BF16) as w2s_t, \
         nc.sbuf_tensor(p.name("hmF"), [128, 32 * 1024], BF16) as hmF_t, \
         nc.sbuf_tensor(p.name("xesT"), [128, 2 * 8 * 1024], BF16) as xesT_t, \
         nc.sbuf_tensor(p.name("xsl"), [128, 2 * D], BF16) as xsl_t, \
         nc.sbuf_tensor(p.name("zev6"), [128, 8 * 512], F32) as zev6_t, \
         nc.sbuf_tensor(p.name("b1eT"), [128, 8 * 32], F32) as b1eT_t, \
         nc.psum_tensor(p.name("pe0"), [128, 512], F32) as pe0_t, \
         nc.psum_tensor(p.name("pe1"), [128, 512], F32) as pe1_t, \
         nc.psum_tensor(p.name("pz0"), [128, 512], F32) as pz0_t, \
         nc.psum_tensor(p.name("pz1"), [128, 512], F32) as pz1_t, \
         nc.psum_tensor(p.name("pz2"), [128, 512], F32) as pz2_t, \
         nc.psum_tensor(p.name("pz3"), [128, 512], F32) as pz3_t, \
         nc.psum_tensor(p.name("pts0"), [128, 128], BF16) as pts0_t, \
         nc.psum_tensor(p.name("pts1"), [128, 128], BF16) as pts1_t:
        whe2 = whe_t[:]; w2s = w2s_t[:]; hmF = hmF_t[:]
        xesT2 = xesT_t[:]; xsl2 = xsl_t[:]; zev6 = zev6_t[:]
        b1eT = b1eT_t[:]
        pe = (pe0_t[:], pe1_t[:])
        pz = (pz0_t[:], pz1_t[:], pz2_t[:], pz3_t[:])
        pts = (pts0_t[:], pts1_t[:])
        s_xld = [p.sem("a0xla"), p.sem("a0xlb")]
        s_xtr = p.get_sem("tr")
        s_xev = p.get_sem("ev")
        s_w2t = [p.sem("a0w2t0"), p.sem("a0w2t1"), p.sem("a0w2t2"), p.sem("a0w2t3")]
        s_ze = [p.get_sem("ze0"), p.get_sem("ze1"), p.get_sem("ze2"), p.get_sem("ze3")]
        s_zsl6 = [p.sem("a0z6a"), p.sem("a0z6b")]
        sch.emit("sync", lambda e: e.dma_start(out=b1eT, in_=weights["b1T"]), incs=[(s_rw, 16)])
        b1_done = w(s_rw)
        scat_done = w(s_sc8)
        mm1_hist = []
        mm1_hist_f = []
        mm2_e_hist = []
        g_tile = 0
        ps_idx = 0
        for ei in range(E):
            slote = ei % 2
            xesT = xesT2[:, slote * 8 * 1024:(slote + 1) * 8 * 1024]
            for mt in range(8):
                xsl = xsl2[:, (mt % 2) * D:(mt % 2 + 1) * D]
                xw = [scat_done] if (ei == 0 and mt == 0) else []
                if ei * 8 + mt >= 2:
                    xw.append((s_xtr, s_xtr.n - 8))
                def xld(e, ei=ei, mt=mt, xsl=xsl):
                    return e.dma_start(out=xsl, in_=d_xslots[ei * 1024 + mt * 128: ei * 1024 + (mt + 1) * 128, :])
                sch.emit("sync", xld, waits=xw, incs=[(s_xld[mt % 2], 16)])
                for k in range(8):
                    pbank = pts[k % 2]
                    tw = [w(s_xld[mt % 2])] if k == 0 else []
                    if s_xev.n >= 2:
                        tw.append((s_xev, s_xev.n - 1))
                    if ei >= 2 and mt == 0 and k == 0:
                        tw.append((s_mm1, mm1_hist[ei - 2]))
                    def xtr(e, k=k, pbank=pbank, xsl=xsl):
                        return e.transpose(out=pbank, in_=xsl[:, k * 128:(k + 1) * 128], identity=ident_bf)
                    sch.emit("tensor", xtr, waits=tw, incs=[(s_xtr, 1)])
                    dst = xesT[:, k * 1024 + mt * 128: k * 1024 + (mt + 1) * 128]
                    def xev(e, dst=dst, pbank=pbank):
                        return e.tensor_copy(out=dst, in_=pbank)
                    sch.emit("vector", xev, waits=[w(s_xtr)], incs=[(s_xev, 1)])
            xesT_ready = w(s_xev)
            for fc in range(8):
                ci = ei * 8 + fc
                slot = ci % 2
                whe = whe2[:, slot * 8 * 512:(slot + 1) * 8 * 512]
                lw_waits = [] if ci < 2 else [(s_mm1, mm1_hist_f[ci - 2])]
                for kc in range(8):
                    def lwh(e, ei=ei, fc=fc, kc=kc, whe=whe):
                        return e.dma_start(out=whe[:, kc * 512:(kc + 1) * 512],
                                           in_=weights["W1"][ei * 1024 + kc * 128: ei * 1024 + (kc + 1) * 128,
                                                             fc * 512:(fc + 1) * 512])
                    sch.emit("gpsimd", lwh, waits=lw_waits if kc == 0 else [], incs=[(s_whe[slot], 16)])
                wh_done = w(s_whe[slot])
                for f4 in range(4):
                    for mc in range(2):
                        idx = ci * 8 + f4 * 2 + mc
                        pb = pe[idx % 2]
                        waits = [wh_done] if (f4 == 0 and mc == 0) else []
                        if fc == 0 and f4 == 0 and mc == 0:
                            waits.append(xesT_ready)
                        if ci == 0 and f4 == 0 and mc == 0:
                            waits.append(b1_done)
                        if idx >= 2:
                            waits.append((s_pev, s_pev.n - 1))
                        def mmh(e, f4=f4, mc=mc, pb=pb, whe=whe, xesT=xesT):
                            inst = None
                            for kc in range(8):
                                inst = e.matmul(out=pb,
                                                lhsT=whe[:, kc * 512 + f4 * 128: kc * 512 + (f4 + 1) * 128],
                                                rhs=xesT[:, kc * 1024 + mc * 512: kc * 1024 + (mc + 1) * 512],
                                                start=(kc == 0), stop=(kc == 7))
                            return inst
                        sch.emit("tensor", mmh, waits=waits, incs=[(s_mm1, 1)])
                        t_h = fc * 4 + f4
                        dst = hmF[:, t_h * 1024 + mc * 512: t_h * 1024 + (mc + 1) * 512]
                        bidx = ei * 32 + fc * 4 + f4
                        gw = [w(s_mm1)]
                        if ei > 0 and fc == 0 and f4 == 0 and mc == 0:
                            gw.append((s_mm2, mm2_e_hist[ei - 1]))
                        def gelh(e, dst=dst, pb=pb, bidx=bidx):
                            return e.activation(out=dst, in_=pb, func=AF.Gelu, bias=b1eT[:, bidx:bidx + 1])
                        sch.emit("scalar", gelh, waits=gw, incs=[(s_pev, 1)])
                mm1_hist_f.append(s_mm1.n)
                if fc == 3 or fc == 7:
                    sch.flush(f"l0a_d5f1_{ei}_{fc}")
            mm1_hist.append(s_mm1.n)
            gel_done = w(s_pev)
            for dc in range(2):
                for mtg in range(2):
                    for t in range(32):
                        par = g_tile % 4
                        lw2 = []
                        if g_tile >= 4:
                            lw2.append((s_mm2, s_mm2.n - 12))
                        def lw2t(e, ei=ei, dc=dc, t=t, par=par):
                            return e.dma_start(out=w2s[:, par * 512:(par + 1) * 512],
                                               in_=weights["W2"][ei * 4096 + t * 128: ei * 4096 + (t + 1) * 128,
                                                                 dc * 512:(dc + 1) * 512])
                        sch.emit("gpsimd", lw2t, waits=lw2, incs=[(s_w2t[par], 16)])
                        for mt in range(4):
                            mw = [w(s_w2t[par])] if mt == 0 else []
                            if t == 0:
                                if dc == 0 and mtg == 0:
                                    mw.append(gel_done)
                                if ps_idx > 0:
                                    mw.append((s_ze[mt], s_ze[mt].n))
                            def mmz(e, mt=mt, mtg=mtg, t=t, par=par):
                                return e.matmul(out=pz[mt],
                                                lhsT=hmF[:, t * 1024 + (mtg * 4 + mt) * 128: t * 1024 + (mtg * 4 + mt + 1) * 128],
                                                rhs=w2s[:, par * 512:(par + 1) * 512],
                                                start=(t == 0), stop=(t == 31))
                            sch.emit("tensor", mmz, waits=mw, incs=[(s_mm2, 1)])
                        g_tile += 1
                    n_mm2 = s_mm2.n
                    for mt in range(4):
                        zt6 = zev6[:, ((ps_idx % 2) * 4 + mt) * 512: ((ps_idx % 2) * 4 + mt + 1) * 512]
                        ew = [(s_mm2, n_mm2 - 3 + mt)]
                        if ps_idx >= 2 and mt == 0:
                            ew.append(w(s_zsl6[ps_idx % 2]))
                        def zev_(e, zt6=zt6, mt=mt):
                            return e.tensor_copy(out=zt6, in_=pz[mt])
                        sch.emit("vector", zev_, waits=ew, incs=[(s_ze[mt], 1)])
                        def zst6(e, ei=ei, dc=dc, mtg=mtg, mt=mt, zt6=zt6):
                            return e.dma_start(
                                out=d_zslots[ei * 1024 + (mtg * 4 + mt) * 128: ei * 1024 + (mtg * 4 + mt + 1) * 128,
                                             dc * 512:(dc + 1) * 512],
                                in_=zt6)
                        sch.emit("sync", zst6, waits=[(s_ze[mt], s_ze[mt].n)], incs=[(s_zsl6[ps_idx % 2], 16)])
                    ps_idx += 1
                sch.flush(f"l0a_d5f2_{ei}_{dc}")
            mm2_e_hist.append(s_mm2.n)
        sch.flush("l0a_d5_end")
        zsl6_done0 = w(s_zsl6[0]); zsl6_done1 = w(s_zsl6[1])

    with nc.sbuf_tensor(p.name("xt7"), [128, 2 * D], F32) as xt7_t, \
         nc.sbuf_tensor(p.name("xm6"), [128, 2 * D], F32) as xm6_t, \
         nc.sbuf_tensor(p.name("zb6"), [128, 2 * D], F32) as zb6_t, \
         nc.psum_tensor(p.name("pb20"), [128, 512], F32) as pb20_t, \
         nc.psum_tensor(p.name("pb21"), [128, 512], F32) as pb21_t:
        xt7 = xt7_t[:]; xm6 = xm6_t[:]; zb6 = zb6_t[:]
        pb2 = (pb20_t[:], pb21_t[:])
        gb6 = (gb6a_e, gb6b_e)
        s_g6 = p.sem("a0g6")
        s_g6v = p.get_sem("msk")
        def g6init(e):
            e.memset(gb6[0], 0.0)
            e.drain()
            return e.memset(gb6[1], 0.0)
        sch.emit("vector", g6init, incs=[(s_g6v, 1)])
        for i in range(NT):
            xmi = xm6[:, (i % 2) * D:(i % 2 + 1) * D]
            z_t = zb6[:, (i % 2) * D:(i % 2 + 1) * D]
            xti = xt7[:, (i % 2) * D:(i % 2 + 1) * D]
            sl = s_xm[i % 2]
            ldw = [] if i < 2 else [(s_za, s_za.n - 2)]
            def ldxm(e, xmi=xmi, i=i):
                return e.dma_start(out=xmi, in_=d_xmid[i * 128:(i + 1) * 128, :])
            sch.emit("sync", ldxm, waits=ldw, incs=[(sl, 16)])
            for hc in range(2):
                pb = pb2[hc]
                mmw = [gates_done] if (i == 0 and hc == 0) else []
                if i > 0:
                    mmw.append((s_za, s_za.n - 1 + hc))
                def bmm(e, i=i, hc=hc, pb=pb):
                    return e.matmul(out=pb, lhsT=meT[:, i * 128:(i + 1) * 128],
                                    rhs=b2all[:, hc * 512:(hc + 1) * 512], start=True, stop=True)
                sch.emit("tensor", bmm, waits=mmw, incs=[(s_mmS, 1)])
            zw0 = [w(s_mmS), w(sl)]
            if i >= 2:
                zw0.append((s_ep, s_ep.n - 2))
            def zs0(e, i=i, xmi=xmi, z_t=z_t):
                return e.tensor_tensor(out=z_t[:, 0:512], in0=pb2[0], in1=xmi[:, 0:512], op=ALU.add)
            sch.emit("vector", zs0, waits=zw0, incs=[(s_za, 1)])
            def zs1(e, i=i, xmi=xmi, z_t=z_t):
                return e.tensor_tensor(out=z_t[:, 512:1024], in0=pb2[1], in1=xmi[:, 512:1024], op=ALU.add)
            sch.emit("vector", zs1, incs=[(s_za, 1)])
            def ldx7(e, xti=xti, i=i):
                return e.dma_start(out=xti, in_=x_src[i * 128:(i + 1) * 128, :])
            sch.emit("sync", ldx7, waits=[] if i < 2 else [(s_ep, s_ep.n - 2), w(s_st6[i % 2])],
                     incs=[(s_xt[i % 2], 16)])
            for sc in range(2):
                gb = gb6[sc]
                gw6 = [zsl6_done0, zsl6_done1, w(s_g6v)] if (i == 0 and sc == 0) else [(s_g6v, s_g6v.n)]
                gixb = gix0 if sc == 0 else gix1
                def g6c(e, i=i, sc=sc, gixb=gixb):
                    return e.tensor_copy(out=gixb, in_=sl_i32[:, 2 * i + sc:2 * i + sc + 1])
                sch.emit("vector", g6c, waits=gw6 + [w(s_g6)], incs=[(s_g6v, 1)])
                def g6(e, gb=gb, gixb=gixb):
                    return e.indirect_dma_start(
                        out=gb, out_offset=None,
                        in_=d_zslots, in_offset=IndirectOffsetOnAxis(gixb, 0))
                sch.emit("gpsimd", g6, waits=[w(s_g6v)], incs=[(s_g6, 16)])
                def g6a(e, i=i, sc=sc, gb=gb, z_t=z_t):
                    return e.scalar_tensor_tensor(out=z_t, in0=gb, scalar=g12p[:, 2 * i + sc:2 * i + sc + 1],
                                                  in1=z_t, op0=ALU.mult, op1=ALU.add)
                sch.emit("vector", g6a, waits=[w(s_g6), (s_za, s_za.n)], incs=[(s_g6v, 1)])
            emit_ln(sch, z_t, scr2, stat, scr, s_stats, done=s_ep,
                    scalar_waits=[w(s_g6v), w(s_ep), w(s_xn)] if i == 0 else [w(s_g6v), w(s_ep)],
                    vector_waits=[w(s_ep)] if i > 0 else [])
            def xadd(e, xti=xti):
                return e.tensor_tensor(out=xti, in0=xti, in1=scr2, op=ALU.add)
            sch.emit("vector", xadd, waits=[w(s_xt[i % 2]), w(s_ep)], incs=[(s_ep, 1)])
            def stx(e, xti=xti, i=i):
                return e.dma_start(out=x_dst[i * 128:(i + 1) * 128, :], in_=xti)
            sch.emit("sync", stx, waits=[w(s_ep)], incs=[(s_st6[i % 2], 16)])
            if i % 4 == 3:
                sch.flush(f"l0a_d6_{i}")
        sch.emit("sync", lambda e: e.nop(), waits=[w(s_st6[0]), w(s_st6[1])], incs=[(s_io, 16)])
        sch.flush("l0a_d6")
    ctx.close()
    return s_io

L0_ATTN_DECL = {
    "norm1_scale": ([1, D], F32), "norm2_scale": ([1, D], F32),
    "Wq": ([1024, 1024], F32), "Wk": ([1024, 1024], F32),
    "Wv": ([1024, 1024], F32), "Wg": ([1024, 1024], F32), "Wo": ([1024, 1024], F32),
    "bq": ([1, D], F32), "bk": ([1, D], F32), "bv": ([1, D], F32), "bg": ([1, D], F32),
    "bo": ([1, D], F32),
    "ropecos": ([T, 32], F32), "ropesin": ([T, 32], F32),
    "router_W": ([1024, 8], F32), "router_b": ([1, 8], F32),
    "ioe": ([1, 8], F32), "coremask": ([8, 1], F32),
    "W1": ([8 * 1024, 4096], F32), "b1T": ([128, 8 * 32], F32),
    "W2": ([8 * 4096, 1024], F32), "b2": ([8, 1024], F32),
}


def l0_attn_weights(inp, core_id, pre=""):
    half = 32
    inv = 1.0 / 10000.0 ** (np.arange(half, dtype=np.float32) / half)
    ang = np.arange(T, dtype=np.float32)[:, None] * inv[None, :]
    cmask = np.zeros((8, 1), np.float32)
    cmask[:core_id, 0] = 1.0
    return {
        pre + "norm1_scale": inp["L0_attn_norm1_scale"].reshape(1, D),
        pre + "norm2_scale": inp["L0_attn_norm2_scale"].reshape(1, D),
        pre + "Wq": inp["L0_attn_Wq"], pre + "Wk": inp["L0_attn_Wk"],
        pre + "Wv": inp["L0_attn_Wv"], pre + "Wg": inp["L0_attn_Wg"],
        pre + "Wo": inp["L0_attn_Wo"],
        pre + "bq": inp["L0_attn_bq"].reshape(1, D), pre + "bk": inp["L0_attn_bk"].reshape(1, D),
        pre + "bv": inp["L0_attn_bv"].reshape(1, D), pre + "bg": inp["L0_attn_bg"].reshape(1, D),
        pre + "bo": inp["L0_attn_bo"].reshape(1, D),
        pre + "ropecos": np.cos(ang), pre + "ropesin": np.sin(ang),
        pre + "router_W": inp["L0_attn_router_W"],
        pre + "router_b": inp["L0_attn_router_b"].reshape(1, 8),
        pre + "ioe": np.arange(8, dtype=np.float32).reshape(1, 8),
        pre + "coremask": cmask,
        pre + "W1": np.ascontiguousarray(inp["L0_attn_W1"].reshape(8 * 1024, 4096)),
        pre + "b1T": np.ascontiguousarray(inp["L0_attn_b1"].reshape(8, 32, 128).transpose(2, 0, 1).reshape(128, 256)),
        pre + "W2": np.ascontiguousarray(inp["L0_attn_W2"].reshape(8 * 4096, 1024)),
        pre + "b2": inp["L0_attn_b2"],
    }


def emit_l0_attn_b(p, x_src, x_dst, weights, consts):
    return emit_l0_attn(p, x_src, x_dst, weights, consts, mode="b")


L0_ATTN_B_DECL = dict(L0_ATTN_DECL)
L0_ATTN_B_DECL["xmid_in"] = ([T, D], F32)
L0_ATTN_B_DECL["gathin"] = ([8, 8], F32)
_PROGS = {}


def _build_l0a_program():
    p = Prog()
    nc = p.nc
    x_in = nc.dram_tensor("x_in", [T, D], F32, kind="ExternalInput").ap()
    ident_in = nc.dram_tensor("ident_in", [128, 128], BF16, kind="ExternalInput").ap()
    identf_in = nc.dram_tensor("identf_in", [128, 128], F32, kind="ExternalInput").ap()
    cvec_in = nc.dram_tensor("cvec_in", [128, 8], F32, kind="ExternalInput").ap()
    wmap = {}
    for name, (shape, dt) in L0_ATTN_DECL.items():
        wmap[name] = nc.dram_tensor("b0_" + name, list(shape), dt, kind="ExternalInput").ap()
    wmap["xmid_out"] = nc.dram_tensor("xmid_out", [T, D], F32, kind="ExternalOutput").ap()
    wmap["cnt_out"] = nc.dram_tensor("cnt_out", [8, 1], F32, kind="ExternalOutput").ap()
    ident_bf = p.sb("ident_bf", [128, 128], BF16)
    ident_f = p.sb("ident_f", [128, 128], F32)
    onesf = p.sb("onesf", [1, 128], F32)
    cvec = p.sb("cvec", [128, 8], F32)
    p.epsap = cvec[:, 0:1]
    p.cv = lambda i: cvec[:, i:i + 1]
    s_io = p.get_sem("io")
    sch = Sched(p)
    sch.emit("sync", lambda e: e.dma_start(out=ident_bf, in_=ident_in), incs=[(s_io, 16)])
    sch.emit("sync", lambda e: e.dma_start(out=ident_f, in_=identf_in), incs=[(s_io, 16)])
    sch.emit("sync", lambda e: e.dma_start(out=cvec, in_=cvec_in), incs=[(s_io, 16)])
    sch.emit("vector", lambda e: e.memset(onesf, 1.0), incs=[(s_io, 1)])
    for eng in ("scalar", "vector", "tensor", "gpsimd"):
        sch.emit(eng, lambda e: e.nop(), waits=[w(s_io)])
    sch.flush("init")
    consts = {"ident_bf": ident_bf, "ident_f": ident_f, "onesf": onesf, "cvec": cvec}
    emit_l0_attn(p, x_in, None, wmap, consts, mode="a")
    return p


def _get_progs():
    if "v2" not in _PROGS:
        _PROGS["v2"] = {
            "l0a": _build_l0a_program(),
            "rest": build_program([
                (emit_l0_attn_b, L0_ATTN_B_DECL),
                (emit_l0_ffn, L0_FFN_DECL),
                (emit_l1_attn, L1_ATTN_DECL),
                (emit_l1_ffn, L1_FFN_DECL),
            ]),
        }
    return _PROGS["v2"]


def kernel(**inputs):
    progs = _get_progs()
    base = base_inputs()
    x = np.ascontiguousarray(inputs["L0_attn_x"], dtype=np.float32)
    n = 8
    l0w = [l0_attn_weights(inputs, core_id=c) for c in range(n)]
    l0fw = l0_ffn_weights(inputs)
    l1aw = l1_attn_weights(inputs)
    l1fw = l1_ffn_weights(inputs)

    in_maps = []
    for c in range(n):
        m = dict(base)
        m["x_in"] = np.ascontiguousarray(x[c])
        for k, v in l0w[c].items():
            m["b0_" + k] = v
        in_maps.append(m)
    resA = run_bass_kernel_spmd(progs["l0a"].nc, in_maps, core_ids=list(range(n)))
    gath = np.ascontiguousarray(
        np.concatenate([resA.results[c]["cnt_out"].reshape(1, 8) for c in range(n)], axis=0))

    in_maps = []
    for c in range(n):
        m = dict(base)
        m["x_in"] = np.ascontiguousarray(x[c])
        for k, v in l0w[c].items():
            m["b0_" + k] = v
        m["b0_xmid_in"] = resA.results[c]["xmid_out"]
        m["b0_gathin"] = gath
        for k, v in l0fw.items():
            m["b1_" + k] = v
        for k, v in l1aw.items():
            m["b2_" + k] = v
        for k, v in l1fw.items():
            m["b3_" + k] = v
        in_maps.append(m)
    res = run_bass_kernel_spmd(progs["rest"].nc, in_maps, core_ids=list(range(n)))
    return np.stack([res.results[c]["x_out"] for c in range(n)]).astype(np.float32)


# version 10 (final)
# speedup vs baseline: 1.3918x; 1.0148x over previous
import numpy as np
from contextlib import ExitStack
import concourse.bass as bass
import concourse.mybir as mybir
from concourse.bass_utils import run_bass_kernel_spmd
from concourse.bass_core_rust import IndirectOffsetOnAxis

F32 = mybir.dt.float32
BF16 = mybir.dt.bfloat16
I32 = mybir.dt.int32
U32 = mybir.dt.uint32
AF = mybir.ActivationFunctionType
ALU = mybir.AluOpType
AX = mybir.AxisListType

T = 2048
D = 1024
NT = T // 128
EPS_LN = 1e-5
ENGINES = ("sync", "tensor", "vector", "scalar", "gpsimd")


class Sem:
    __slots__ = ("h", "n", "name")
    def __init__(self, ctx, nc, name):
        self.h = ctx.enter_context(nc.semaphore(name))
        self.n = 0
        self.name = name


class Prog:
    def __init__(self):
        self.nc = bass.Bass("TRN2", target_bir_lowering=False, num_devices=8)
        self.ctx = ExitStack()
        self.uid = 0

    def sem(self, name):
        return Sem(self.ctx, self.nc, self.name(name))

    def get_sem(self, name):
        if not hasattr(self, "_sems"):
            self._sems = {}
        if name not in self._sems:
            self._sems[name] = Sem(self.ctx, self.nc, name)
        return self._sems[name]

    def sb(self, name, shape, dtype):
        return self.ctx.enter_context(self.nc.sbuf_tensor(self.name(name), shape, dtype))[:]

    def ps(self, name, shape, dtype):
        return self.ctx.enter_context(self.nc.psum_tensor(self.name(name), shape, dtype))[:]

    def name(self, base):
        self.uid += 1
        return f"{base}_{self.uid}"


class Sched:
    def __init__(self, p: Prog):
        self.p = p
        self.items = {e: [] for e in ENGINES}
        if not hasattr(p, "_waited"):
            p._waited = {e: {} for e in ENGINES}

    def emit(self, eng, fn, waits=(), incs=()):
        for s, k in incs:
            s.n += k
        seen = self.p._waited[eng]
        best = {}
        for s, v in waits:
            if v > 0 and v > best.get(s.h, -1):
                best[s.h] = v
        ws = []
        for h, v in best.items():
            if v > seen.get(h, 0):
                ws.append((h, v))
                seen[h] = v
        self.items[eng].append((fn, ws, [(s.h, k) for s, k in incs]))

    def flush(self, name=None):
        p = self.p
        if not any(self.items.values()):
            return
        with p.nc.Block(name) as blk:
            for eng in ENGINES:
                items = self.items[eng]
                if not items:
                    continue
                def body(e, items=items):
                    for fn, waits, incs in items:
                        for h, v in waits:
                            e.wait_ge(h, v)
                        inst = fn(e)
                        for h, k in incs:
                            if inst is None:
                                inst = e.nop()
                            inst.then_inc(h, k)
                getattr(blk, eng)(body)
        self.items = {e: [] for e in ENGINES}


def w(s: Sem, v=None):
    return (s, s.n if v is None else v)


def _emit_ln_inner(sch, x_ap, out_ap, stat, scr, s_stats, done=None,
            gamma=None, beta=None, scalar_waits=(), vector_waits=(), eps_ap=None):
    sum_ = stat[:, 0:1]; negmean = stat[:, 1:2]; ssq = stat[:, 2:3]; rstd = stat[:, 3:4]

    def sc(e):
        e.activation(out=scr, in_=x_ap, func=AF.Identity, accum_out=sum_)
        e.drain()
        e.activation(out=negmean, in_=sum_, func=AF.Identity, scale=sch.p.cv(2))
        e.drain()
        e.activation(out=scr, in_=x_ap, func=AF.Square, bias=negmean, accum_out=ssq)
        e.drain()
        return e.activation(out=rstd, in_=ssq, func=AF.Sqrt, scale=sch.p.cv(3), bias=eps_ap)
    sch.emit("scalar", sc, waits=list(scalar_waits), incs=[(s_stats, 1)])

    def ve(e):
        e.reciprocal(out=rstd, in_=rstd)
        e.drain()
        if gamma is None and beta is None:
            return e.tensor_scalar(out=out_ap, in0=x_ap, scalar1=negmean, scalar2=rstd,
                                   op0=ALU.add, op1=ALU.mult)
        e.tensor_scalar(out=scr, in0=x_ap, scalar1=negmean, scalar2=rstd,
                        op0=ALU.add, op1=ALU.mult)
        e.drain()
        if gamma is not None:
            e.tensor_tensor(out=(scr if beta is not None else out_ap), in0=scr, in1=gamma, op=ALU.mult)
            e.drain()
        if beta is not None:
            return e.tensor_tensor(out=out_ap, in0=scr, in1=beta, op=ALU.add)
    incs = [(done, 1)] if done is not None else []
    sch.emit("vector", ve, waits=[w(s_stats)] + list(vector_waits), incs=incs)


def emit_ln(sch, x_ap, out_ap, stat, scr, s_stats, done=None,
            gamma=None, beta=None, scalar_waits=(), vector_waits=()):
    _emit_ln_inner(sch, x_ap, out_ap, stat, scr, s_stats, done=done, gamma=gamma,
                   beta=beta, scalar_waits=scalar_waits, vector_waits=vector_waits,
                   eps_ap=sch.p.epsap)


def emit_bcast_load(sch, dram_row, sbuf_tile, s_ld, eng="sync", waits=()):
    def f(e, dram_row=dram_row, sbuf_tile=sbuf_tile):
        return e.dma_start(out=sbuf_tile, in_=dram_row.to_broadcast([sbuf_tile.shape[0]] + list(dram_row.shape[1:])))
    sch.emit(eng, f, waits=waits, incs=[(s_ld, 16)])


def emit_l1_ffn(p, x_src, x_dst, weights, consts):
    ident_bf = consts["ident_bf"]
    nc = p.nc
    TH = 1024
    NTH = TH // 128
    sch = Sched(p)
    ctx = ExitStack()
    sb = lambda n, s, d: ctx.enter_context(nc.sbuf_tensor(p.name(n), s, d))[:]
    ps = lambda n, s, d: ctx.enter_context(nc.psum_tensor(p.name(n), s, d))[:]

    g_bc = sb("g_bc", [128, D], F32)
    b_bc = sb("b_bc", [128, D], F32)
    b2_bc = sb("b2_bc", [128, D], F32)
    b1T = sb("b1T", [128, 32], F32)
    hT = sb("hT", [128, 8 * TH], BF16)
    z_acc = sb("z_acc", [128, NTH * D], F32)
    scr = sb("scr", [128, D], F32)
    scr2 = sb("scr2", [128, D], F32)
    xtmp = sb("xtmp", [128, D], F32)
    stat = sb("stat", [128, 8], F32)
    xn_bf = sb("xn_bf", [128, D], BF16)
    w1q = [sb("w1q", [128, 512], I32) for _ in range(2)]
    w1f = sb("w1f", [128, 512], F32)
    w1s = [sb("w1s", [128, 512], F32) for _ in range(2)]
    w1c = sb("w1c", [128, 8 * 512], BF16)
    w2q = [sb("w2q", [128, 1024], I32) for _ in range(2)]
    w2f = sb("w2f", [128, 1024], F32)
    w2s = [sb("w2s", [128, 1024], F32) for _ in range(2)]
    w2c = sb("w2c", [128, 4 * 1024], BF16)
    y2 = sb("y2", [128, 8 * 512], BF16)
    pt0 = ps("pt0", [128, 128], BF16)
    pt1 = ps("pt1", [128, 128], BF16)
    py = [ps("py", [128, 512], F32) for _ in range(2)]
    pz = [ps("pz", [128, 512], F32) for _ in range(2)]

    x_res = sb("x_res", [128, NT * D], F32)
    s_io = p.get_sem("io")
    s_ld = p.sem("ld")
    s_stats = p.get_sem("st")
    s_xn = p.get_sem("xn")
    s_tr = p.get_sem("tr")
    s_evac = p.get_sem("ev")
    s_w1 = p.get_sem("w1")
    s_w2 = p.get_sem("w2")
    s_wq = [p.sem("wqa"), p.sem("wqb")]
    s_ws = [p.sem("wsa"), p.sem("wsb")]
    s_wq2 = [p.sem("wq2a"), p.sem("wq2b")]
    s_ws2 = [p.sem("ws2a"), p.sem("ws2b")]
    s_dq1 = p.get_sem("dq1")
    s_dq2 = p.get_sem("dq2")
    s_mmy = p.get_sem("mmy")
    s_silu = p.get_sem("silu")
    s_mmz = p.get_sem("mmz")
    s_zacc = p.get_sem("za")
    s_ep = p.get_sem("ep")

    def ldx(e):
        return e.dma_start(out=x_res.rearrange("p (n d) -> p n d", d=D),
                           in_=x_src.rearrange("(n p) d -> p n d", p=128))
    sch.emit("sync", ldx, waits=[w(s_io)], incs=[(s_io, 16)])
    emit_bcast_load(sch, weights["ln_gamma"], g_bc, s_ld)
    emit_bcast_load(sch, weights["ln_beta"], b_bc, s_ld)
    emit_bcast_load(sch, weights["fc2_bias"], b2_bc, s_ld)
    sch.emit("sync", lambda e: e.dma_start(out=b1T, in_=weights["fc1_biasT"]),
             incs=[(s_ld, 16)])
    ld_done = w(s_ld)
    first_scalar_wait = [ld_done, w(s_io)]

    for half in range(2):
        for ii in range(NTH):
            i = half * NTH + ii
            x_t = x_res[:, i * D:(i + 1) * D]
            z_t = z_acc[:, ii * D:(ii + 1) * D]
            prev_half_fence = [w(s_ep)] if (half > 0 and ii == 0) else []
            emit_ln(sch, x_t, xtmp, stat, scr, s_stats, done=s_xn,
                    scalar_waits=(first_scalar_wait if i == 0 else [w(s_xn)]))
            first_scalar_wait = None
            emit_ln(sch, xtmp, xn_bf, stat, scr, s_stats, done=s_xn,
                    gamma=g_bc, beta=b_bc,
                    scalar_waits=[w(s_xn)], vector_waits=[w(s_tr)])
            def zseed(e, z_t=z_t, xtmp=xtmp):
                return e.scalar_tensor_tensor(out=z_t, in0=xtmp, scalar=2.0,
                                              in1=b2_bc, op0=ALU.mult, op1=ALU.add)
            sch.emit("vector", zseed, waits=prev_half_fence, incs=[(s_zacc, 1)])
            for k in range(8):
                pbank = pt0 if (k % 2 == 0) else pt1
                evac_snapshot = s_evac.n
                waits = [w(s_xn)] if k == 0 else []
                if evac_snapshot >= 2:
                    waits.append((s_evac, evac_snapshot - 1))
                def tr(e, k=k, pbank=pbank):
                    return e.transpose(out=pbank, in_=xn_bf[:, k * 128:(k + 1) * 128], identity=ident_bf)
                sch.emit("tensor", tr, waits=waits, incs=[(s_tr, 1)])
                dst = hT[:, k * TH + ii * 128: k * TH + (ii + 1) * 128]
                def ev(e, dst=dst, pbank=pbank):
                    return e.tensor_copy(out=dst, in_=pbank)
                sch.emit("vector", ev, waits=[w(s_tr)], incs=[(s_evac, 1)])
        sch.flush(f"l1ffn_prep{half}")
        hT_ready = w(s_evac)
        zseed_done = w(s_zacc)

        for n in range(8):
            mmy_fence = w(s_mmy)
            mmz_fence = w(s_mmz)
            for k in range(8):
                qb = w1q[k % 2]; sbuf_s = w1s[k % 2]
                kk = (half * 8 + n) * 8 + k
                def ldq(e, n=n, k=k, qb=qb):
                    return e.dma_start(out=qb, in_=weights["fc1_q"][k * 128:(k + 1) * 128, n * 512:(n + 1) * 512])
                sch.emit("sync", ldq, waits=[(s_dq1, s_dq1.n - 1)] if kk >= 2 else [], incs=[(s_wq[k % 2], 16)])
                def lds(e, n=n, k=k, sbuf_s=sbuf_s):
                    return e.dma_start(out=sbuf_s, in_=weights["fc1_scales"][k:k + 1, n * 512:(n + 1) * 512].to_broadcast([128, 512]))
                sch.emit("sync", lds, waits=[(s_w1, s_w1.n - 1)] if kk >= 2 else [], incs=[(s_ws[k % 2], 16)])
                def deq_s(e, qb=qb):
                    return e.activation(out=w1f, in_=qb, func=AF.Identity, bias=p.cv(1))
                sch.emit("scalar", deq_s,
                         waits=[w(s_wq[k % 2])] + ([w(s_w1)] if kk > 0 else []),
                         incs=[(s_dq1, 1)])
                def deq_v(e, k=k, sbuf_s=sbuf_s):
                    return e.tensor_tensor(out=w1c[:, k * 512:(k + 1) * 512], in0=w1f, in1=sbuf_s, op=ALU.mult)
                vw = [w(s_dq1), w(s_ws[k % 2])]
                if k == 0 and (n > 0 or half > 0):
                    vw.append(mmy_fence)
                sch.emit("vector", deq_v, waits=vw, incs=[(s_w1, 1)])
            w1_ready = w(s_w1)

            for j in range(4):
                for g in range(2):
                    idx = (half * 8 + n) * 8 + j * 2 + g
                    pyb = py[idx % 2]
                    waits = [w1_ready, hT_ready] if (j == 0 and g == 0) else []
                    if idx >= 2:
                        waits.append((s_silu, s_silu.n - 1))
                    def mm(e, j=j, g=g, pyb=pyb):
                        inst = None
                        for k in range(8):
                            inst = e.matmul(out=pyb,
                                            lhsT=w1c[:, k * 512 + j * 128: k * 512 + (j + 1) * 128],
                                            rhs=hT[:, k * TH + g * 512: k * TH + (g + 1) * 512],
                                            start=(k == 0), stop=(k == 7))
                        return inst
                    sch.emit("tensor", mm, waits=waits, incs=[(s_mmy, 1)])
                    ntile_abs = n * 4 + j
                    def silu(e, j=j, g=g, pyb=pyb, ntile_abs=ntile_abs):
                        return e.activation(out=y2[:, (j * 2 + g) * 512:(j * 2 + g + 1) * 512],
                                            in_=pyb, func=AF.Silu,
                                            bias=b1T[:, ntile_abs:ntile_abs + 1])
                    sw = [w(s_mmy)]
                    if j == 0 and g == 0 and (n > 0 or half > 0):
                        sw.append(mmz_fence)
                    sch.emit("scalar", silu, waits=sw, incs=[(s_silu, 1)])

            for jj in range(4):
                qb = w2q[jj % 2]; sbuf_s = w2s[jj % 2]
                gidx = n * 4 + jj
                ggidx = half * 32 + gidx
                def ldq2(e, gidx=gidx, qb=qb):
                    return e.dma_start(out=qb, in_=weights["fc2_q"][gidx * 128:(gidx + 1) * 128, :])
                sch.emit("sync", ldq2, waits=[(s_dq2, s_dq2.n - 1)] if ggidx >= 2 else [], incs=[(s_wq2[jj % 2], 16)])
                def lds2(e, gidx=gidx, sbuf_s=sbuf_s):
                    return e.dma_start(out=sbuf_s, in_=weights["fc2_scales"][gidx:gidx + 1, :].to_broadcast([128, 1024]))
                sch.emit("sync", lds2, waits=[(s_w2, s_w2.n - 1)] if ggidx >= 2 else [], incs=[(s_ws2[jj % 2], 16)])
                def deq2_s(e, qb=qb):
                    return e.activation(out=w2f, in_=qb, func=AF.Identity, bias=p.cv(1))
                sch.emit("scalar", deq2_s,
                         waits=[w(s_wq2[jj % 2])] + ([w(s_w2)] if ggidx > 0 else []),
                         incs=[(s_dq2, 1)])
                def deq2_v(e, jj=jj, sbuf_s=sbuf_s):
                    return e.tensor_tensor(out=w2c[:, jj * 1024:(jj + 1) * 1024], in0=w2f, in1=sbuf_s, op=ALU.mult)
                vw = [w(s_dq2), w(s_ws2[jj % 2])]
                if jj == 0 and (n > 0 or half > 0):
                    vw.append(mmz_fence)
                sch.emit("vector", deq2_v, waits=vw, incs=[(s_w2, 1)])
            w2_ready = w(s_w2)

            silu_all = w(s_silu)
            for g in range(2):
                for mt in range(4):
                    for hc in range(2):
                        idx = (half * 8 + n) * 16 + g * 8 + mt * 2 + hc
                        pzb = pz[idx % 2]
                        waits = [silu_all, w2_ready] if (g == 0 and mt == 0 and hc == 0) else []
                        if idx >= 2:
                            waits.append((s_zacc, s_zacc.n - 1))
                        if n == 0 and g == 0 and mt == 0 and hc == 0:
                            waits.append(zseed_done)
                        def mm2(e, g=g, mt=mt, hc=hc, pzb=pzb):
                            inst = None
                            for j in range(4):
                                inst = e.matmul(out=pzb,
                                                lhsT=y2[:, (j * 2 + g) * 512 + mt * 128:(j * 2 + g) * 512 + (mt + 1) * 128],
                                                rhs=w2c[:, j * 1024 + hc * 512: j * 1024 + (hc + 1) * 512],
                                                start=(j == 0), stop=(j == 3))
                            return inst
                        sch.emit("tensor", mm2, waits=waits, incs=[(s_mmz, 1)])
                        mtile = g * 4 + mt
                        dst = z_acc[:, mtile * D + hc * 512: mtile * D + (hc + 1) * 512]
                        def zadd(e, dst=dst, pzb=pzb):
                            return e.tensor_tensor(out=dst, in0=dst, in1=pzb, op=ALU.add)
                        sch.emit("vector", zadd, waits=[w(s_mmz)], incs=[(s_zacc, 1)])
            sch.flush(f"l1ffn_h{half}n{n}")

        z_done = w(s_zacc)
        for ii in range(NTH):
            i = half * NTH + ii
            x_t = x_res[:, i * D:(i + 1) * D]
            z_t = z_acc[:, ii * D:(ii + 1) * D]
            def ep1(e, z_t=z_t):
                return e.tensor_scalar(out=scr2, in0=z_t, scalar1=0.5, scalar2=None, op0=ALU.mult)
            sch.emit("vector", ep1, waits=[z_done] if ii == 0 else [], incs=[(s_ep, 1)])
            emit_ln(sch, scr2, scr2, stat, scr, s_stats, done=s_ep,
                    scalar_waits=[w(s_ep)])
            def ep2(e, x_t=x_t):
                return e.tensor_tensor(out=x_t, in0=x_t, in1=scr2, op=ALU.add)
            sch.emit("vector", ep2, incs=[(s_ep, 1)])
        sch.flush(f"l1ffn_ep{half}")
    def stx(e):
        return e.dma_start(out=x_dst.rearrange("(n p) d -> p n d", p=128),
                           in_=x_res.rearrange("p (n d) -> p n d", d=D))
    sch.emit("sync", stx, waits=[w(s_ep)], incs=[(s_io, 16)])
    sch.emit("sync", lambda e: e.nop(), waits=[w(s_io)])
    sch.flush("l1ffn_store")
    ctx.close()
    return s_io



def emit_l1_attn(p, x_src, x_dst, weights, consts):
    nc = p.nc
    ident_bf = consts["ident_bf"]
    NH = 16
    sch = Sched(p)
    ctx = ExitStack()
    sb = lambda n, s, d: ctx.enter_context(nc.sbuf_tensor(p.name(n), s, d))[:]
    ps = lambda n, s, d: ctx.enter_context(nc.psum_tensor(p.name(n), s, d))[:]

    qT = sb("qT", [128, 8 * T], BF16)
    kT = sb("kT", [128, 8 * T], BF16)
    v_ = sb("v", [128, NT * D], BF16)
    oT = sb("oT", [128, 8 * T], BF16)
    pen9 = sb("pen9", [128, 128], F32)
    bqT = sb("bqT", [128, 8], F32)
    bo_bc = sb("bo_bc", [128, D], F32)
    stat = sb("stat", [128, 8], F32)
    scr = sb("scr", [128, D], F32)
    scr2 = sb("scr2", [128, D], F32)
    lrow = sb("lrow", [128, 4], F32)

    s_io = p.get_sem("io")
    s_ld = p.sem("ld")
    s_stats = p.get_sem("st")
    s_xn = p.get_sem("xn")
    s_tr = p.get_sem("tr")
    s_evac = p.get_sem("ev")
    s_wld = p.sem("wld")
    s_xt = [p.sem("xta"), p.sem("xtb")]
    s_st = [p.sem("sta"), p.sem("stb")]
    s_proj = p.get_sem("mmy")
    s_pevac = p.get_sem("silu")
    s_ab = p.get_sem("ab")
    s_smm = p.get_sem("mmz")
    s_sevac = p.get_sem("za")
    s_devac = p.get_sem("zad")
    s_max = p.get_sem("w1")
    s_exp = p.get_sem("dq1")
    s_ptr = p.get_sem("tr2")
    s_ptev = p.get_sem("ev2")
    s_pv = p.get_sem("w2")
    s_o = p.get_sem("dq2")
    s_oT = p.get_sem("tr3")
    s_zmm = p.get_sem("zmm")
    s_ep = p.get_sem("ep")

    with nc.sbuf_tensor(p.name("xnT"), [128, 8 * T], BF16) as xnT_t, \
         nc.sbuf_tensor(p.name("xt"), [128, 2 * D], F32) as xt_t, \
         nc.sbuf_tensor(p.name("wbf"), [128, 8 * D], BF16) as wbf_t, \
         nc.sbuf_tensor(p.name("xnbf"), [128, D], BF16) as xnbf_t, \
         nc.psum_tensor(p.name("pt0"), [128, 128], BF16) as pt0_t, \
         nc.psum_tensor(p.name("pt1"), [128, 128], BF16) as pt1_t, \
         nc.psum_tensor(p.name("pp0"), [128, 512], F32) as pp0_t, \
         nc.psum_tensor(p.name("pp1"), [128, 512], F32) as pp1_t:
        xnT = xnT_t[:]; wbf = wbf_t[:]; xnbf = xnbf_t[:]
        pt = (pt0_t[:], pt1_t[:])
        pp = (pp0_t[:], pp1_t[:])
        xt = xt_t[:]

        sch.emit("sync", lambda e: e.dma_start(out=pen9, in_=weights["pen9"]), incs=[(s_ld, 16)])
        sch.emit("sync", lambda e: e.dma_start(out=bqT, in_=weights["bqT"]), incs=[(s_ld, 16)])
        emit_bcast_load(sch, weights["bo"], bo_bc, s_ld)
        consts_ready = w(s_ld)
        sch.emit("scalar", lambda e: e.nop(), waits=[consts_ready])
        sch.emit("vector", lambda e: e.nop(), waits=[consts_ready])

        for i in range(NT):
            xti = xt[:, (i % 2) * D:(i % 2 + 1) * D]
            ldw = []
            if i >= 2:
                ldw = [(s_xn, s_xn.n - 1)]
            def ldx(e, xti=xti, i=i):
                return e.dma_start(out=xti, in_=x_src[i * 128:(i + 1) * 128, :])
            sl = s_xt[i % 2]
            sch.emit("sync", ldx, waits=([w(s_io)] if i == 0 else ldw), incs=[(sl, 16)])
            emit_ln(sch, xti, xnbf, stat, scr, s_stats, done=s_xn,
                    scalar_waits=[w(sl), w(s_xn)], vector_waits=[w(s_tr)])
            for k in range(8):
                pbank = pt[k % 2]
                evac_snapshot = s_evac.n
                waits = [w(s_xn)] if k == 0 else []
                if evac_snapshot >= 2:
                    waits.append((s_evac, evac_snapshot - 1))
                def tr(e, k=k, pbank=pbank):
                    return e.transpose(out=pbank, in_=xnbf[:, k * 128:(k + 1) * 128], identity=ident_bf)
                sch.emit("tensor", tr, waits=waits, incs=[(s_tr, 1)])
                dst = xnT[:, k * T + i * 128: k * T + (i + 1) * 128]
                def ev(e, dst=dst, pbank=pbank):
                    return e.tensor_copy(out=dst, in_=pbank)
                sch.emit("vector", ev, waits=[w(s_tr)], incs=[(s_evac, 1)])
        sch.flush("l1a_ln")
        xnT_ready = w(s_evac)

        for pi, (wname, outbuf, kind) in enumerate([
                ("Wq", qT, "q"), ("Wk", kT, "k"), ("Wv", v_, "v")]):
            wdap = weights[wname]
            wfence = w(s_proj)
            for kc in range(8):
                def ldw_(e, wdap=wdap, kc=kc):
                    return e.dma_start(out=wbf[:, kc * D:(kc + 1) * D],
                                       in_=wdap[kc * 128:(kc + 1) * 128, :])
                sch.emit("gpsimd", ldw_, waits=[wfence] if kc == 0 else [], incs=[(s_wld, 16)])
            wld_done = w(s_wld)
            if kind in ("q", "k"):
                for c in range(8):
                    for tc in range(4):
                        idx = pi * 32 + c * 4 + tc
                        pb = pp[idx % 2]
                        waits = [wld_done, xnT_ready] if (c == 0 and tc == 0) else []
                        if True:
                            waits = waits + [(s_pevac, s_pevac.n - 1)] if s_pevac.n >= 2 else waits
                        def mm(e, c=c, tc=tc, pb=pb):
                            inst = None
                            for kc in range(8):
                                inst = e.matmul(out=pb,
                                                lhsT=wbf[:, kc * D + c * 128: kc * D + (c + 1) * 128],
                                                rhs=xnT[:, kc * T + tc * 512: kc * T + (tc + 1) * 512],
                                                start=(kc == 0), stop=(kc == 7))
                            return inst
                        sch.emit("tensor", mm, waits=waits, incs=[(s_proj, 1)])
                        dst = outbuf[:, c * T + tc * 512: c * T + (tc + 1) * 512]
                        if kind == "q":
                            def evq(e, dst=dst, pb=pb, c=c):
                                return e.activation(out=dst, in_=pb, func=AF.Identity,
                                                    bias=bqT[:, c:c + 1])
                            sch.emit("scalar", evq, waits=[w(s_proj)], incs=[(s_pevac, 1)])
                        else:
                            def evk(e, dst=dst, pb=pb):
                                return e.activation(out=dst, in_=pb, func=AF.Identity)
                            sch.emit("scalar", evk, waits=[w(s_proj)], incs=[(s_pevac, 1)])
            else:
                for tt in range(16):
                    for hc in range(2):
                        idx = c_idx = tt * 2 + hc
                        pb = pp[idx % 2]
                        waits = [wld_done] if idx == 0 else []
                        if s_pevac.n >= 2:
                            waits = waits + [(s_pevac, s_pevac.n - 1)]
                        def mmv(e, tt=tt, hc=hc, pb=pb):
                            inst = None
                            for kc in range(8):
                                inst = e.matmul(out=pb,
                                                lhsT=xnT[:, kc * T + tt * 128: kc * T + (tt + 1) * 128],
                                                rhs=wbf[:, kc * D + hc * 512: kc * D + (hc + 1) * 512],
                                                start=(kc == 0), stop=(kc == 7))
                            return inst
                        sch.emit("tensor", mmv, waits=waits, incs=[(s_proj, 1)])
                        dst = v_[:, tt * D + hc * 512: tt * D + (hc + 1) * 512]
                        def evv(e, dst=dst, pb=pb):
                            return e.activation(out=dst, in_=pb, func=AF.Identity)
                        sch.emit("scalar", evv, waits=[w(s_proj)], incs=[(s_pevac, 1)])
        sch.flush("l1a_proj")
    proj_done = w(s_pevac)

    with nc.sbuf_tensor(p.name("S_row"), [128, 2 * T], F32) as S_t, \
         nc.sbuf_tensor(p.name("P_row"), [128, 2 * T], BF16) as P_t, \
         nc.sbuf_tensor(p.name("PT"), [128, 2 * 512], BF16) as PT_t, \
         nc.sbuf_tensor(p.name("alibi"), [128, T], F32) as arow_t, \
         nc.sbuf_tensor(p.name("obf"), [128, 2 * 64], BF16) as obf_t, \
         nc.psum_tensor(p.name("ps0"), [128, 512], F32) as ps0_t, \
         nc.psum_tensor(p.name("ps1"), [128, 512], F32) as ps1_t, \
         nc.psum_tensor(p.name("ppv"), [128, 64], F32) as ppv_t, \
         nc.psum_tensor(p.name("ptp0"), [128, 512], BF16) as ptp0_t, \
         nc.psum_tensor(p.name("ptp1"), [128, 512], BF16) as ptp1_t, \
         nc.psum_tensor(p.name("pot"), [64, 128], BF16) as pot_t:
        S_row2 = S_t[:]; P_row2 = P_t[:]; PT2 = PT_t[:]
        alibi_h = arow_t[:]; obf2 = obf_t[:]
        psb = (ps0_t[:], ps1_t[:])
        ppv = ppv_t[:]
        ptp = (ptp0_t[:], ptp1_t[:])
        pot = pot_t[:]

        s_abd = p.sem("abd")
        PTW = 512

        def stageA(h, q):
            par = (h * NT + q) % 2
            S_row = S_row2[:, par * T:par * T + T]
            P_row = P_row2[:, par * T:par * T + T]
            l_ = lrow[:, 2 * par:2 * par + 1]
            negm = lrow[:, 2 * par + 1:2 * par + 2]
            klen = (q + 1) * 128
            rem = klen % 512
            if rem == 0:
                rem = 512
            nfull = (klen - rem) // 512
            widths = [512] * nfull + [rem]
            strip = h // 2
            pbase = (h % 2) * 64
            qsl = qT[pbase:pbase + 64, strip * T + q * 128: strip * T + (q + 1) * 128]
            for bi, bw in enumerate(widths):
                boff = bi * 512
                pb = psb[(s_smm.n) % 2]
                mmw = []
                if h == 0 and q == 0 and bi == 0:
                    mmw = [proj_done, w(s_ld)]
                if bi == 0 and q == 0:
                    mmw = mmw + [(s_abd, 16 * (h + 1))]
                if s_smm.n >= 2:
                    mmw = mmw + [(s_sevac, s_sevac.n - 1)]
                def smm(e, qsl=qsl, pb=pb, strip=strip, pbase=pbase, boff=boff, bw=bw):
                    return e.matmul(out=pb[:, 0:bw], lhsT=qsl,
                                    rhs=kT[pbase:pbase + 64, strip * T + boff: strip * T + boff + bw],
                                    start=True, stop=True)
                sch.emit("tensor", smm, waits=mmw, incs=[(s_smm, 1)])
                dst = S_row[:, boff:boff + bw]
                last = bi == len(widths) - 1
                def sev(e, dst=dst, pb=pb, bw=bw, boff=boff, last=last, klen=klen, S_row=S_row):
                    e.scalar_tensor_tensor(out=dst, in0=pb[:, 0:bw], scalar=0.125,
                                           in1=alibi_h[:, boff:boff + bw],
                                           op0=ALU.mult, op1=ALU.add)
                    if last:
                        e.drain()
                        e.tensor_tensor(out=S_row[:, klen - 128:klen],
                                        in0=S_row[:, klen - 128:klen], in1=pen9, op=ALU.add)
                    return e.drain()
                evw = [w(s_smm)]
                if s_exp.n >= 1:
                    evw.append((s_exp, s_exp.n - 1))
                sch.emit("vector", sev, waits=evw, incs=[(s_sevac, 1)])
            def vmax(e, S_row=S_row, negm=negm, klen=klen):
                e.tensor_reduce(out=negm, in_=S_row[:, 0:klen], axis=AX.X, op=ALU.max)
                e.drain()
                return e.tensor_scalar(out=negm, in0=negm, scalar1=-1.0, scalar2=None, op0=ALU.mult)
            vmw = [w(s_sevac)]
            if s_exp.n >= 1:
                vmw.append((s_exp, s_exp.n - 1))
            sch.emit("vector", vmax, waits=vmw, incs=[(s_max, 1)])
            def sexp(e, S_row=S_row, P_row=P_row, negm=negm, l_=l_, klen=klen):
                return e.activation(out=P_row[:, 0:klen], in_=S_row[:, 0:klen],
                                    func=AF.Exp, bias=negm, accum_out=l_)
            sxw = [w(s_max)]
            if len(ptr_hist) >= 1:
                sxw.append((s_ptr, ptr_hist[-1]))
            if s_o.n >= 1:
                sxw.append((s_o, s_o.n))
            sch.emit("scalar", sexp, waits=sxw, incs=[(s_exp, 1)])
            return dict(exp=s_exp.n)

        def stageB(h, q, tok):
            par = (h * NT + q) % 2
            P_row = P_row2[:, par * T:par * T + T]
            l_ = lrow[:, 2 * par:2 * par + 1]
            klen = (q + 1) * 128
            nc_chunks = klen // 128
            strip = h // 2
            pbase = (h % 2) * 64
            for g0 in range(0, nc_chunks, 4):
                gn = min(4, nc_chunks - g0)
                bank = ptp[(s_ptev.n) % 2]
                for j in range(gn):
                    c = g0 + j
                    ptw = []
                    if j == 0:
                        ptw.append((s_exp, tok["exp"]))
                        if s_ptev.n >= 2:
                            ptw.append((s_ptev, s_ptev.n - 1))
                    def ptr_(e, P_row=P_row, c=c, bank=bank, j=j):
                        return e.transpose(out=bank[:, j * 128:(j + 1) * 128],
                                           in_=P_row[:, c * 128:(c + 1) * 128], identity=ident_bf)
                    sch.emit("tensor", ptr_, waits=ptw, incs=[(s_ptr, 1)])
                dst = PT2[:, (s_ptev.n % 2) * PTW:(s_ptev.n % 2) * PTW + gn * 128]
                def ptev(e, dst=dst, bank=bank, gn=gn):
                    return e.copy(out=dst, in_=bank[:, 0:gn * 128])
                pw = [w(s_ptr)]
                if len(pv_bank_hist) >= 2:
                    pw.append((s_pv, pv_bank_hist[-2]))
                sch.emit("scalar", ptev, waits=pw, incs=[(s_ptev, 1)])
                pv_bank_hist.append(s_pv.n + gn)
                for j in range(gn):
                    c = g0 + j
                    def pvm(e, c=c, h=h, dst=dst, j=j, nc_chunks=nc_chunks):
                        return e.matmul(out=ppv, lhsT=dst[:, j * 128:(j + 1) * 128],
                                        rhs=v_[:, c * D + h * 64: c * D + h * 64 + 64],
                                        start=(c == 0), stop=(c == nc_chunks - 1))
                    pvw = [w(s_ptev)] if j == 0 else []
                    if c == 0 and s_o.n >= 1:
                        pvw.append((s_o, s_o.n))
                    sch.emit("tensor", pvm, waits=pvw, incs=[(s_pv, 1)])
            ptr_hist.append(s_ptr.n)
            obf = obf2[:, par * 64:par * 64 + 64]
            def vo(e, l_=l_, obf=obf):
                e.reciprocal(out=l_, in_=l_)
                e.drain()
                return e.tensor_scalar(out=obf, in0=ppv, scalar1=l_, scalar2=None, op0=ALU.mult)
            sch.emit("vector", vo, waits=[w(s_pv)] + ([(s_oT, s_oT.n - 1)] if s_o.n >= 2 else []),
                     incs=[(s_o, 1)])
            def otr(e, obf=obf):
                return e.transpose(out=pot, in_=obf, identity=ident_bf)
            sch.emit("tensor", otr, waits=[w(s_o), (s_oT, s_oT.n)], incs=[(s_oT, 1)])
            dsto = oT[pbase:pbase + 64, strip * T + q * 128: strip * T + (q + 1) * 128]
            def otev(e, dsto=dsto):
                return e.tensor_copy(out=dsto, in_=pot)
            sch.emit("vector", otev, waits=[w(s_oT)], incs=[(s_oT, 1)])

        ptr_hist = []
        pv_bank_hist = []
        pending = None
        for h in range(NH):
            def bab(e, h=h):
                return e.dma_start(out=alibi_h, in_=weights["alibi"][h:h + 1, :].to_broadcast([128, T]))
            abw = [proj_done] if h == 0 else [(s_smm, s_smm.n)]
            sch.emit("sync", bab, waits=abw, incs=[(s_abd, 16)])
            for q in range(NT):
                tok = stageA(h, q)
                if pending is not None:
                    stageB(*pending)
                pending = (h, q, tok)
            sch.flush(f"l1a_h{h}")
        stageB(*pending)
    flash_done = w(s_oT)

    with nc.sbuf_tensor(p.name("wof"), [128, 8 * D], BF16) as wof_t, \
         nc.sbuf_tensor(p.name("xt2"), [128, 2 * D], F32) as xt2_t, \
         nc.psum_tensor(p.name("pz0"), [128, 512], F32) as pz0_t, \
         nc.psum_tensor(p.name("pz1"), [128, 512], F32) as pz1_t:
        wof = wof_t[:]; xt2 = xt2_t[:]
        pz = (pz0_t[:], pz1_t[:])
        for kc in range(8):
            def ldwo(e, kc=kc):
                return e.dma_start(out=wof[:, kc * D:(kc + 1) * D],
                                   in_=weights["Wo"][kc * 128:(kc + 1) * 128, :])
            sch.emit("gpsimd", ldwo, incs=[(s_wld, 16)])
        wo_done = w(s_wld)
        for i in range(NT):
            xti = xt2[:, (i % 2) * D:(i % 2 + 1) * D]
            ldw2 = [] if i < 2 else [(s_ep, s_ep.n - 2), w(s_st[i % 2])]
            sl2 = s_xt[i % 2]
            def ldx2(e, xti=xti, i=i):
                return e.dma_start(out=xti, in_=x_src[i * 128:(i + 1) * 128, :])
            sch.emit("sync", ldx2, waits=ldw2, incs=[(sl2, 16)])
            for hc in range(2):
                pb = pz[hc]
                mmw = [wo_done, flash_done] if (i == 0 and hc == 0) else []
                if i > 0:
                    mmw = mmw + [(s_ep, s_ep.n - (1 + hc))]
                def zmm(e, i=i, hc=hc, pb=pb):
                    inst = None
                    for c in range(8):
                        inst = e.matmul(out=pb, lhsT=oT[:, c * T + i * 128: c * T + (i + 1) * 128],
                                        rhs=wof[:, c * D + hc * 512: c * D + (hc + 1) * 512],
                                        start=(c == 0), stop=(c == 7))
                    return inst
                sch.emit("tensor", zmm, waits=mmw, incs=[(s_zmm, 1)])
            def zev(e, i=i):
                e.tensor_tensor(out=scr2[:, 0:512], in0=pz[0], in1=bo_bc[:, 0:512], op=ALU.add)
                return e.tensor_tensor(out=scr2[:, 512:1024], in0=pz[1], in1=bo_bc[:, 512:1024], op=ALU.add)
            sch.emit("vector", zev, waits=[w(s_zmm), w(s_ep)], incs=[(s_ep, 2)])
            emit_ln(sch, scr2, scr2, stat, scr, s_stats, done=s_ep,
                    scalar_waits=[w(s_ep)])
            def xadd(e, xti=xti):
                return e.tensor_tensor(out=xti, in0=xti, in1=scr2, op=ALU.add)
            sch.emit("vector", xadd, waits=[w(sl2), w(s_ep)], incs=[(s_ep, 1)])
            def stx(e, xti=xti, i=i):
                return e.dma_start(out=x_dst[i * 128:(i + 1) * 128, :], in_=xti)
            sch.emit("sync", stx, waits=[w(s_ep)], incs=[(s_st[i % 2], 16)])
        sch.emit("sync", lambda e: e.nop(), waits=[w(s_st[0]), w(s_st[1])], incs=[(s_io, 16)])
        sch.flush("l1a_z")
    ctx.close()
    return s_io


def make_cvec():
    cv = np.zeros((128, 8), dtype=np.float32)
    cv[:, 0] = EPS_LN
    cv[:, 1] = -128.0
    cv[:, 2] = -1.0 / D
    cv[:, 3] = 1.0 / D
    cv[:, 4] = 1.0
    cv[:, 5] = 0.5
    cv[:, 6] = 0.125
    return cv


def build_program(blocks):
    p = Prog()
    nc = p.nc
    x_in = nc.dram_tensor("x_in", [T, D], F32, kind="ExternalInput").ap()
    x_out = nc.dram_tensor("x_out", [T, D], F32, kind="ExternalOutput").ap()
    ident_in = nc.dram_tensor("ident_in", [128, 128], BF16, kind="ExternalInput").ap()
    identf_in = nc.dram_tensor("identf_in", [128, 128], F32, kind="ExternalInput").ap()
    cvec_in = nc.dram_tensor("cvec_in", [128, 8], F32, kind="ExternalInput").ap()

    ident_bf = p.sb("ident_bf", [128, 128], BF16)
    ident_f = p.sb("ident_f", [128, 128], F32)
    onesf = p.sb("onesf", [1, 128], F32)
    cvec = p.sb("cvec", [128, 8], F32)
    p.epsap = cvec[:, 0:1]
    p.cv = lambda i: cvec[:, i:i + 1]

    s_io = p.get_sem("io")
    sch = Sched(p)
    sch.emit("sync", lambda e: e.dma_start(out=ident_bf, in_=ident_in), incs=[(s_io, 16)])
    sch.emit("sync", lambda e: e.dma_start(out=ident_f, in_=identf_in), incs=[(s_io, 16)])
    sch.emit("sync", lambda e: e.dma_start(out=cvec, in_=cvec_in), incs=[(s_io, 16)])
    sch.emit("vector", lambda e: e.memset(onesf, 1.0), incs=[(s_io, 1)])
    for eng in ("scalar", "vector", "tensor", "gpsimd"):
        sch.emit(eng, lambda e: e.nop(), waits=[w(s_io)])
    sch.flush("init")

    consts = {"ident_bf": ident_bf, "ident_f": ident_f, "onesf": onesf, "cvec": cvec}
    n = len(blocks)
    bufs = [x_in]
    for i in range(n - 1):
        bufs.append(nc.dram_tensor(f"xbuf{i}", [T, D], F32, kind="Internal").ap())
    bufs.append(x_out)

    for i, (emit_fn, wdecl) in enumerate(blocks):
        weights = {}
        for name, (shape, dtype) in wdecl.items():
            weights[name] = nc.dram_tensor(f"b{i}_{name}", list(shape), dtype, kind="ExternalInput").ap()
        emit_fn(p, bufs[i], bufs[i + 1], weights, consts)
    p.ctx.close()
    return p


L1_FFN_DECL = {
    "ln_gamma": ([1, D], F32), "ln_beta": ([1, D], F32),
    "fc1_q": ([1024, 4096], I32), "fc1_scales": ([8, 4096], F32),
    "fc1_biasT": ([128, 32], F32), "fc2_q": ([4096, 1024], I32),
    "fc2_scales": ([32, 1024], F32), "fc2_bias": ([1, D], F32),
}

L1_ATTN_DECL = {
    "Wq": ([1024, 1024], F32), "Wk": ([1024, 1024], F32),
    "Wv": ([1024, 1024], F32), "Wo": ([1024, 1024], F32),
    "bqT": ([128, 8], F32), "bo": ([1, D], F32),
    "pen9": ([128, 128], F32), "alibi": ([16, T], F32),
}


def l1_ffn_weights(inp, pre=""):
    return {
        pre + "ln_gamma": inp["L1_ffn_ln_gamma"].reshape(1, D),
        pre + "ln_beta": inp["L1_ffn_ln_beta"].reshape(1, D),
        pre + "fc1_q": inp["L1_ffn_fc1_q"],
        pre + "fc1_scales": inp["L1_ffn_fc1_scales"],
        pre + "fc1_biasT": np.ascontiguousarray(inp["L1_ffn_fc1_bias"].reshape(32, 128).T),
        pre + "fc2_q": inp["L1_ffn_fc2_q"],
        pre + "fc2_scales": inp["L1_ffn_fc2_scales"],
        pre + "fc2_bias": inp["L1_ffn_fc2_bias"].reshape(1, D),
    }


def l1_attn_weights(inp, pre=""):
    NH = 16
    slopes = np.power(2.0 ** (-8.0 / NH), np.arange(1, NH + 1, dtype=np.float32))
    tk = np.arange(T, dtype=np.float64)
    pen9 = np.where(np.arange(128)[None, :] <= np.arange(128)[:, None], 0.0, -1e9).astype(np.float32)
    return {
        pre + "Wq": inp["L1_attn_Wq"], pre + "Wk": inp["L1_attn_Wk"],
        pre + "Wv": inp["L1_attn_Wv"], pre + "Wo": inp["L1_attn_Wo"],
        pre + "bqT": np.ascontiguousarray(inp["L1_attn_bq"].reshape(8, 128).T),
        pre + "bo": inp["L1_attn_bo"].reshape(1, D),
        pre + "pen9": pen9,
        pre + "alibi": (slopes[:, None].astype(np.float64) * tk[None, :]).astype(np.float32),
    }


def base_inputs():
    import ml_dtypes
    return {"ident_in": np.eye(128, dtype=ml_dtypes.bfloat16),
            "identf_in": np.eye(128, dtype=np.float32), "cvec_in": make_cvec()}


def emit_l0_ffn(p, x_src, x_dst, weights, consts):
    nc = p.nc
    ident_bf = consts["ident_bf"]
    ident_f = consts["ident_f"]
    onesf = consts["onesf"]
    LOGK = float(np.log(np.float32(1152.0)))
    SQD = 32.0
    sch = Sched(p)
    ctx = ExitStack()
    sb = lambda n, s, d: ctx.enter_context(nc.sbuf_tensor(p.name(n), s, d))[:]
    dram = lambda n, s, d: nc.dram_tensor(p.name(n), s, d, kind="Internal").ap()

    z_acc = sb("z_acc", [128, NT * D], F32)
    scr = sb("scr", [128, D], F32)
    scr2 = sb("scr2", [128, D], F32)
    stat = sb("stat", [128, 8], F32)
    s_sb = sb("s_sb", [128, 16], F32)
    maskcol = sb("maskcol", [128, 16], F32)
    maskbc = sb("maskbc", [128, T], F32)
    b2l_bc = sb("b2l_bc", [128, D], F32)
    b2h_bc = sb("b2h_bc", [128, D], F32)
    b1lT = sb("b1lT", [128, 4], F32)
    b1hT = sb("b1hT", [128, 32], F32)
    xnhT = sb("xnhT", [128, 8 * T], BF16)
    s32 = sb("s32", [32, 64], F32)
    b32 = sb("b32", [32, 64], F32)
    t32 = sb("t32", [32, 64], F32)
    u32 = sb("u32", [32, 64], F32)
    m32 = sb("m32", [32, 64], F32)
    pos32 = sb("pos32", [32, 64], F32)
    cum32 = sb("cum32", [32, 64], F32)
    zero32 = sb("zero32", [32, 64], F32)
    un32 = sb("un32", [32, 64], F32)
    sq32 = sb("sq32", [32, 32], F32)
    tq32 = sb("tq32", [32, 32], F32)
    srow = sb("srow", [1, 64], F32)
    orow = sb("orow", [1, 32], F32)
    lohi = sb("lohi", [1, 8], F32)
    pu1 = sb("pu1", [1, 1], U32)
    maskrow = sb("maskrow", [1, 512], F32)
    s16 = sb("s16", [16, 128], F32)

    d_s = dram("l0f_s", [16, 128], F32)
    d_m = dram("l0f_m", [32, 64], F32)

    s_io = p.get_sem("io")
    s_ld = p.sem("l0ld")
    s_xt = [p.sem("l0xa"), p.sem("l0xb")]
    s_st6 = [p.sem("l0sa"), p.sem("l0sb")]
    s_stats = p.get_sem("st")
    s_xn = p.get_sem("xn")
    s_tr = p.get_sem("tr")
    s_evac = p.get_sem("ev")
    s_sv = p.get_sem("sv")
    s_desc = p.get_sem("desc")
    s_dsc = p.get_sem("dsc")
    s_sdma = p.sem("l0sd")
    s_mdma = p.sem("l0md")
    s_wl = p.sem("l0wl")
    s_wh = [p.sem("l0wha"), p.sem("l0whb")]
    s_w2h = [p.sem("l0w2a"), p.sem("l0w2b")]
    s_mm1 = p.get_sem("mmy")
    s_gel = p.get_sem("silu")
    s_mm2 = p.get_sem("mmz")
    s_za = p.get_sem("za")
    s_msk = p.get_sem("msk")
    s_mbc = p.get_sem("mbc")
    s_ep = p.get_sem("ep")

    emit_bcast_load(sch, weights["b2_l"], b2l_bc, s_ld)
    emit_bcast_load(sch, weights["b2_h"], b2h_bc, s_ld)
    sch.emit("sync", lambda e: e.dma_start(out=b1lT, in_=weights["b1_lT"]), incs=[(s_ld, 16)])
    sch.emit("sync", lambda e: e.dma_start(out=b1hT, in_=weights["b1_hT"]), incs=[(s_ld, 16)])
    def vinit(e):
        e.memset(orow, 1.0)
        e.drain()
        e.memset(zero32, 0.0)
        e.drain()
        e.memset(sq32, 0.0)
        e.drain()
        e.memset(tq32, 0.0)
        e.drain()
        e.memset(srow, 0.0)
        e.drain()
        return e.memset(lohi, 0.0)
    sch.emit("vector", vinit, incs=[(s_desc, 1)])

    xnlT_cm = nc.sbuf_tensor(p.name("xnlT"), [128, 8 * T], BF16)
    xnlT = xnlT_cm.__enter__()[:]
    with nc.sbuf_tensor(p.name("xt"), [128, 2 * D], F32) as xt_t, \
         nc.sbuf_tensor(p.name("xinb"), [128, 2 * D], F32) as xin_t, \
         nc.sbuf_tensor(p.name("xnb"), [128, D], BF16) as xnb_t, \
         nc.sbuf_tensor(p.name("rbc3"), [128, 3 * D], F32) as rbc3_t, \
         nc.psum_tensor(p.name("pt0"), [128, 128], BF16) as pt0_t, \
         nc.psum_tensor(p.name("pt1"), [128, 128], BF16) as pt1_t:
        xt2 = xt_t[:]; xin2 = xin_t[:]; xnb = xnb_t[:]
        rbc3 = rbc3_t[:]
        r_bc = rbc3[:, 0:D]; gl_bc = rbc3[:, D:2 * D]; gh_bc = rbc3[:, 2 * D:3 * D]
        emit_bcast_load(sch, weights["routing_token"], r_bc, s_ld)
        emit_bcast_load(sch, weights["gamma_l"], gl_bc, s_ld)
        emit_bcast_load(sch, weights["gamma_h"], gh_bc, s_ld)
        ld_done = w(s_ld)
        sch.emit("scalar", lambda e: e.nop(), waits=[ld_done])
        sch.emit("vector", lambda e: e.nop(), waits=[ld_done])
        sch.emit("gpsimd", lambda e: e.nop(), waits=[ld_done])
        pt = (pt0_t[:], pt1_t[:])
        for i in range(NT):
            xti = xt2[:, (i % 2) * D:(i % 2 + 1) * D]
            xini = xin2[:, (i % 2) * D:(i % 2 + 1) * D]
            sl = s_xt[i % 2]
            ldw = [w(s_io)] if i == 0 else ([] if i < 2 else [(s_xn, s_xn.n - 2)])
            def ldx(e, xti=xti, i=i):
                return e.dma_start(out=xti, in_=x_src[i * 128:(i + 1) * 128, :])
            sch.emit("sync", ldx, waits=ldw, incs=[(sl, 16)])
            emit_ln(sch, xti, xini, stat, scr, s_stats, done=s_xn,
                    scalar_waits=[w(sl), w(s_xn), w(s_sv)])
            def sv(e, xini=xini, i=i):
                e.tensor_tensor(out=scr, in0=xini, in1=r_bc, op=ALU.mult)
                e.drain()
                return e.tensor_reduce(out=s_sb[:, i:i + 1], in_=scr, axis=AX.X, op=ALU.add)
            sch.emit("vector", sv, waits=[w(s_xn)], incs=[(s_sv, 1)])
            ssq = stat[:, 4:5]; rn = stat[:, 5:6]
            def rms_s(e, xini=xini, ssq=ssq):
                e.activation(out=scr2, in_=xini, func=AF.Square, accum_out=ssq)
                e.drain()
                return e.activation(out=ssq, in_=ssq, func=AF.Sqrt)
            sch.emit("scalar", rms_s, waits=[w(s_sv)], incs=[(s_dsc, 1)])
            def rms_v(e, ssq=ssq, rn=rn):
                e.tensor_scalar(out=rn, in0=ssq, scalar1=1e-12, scalar2=None, op0=ALU.max)
                e.drain()
                e.reciprocal(out=rn, in_=rn)
                e.drain()
                return e.tensor_scalar(out=rn, in0=rn, scalar1=SQD, scalar2=None, op0=ALU.mult)
            sch.emit("vector", rms_v, waits=[w(s_dsc)], incs=[(s_desc, 1)])
            for gbc, dstT in ((gl_bc, xnlT), (gh_bc, xnhT)):
                def mknorm(e, xini=xini, gbc=gbc, rn=rn):
                    e.tensor_scalar(out=scr2, in0=xini, scalar1=rn, scalar2=None, op0=ALU.mult)
                    e.drain()
                    return e.tensor_tensor(out=xnb, in0=scr2, in1=gbc, op=ALU.mult)
                sch.emit("vector", mknorm, waits=[w(s_desc), w(s_tr)], incs=[(s_xn, 1)])
                for k in range(8):
                    pbank = pt[k % 2]
                    evac_snapshot = s_evac.n
                    waits = [w(s_xn)] if k == 0 else []
                    if evac_snapshot >= 2:
                        waits.append((s_evac, evac_snapshot - 1))
                    def tr(e, k=k, pbank=pbank):
                        return e.transpose(out=pbank, in_=xnb[:, k * 128:(k + 1) * 128], identity=ident_bf)
                    sch.emit("tensor", tr, waits=waits, incs=[(s_tr, 1)])
                    dst = dstT[:, k * T + i * 128: k * T + (i + 1) * 128]
                    def ev(e, dst=dst, pbank=pbank):
                        return e.tensor_copy(out=dst, in_=pbank)
                    sch.emit("vector", ev, waits=[w(s_tr)], incs=[(s_evac, 1)])
        sch.flush("l0f_p1")
    xT_ready = w(s_evac)

    with nc.psum_tensor(p.name("ptsf"), [16, 128], F32) as ptsf_t:
        ptsf = ptsf_t[:]
        def str_(e):
            return e.transpose(out=ptsf, in_=s_sb, identity=ident_f)
        sch.emit("tensor", str_, waits=[w(s_sv)], incs=[(s_tr, 1)])
        def sev_(e):
            return e.tensor_copy(out=s16, in_=ptsf)
        sch.emit("vector", sev_, waits=[w(s_tr)], incs=[(s_desc, 1)])
        def sst(e):
            return e.dma_start(out=d_s, in_=s16)
        sch.emit("sync", sst, waits=[w(s_desc)], incs=[(s_sdma, 16)])
        def sld(e):
            return e.dma_start(out=s32, in_=d_s.rearrange("a b -> (a b)").rearrange("(P F) -> P F", P=32))
        sch.emit("sync", sld, waits=[w(s_sdma)], incs=[(s_sdma, 16)])
        sch.flush("l0f_p2")
    s32_ready = w(s_sdma)

    with nc.sbuf_tensor(p.name("wl"), [128, 8 * 512], BF16) as wl_t, \
         nc.sbuf_tensor(p.name("w2l"), [128, 4 * D], BF16) as w2l_t, \
         nc.sbuf_tensor(p.name("hlT"), [128, 4 * T], BF16) as hlT_t, \
         nc.psum_tensor(p.name("pl0"), [128, 512], F32) as pl0_t, \
         nc.psum_tensor(p.name("pl1"), [128, 512], F32) as pl1_t:
        wl = wl_t[:]; w2l = w2l_t[:]; hlT = hlT_t[:]
        pl = (pl0_t[:], pl1_t[:])
        for kc in range(8):
            def lw(e, kc=kc):
                return e.dma_start(out=wl[:, kc * 512:(kc + 1) * 512],
                                   in_=weights["w1_lT"][kc * 128:(kc + 1) * 128, :])
            sch.emit("gpsimd", lw, incs=[(s_wl, 16)])
        for j in range(4):
            def lw2(e, j=j):
                return e.dma_start(out=w2l[:, j * D:(j + 1) * D],
                                   in_=weights["w2_lT"][j * 128:(j + 1) * 128, :])
            sch.emit("gpsimd", lw2, incs=[(s_wl, 16)])
        wl_done = w(s_wl)
        for f4 in range(4):
            for mc in range(4):
                idx = f4 * 4 + mc
                pb = pl[idx % 2]
                waits = [wl_done, xT_ready] if idx == 0 else []
                if idx >= 2:
                    waits.append((s_gel, s_gel.n - 1))
                def mm(e, f4=f4, mc=mc, pb=pb):
                    inst = None
                    for kc in range(8):
                        inst = e.matmul(out=pb,
                                        lhsT=wl[:, kc * 512 + f4 * 128: kc * 512 + (f4 + 1) * 128],
                                        rhs=xnlT[:, kc * T + mc * 512: kc * T + (mc + 1) * 512],
                                        start=(kc == 0), stop=(kc == 7))
                    return inst
                sch.emit("tensor", mm, waits=waits, incs=[(s_mm1, 1)])
                dst = hlT[:, f4 * T + mc * 512: f4 * T + (mc + 1) * 512]
                def gel(e, dst=dst, pb=pb, f4=f4):
                    return e.activation(out=dst, in_=pb, func=AF.Gelu, bias=b1lT[:, f4:f4 + 1])
                sch.emit("scalar", gel, waits=[w(s_mm1)], incs=[(s_gel, 1)])
        gel_done = w(s_gel)
        for mt in range(16):
            for dc in range(2):
                idx = mt * 2 + dc
                pb = pl[idx % 2]
                waits = [gel_done] if idx == 0 else []
                if idx >= 2:
                    waits.append((s_za, s_za.n - 1))
                def mm2(e, mt=mt, dc=dc, pb=pb):
                    inst = None
                    for f4 in range(4):
                        inst = e.matmul(out=pb,
                                        lhsT=hlT[:, f4 * T + mt * 128: f4 * T + (mt + 1) * 128],
                                        rhs=w2l[:, f4 * D + dc * 512: f4 * D + (dc + 1) * 512],
                                        start=(f4 == 0), stop=(f4 == 3))
                    return inst
                sch.emit("tensor", mm2, waits=waits, incs=[(s_mm2, 1)])
                dst = z_acc[:, mt * D + dc * 512: mt * D + (dc + 1) * 512]
                def zseed(e, dst=dst, pb=pb, dc=dc):
                    return e.tensor_tensor(out=dst, in0=pb, in1=b2l_bc[:, dc * 512:(dc + 1) * 512], op=ALU.add)
                sch.emit("vector", zseed, waits=[w(s_mm2)], incs=[(s_za, 1)])
        sch.flush("l0f_p4")

    xnlT_cm.__exit__(None, None, None)

    def col_to_row(e, col_ap, row_dst):
        e.tensor_copy(out=sq32[:, 0:1], in_=col_ap)
        e.drain()
        e.transpose(out=tq32, in_=sq32)
        e.drain()
        e.tensor_copy(out=row_dst, in_=tq32[0:1, 0:32])
        return e.drain()

    def scalar_to_col(e, sc_ap, col_dst):
        e.tensor_scalar(out=tq32[0:1, 0:32], in0=orow[0:1, 0:32], scalar1=sc_ap, scalar2=None, op0=ALU.mult)
        e.drain()
        e.transpose(out=sq32, in_=tq32)
        e.drain()
        e.tensor_copy(out=col_dst, in_=sq32[:, 0:1])
        return e.drain()

    def d0(e):
        return e.tensor_scalar(out=b32, in0=s32, scalar1=-1.0, scalar2=None, op0=ALU.mult)
    sch.emit("vector", d0, waits=[s32_ready], incs=[(s_desc, 1)])
    psum_ = stat[0:32, 6:7]
    acol = stat[0:32, 7:8]
    for it in range(50):
        def d1(e):
            return e.tensor_add(out=t32, in0=s32, in1=b32)
        sch.emit("vector", d1, waits=[w(s_desc)], incs=[(s_desc, 1)])
        def d2(e):
            return e.activation(out=t32, in_=t32, func=AF.Exp, accum_out=psum_)
        sch.emit("scalar", d2, waits=[w(s_desc)] + ([w(s_dsc)] if it > 0 else []), incs=[(s_dsc, 1)])
        def d3(e):
            col_to_row(e, psum_, srow[0:1, 0:32])
            e.tensor_reduce(out=lohi[:, 7:8], in_=srow[0:1, 0:32], axis=AX.X, op=ALU.add)
            return e.drain()
        sch.emit("vector", d3, waits=[w(s_dsc)], incs=[(s_desc, 1)])
        def d4(e):
            return e.activation(out=lohi[:, 7:8], in_=lohi[:, 7:8], func=AF.Ln)
        sch.emit("scalar", d4, waits=[w(s_desc)], incs=[(s_dsc, 1)])
        def d5(e):
            e.tensor_scalar(out=lohi[:, 7:8], in0=lohi[:, 7:8], scalar1=-1.0, scalar2=LOGK,
                            op0=ALU.mult, op1=ALU.add)
            e.drain()
            scalar_to_col(e, lohi[:, 7:8], acol)
            e.tensor_scalar(out=b32, in0=s32, scalar1=acol, scalar2=None, op0=ALU.add)
            e.drain()
            return e.tensor_scalar(out=b32, in0=b32, scalar1=0.0, scalar2=-1.0, op0=ALU.max, op1=ALU.mult)
        sch.emit("vector", d5, waits=[w(s_dsc)], incs=[(s_desc, 1)])
    def du(e):
        return e.tensor_scalar(out=u32, in0=s32, scalar1=acol, scalar2=None, op0=ALU.add)
    sch.emit("vector", du, waits=[w(s_desc)], incs=[(s_desc, 1)])
    sch.flush("l0f_p3")

    def m0(e):
        e.tensor_scalar(out=pos32, in0=u32, scalar1=0.0, scalar2=None, op0=ALU.is_ge)
        e.drain()
        e.tensor_tensor_scan(out=cum32, data0=pos32, data1=zero32, initial=0.0,
                             op0=ALU.add, op1=ALU.add)
        e.drain()
        col_to_row(e, cum32[:, 63:64], srow[0:1, 0:32])
        e.tensor_tensor_scan(out=srow[0:1, 0:32], data0=srow[0:1, 0:32], data1=orow[0:1, 0:32],
                             op0=ALU.add, op1=ALU.mult, initial=0.0)
        e.drain()
        e.tensor_copy(out=lohi[:, 6:7], in_=srow[0:1, 31:32])
        e.drain()
        e.memset(srow[:, 32:33], 0.0)
        e.drain()
        e.tensor_copy(out=srow[:, 33:64], in_=srow[0:1, 0:31])
        e.drain()
        e.tensor_copy(out=tq32[0:1, 0:32], in_=srow[0:1, 32:64])
        e.drain()
        e.transpose(out=sq32, in_=tq32)
        e.drain()
        e.tensor_scalar(out=cum32, in0=cum32, scalar1=sq32[:, 0:1], scalar2=None, op0=ALU.add)
        e.drain()
        e.tensor_scalar(out=m32, in0=cum32, scalar1=1024.0, scalar2=None, op0=ALU.is_le)
        e.drain()
        e.tensor_tensor(out=m32, in0=m32, in1=pos32, op=ALU.mult)
        e.drain()
        e.tensor_scalar(out=lohi[:, 4:5], in0=lohi[:, 6:7], scalar1=-1.0, scalar2=1024.0,
                        op0=ALU.mult, op1=ALU.add)
        e.drain()
        e.tensor_scalar(out=lohi[:, 5:6], in0=lohi[:, 4:5], scalar1=0.0, scalar2=None, op0=ALU.is_gt)
        e.drain()
        e.tensor_scalar(out=t32, in0=pos32, scalar1=-1e38, scalar2=None, op0=ALU.mult)
        e.drain()
        e.tensor_tensor(out=un32, in0=u32, in1=t32, op=ALU.min)
        e.drain()
        e.tensor_tensor(out=t32, in0=u32, in1=pos32, op=ALU.mult)
        e.drain()
        e.tensor_sub(out=t32, in0=u32, in1=t32)
        e.drain()
        e.tensor_reduce(out=stat[0:32, 6:7], in_=t32, axis=AX.X, op=ALU.min)
        e.drain()
        col_to_row(e, stat[0:32, 6:7], srow[0:1, 0:32])
        e.tensor_reduce(out=lohi[:, 0:1], in_=srow[0:1, 0:32], axis=AX.X, op=ALU.min)
        e.drain()
        return e.memset(lohi[:, 1:2], 0.0)
    sch.emit("vector", m0, waits=[w(s_desc)], incs=[(s_desc, 1)])
    mcol = stat[0:32, 6:7]
    for r_ in range(45):
        def br(e):
            e.tensor_add(out=lohi[:, 2:3], in0=lohi[:, 0:1], in1=lohi[:, 1:2])
            e.drain()
            e.tensor_scalar(out=lohi[:, 2:3], in0=lohi[:, 2:3], scalar1=0.5, scalar2=None, op0=ALU.mult)
            e.drain()
            scalar_to_col(e, lohi[:, 2:3], mcol)
            e.tensor_scalar(out=t32, in0=un32, scalar1=mcol, scalar2=None, op0=ALU.is_gt)
            e.drain()
            e.tensor_reduce(out=stat[0:32, 7:8], in_=t32, axis=AX.X, op=ALU.add)
            e.drain()
            col_to_row(e, stat[0:32, 7:8], srow[0:1, 0:32])
            e.tensor_reduce(out=lohi[:, 3:4], in_=srow[0:1, 0:32], axis=AX.X, op=ALU.add)
            e.drain()
            e.tensor_tensor(out=pu1, in0=lohi[:, 3:4], in1=lohi[:, 4:5], op=ALU.is_ge)
            e.drain()
            e.select(out=lohi[:, 0:1], mask=pu1, on_true=lohi[:, 2:3], on_false=lohi[:, 0:1], add_drain=True)
            e.drain()
            e.select(out=lohi[:, 1:2], mask=pu1, on_true=lohi[:, 1:2], on_false=lohi[:, 2:3], add_drain=True)
            return e.drain()
        sch.emit("vector", br, waits=[w(s_desc)], incs=[(s_desc, 1)])
    def mfin(e):
        scalar_to_col(e, lohi[:, 0:1], mcol)
        e.tensor_scalar(out=t32, in0=un32, scalar1=mcol, scalar2=None, op0=ALU.is_gt)
        e.drain()
        scalar_to_col(e, lohi[:, 5:6], mcol)
        e.tensor_scalar(out=t32, in0=t32, scalar1=mcol, scalar2=None, op0=ALU.mult)
        e.drain()
        e.tensor_add(out=m32, in0=m32, in1=t32)
        return e.drain()
    sch.emit("vector", mfin, waits=[w(s_desc)], incs=[(s_desc, 1)])
    def mst(e):
        return e.dma_start(out=d_m, in_=m32)
    sch.emit("sync", mst, waits=[w(s_desc)], incs=[(s_mdma, 16)])
    def mld2(e):
        return e.dma_start(out=s16, in_=d_m.rearrange("a b -> (a b)").rearrange("(a b) -> a b", a=16))
    sch.emit("sync", mld2, waits=[w(s_mdma)], incs=[(s_mdma, 16)])
    with nc.psum_tensor(p.name("ptm"), [128, 16], F32) as ptm_t, \
         nc.psum_tensor(p.name("pmb"), [128, 512], F32) as pmb_t:
        ptm = ptm_t[:]; pmb = pmb_t[:]
        def mtr(e):
            return e.transpose(out=ptm, in_=s16, identity=ident_f[0:16, 0:16])
        sch.emit("tensor", mtr, waits=[w(s_mdma)], incs=[(s_tr, 1)])
        def mtc(e):
            return e.tensor_copy(out=maskcol, in_=ptm)
        sch.emit("vector", mtc, waits=[w(s_tr)], incs=[(s_mbc, 1)])
        dmrow = d_m.rearrange("a b -> (a b)").rearrange("(c t) -> c t", c=4)
        for c in range(4):
            def mldr(e, c=c):
                return e.dma_start(out=maskrow, in_=dmrow[c:c + 1, :])
            sch.emit("sync", mldr, waits=[w(s_mdma)] + ([(s_mbc, s_mbc.n)] if c > 0 else []),
                     incs=[(s_mdma, 16)])
            def mbm(e, c=c):
                return e.matmul(out=pmb, lhsT=onesf, rhs=maskrow,
                                start=True, stop=True)
            sch.emit("tensor", mbm, waits=[w(s_mdma)] + ([(s_mbc, s_mbc.n)] if c > 0 else []),
                     incs=[(s_tr, 1)])
            def mbe(e, c=c):
                return e.tensor_copy(out=maskbc[:, c * 512:(c + 1) * 512], in_=pmb)
            sch.emit("vector", mbe, waits=[w(s_tr)], incs=[(s_mbc, 1)])
        sch.flush("l0f_p3b")
    mask_ready = w(s_mbc)

    with nc.sbuf_tensor(p.name("whh"), [128, 2 * 8 * 512], BF16) as whh_t, \
         nc.sbuf_tensor(p.name("w2h"), [128, 2 * 4 * D], BF16) as w2h_t, \
         nc.sbuf_tensor(p.name("hhT"), [128, 4 * T], BF16) as hhT_t, \
         nc.psum_tensor(p.name("ph0"), [128, 512], F32) as ph0_t, \
         nc.psum_tensor(p.name("ph1"), [128, 512], F32) as ph1_t:
        whh2 = whh_t[:]; w2h2 = w2h_t[:]; hhT = hhT_t[:]
        ph = (ph0_t[:], ph1_t[:])
        mm1_hist = []
        mm2_hist = []
        for fc in range(8):
            slot = fc % 2
            whh = whh2[:, slot * 8 * 512:(slot + 1) * 8 * 512]
            w2h = w2h2[:, slot * 4 * D:(slot + 1) * 4 * D]
            lw_waits = [] if fc < 2 else [(s_mm1, mm1_hist[fc - 2])]
            for kc in range(8):
                def lwh(e, fc=fc, kc=kc, whh=whh):
                    return e.dma_start(out=whh[:, kc * 512:(kc + 1) * 512],
                                       in_=weights["w1_hT"][kc * 128:(kc + 1) * 128, fc * 512:(fc + 1) * 512])
                sch.emit("gpsimd", lwh, waits=lw_waits if kc == 0 else [], incs=[(s_wh[slot], 16)])
            lw2_waits = [] if fc < 2 else [(s_mm2, mm2_hist[fc - 2])]
            for j in range(4):
                def lw2h(e, fc=fc, j=j, w2h=w2h):
                    return e.dma_start(out=w2h[:, j * D:(j + 1) * D],
                                       in_=weights["w2_hT"][fc * 512 + j * 128: fc * 512 + (j + 1) * 128, :])
                sch.emit("gpsimd", lw2h, waits=lw2_waits if j == 0 else [], incs=[(s_w2h[slot], 16)])
            wh_done = w(s_wh[slot])
            w2h_done = w(s_w2h[slot])
            mm2_fence = w(s_mm2)
            for f4 in range(4):
                for mc in range(4):
                    idx = fc * 16 + f4 * 4 + mc
                    pb = ph[idx % 2]
                    waits = [wh_done] if (f4 == 0 and mc == 0) else []
                    if idx >= 2:
                        waits.append((s_gel, s_gel.n - 1))
                    def mmh(e, f4=f4, mc=mc, pb=pb, whh=whh):
                        inst = None
                        for kc in range(8):
                            inst = e.matmul(out=pb,
                                            lhsT=whh[:, kc * 512 + f4 * 128: kc * 512 + (f4 + 1) * 128],
                                            rhs=xnhT[:, kc * T + mc * 512: kc * T + (mc + 1) * 512],
                                            start=(kc == 0), stop=(kc == 7))
                        return inst
                    sch.emit("tensor", mmh, waits=waits, incs=[(s_mm1, 1)])
                    dst = hhT[:, f4 * T + mc * 512: f4 * T + (mc + 1) * 512]
                    bidx = fc * 4 + f4
                    gw = [w(s_mm1)]
                    if f4 == 0 and mc == 0 and fc > 0:
                        gw.append(mm2_fence)
                        gw.append(w(s_msk))
                    def gelh(e, dst=dst, pb=pb, bidx=bidx):
                        return e.activation(out=dst, in_=pb, func=AF.Gelu, bias=b1hT[:, bidx:bidx + 1])
                    sch.emit("scalar", gelh, waits=gw, incs=[(s_gel, 1)])
            mm1_hist.append(s_mm1.n)
            gel_done_h = w(s_gel)
            for f4 in range(4):
                def mskp(e, f4=f4):
                    return e.tensor_tensor(out=hhT[:, f4 * T:(f4 + 1) * T],
                                           in0=hhT[:, f4 * T:(f4 + 1) * T],
                                           in1=maskbc, op=ALU.mult)
                sch.emit("vector", mskp, waits=([gel_done_h, mask_ready] if f4 == 0 else []),
                         incs=[(s_msk, 1)])
            msk_done = w(s_msk)
            for mt in range(16):
                for dc in range(2):
                    idx = fc * 32 + mt * 2 + dc
                    pb = ph[idx % 2]
                    waits = [msk_done, w2h_done] if (mt == 0 and dc == 0) else []
                    if idx >= 2:
                        waits.append((s_za, s_za.n - 1))
                    def mmh2(e, mt=mt, dc=dc, pb=pb, w2h=w2h):
                        inst = None
                        for f4 in range(4):
                            inst = e.matmul(out=pb,
                                            lhsT=hhT[:, f4 * T + mt * 128: f4 * T + (mt + 1) * 128],
                                            rhs=w2h[:, f4 * D + dc * 512: f4 * D + (dc + 1) * 512],
                                            start=(f4 == 0), stop=(f4 == 3))
                        return inst
                    sch.emit("tensor", mmh2, waits=waits, incs=[(s_mm2, 1)])
                    dst = z_acc[:, mt * D + dc * 512: mt * D + (dc + 1) * 512]
                    def zadd(e, dst=dst, pb=pb):
                        return e.tensor_tensor(out=dst, in0=dst, in1=pb, op=ALU.add)
                    sch.emit("vector", zadd, waits=[w(s_mm2)], incs=[(s_za, 1)])
            mm2_hist.append(s_mm2.n)
            sch.flush(f"l0f_p5_{fc}")

    with nc.sbuf_tensor(p.name("xt6"), [128, 2 * D], F32) as xt6_t:
        xt6 = xt6_t[:]
        z_done = w(s_za)
        for i in range(NT):
            xti = xt6[:, (i % 2) * D:(i % 2 + 1) * D]
            sl = s_xt[i % 2]
            ldw = [w(s_za, 0)] if False else ([] if i < 2 else [(s_ep, s_ep.n - 2), w(s_st6[i % 2])])
            def ldx6(e, xti=xti, i=i):
                return e.dma_start(out=xti, in_=x_src[i * 128:(i + 1) * 128, :])
            sch.emit("sync", ldx6, waits=ldw, incs=[(sl, 16)])
            z_t = z_acc[:, i * D:(i + 1) * D]
            def brank(e, z_t=z_t, i=i):
                return e.scalar_tensor_tensor(out=z_t, in0=b2h_bc, scalar=maskcol[:, i:i + 1],
                                              in1=z_t, op0=ALU.mult, op1=ALU.add)
            bw_ = [z_done, mask_ready] if i == 0 else [w(s_ep)]
            sch.emit("vector", brank, waits=bw_, incs=[(s_ep, 1)])
            emit_ln(sch, z_t, scr2, stat, scr, s_stats, done=s_ep,
                    scalar_waits=[w(s_ep)])
            def xadd(e, xti=xti):
                return e.tensor_tensor(out=xti, in0=xti, in1=scr2, op=ALU.add)
            sch.emit("vector", xadd, waits=[w(sl), w(s_ep)], incs=[(s_ep, 1)])
            def stx(e, xti=xti, i=i):
                return e.dma_start(out=x_dst[i * 128:(i + 1) * 128, :], in_=xti)
            sch.emit("sync", stx, waits=[w(s_ep)], incs=[(s_st6[i % 2], 16)])
        sch.emit("sync", lambda e: e.nop(), waits=[w(s_st6[0]), w(s_st6[1])], incs=[(s_io, 16)])
        sch.flush("l0f_p6")
    ctx.close()
    return s_io


L0_FFN_DECL = {
    "routing_token": ([1, D], F32), "gamma_l": ([1, D], F32), "gamma_h": ([1, D], F32),
    "w1_lT": ([1024, 512], F32), "b1_lT": ([128, 4], F32),
    "w2_lT": ([512, 1024], F32), "b2_l": ([1, D], F32),
    "w1_hT": ([1024, 4096], F32), "b1_hT": ([128, 32], F32),
    "w2_hT": ([4096, 1024], F32), "b2_h": ([1, D], F32),
}


def l0_ffn_weights(inp, pre=""):
    return {
        pre + "routing_token": inp["L0_ffn_routing_token"].reshape(1, D),
        pre + "gamma_l": inp["L0_ffn_gamma_l"].reshape(1, D),
        pre + "gamma_h": inp["L0_ffn_gamma_h"].reshape(1, D),
        pre + "w1_lT": np.ascontiguousarray(inp["L0_ffn_w1_l"].T),
        pre + "b1_lT": np.ascontiguousarray(inp["L0_ffn_b1_l"].reshape(4, 128).T),
        pre + "w2_lT": np.ascontiguousarray(inp["L0_ffn_w2_l"].T),
        pre + "b2_l": inp["L0_ffn_b2_l"].reshape(1, D),
        pre + "w1_hT": np.ascontiguousarray(inp["L0_ffn_w1_h"].T),
        pre + "b1_hT": np.ascontiguousarray(inp["L0_ffn_b1_h"].reshape(32, 128).T),
        pre + "w2_hT": np.ascontiguousarray(inp["L0_ffn_w2_h"].T),
        pre + "b2_h": inp["L0_ffn_b2_h"].reshape(1, D),
    }


def emit_l0_attn(p, x_src, x_dst, weights, consts, use_collective=True, n_cores=8, mode="mono"):
    nc = p.nc
    ident_bf = consts["ident_bf"]
    ident_f = consts["ident_f"]
    NH, DK, CH = 16, 64, 64
    NCH = T // CH
    E = 8
    CAP = 5120.0
    sch = Sched(p)
    ctx = ExitStack()
    sb = lambda n, s, d: ctx.enter_context(nc.sbuf_tensor(p.name(n), s, d))[:]
    dram = lambda n, s, d: nc.dram_tensor(p.name(n), s, d, kind="Internal").ap()

    if mode == "a":
        d_xmid = weights["xmid_out"]
    elif mode == "b":
        d_xmid = weights["xmid_in"]
    else:
        d_xmid = dram("l0a_xmid", [T, D], F32)
    d_cnt = dram("l0a_cnt", [E, 1], F32)
    d_gath = dram("l0a_gath", [n_cores * E, 1], F32)

    scr = sb("scr", [128, D], F32)
    scr2 = sb("scr2", [128, D], F32)
    stat = sb("stat", [128, 8], F32)

    s_io = p.get_sem("io")
    s_ld = p.sem("a0ld")
    s_xt = [p.sem("a0xa"), p.sem("a0xb")]
    s_st6 = [p.sem("a0sa"), p.sem("a0sb")]
    s_stats = p.get_sem("st")
    s_xn = p.get_sem("xn")
    s_tr = p.get_sem("tr")
    s_evac = p.get_sem("ev")
    s_wld = p.sem("a0wld")
    s_w2ld = [p.sem("a0w2a"), p.sem("a0w2b")]
    s_proj = p.get_sem("mmy")
    s_pev = p.get_sem("silu")
    s_rope = p.get_sem("desc")
    s_phis = p.get_sem("dsc")
    s_mmS = p.get_sem("mmz")
    s_Sacc = p.get_sem("za")
    s_mmN = p.get_sem("w1")
    s_epi = p.get_sem("w2")
    s_aoT = p.get_sem("ab")
    s_gt = p.get_sem("msk")
    s_gtT = p.get_sem("mbc")
    s_coll = p.sem("a0coll")
    s_cldma = p.sem("a0cd")
    s_za = p.get_sem("za")
    s_mm1 = p.get_sem("zmm")
    s_mm2 = p.get_sem("mmzz")
    s_ep = p.get_sem("ep")

    if mode != "b":
        aoT_cm = nc.sbuf_tensor(p.name("aoT"), [128, 8 * T], BF16)
        aoT = aoT_cm.__enter__()[:]
    if mode != "b":
        NHD = NH * 72
        bs_ctx = ExitStack()
        _sb = lambda n, sh, d: bs_ctx.enter_context(nc.sbuf_tensor(p.name(n), sh, d))[:]
        _ps = lambda n, sh, d: bs_ctx.enter_context(nc.psum_tensor(p.name(n), sh, d))[:]
        if True:
            wall = _sb("wqkvg", [128, 4 * 8 * D], BF16)
            bqkg = _sb("bqkg", [128, 4 * D], F32)
            xt2 = _sb("xt", [128, 2 * D], F32)
            xin2 = _sb("xinb", [128, 2 * D], F32)
            xnb = _sb("xnb", [128, D], BF16)
            n1bc = _sb("n1bc", [128, D], F32)
            rcos = _sb("rcos", [128, NT * 32], F32)
            rsin = _sb("rsin", [128, NT * 32], F32)
            rsc = _sb("rsc", [128, 2 * 512], F32)
            rmsTt2 = _sb("rmsTt", [128, 2 * 8 * 128], BF16)
            qkvg2 = _sb("qkvg", [128, 2 * 4 * D], BF16)
            SaccS = _sb("SaccS", [128, 512], F32)
            SaccZ = _sb("SaccZ", [128, 8], F32)
            SbfS = _sb("SbfS", [128, 512], BF16)
            SbfZ = _sb("SbfZ", [128, 8], BF16)
            ones64 = _sb("ones64", [128, 1], BF16)
            qtb2 = _sb("qtb", [128, 2 * 512], BF16)
            gext = _sb("gext", [128, 512], F32)
            nscb = _sb("nscb", [128, 512], F32)
            recb = _sb("recb", [128, 8], F32)
            aobb = _sb("aobb", [128, 512], BF16)
            pt = (_ps("pt0", [128, 128], BF16), _ps("pt1", [128, 128], BF16))
            pp = (_ps("pp0", [128, 512], F32), _ps("pp1", [128, 512], F32))
            pSb = _ps("pSb", [128, 512], F32)
            pZb = _ps("pZb", [128, 8], F32)
            pNb = _ps("pNb", [128, 512], F32)
            pNzb = _ps("pNzb", [128, 8], F32)

            emit_bcast_load(sch, weights["norm1_scale"], n1bc, s_ld)
            sch.emit("sync", lambda e: e.dma_start(
                out=rcos.rearrange("p (n f) -> p n f", f=32),
                in_=weights["ropecos"].rearrange("(n p) f -> p n f", p=128)), incs=[(s_ld, 16)])
            sch.emit("sync", lambda e: e.dma_start(
                out=rsin.rearrange("p (n f) -> p n f", f=32),
                in_=weights["ropesin"].rearrange("(n p) f -> p n f", p=128)), incs=[(s_ld, 16)])
            emit_bcast_load(sch, weights["bq"], bqkg[:, 0:D], s_ld)
            emit_bcast_load(sch, weights["bk"], bqkg[:, D:2 * D], s_ld)
            emit_bcast_load(sch, weights["bv"], bqkg[:, 2 * D:3 * D], s_ld)
            emit_bcast_load(sch, weights["bg"], bqkg[:, 3 * D:4 * D], s_ld)
            for pi, wname in enumerate(["Wq", "Wk", "Wv", "Wg"]):
                for kc in range(8):
                    def ldw_(e, wname=wname, pi=pi, kc=kc):
                        return e.dma_start(out=wall[:, (pi * 8 + kc) * D:(pi * 8 + kc + 1) * D],
                                           in_=weights[wname][kc * 128:(kc + 1) * 128, :])
                    sch.emit("gpsimd", ldw_, incs=[(s_wld, 16)])
            ld_done = w(s_ld)
            wld_done = w(s_wld)
            sch.emit("scalar", lambda e: e.nop(), waits=[ld_done])
            def vi(e):
                e.memset(ones64, 1.0)
                e.drain()
                e.memset(SaccS, 0.0)
                e.drain()
                return e.memset(SaccZ, 0.0)
            sch.emit("vector", vi, waits=[ld_done], incs=[(s_Sacc, 1)])

            pev_hist = []
            rsc_hist = []
            nw_hist = []
            sadd_hist = []
            epi_hist = []
            qev_hist = []
            for tt in range(NT):
                tp = tt % 2
                xti = xt2[:, tp * D:(tp + 1) * D]
                xini = xin2[:, tp * D:(tp + 1) * D]
                sl = s_xt[tp]
                ldw = [w(s_io)] if tt == 0 else ([] if tt < 2 else [(s_xn, s_xn.n - 2)])
                def ldx(e, xti=xti, tt=tt):
                    return e.dma_start(out=xti, in_=x_src[tt * 128:(tt + 1) * 128, :])
                sch.emit("sync", ldx, waits=ldw, incs=[(sl, 16)])
                emit_ln(sch, xti, xini, stat, scr2, s_stats, done=s_xn,
                        scalar_waits=[w(sl), w(s_xn)])
                ssq = stat[:, 4:5]; rr = stat[:, 5:6]
                def rms_s(e, xini=xini, ssq=ssq):
                    e.activation(out=scr2, in_=xini, func=AF.Square, accum_out=ssq)
                    e.drain()
                    return e.activation(out=ssq, in_=ssq, func=AF.Sqrt, scale=p.cv(3))
                sch.emit("scalar", rms_s, waits=[w(s_xn)], incs=[(s_phis, 1)])
                def rms_v(e, rr=rr, ssq=ssq):
                    e.reciprocal(out=rr, in_=ssq)
                    return e.drain()
                sch.emit("vector", rms_v, waits=[w(s_phis)], incs=[(s_rope, 1)])
                def mknorm(e, xini=xini, rr=rr):
                    e.tensor_scalar(out=scr2, in0=xini, scalar1=rr, scalar2=None, op0=ALU.mult)
                    e.drain()
                    return e.tensor_tensor(out=xnb, in0=scr2, in1=n1bc, op=ALU.mult)
                sch.emit("vector", mknorm, waits=[w(s_rope), w(s_tr)], incs=[(s_xn, 1)])
                rmsTt = rmsTt2[:, tp * 8 * 128:(tp + 1) * 8 * 128]
                for k in range(8):
                    pbank = pt[k % 2]
                    evac_snapshot = s_evac.n
                    waits = [w(s_xn)] if k == 0 else []
                    if k == 0 and tt > 0:
                        waits.append((s_aoT, s_aoT.n))
                    if evac_snapshot >= 2:
                        waits.append((s_evac, evac_snapshot - 1))
                    def tr(e, k=k, pbank=pbank):
                        return e.transpose(out=pbank, in_=xnb[:, k * 128:(k + 1) * 128], identity=ident_bf)
                    sch.emit("tensor", tr, waits=waits, incs=[(s_tr, 1)])
                    dst = rmsTt[:, k * 128:(k + 1) * 128]
                    def ev(e, dst=dst, pbank=pbank):
                        return e.tensor_copy(out=dst, in_=pbank)
                    sch.emit("vector", ev, waits=[w(s_tr)], incs=[(s_evac, 1)])
                rmsTt_ready = w(s_evac)
                qkvg = qkvg2[:, tp * 4 * D:(tp + 1) * 4 * D]
                for pi in range(4):
                    for hc in range(2):
                        idx = tt * 8 + pi * 2 + hc
                        pb = pp[idx % 2]
                        waits = [rmsTt_ready]
                        if pi == 0 and hc == 0 and tt == 0:
                            waits.append(wld_done)
                        if idx >= 2:
                            waits.append((s_pev, pev_hist[idx - 2]))
                        def mmv(e, pi=pi, hc=hc, pb=pb, rmsTt=rmsTt):
                            inst = None
                            for kc in range(8):
                                inst = e.matmul(out=pb,
                                                lhsT=rmsTt[:, kc * 128:(kc + 1) * 128],
                                                rhs=wall[:, (pi * 8 + kc) * D + hc * 512:(pi * 8 + kc) * D + (hc + 1) * 512],
                                                start=(kc == 0), stop=(kc == 7))
                            return inst
                        sch.emit("tensor", mmv, waits=waits, incs=[(s_proj, 1)])
                        dst = qkvg[:, pi * D + hc * 512: pi * D + (hc + 1) * 512]
                        bsl = bqkg[:, pi * D + hc * 512: pi * D + (hc + 1) * 512]
                        rbuf = rsc[:, (idx % 2) * 512:(idx % 2 + 1) * 512]
                        if pi == 3:
                            def evg1(e, pb=pb, bsl=bsl, rbuf=rbuf):
                                return e.tensor_tensor(out=rbuf, in0=pb, in1=bsl, op=ALU.add)
                            gvw = [w(s_proj)]
                            if len(rsc_hist) >= 2:
                                gvw.append((s_phis, rsc_hist[-2]))
                            sch.emit("vector", evg1, waits=gvw, incs=[(s_pev, 1)])
                            pev_hist.append(s_pev.n)
                            def evg2(e, dst=dst, rbuf=rbuf):
                                return e.activation(out=dst, in_=rbuf, func=AF.Sigmoid)
                            sch.emit("scalar", evg2, waits=[w(s_pev)], incs=[(s_phis, 1)])
                            rsc_hist.append(s_phis.n)
                        elif pi == 2:
                            def evv(e, dst=dst, pb=pb, bsl=bsl):
                                return e.tensor_tensor(out=dst, in0=pb, in1=bsl, op=ALU.add)
                            sch.emit("vector", evv, waits=[w(s_proj)], incs=[(s_pev, 1)])
                            pev_hist.append(s_pev.n)
                        else:
                            def evqk(e, pb=pb, bsl=bsl, rbuf=rbuf):
                                return e.tensor_tensor(out=rbuf, in0=pb, in1=bsl, op=ALU.add)
                            evw = [w(s_proj)]
                            if len(rsc_hist) >= 2:
                                evw.append((s_phis, rsc_hist[-2]))
                            sch.emit("vector", evqk, waits=evw, incs=[(s_pev, 1)])
                            pev_hist.append(s_pev.n)
                            cossl = rcos[:, tt * 32:(tt + 1) * 32]
                            sinsl = rsin[:, tt * 32:(tt + 1) * 32]
                            def rope(e, rbuf=rbuf, cossl=cossl, sinsl=sinsl):
                                v3 = lambda ap: ap.rearrange("p (h f) -> p h f", f=64)
                                lo = v3(rbuf)[:, :, 0:32]
                                hi = v3(rbuf)[:, :, 32:64]
                                cb = cossl.unsqueeze(1).to_broadcast([128, 8, 32])
                                sbx = sinsl.unsqueeze(1).to_broadcast([128, 8, 32])
                                s1 = v3(scr[:, 0:512])[:, :, 0:32]
                                s2 = v3(scr[:, 0:512])[:, :, 32:64]
                                e.tensor_tensor(out=s1, in0=lo, in1=cb, op=ALU.mult)
                                e.drain()
                                e.tensor_tensor(out=s2, in0=hi, in1=sbx, op=ALU.mult)
                                e.drain()
                                e.tensor_sub(out=s1, in0=s1, in1=s2)
                                e.drain()
                                e.tensor_tensor(out=s2, in0=lo, in1=sbx, op=ALU.mult)
                                e.drain()
                                e.tensor_tensor(out=lo, in0=hi, in1=cb, op=ALU.mult)
                                e.drain()
                                e.tensor_add(out=s2, in0=s2, in1=lo)
                                e.drain()
                                e.tensor_scalar(out=scr2[:, 0:512], in0=scr[:, 0:512], scalar1=0.0, scalar2=None, op0=ALU.max)
                                e.drain()
                                e.tensor_sub(out=rbuf, in0=scr[:, 0:512], in1=scr2[:, 0:512])
                                return e.drain()
                            sch.emit("vector", rope, waits=[w(s_pev), w(s_phis)], incs=[(s_rope, 1)])
                            def phiexp(e, rbuf=rbuf):
                                return e.activation(out=rbuf, in_=rbuf, func=AF.Exp)
                            sch.emit("scalar", phiexp, waits=[w(s_rope)], incs=[(s_phis, 1)])
                            rsc_hist.append(s_phis.n)
                            def phiadd(e, dst=dst, rbuf=rbuf):
                                return e.tensor_add(out=dst, in0=rbuf, in1=scr2[:, 0:512])
                            sch.emit("vector", phiadd, waits=[w(s_phis)], incs=[(s_rope, 1)])
                proj_tile_done = w(s_rope)
                pev_tile_done = w(s_pev)
                g_ready = w(s_phis)
                q_tm = qkvg[:, 0:D]
                k_tm = qkvg[:, D:2 * D]
                v_tm = qkvg[:, 2 * D:3 * D]
                g_tm = qkvg[:, 3 * D:4 * D]
                aoT3 = aoT.rearrange("p (s t) -> p s t", s=8)
                s_qq = p.get_sem("qq")
                for bi in range(8):
                    cc = bi // 4
                    u = bi % 4
                    pbase = cc * 64
                    bank = pt[bi % 2]
                    qw = []
                    if bi == 0:
                        qw = [proj_tile_done, (s_evac, s_evac.n)]
                        if tt > 0:
                            qw.append((s_mmN, nw_hist[-1]))
                    elif bi >= 2:
                        qw = [(s_qq, s_qq.n - 1)]
                    def qtrb(e, cc=cc, u=u, pbase=pbase, bank=bank):
                        e.transpose(out=bank[:, 0:64],
                                    in_=q_tm[pbase:pbase + 64, (4 * u) * 64:(4 * u) * 64 + 128],
                                    identity=ident_bf[pbase:pbase + 64, pbase:pbase + 64])
                        return e.transpose(out=bank[:, 64:128],
                                           in_=q_tm[pbase:pbase + 64, (4 * u + 2) * 64:(4 * u + 2) * 64 + 128],
                                           identity=ident_bf[pbase:pbase + 64, pbase:pbase + 64])
                    sch.emit("tensor", qtrb, waits=qw, incs=[(s_mmN, 1)])
                    def qevb(e, cc=cc, u=u, bank=bank):
                        return e.tensor_copy(out=qtb2[:, cc * 512 + (2 * u) * 64: cc * 512 + (2 * u) * 64 + 128],
                                             in_=bank)
                    sch.emit("vector", qevb, waits=[w(s_mmN)], incs=[(s_qq, 1)])
                qev_hist.append(s_qq.n)
                for cc in range(2):
                    c = tt * 2 + cc
                    pbase = cc * 64
                    qtb = qtb2[:, cc * 512:(cc + 1) * 512]
                    g3 = g_tm.rearrange("p (j two f) -> p two j f", two=2, f=64)
                    def gxc(e, pbase=pbase, g3=g3):
                        e.tensor_copy(out=gext[0:64, :].rearrange("p (one j f) -> p one j f", one=1, f=64),
                                      in_=g3[pbase:pbase + 64, 0:1, :, :])
                        e.drain()
                        return e.tensor_copy(out=gext[64:128, :].rearrange("p (one j f) -> p one j f", one=1, f=64),
                                             in_=g3[pbase:pbase + 64, 1:2, :, :])
                    sch.emit("vector", gxc, waits=([g_ready] if cc == 0 else []) + [(s_epi, s_epi.n)], incs=[(s_epi, 1)])
                    mw = []
                    if cc == 0 and tt == 0:
                        mw = [pev_tile_done, proj_tile_done, w(s_Sacc)]
                    elif cc == 0:
                        mw = [pev_tile_done, proj_tile_done, (s_Sacc, sadd_hist[-1])]
                    else:
                        mw = [(s_Sacc, sadd_hist[-1])]
                    def mmS(e, pbase=pbase):
                        inst = None
                        for pq in range(2):
                            for j in range(8):
                                h = 2 * j + pq
                                e.matmul(out=pSb[pq * 64:pq * 64 + 64, j * 64:(j + 1) * 64],
                                         lhsT=k_tm[pbase:pbase + 64, h * 64:(h + 1) * 64],
                                         rhs=v_tm[pbase:pbase + 64, h * 64:(h + 1) * 64],
                                         start=True, stop=True)
                                inst = e.matmul(out=pZb[pq * 64:pq * 64 + 64, j:j + 1],
                                                lhsT=k_tm[pbase:pbase + 64, h * 64:(h + 1) * 64],
                                                rhs=ones64[pbase:pbase + 64, :],
                                                start=True, stop=True)
                        return inst
                    sch.emit("tensor", mmS, waits=mw, incs=[(s_mmS, 1)])
                    def sadd(e):
                        e.tensor_tensor(out=SaccS, in0=SaccS, in1=pSb, op=ALU.add)
                        e.drain()
                        return e.tensor_tensor(out=SaccZ, in0=SaccZ, in1=pZb, op=ALU.add)
                    sch.emit("vector", sadd, waits=[w(s_mmS), (s_Sacc, s_Sacc.n)], incs=[(s_Sacc, 1)])
                    sadd_hist.append(s_Sacc.n)
                    scw = ([(s_mmN, nw_hist[-1])] if c > 0 else []) + [(s_Sacc, s_Sacc.n)]
                    def scast(e):
                        e.tensor_copy(out=SbfS, in_=SaccS)
                        e.drain()
                        return e.tensor_copy(out=SbfZ, in_=SaccZ)
                    sch.emit("vector", scast, waits=scw, incs=[(s_Sacc, 1)])
                    nw = [w(s_Sacc)]
                    if cc == 0:
                        nw.append((s_qq, qev_hist[-1]))
                    if c > 0:
                        nw.append((s_epi, epi_hist[-1]))
                    def mmN(e, qtb=qtb):
                        inst = None
                        for pq in range(2):
                            for j in range(8):
                                e.matmul(out=pNb[pq * 64:pq * 64 + 64, j * 64:(j + 1) * 64],
                                         lhsT=qtb[pq * 64:pq * 64 + 64, j * 64:(j + 1) * 64],
                                         rhs=SbfS[pq * 64:pq * 64 + 64, j * 64:(j + 1) * 64],
                                         start=True, stop=True)
                                inst = e.matmul(out=pNzb[pq * 64:pq * 64 + 64, j:j + 1],
                                                lhsT=qtb[pq * 64:pq * 64 + 64, j * 64:(j + 1) * 64],
                                                rhs=SbfZ[pq * 64:pq * 64 + 64, j:j + 1],
                                                start=True, stop=True)
                        return inst
                    sch.emit("tensor", mmN, waits=nw, incs=[(s_mmN, 1)])
                    nw_hist.append(s_mmN.n)
                    def epi1(e):
                        e.tensor_scalar(out=recb, in0=pNzb, scalar1=1e-6, scalar2=None, op0=ALU.add)
                        e.drain()
                        return e.reciprocal(out=recb, in_=recb)
                    sch.emit("vector", epi1, waits=[w(s_mmN)], incs=[(s_epi, 1)])
                    ew2 = ([(s_aoT, s_aoT.n)] if c > 0 else []) + [(s_epi, s_epi.n)]
                    def epi2(e):
                        e.tensor_tensor(out=nscb.rearrange("p (j f) -> p j f", f=64),
                                        in0=pNb.rearrange("p (j f) -> p j f", f=64),
                                        in1=recb.unsqueeze(2).to_broadcast([128, 8, 64]),
                                        op=ALU.mult)
                        e.drain()
                        return e.tensor_tensor(out=aobb, in0=nscb, in1=gext, op=ALU.mult)
                    sch.emit("vector", epi2, waits=ew2, incs=[(s_epi, 1)])
                    epi_hist.append(s_epi.n)
                    for r in range(4):
                        ow = [w(s_epi)]
                        if r > 0 or c > 0:
                            ow.append((s_aoT, s_aoT.n))
                        def otr(e, r=r):
                            inst = None
                            for pq in range(2):
                                for jj in range(2):
                                    j = 2 * r + jj
                                    inst = e.transpose(out=pt[1][pq * 64:pq * 64 + 64, jj * 64:(jj + 1) * 64],
                                                       in_=aobb[pq * 64:pq * 64 + 64, j * 64:(j + 1) * 64],
                                                       identity=ident_bf[pq * 64:pq * 64 + 64, pq * 64:pq * 64 + 64])
                            return inst
                        sch.emit("tensor", otr, waits=ow, incs=[(s_aoT, 1)])
                        def otev(e, r=r, c=c):
                            e.tensor_copy(out=aoT3[0:64, 2 * r:2 * r + 2, c * 64:(c + 1) * 64],
                                          in_=pt[1][0:64, 0:128].rearrange("p (s t) -> p s t", s=2))
                            e.drain()
                            return e.tensor_copy(out=aoT3[64:128, 2 * r:2 * r + 2, c * 64:(c + 1) * 64],
                                                 in_=pt[1][64:128, 0:128].rearrange("p (s t) -> p s t", s=2))
                        sch.emit("vector", otev, waits=[w(s_aoT)], incs=[(s_aoT, 1)])
                sch.flush(f"l0a_bs{tt}")
            if tt == NT - 1:
                pass
        scan_done = w(s_aoT)

        bs_ctx.close()

        with nc.sbuf_tensor(p.name("wof"), [128, 8 * D], BF16) as wof_t, \
             nc.sbuf_tensor(p.name("bobc"), [128, D], F32) as bobc_t, \
             nc.sbuf_tensor(p.name("xt2"), [128, 2 * D], F32) as xt2_t, \
             nc.sbuf_tensor(p.name("xin3"), [128, D], F32) as xin3_t, \
             nc.psum_tensor(p.name("pz0"), [128, 512], F32) as pz0_t, \
             nc.psum_tensor(p.name("pz1"), [128, 512], F32) as pz1_t:
            wof = wof_t[:]; bo_bcast = bobc_t[:]; xt3 = xt2_t[:]; xin3 = xin3_t[:]
            pz = (pz0_t[:], pz1_t[:])
            for kc in range(8):
                def ldwo(e, kc=kc):
                    return e.dma_start(out=wof[:, kc * D:(kc + 1) * D],
                                       in_=weights["Wo"][kc * 128:(kc + 1) * 128, :])
                sch.emit("gpsimd", ldwo, incs=[(s_wld, 16)])
            s_bo = p.sem("a0bo")
            emit_bcast_load(sch, weights["bo"], bo_bcast, s_bo)
            wo_done = w(s_wld)
            bo_done = w(s_bo)
            for i in range(NT):
                xti = xt3[:, (i % 2) * D:(i % 2 + 1) * D]
                sl = s_xt[i % 2]
                ldw2 = [] if i < 2 else [(s_xn, s_xn.n - 2)]
                def ldx2(e, xti=xti, i=i):
                    return e.dma_start(out=xti, in_=x_src[i * 128:(i + 1) * 128, :])
                sch.emit("sync", ldx2, waits=ldw2, incs=[(sl, 16)])
                emit_ln(sch, xti, xin3, stat, scr, s_stats, done=s_xn,
                        scalar_waits=[w(sl), w(s_xn)], vector_waits=[w(s_ep)])
                for hc in range(2):
                    pb = pz[hc]
                    mmw = [wo_done, scan_done] if (i == 0 and hc == 0) else []
                    if i > 0:
                        mmw = mmw + [(s_ep, s_ep.n - (1 + hc))]
                    def zmm(e, i=i, hc=hc, pb=pb):
                        inst = None
                        for c8 in range(8):
                            inst = e.matmul(out=pb, lhsT=aoT[:, c8 * T + i * 128: c8 * T + (i + 1) * 128],
                                            rhs=wof[:, c8 * D + hc * 512: c8 * D + (hc + 1) * 512],
                                            start=(c8 == 0), stop=(c8 == 7))
                        return inst
                    sch.emit("tensor", zmm, waits=mmw, incs=[(s_gt, 1)])
                def zev(e, i=i):
                    e.tensor_tensor(out=scr2[:, 0:512], in0=pz[0], in1=bo_bcast[:, 0:512], op=ALU.add)
                    e.drain()
                    e.tensor_tensor(out=scr2[:, 512:1024], in0=pz[1], in1=bo_bcast[:, 512:1024], op=ALU.add)
                    e.drain()
                    return e.tensor_add(out=scr2, in0=scr2, in1=xin3)
                zvw = [w(s_gt), w(s_xn)]
                if i == 0:
                    zvw.append(bo_done)
                sch.emit("vector", zev, waits=zvw, incs=[(s_ep, 2)])
                def stm(e, i=i):
                    return e.dma_start(out=d_xmid[i * 128:(i + 1) * 128, :], in_=scr2)
                sch.emit("sync", stm, waits=[w(s_ep), (s_st6[i % 2], s_st6[i % 2].n)], incs=[(s_st6[i % 2], 16)])
                sch.emit("vector", lambda e: e.nop(), waits=[w(s_st6[i % 2])], incs=[(s_ep, 1)])
            sch.flush("l0a_phC")
        aoT_cm.__exit__(None, None, None)


    s_rw = p.sem("a0rw")
    s_xm = [p.sem("a0ma"), p.sem("a0mb")]
    s_whe = [p.sem("a0wea"), p.sem("a0web")]
    sl_i32 = sb("sl_i32", [128, 2 * NT], I32)
    g12p = sb("g12p", [128, 2 * NT], F32)
    d_xslots = dram("l0a_xsl", [8 * 1024 + 128, D], BF16)
    d_zslots = dram("l0a_zsl", [8 * 1024 + 128, D], F32)
    gts = sb("gts", [128, NT * 8], F32)
    zrow8 = sb("zrow8", [8, T], F32)
    gb6a_e = sb("gb6a", [128, D], F32)
    gb6b_e = sb("gb6b", [128, D], F32)
    six0 = sb("six0", [128, 1], I32)
    six1 = sb("six1", [128, 1], I32)
    gix0 = sb("gix0", [128, 1], I32)
    gix1 = sb("gix1", [128, 1], I32)
    offrow = sb("offrow", [1, 8], F32)
    offbc = sb("offbc", [128, 8], F32)
    meT = sb("meT", [8, T], F32)
    exT = sb("exT", [8, T], F32)
    eq1s = sb("eq1s", [128, NT * 8], F32)
    eq2s = sb("eq2s", [128, NT * 8], F32)
    g12 = sb("g12", [128, NT * 2], F32)
    ioe_bc = sb("ioe_bc", [128, 8], F32)
    rb_bc = sb("rb_bc", [128, 8], F32)
    b2all = sb("b2all", [8, D], F32)
    cmask = sb("cmask", [8, 1], F32)
    gth = sb("gth", [8, 8], F32)
    offc = sb("offc", [8, 1], F32)
    lgs = sb("lgs", [128, 8], F32)
    prb = sb("prb", [128, 8], F32)
    pr2 = sb("pr2", [128, 8], F32)
    tp8 = sb("tp8", [128, 8], F32)
    iu1 = sb("iu1", [128, 8], U32)
    iu2 = sb("iu2", [128, 8], U32)
    mr8 = sb("mr8", [128, 8], F32)
    xmn_tm_cm = nc.sbuf_tensor(p.name("xmn_tm"), [128, NT * D], BF16)
    xmn_tm = xmn_tm_cm.__enter__()[:]
    xmnT_cm = nc.sbuf_tensor(p.name("xmnT"), [128, 8 * T], BF16)
    xmnT = xmnT_cm.__enter__()[:]

    s_zf = p.sem("a0zf")
    s_sc8 = p.sem("a0sc8")
    s_zm = p.sem("a0zm")
    zt_cm = nc.sbuf_tensor(p.name("zbf"), [128, D], BF16)
    zbf = zt_cm.__enter__()[:]
    def zfill0(e):
        e.memset(zbf, 0.0)
        e.drain()
        return e.memset(scr2, 0.0)
    sch.emit("vector", zfill0, incs=[(s_zm, 1)])
    for r_ in range(65):
        def zfill(e, r_=r_):
            return e.dma_start(out=d_xslots[r_ * 128:(r_ + 1) * 128, :], in_=zbf)
        sch.emit("sync", zfill, waits=[w(s_zm)] if r_ == 0 else [], incs=[(s_zf, 16)])
    def zfill2(e):
        return e.dma_start(out=d_zslots[8192:8320, :], in_=scr2)
    sch.emit("sync", zfill2, waits=[w(s_zm)], incs=[(s_zf, 16)])
    emit_bcast_load(sch, weights["ioe"], ioe_bc, s_rw)
    emit_bcast_load(sch, weights["router_b"], rb_bc, s_rw)
    sch.emit("sync", lambda e: e.dma_start(out=b2all, in_=weights["b2"]), incs=[(s_rw, 16)])
    sch.emit("sync", lambda e: e.dma_start(out=cmask, in_=weights["coremask"]), incs=[(s_rw, 16)])

    with nc.sbuf_tensor(p.name("xm"), [128, 2 * D], F32) as xm_t, \
         nc.sbuf_tensor(p.name("xnb2"), [128, D], BF16) as xnb2_t, \
         nc.sbuf_tensor(p.name("n2bc"), [128, D], F32) as n2bc_t, \
         nc.sbuf_tensor(p.name("rwbf"), [128, 8 * 8], BF16) as rwbf_t, \
         nc.psum_tensor(p.name("pt0b"), [128, 128], BF16) as pt0_t, \
         nc.psum_tensor(p.name("pt1b"), [128, 128], BF16) as pt1_t, \
         nc.psum_tensor(p.name("plg"), [128, 8], F32) as plg_t, \
         nc.psum_tensor(p.name("pme"), [8, 128], F32) as pme_t:
        xm2 = xm_t[:]; xnb2 = xnb2_t[:]; n2bc = n2bc_t[:]; rwbf = rwbf_t[:]
        pt = (pt0_t[:], pt1_t[:])
        plg = plg_t[:]; pme = pme_t[:]
        emit_bcast_load(sch, weights["norm2_scale"], n2bc, s_rw)
        s_rwg = p.sem("a0rwg")
        sch.emit("gpsimd", lambda e: e.dma_start(
            out=rwbf.rearrange("p (kc e) -> p kc e", e=8),
            in_=weights["router_W"].rearrange("(kc p) e -> p kc e", p=128)), incs=[(s_rwg, 16)])
        rw_done = w(s_rw)
        rwg_done = w(s_rwg)
        sch.emit("scalar", lambda e: e.nop(), waits=[rw_done])
        sch.emit("vector", lambda e: e.nop(), waits=[rw_done])
        for i in range(NT):
            xmi = xm2[:, (i % 2) * D:(i % 2 + 1) * D]
            sl = s_xm[i % 2]
            ldw = [] if i < 2 else [(s_xn, s_xn.n - 1)]
            def ldxm(e, xmi=xmi, i=i):
                return e.dma_start(out=xmi, in_=d_xmid[i * 128:(i + 1) * 128, :])
            sch.emit("sync", ldxm, waits=ldw, incs=[(sl, 16)])
            ssq = stat[:, 4:5]; rr = stat[:, 5:6]
            def rms_s(e, xmi=xmi, ssq=ssq):
                e.activation(out=scr2, in_=xmi, func=AF.Square, accum_out=ssq)
                e.drain()
                return e.activation(out=ssq, in_=ssq, func=AF.Sqrt, scale=p.cv(3))
            sch.emit("scalar", rms_s, waits=[w(sl), w(s_xn), w(s_gt)] + ([w(s_zf)] if i == 0 else []),
                     incs=[(s_phis, 1)])
            def rms_v(e, rr=rr, ssq=ssq):
                e.reciprocal(out=rr, in_=ssq)
                return e.drain()
            sch.emit("vector", rms_v, waits=[w(s_phis)], incs=[(s_rope, 1)])
            def mknorm(e, xmi=xmi, rr=rr, i=i):
                e.tensor_scalar(out=scr2, in0=xmi, scalar1=rr, scalar2=None, op0=ALU.mult)
                e.drain()
                e.tensor_tensor(out=xnb2, in0=scr2, in1=n2bc, op=ALU.mult)
                e.drain()
                return e.tensor_copy(out=xmn_tm[:, i * D:(i + 1) * D], in_=xnb2)
            sch.emit("vector", mknorm, waits=[w(s_rope), w(s_tr)], incs=[(s_xn, 1)])
            for k in range(8):
                pbank = pt[k % 2]
                evac_snapshot = s_evac.n
                waits = [w(s_xn)] if k == 0 else []
                if evac_snapshot >= 2:
                    waits.append((s_evac, evac_snapshot - 1))
                def tr(e, k=k, pbank=pbank):
                    return e.transpose(out=pbank, in_=xnb2[:, k * 128:(k + 1) * 128], identity=ident_bf)
                sch.emit("tensor", tr, waits=waits, incs=[(s_tr, 1)])
                dst = xmnT[:, k * T + i * 128: k * T + (i + 1) * 128]
                def ev(e, dst=dst, pbank=pbank):
                    return e.tensor_copy(out=dst, in_=pbank)
                sch.emit("vector", ev, waits=[w(s_tr)], incs=[(s_evac, 1)])
            def lmm(e, i=i):
                inst = None
                for kc in range(8):
                    inst = e.matmul(out=plg, lhsT=xmnT[:, kc * T + i * 128: kc * T + (i + 1) * 128],
                                    rhs=rwbf[:, kc * 8:(kc + 1) * 8],
                                    start=(kc == 0), stop=(kc == 7))
                return inst
            sch.emit("tensor", lmm, waits=[w(s_evac)] + ([(s_gt, s_gt.n)] if i > 0 else [rw_done, rwg_done]),
                     incs=[(s_mmS, 1)])
            def sm1(e):
                e.tensor_tensor(out=lgs, in0=plg, in1=rb_bc, op=ALU.add)
                e.drain()
                e.tensor_reduce(out=stat[:, 6:7], in_=lgs, axis=AX.X, op=ALU.max, negate=True)
                return e.drain()
            sch.emit("vector", sm1, waits=[w(s_mmS), w(s_phis)], incs=[(s_gt, 1)])
            def sm2(e):
                return e.activation(out=prb, in_=lgs, func=AF.Exp, bias=stat[:, 6:7], accum_out=stat[:, 7:8])
            sch.emit("scalar", sm2, waits=[w(s_gt)], incs=[(s_phis, 1)])
            def top2(e, i=i):
                e.reciprocal(out=stat[:, 7:8], in_=stat[:, 7:8])
                e.drain()
                e.tensor_scalar(out=prb, in0=prb, scalar1=stat[:, 7:8], scalar2=None, op0=ALU.mult)
                e.drain()
                e.memset(mr8, -1e30)
                e.drain()
                e.tensor_reduce(out=tp8[:, 0:1], in_=prb, axis=AX.X, op=ALU.max)
                e.drain()
                e.tensor_copy(out=mr8[:, 0:1], in_=tp8[:, 0:1])
                e.drain()
                e.match_replace(out=pr2, in_to_replace=mr8, in_values=prb, imm_value=-1e30)
                e.drain()
                e.tensor_reduce(out=tp8[:, 1:2], in_=pr2, axis=AX.X, op=ALU.max)
                e.drain()
                e.max_index(out=iu1, in_max=mr8, in_values=prb)
                e.drain()
                e.tensor_copy(out=mr8[:, 0:1], in_=tp8[:, 1:2])
                e.drain()
                e.max_index(out=iu2, in_max=mr8, in_values=pr2)
                e.drain()
                e.tensor_add(out=tp8[:, 2:3], in0=tp8[:, 0:1], in1=tp8[:, 1:2])
                e.drain()
                e.reciprocal(out=tp8[:, 2:3], in_=tp8[:, 2:3])
                e.drain()
                e.tensor_tensor(out=g12[:, i * 2:i * 2 + 1], in0=tp8[:, 0:1], in1=tp8[:, 2:3], op=ALU.mult)
                e.drain()
                e.tensor_tensor(out=g12[:, i * 2 + 1:i * 2 + 2], in0=tp8[:, 1:2], in1=tp8[:, 2:3], op=ALU.mult)
                e.drain()
                e.tensor_copy(out=tp8[:, 3:4], in_=iu1[:, 0:1])
                e.drain()
                e.tensor_copy(out=tp8[:, 4:5], in_=iu2[:, 0:1])
                e.drain()
                e.tensor_scalar(out=eq1s[:, i * 8:(i + 1) * 8], in0=ioe_bc, scalar1=tp8[:, 3:4], scalar2=None, op0=ALU.is_equal)
                e.drain()
                e.tensor_scalar(out=eq2s[:, i * 8:(i + 1) * 8], in0=ioe_bc, scalar1=tp8[:, 4:5], scalar2=None, op0=ALU.is_equal)
                e.drain()
                return e.tensor_add(out=lgs, in0=eq1s[:, i * 8:(i + 1) * 8], in1=eq2s[:, i * 8:(i + 1) * 8])
            sch.emit("vector", top2, waits=[w(s_phis)], incs=[(s_gt, 1)])
            def metr(e):
                return e.transpose(out=pme, in_=lgs, identity=ident_f)
            sch.emit("tensor", metr, waits=[w(s_gt)] + ([(s_gtT, s_gtT.n)] if i > 0 else []), incs=[(s_mmS, 1)])
            def meev(e, i=i):
                return e.tensor_copy(out=meT[:, i * 128:(i + 1) * 128], in_=pme)
            sch.emit("vector", meev, waits=[w(s_mmS)], incs=[(s_gtT, 1)])
        sch.flush("l0a_d1")

        def csum3(e):
            e.tensor_tensor_scan(out=exT, data0=meT, data1=zrow8, initial=0.0,
                                 op0=ALU.add, op1=ALU.add)
            e.drain()
            e.tensor_copy(out=gth[:, 0:1], in_=exT[:, T - 1:T])
            e.drain()
            e.tensor_sub(out=exT, in0=exT, in1=meT)
            return e.drain()

        def zr(e):
            e.memset(zrow8, 0.0)
            e.drain()
            return e.memset(gth, 0.0)
        sch.emit("vector", zr, incs=[(s_gtT, 1)])
        sch.emit("vector", csum3, waits=[w(s_gtT)], incs=[(s_gtT, 1)])
        def cst(e):
            return e.dma_start(out=(weights["cnt_out"] if mode == "a" else d_cnt), in_=gth[:, 0:1])
        sch.emit("sync", cst, waits=[w(s_gtT)], incs=[(s_cldma, 16)])
        if mode == "a":
            sch.emit("sync", lambda e: e.nop(), waits=[w(s_cldma)], incs=[(s_io, 16)])
            sch.flush("l0a_a_end")
            return s_io
        if mode == "b":
            def cldb(e):
                return e.dma_start(out=gth, in_=weights["gathin"])
            sch.emit("sync", cldb, waits=[w(s_cldma), w(s_gtT)], incs=[(s_cldma, 16)])
        elif use_collective:
            def coll(e):
                return e.collective_compute("AllGather", ALU.bypass,
                                            replica_groups=[list(range(n_cores))],
                                            ins=[d_cnt], outs=[d_gath])
            sch.emit("gpsimd", coll, waits=[w(s_cldma)], incs=[(s_coll, 16)])
            def cld(e):
                return e.dma_start(out=gth, in_=d_gath.rearrange("(r e) o -> r (e o)", r=n_cores))
            sch.emit("sync", cld, waits=[w(s_coll), w(s_gtT)], incs=[(s_cldma, 16)])
        else:
            def cld1(e):
                return e.dma_start(out=gth[0:1, :], in_=d_cnt.rearrange("(o e) u -> o (e u)", o=1))
            sch.emit("sync", cld1, waits=[w(s_cldma), w(s_gtT)], incs=[(s_cldma, 16)])
        with nc.psum_tensor(p.name("poff"), [8, 1], F32) as poff_t, \
             nc.psum_tensor(p.name("poff2"), [1, 8], F32) as poff2_t, \
             nc.psum_tensor(p.name("pofb"), [128, 8], F32) as pofb_t:
            poff = poff_t[:]
            poff2 = poff2_t[:]
            pofb = pofb_t[:]
            def offmm(e):
                return e.matmul(out=poff, lhsT=gth, rhs=cmask, start=True, stop=True)
            sch.emit("tensor", offmm, waits=[w(s_cldma), rw_done], incs=[(s_mmS, 1)])
            def offev(e):
                return e.tensor_copy(out=offc, in_=poff)
            sch.emit("vector", offev, waits=[w(s_mmS)], incs=[(s_gtT, 1)])
            def offtr(e):
                return e.transpose(out=pme[0:1, 0:8] if False else poff2, in_=offc, identity=ident_f[0:8, 0:8])
            sch.emit("tensor", offtr, waits=[w(s_gtT)], incs=[(s_mmS, 1)])
            def offr(e):
                return e.tensor_copy(out=offrow, in_=poff2)
            sch.emit("vector", offr, waits=[w(s_mmS)], incs=[(s_gtT, 1)])
            def offbm(e):
                return e.matmul(out=pofb, lhsT=consts["onesf"], rhs=offrow, start=True, stop=True)
            sch.emit("tensor", offbm, waits=[w(s_gtT)], incs=[(s_mmS, 1)])
            def offbe(e):
                return e.tensor_copy(out=offbc, in_=pofb)
            sch.emit("vector", offbe, waits=[w(s_mmS)], incs=[(s_gtT, 1)])
            sch.flush("l0a_d2")

        with nc.psum_tensor(p.name("pex"), [128, 8], F32) as pex_t:
            pex = pex_t[:]
            for i in range(NT):
                def extr(e, i=i):
                    return e.transpose(out=pex, in_=exT[:, i * 128:(i + 1) * 128], identity=ident_f[0:8, 0:8])
                sch.emit("tensor", extr, waits=[w(s_gtT)] + ([(s_gt, s_gt.n)] if i > 0 else []),
                         incs=[(s_mmS, 1)])
                def capg(e, i=i):
                    eq1 = eq1s[:, i * 8:(i + 1) * 8]
                    eq2 = eq2s[:, i * 8:(i + 1) * 8]
                    e.tensor_tensor(out=lgs, in0=eq1, in1=pex, op=ALU.mult)
                    e.drain()
                    e.tensor_reduce(out=tp8[:, 0:1], in_=lgs, axis=AX.X, op=ALU.add)
                    e.drain()
                    e.tensor_tensor(out=lgs, in0=eq2, in1=pex, op=ALU.mult)
                    e.drain()
                    e.tensor_reduce(out=tp8[:, 1:2], in_=lgs, axis=AX.X, op=ALU.add)
                    e.drain()
                    e.tensor_tensor(out=lgs, in0=eq1, in1=offbc, op=ALU.mult)
                    e.drain()
                    e.tensor_reduce(out=tp8[:, 6:7], in_=lgs, axis=AX.X, op=ALU.add)
                    e.drain()
                    e.tensor_tensor(out=lgs, in0=eq2, in1=offbc, op=ALU.mult)
                    e.drain()
                    e.tensor_reduce(out=tp8[:, 7:8], in_=lgs, axis=AX.X, op=ALU.add)
                    e.drain()
                    e.tensor_tensor(out=tp8[:, 6:7], in0=tp8[:, 6:7], in1=tp8[:, 0:1], op=ALU.add)
                    e.drain()
                    e.tensor_tensor(out=tp8[:, 7:8], in0=tp8[:, 7:8], in1=tp8[:, 1:2], op=ALU.add)
                    e.drain()
                    e.tensor_scalar(out=tp8[:, 2:3], in0=tp8[:, 6:7], scalar1=CAP, scalar2=None, op0=ALU.is_lt)
                    e.drain()
                    e.tensor_scalar(out=tp8[:, 3:4], in0=tp8[:, 7:8], scalar1=CAP, scalar2=None, op0=ALU.is_lt)
                    e.drain()
                    e.tensor_scalar(out=stat[:, 4:5], in0=tp8[:, 0:1], scalar1=1024.0, scalar2=None, op0=ALU.is_lt)
                    e.drain()
                    e.tensor_scalar(out=stat[:, 5:6], in0=tp8[:, 1:2], scalar1=1024.0, scalar2=None, op0=ALU.is_lt)
                    e.drain()
                    e.tensor_tensor(out=tp8[:, 2:3], in0=tp8[:, 2:3], in1=stat[:, 4:5], op=ALU.mult)
                    e.drain()
                    e.tensor_tensor(out=tp8[:, 3:4], in0=tp8[:, 3:4], in1=stat[:, 5:6], op=ALU.mult)
                    e.drain()
                    e.tensor_tensor(out=g12p[:, 2 * i:2 * i + 1], in0=g12[:, i * 2:i * 2 + 1], in1=tp8[:, 2:3], op=ALU.mult)
                    e.drain()
                    e.tensor_tensor(out=g12p[:, 2 * i + 1:2 * i + 2], in0=g12[:, i * 2 + 1:i * 2 + 2], in1=tp8[:, 3:4], op=ALU.mult)
                    e.drain()
                    e.tensor_tensor(out=lgs, in0=eq1, in1=ioe_bc, op=ALU.mult)
                    e.drain()
                    e.tensor_reduce(out=tp8[:, 4:5], in_=lgs, axis=AX.X, op=ALU.add)
                    e.drain()
                    e.tensor_tensor(out=lgs, in0=eq2, in1=ioe_bc, op=ALU.mult)
                    e.drain()
                    e.tensor_reduce(out=tp8[:, 5:6], in_=lgs, axis=AX.X, op=ALU.add)
                    e.drain()
                    e.tensor_scalar(out=tp8[:, 4:5], in0=tp8[:, 4:5], scalar1=1024.0, scalar2=tp8[:, 0:1], op0=ALU.mult, op1=ALU.add)
                    e.drain()
                    e.tensor_scalar(out=tp8[:, 5:6], in0=tp8[:, 5:6], scalar1=1024.0, scalar2=tp8[:, 1:2], op0=ALU.mult, op1=ALU.add)
                    e.drain()
                    e.tensor_scalar(out=stat[:, 4:5], in0=stat[:, 4:5], scalar1=-1.0, scalar2=1.0, op0=ALU.mult, op1=ALU.add)
                    e.drain()
                    e.tensor_scalar(out=stat[:, 5:6], in0=stat[:, 5:6], scalar1=-1.0, scalar2=1.0, op0=ALU.mult, op1=ALU.add)
                    e.drain()
                    e.scalar_tensor_tensor(out=tp8[:, 4:5], in0=stat[:, 4:5], scalar=1e9, in1=tp8[:, 4:5], op0=ALU.mult, op1=ALU.add)
                    e.drain()
                    e.scalar_tensor_tensor(out=tp8[:, 5:6], in0=stat[:, 5:6], scalar=1e9, in1=tp8[:, 5:6], op0=ALU.mult, op1=ALU.add)
                    e.drain()
                    e.tensor_scalar(out=tp8[:, 4:5], in0=tp8[:, 4:5], scalar1=8192.0, scalar2=None, op0=ALU.min)
                    e.drain()
                    e.tensor_scalar(out=tp8[:, 5:6], in0=tp8[:, 5:6], scalar1=8192.0, scalar2=None, op0=ALU.min)
                    e.drain()
                    e.tensor_copy(out=sl_i32[:, 2 * i:2 * i + 1], in_=tp8[:, 4:5])
                    e.drain()
                    e.tensor_copy(out=sl_i32[:, 2 * i + 1:2 * i + 2], in_=tp8[:, 5:6])
                    e.drain()
                    e.tensor_scalar(out=lgs, in0=eq1, scalar1=g12p[:, 2 * i:2 * i + 1], scalar2=None, op0=ALU.mult)
                    e.drain()
                    e.tensor_scalar(out=prb, in0=eq2, scalar1=g12p[:, 2 * i + 1:2 * i + 2], scalar2=None, op0=ALU.mult)
                    e.drain()
                    return e.tensor_add(out=gts[:, i * 8:(i + 1) * 8], in0=lgs, in1=prb)
                sch.emit("vector", capg, waits=[w(s_mmS)], incs=[(s_gt, 1)])
                for sc in range(2):
                    sixb = six0 if sc == 0 else six1
                    def scpy(e, i=i, sc=sc, sixb=sixb):
                        return e.tensor_copy(out=sixb, in_=sl_i32[:, 2 * i + sc:2 * i + sc + 1])
                    sch.emit("vector", scpy, waits=[w(s_gt), w(s_sc8)], incs=[(s_gt, 1)])
                    def scat(e, i=i, sc=sc, sixb=sixb):
                        return e.indirect_dma_start(
                            out=d_xslots, out_offset=IndirectOffsetOnAxis(sixb, 0),
                            in_=xmn_tm[:, i * D:(i + 1) * D], in_offset=None)
                    sch.emit("gpsimd", scat, waits=[w(s_gt), w(s_zf)], incs=[(s_sc8, 16)])
                def gtr(e, i=i):
                    return e.transpose(out=pme, in_=gts[:, i * 8:(i + 1) * 8], identity=ident_f)
                sch.emit("tensor", gtr, waits=[w(s_gt)] + [(s_gtT, s_gtT.n)], incs=[(s_mmS, 1)])
                def gev(e, i=i):
                    return e.tensor_copy(out=meT[:, i * 128:(i + 1) * 128], in_=pme)
                sch.emit("vector", gev, waits=[w(s_mmS)], incs=[(s_gtT, 1)])
            sch.flush("l0a_d3")
    zt_cm.__exit__(None, None, None)
    xmnT_cm.__exit__(None, None, None)
    xmn_tm_cm.__exit__(None, None, None)
    gates_done = w(s_gtT)

    with nc.sbuf_tensor(p.name("whe"), [128, 2 * 8 * 512], BF16) as whe_t, \
         nc.sbuf_tensor(p.name("w2s"), [128, 4 * 512], BF16) as w2s_t, \
         nc.sbuf_tensor(p.name("hmF"), [128, 32 * 1024], BF16) as hmF_t, \
         nc.sbuf_tensor(p.name("xesT"), [128, 2 * 8 * 1024], BF16) as xesT_t, \
         nc.sbuf_tensor(p.name("xsl"), [128, 2 * D], BF16) as xsl_t, \
         nc.sbuf_tensor(p.name("zev6"), [128, 8 * 512], F32) as zev6_t, \
         nc.sbuf_tensor(p.name("b1eT"), [128, 8 * 32], F32) as b1eT_t, \
         nc.psum_tensor(p.name("pe0"), [128, 512], F32) as pe0_t, \
         nc.psum_tensor(p.name("pe1"), [128, 512], F32) as pe1_t, \
         nc.psum_tensor(p.name("pz0"), [128, 512], F32) as pz0_t, \
         nc.psum_tensor(p.name("pz1"), [128, 512], F32) as pz1_t, \
         nc.psum_tensor(p.name("pz2"), [128, 512], F32) as pz2_t, \
         nc.psum_tensor(p.name("pz3"), [128, 512], F32) as pz3_t, \
         nc.psum_tensor(p.name("pts0"), [128, 128], BF16) as pts0_t, \
         nc.psum_tensor(p.name("pts1"), [128, 128], BF16) as pts1_t:
        whe2 = whe_t[:]; w2s = w2s_t[:]; hmF = hmF_t[:]
        xesT2 = xesT_t[:]; xsl2 = xsl_t[:]; zev6 = zev6_t[:]
        b1eT = b1eT_t[:]
        pe = (pe0_t[:], pe1_t[:])
        pz = (pz0_t[:], pz1_t[:], pz2_t[:], pz3_t[:])
        pts = (pts0_t[:], pts1_t[:])
        s_xld = [p.sem("a0xla"), p.sem("a0xlb")]
        s_xtr = p.get_sem("tr")
        s_xev = p.get_sem("ev")
        s_w2t = [p.sem("a0w2t0"), p.sem("a0w2t1"), p.sem("a0w2t2"), p.sem("a0w2t3")]
        s_ze = [p.get_sem("ze0"), p.get_sem("ze1"), p.get_sem("ze2"), p.get_sem("ze3")]
        s_zsl6 = [p.sem("a0z6a"), p.sem("a0z6b")]
        sch.emit("sync", lambda e: e.dma_start(out=b1eT, in_=weights["b1T"]), incs=[(s_rw, 16)])
        b1_done = w(s_rw)
        scat_done = w(s_sc8)
        mm1_hist = []
        mm1_hist_f = []
        mm2_e_hist = []
        g_tile = 0
        ps_idx = 0
        for ei in range(E):
            slote = ei % 2
            xesT = xesT2[:, slote * 8 * 1024:(slote + 1) * 8 * 1024]
            for mt in range(8):
                xsl = xsl2[:, (mt % 2) * D:(mt % 2 + 1) * D]
                xw = [scat_done] if (ei == 0 and mt == 0) else []
                if ei * 8 + mt >= 2:
                    xw.append((s_xtr, s_xtr.n - 8))
                def xld(e, ei=ei, mt=mt, xsl=xsl):
                    return e.dma_start(out=xsl, in_=d_xslots[ei * 1024 + mt * 128: ei * 1024 + (mt + 1) * 128, :])
                sch.emit("sync", xld, waits=xw, incs=[(s_xld[mt % 2], 16)])
                for k in range(8):
                    pbank = pts[k % 2]
                    tw = [w(s_xld[mt % 2])] if k == 0 else []
                    if s_xev.n >= 2:
                        tw.append((s_xev, s_xev.n - 1))
                    if ei >= 2 and mt == 0 and k == 0:
                        tw.append((s_mm1, mm1_hist[ei - 2]))
                    def xtr(e, k=k, pbank=pbank, xsl=xsl):
                        return e.transpose(out=pbank, in_=xsl[:, k * 128:(k + 1) * 128], identity=ident_bf)
                    sch.emit("tensor", xtr, waits=tw, incs=[(s_xtr, 1)])
                    dst = xesT[:, k * 1024 + mt * 128: k * 1024 + (mt + 1) * 128]
                    def xev(e, dst=dst, pbank=pbank):
                        return e.tensor_copy(out=dst, in_=pbank)
                    sch.emit("vector", xev, waits=[w(s_xtr)], incs=[(s_xev, 1)])
            xesT_ready = w(s_xev)
            for fc in range(8):
                ci = ei * 8 + fc
                slot = ci % 2
                whe = whe2[:, slot * 8 * 512:(slot + 1) * 8 * 512]
                lw_waits = [] if ci < 2 else [(s_mm1, mm1_hist_f[ci - 2])]
                for kc in range(8):
                    def lwh(e, ei=ei, fc=fc, kc=kc, whe=whe):
                        return e.dma_start(out=whe[:, kc * 512:(kc + 1) * 512],
                                           in_=weights["W1"][ei * 1024 + kc * 128: ei * 1024 + (kc + 1) * 128,
                                                             fc * 512:(fc + 1) * 512])
                    sch.emit("gpsimd", lwh, waits=lw_waits if kc == 0 else [], incs=[(s_whe[slot], 16)])
                wh_done = w(s_whe[slot])
                for f4 in range(4):
                    for mc in range(2):
                        idx = ci * 8 + f4 * 2 + mc
                        pb = pe[idx % 2]
                        waits = [wh_done] if (f4 == 0 and mc == 0) else []
                        if fc == 0 and f4 == 0 and mc == 0:
                            waits.append(xesT_ready)
                        if ci == 0 and f4 == 0 and mc == 0:
                            waits.append(b1_done)
                        if idx >= 2:
                            waits.append((s_pev, s_pev.n - 1))
                        def mmh(e, f4=f4, mc=mc, pb=pb, whe=whe, xesT=xesT):
                            inst = None
                            for kc in range(8):
                                inst = e.matmul(out=pb,
                                                lhsT=whe[:, kc * 512 + f4 * 128: kc * 512 + (f4 + 1) * 128],
                                                rhs=xesT[:, kc * 1024 + mc * 512: kc * 1024 + (mc + 1) * 512],
                                                start=(kc == 0), stop=(kc == 7))
                            return inst
                        sch.emit("tensor", mmh, waits=waits, incs=[(s_mm1, 1)])
                        t_h = fc * 4 + f4
                        dst = hmF[:, t_h * 1024 + mc * 512: t_h * 1024 + (mc + 1) * 512]
                        bidx = ei * 32 + fc * 4 + f4
                        gw = [w(s_mm1)]
                        if ei > 0 and fc == 0 and f4 == 0 and mc == 0:
                            gw.append((s_mm2, mm2_e_hist[ei - 1]))
                        def gelh(e, dst=dst, pb=pb, bidx=bidx):
                            return e.activation(out=dst, in_=pb, func=AF.Gelu, bias=b1eT[:, bidx:bidx + 1])
                        sch.emit("scalar", gelh, waits=gw, incs=[(s_pev, 1)])
                mm1_hist_f.append(s_mm1.n)
                if fc == 3 or fc == 7:
                    sch.flush(f"l0a_d5f1_{ei}_{fc}")
            mm1_hist.append(s_mm1.n)
            gel_done = w(s_pev)
            for dc in range(2):
                for mtg in range(2):
                    for t in range(32):
                        par = g_tile % 4
                        lw2 = []
                        if g_tile >= 4:
                            lw2.append((s_mm2, s_mm2.n - 12))
                        def lw2t(e, ei=ei, dc=dc, t=t, par=par):
                            return e.dma_start(out=w2s[:, par * 512:(par + 1) * 512],
                                               in_=weights["W2"][ei * 4096 + t * 128: ei * 4096 + (t + 1) * 128,
                                                                 dc * 512:(dc + 1) * 512])
                        sch.emit("gpsimd", lw2t, waits=lw2, incs=[(s_w2t[par], 16)])
                        for mt in range(4):
                            mw = [w(s_w2t[par])] if mt == 0 else []
                            if t == 0:
                                if dc == 0 and mtg == 0:
                                    mw.append(gel_done)
                                if ps_idx > 0:
                                    mw.append((s_ze[mt], s_ze[mt].n))
                            def mmz(e, mt=mt, mtg=mtg, t=t, par=par):
                                return e.matmul(out=pz[mt],
                                                lhsT=hmF[:, t * 1024 + (mtg * 4 + mt) * 128: t * 1024 + (mtg * 4 + mt + 1) * 128],
                                                rhs=w2s[:, par * 512:(par + 1) * 512],
                                                start=(t == 0), stop=(t == 31))
                            sch.emit("tensor", mmz, waits=mw, incs=[(s_mm2, 1)])
                        g_tile += 1
                    n_mm2 = s_mm2.n
                    for mt in range(4):
                        zt6 = zev6[:, ((ps_idx % 2) * 4 + mt) * 512: ((ps_idx % 2) * 4 + mt + 1) * 512]
                        ew = [(s_mm2, n_mm2 - 3 + mt)]
                        if ps_idx >= 2 and mt == 0:
                            ew.append(w(s_zsl6[ps_idx % 2]))
                        def zev_(e, zt6=zt6, mt=mt):
                            return e.tensor_copy(out=zt6, in_=pz[mt])
                        sch.emit("vector", zev_, waits=ew, incs=[(s_ze[mt], 1)])
                        def zst6(e, ei=ei, dc=dc, mtg=mtg, mt=mt, zt6=zt6):
                            return e.dma_start(
                                out=d_zslots[ei * 1024 + (mtg * 4 + mt) * 128: ei * 1024 + (mtg * 4 + mt + 1) * 128,
                                             dc * 512:(dc + 1) * 512],
                                in_=zt6)
                        sch.emit("sync", zst6, waits=[(s_ze[mt], s_ze[mt].n)], incs=[(s_zsl6[ps_idx % 2], 16)])
                    ps_idx += 1
                sch.flush(f"l0a_d5f2_{ei}_{dc}")
            mm2_e_hist.append(s_mm2.n)
        sch.flush("l0a_d5_end")
        zsl6_done0 = w(s_zsl6[0]); zsl6_done1 = w(s_zsl6[1])

    with nc.sbuf_tensor(p.name("xt7"), [128, 2 * D], F32) as xt7_t, \
         nc.sbuf_tensor(p.name("xm6"), [128, 2 * D], F32) as xm6_t, \
         nc.sbuf_tensor(p.name("zb6"), [128, 2 * D], F32) as zb6_t, \
         nc.psum_tensor(p.name("pb20"), [128, 512], F32) as pb20_t, \
         nc.psum_tensor(p.name("pb21"), [128, 512], F32) as pb21_t:
        xt7 = xt7_t[:]; xm6 = xm6_t[:]; zb6 = zb6_t[:]
        pb2 = (pb20_t[:], pb21_t[:])
        gb6 = (gb6a_e, gb6b_e)
        s_g6 = p.sem("a0g6")
        s_g6v = p.get_sem("msk")
        def g6init(e):
            e.memset(gb6[0], 0.0)
            e.drain()
            return e.memset(gb6[1], 0.0)
        sch.emit("vector", g6init, incs=[(s_g6v, 1)])
        for i in range(NT):
            xmi = xm6[:, (i % 2) * D:(i % 2 + 1) * D]
            z_t = zb6[:, (i % 2) * D:(i % 2 + 1) * D]
            xti = xt7[:, (i % 2) * D:(i % 2 + 1) * D]
            sl = s_xm[i % 2]
            ldw = [] if i < 2 else [(s_za, s_za.n - 2)]
            def ldxm(e, xmi=xmi, i=i):
                return e.dma_start(out=xmi, in_=d_xmid[i * 128:(i + 1) * 128, :])
            sch.emit("sync", ldxm, waits=ldw, incs=[(sl, 16)])
            for hc in range(2):
                pb = pb2[hc]
                mmw = [gates_done] if (i == 0 and hc == 0) else []
                if i > 0:
                    mmw.append((s_za, s_za.n - 1 + hc))
                def bmm(e, i=i, hc=hc, pb=pb):
                    return e.matmul(out=pb, lhsT=meT[:, i * 128:(i + 1) * 128],
                                    rhs=b2all[:, hc * 512:(hc + 1) * 512], start=True, stop=True)
                sch.emit("tensor", bmm, waits=mmw, incs=[(s_mmS, 1)])
            zw0 = [w(s_mmS), w(sl)]
            if i >= 2:
                zw0.append((s_ep, s_ep.n - 2))
            def zs0(e, i=i, xmi=xmi, z_t=z_t):
                return e.tensor_tensor(out=z_t[:, 0:512], in0=pb2[0], in1=xmi[:, 0:512], op=ALU.add)
            sch.emit("vector", zs0, waits=zw0, incs=[(s_za, 1)])
            def zs1(e, i=i, xmi=xmi, z_t=z_t):
                return e.tensor_tensor(out=z_t[:, 512:1024], in0=pb2[1], in1=xmi[:, 512:1024], op=ALU.add)
            sch.emit("vector", zs1, incs=[(s_za, 1)])
            def ldx7(e, xti=xti, i=i):
                return e.dma_start(out=xti, in_=x_src[i * 128:(i + 1) * 128, :])
            sch.emit("sync", ldx7, waits=[] if i < 2 else [(s_ep, s_ep.n - 2), w(s_st6[i % 2])],
                     incs=[(s_xt[i % 2], 16)])
            for sc in range(2):
                gb = gb6[sc]
                gw6 = [zsl6_done0, zsl6_done1, w(s_g6v)] if (i == 0 and sc == 0) else [(s_g6v, s_g6v.n)]
                gixb = gix0 if sc == 0 else gix1
                def g6c(e, i=i, sc=sc, gixb=gixb):
                    return e.tensor_copy(out=gixb, in_=sl_i32[:, 2 * i + sc:2 * i + sc + 1])
                sch.emit("vector", g6c, waits=gw6 + [w(s_g6)], incs=[(s_g6v, 1)])
                def g6(e, gb=gb, gixb=gixb):
                    return e.indirect_dma_start(
                        out=gb, out_offset=None,
                        in_=d_zslots, in_offset=IndirectOffsetOnAxis(gixb, 0))
                sch.emit("gpsimd", g6, waits=[w(s_g6v)], incs=[(s_g6, 16)])
                def g6a(e, i=i, sc=sc, gb=gb, z_t=z_t):
                    return e.scalar_tensor_tensor(out=z_t, in0=gb, scalar=g12p[:, 2 * i + sc:2 * i + sc + 1],
                                                  in1=z_t, op0=ALU.mult, op1=ALU.add)
                sch.emit("vector", g6a, waits=[w(s_g6), (s_za, s_za.n)], incs=[(s_g6v, 1)])
            emit_ln(sch, z_t, scr2, stat, scr, s_stats, done=s_ep,
                    scalar_waits=[w(s_g6v), w(s_ep), w(s_xn)] if i == 0 else [w(s_g6v), w(s_ep)],
                    vector_waits=[w(s_ep)] if i > 0 else [])
            def xadd(e, xti=xti):
                return e.tensor_tensor(out=xti, in0=xti, in1=scr2, op=ALU.add)
            sch.emit("vector", xadd, waits=[w(s_xt[i % 2]), w(s_ep)], incs=[(s_ep, 1)])
            def stx(e, xti=xti, i=i):
                return e.dma_start(out=x_dst[i * 128:(i + 1) * 128, :], in_=xti)
            sch.emit("sync", stx, waits=[w(s_ep)], incs=[(s_st6[i % 2], 16)])
            if i % 4 == 3:
                sch.flush(f"l0a_d6_{i}")
        sch.emit("sync", lambda e: e.nop(), waits=[w(s_st6[0]), w(s_st6[1])], incs=[(s_io, 16)])
        sch.flush("l0a_d6")
    ctx.close()
    return s_io

L0_ATTN_DECL = {
    "norm1_scale": ([1, D], F32), "norm2_scale": ([1, D], F32),
    "Wq": ([1024, 1024], F32), "Wk": ([1024, 1024], F32),
    "Wv": ([1024, 1024], F32), "Wg": ([1024, 1024], F32), "Wo": ([1024, 1024], F32),
    "bq": ([1, D], F32), "bk": ([1, D], F32), "bv": ([1, D], F32), "bg": ([1, D], F32),
    "bo": ([1, D], F32),
    "ropecos": ([T, 32], F32), "ropesin": ([T, 32], F32),
    "router_W": ([1024, 8], F32), "router_b": ([1, 8], F32),
    "ioe": ([1, 8], F32), "coremask": ([8, 1], F32),
    "W1": ([8 * 1024, 4096], F32), "b1T": ([128, 8 * 32], F32),
    "W2": ([8 * 4096, 1024], F32), "b2": ([8, 1024], F32),
}


def l0_attn_weights(inp, core_id, pre=""):
    half = 32
    inv = 1.0 / 10000.0 ** (np.arange(half, dtype=np.float32) / half)
    ang = np.arange(T, dtype=np.float32)[:, None] * inv[None, :]
    cmask = np.zeros((8, 1), np.float32)
    cmask[:core_id, 0] = 1.0
    return {
        pre + "norm1_scale": inp["L0_attn_norm1_scale"].reshape(1, D),
        pre + "norm2_scale": inp["L0_attn_norm2_scale"].reshape(1, D),
        pre + "Wq": inp["L0_attn_Wq"], pre + "Wk": inp["L0_attn_Wk"],
        pre + "Wv": inp["L0_attn_Wv"], pre + "Wg": inp["L0_attn_Wg"],
        pre + "Wo": inp["L0_attn_Wo"],
        pre + "bq": inp["L0_attn_bq"].reshape(1, D), pre + "bk": inp["L0_attn_bk"].reshape(1, D),
        pre + "bv": inp["L0_attn_bv"].reshape(1, D), pre + "bg": inp["L0_attn_bg"].reshape(1, D),
        pre + "bo": inp["L0_attn_bo"].reshape(1, D),
        pre + "ropecos": np.cos(ang), pre + "ropesin": np.sin(ang),
        pre + "router_W": inp["L0_attn_router_W"],
        pre + "router_b": inp["L0_attn_router_b"].reshape(1, 8),
        pre + "ioe": np.arange(8, dtype=np.float32).reshape(1, 8),
        pre + "coremask": cmask,
        pre + "W1": np.ascontiguousarray(inp["L0_attn_W1"].reshape(8 * 1024, 4096)),
        pre + "b1T": np.ascontiguousarray(inp["L0_attn_b1"].reshape(8, 32, 128).transpose(2, 0, 1).reshape(128, 256)),
        pre + "W2": np.ascontiguousarray(inp["L0_attn_W2"].reshape(8 * 4096, 1024)),
        pre + "b2": inp["L0_attn_b2"],
    }


def emit_l0_attn_b(p, x_src, x_dst, weights, consts):
    return emit_l0_attn(p, x_src, x_dst, weights, consts, mode="b")


L0_ATTN_B_DECL = dict(L0_ATTN_DECL)
L0_ATTN_B_DECL["xmid_in"] = ([T, D], F32)
L0_ATTN_B_DECL["gathin"] = ([8, 8], F32)
_PROGS = {}


def _build_l0a_program():
    p = Prog()
    nc = p.nc
    x_in = nc.dram_tensor("x_in", [T, D], F32, kind="ExternalInput").ap()
    ident_in = nc.dram_tensor("ident_in", [128, 128], BF16, kind="ExternalInput").ap()
    identf_in = nc.dram_tensor("identf_in", [128, 128], F32, kind="ExternalInput").ap()
    cvec_in = nc.dram_tensor("cvec_in", [128, 8], F32, kind="ExternalInput").ap()
    wmap = {}
    for name, (shape, dt) in L0_ATTN_DECL.items():
        wmap[name] = nc.dram_tensor("b0_" + name, list(shape), dt, kind="ExternalInput").ap()
    wmap["xmid_out"] = nc.dram_tensor("xmid_out", [T, D], F32, kind="ExternalOutput").ap()
    wmap["cnt_out"] = nc.dram_tensor("cnt_out", [8, 1], F32, kind="ExternalOutput").ap()
    ident_bf = p.sb("ident_bf", [128, 128], BF16)
    ident_f = p.sb("ident_f", [128, 128], F32)
    onesf = p.sb("onesf", [1, 128], F32)
    cvec = p.sb("cvec", [128, 8], F32)
    p.epsap = cvec[:, 0:1]
    p.cv = lambda i: cvec[:, i:i + 1]
    s_io = p.get_sem("io")
    sch = Sched(p)
    sch.emit("sync", lambda e: e.dma_start(out=ident_bf, in_=ident_in), incs=[(s_io, 16)])
    sch.emit("sync", lambda e: e.dma_start(out=ident_f, in_=identf_in), incs=[(s_io, 16)])
    sch.emit("sync", lambda e: e.dma_start(out=cvec, in_=cvec_in), incs=[(s_io, 16)])
    sch.emit("vector", lambda e: e.memset(onesf, 1.0), incs=[(s_io, 1)])
    for eng in ("scalar", "vector", "tensor", "gpsimd"):
        sch.emit(eng, lambda e: e.nop(), waits=[w(s_io)])
    sch.flush("init")
    consts = {"ident_bf": ident_bf, "ident_f": ident_f, "onesf": onesf, "cvec": cvec}
    emit_l0_attn(p, x_in, None, wmap, consts, mode="a")
    return p


def _get_progs():
    if "v2" not in _PROGS:
        _PROGS["v2"] = {
            "l0a": _build_l0a_program(),
            "rest": build_program([
                (emit_l0_attn_b, L0_ATTN_B_DECL),
                (emit_l0_ffn, L0_FFN_DECL),
                (emit_l1_attn, L1_ATTN_DECL),
                (emit_l1_ffn, L1_FFN_DECL),
            ]),
        }
    return _PROGS["v2"]


def kernel(**inputs):
    progs = _get_progs()
    base = base_inputs()
    x = np.ascontiguousarray(inputs["L0_attn_x"], dtype=np.float32)
    n = 8
    l0w = [l0_attn_weights(inputs, core_id=c) for c in range(n)]
    l0fw = l0_ffn_weights(inputs)
    l1aw = l1_attn_weights(inputs)
    l1fw = l1_ffn_weights(inputs)

    in_maps = []
    for c in range(n):
        m = dict(base)
        m["x_in"] = np.ascontiguousarray(x[c])
        for k, v in l0w[c].items():
            m["b0_" + k] = v
        in_maps.append(m)
    resA = run_bass_kernel_spmd(progs["l0a"].nc, in_maps, core_ids=list(range(n)))
    gath = np.ascontiguousarray(
        np.concatenate([resA.results[c]["cnt_out"].reshape(1, 8) for c in range(n)], axis=0))

    in_maps = []
    for c in range(n):
        m = dict(base)
        m["x_in"] = np.ascontiguousarray(x[c])
        for k, v in l0w[c].items():
            m["b0_" + k] = v
        m["b0_xmid_in"] = resA.results[c]["xmid_out"]
        m["b0_gathin"] = gath
        for k, v in l0fw.items():
            m["b1_" + k] = v
        for k, v in l1aw.items():
            m["b2_" + k] = v
        for k, v in l1fw.items():
            m["b3_" + k] = v
        in_maps.append(m)
    res = run_bass_kernel_spmd(progs["rest"].nc, in_maps, core_ids=list(range(n)))
    return np.stack([res.results[c]["x_out"] for c in range(n)]).astype(np.float32)
